# version 01
# baseline (speedup 1.0000x reference)
import contextlib
import numpy as np
import ml_dtypes
import concourse.bass as bass
import concourse.mybir as mybir
from concourse.bass_utils import run_bass_kernel_spmd

F32 = mybir.dt.float32
BF16 = mybir.dt.bfloat16
I32 = mybir.dt.int32
AF = mybir.ActivationFunctionType
ALU = mybir.AluOpType
AXX = mybir.AxisListType.X

B, N, D = 4, 4096, 1024
HALF = 2048
HEADS, DH = 16, 64
WIN = 256
FFN_DEPTH = 6
DIM_INNER = 2730
W1C = 2 * DIM_INNER
SLAB = 258
DOMF = HALF + SLAB
N_EXP = 8
MOE_K = 1024
L1_K = 1228
NEG_BIG = -30000.0
NT = N // 128
NTO = HALF // 128

DEBUG_DUMPS = ()
STOP_AFTER = None


def _sigma_perm():
    p = np.zeros(D, dtype=np.int64)
    for h in range(HEADS):
        for d in range(DH):
            p[h * DH + d] = d * HEADS + h
    return p


def _rotary_tables():
    half = np.arange(0, DH, 2, dtype=np.float32)
    inv_freq = (1.0 / 10000.0 ** (half / DH)).astype(np.float32)
    t = np.arange(N, dtype=np.float32)
    freqs = t[:, None] * inv_freq[None, :]
    freqs = np.concatenate([freqs, freqs], -1)
    base_scale = ((half + 0.4 * DH) / (1.4 * DH)).astype(np.float32)
    power = (t - N // 2) / (WIN // 2)
    scale = base_scale[None, :] ** power[:, None]
    scale = np.concatenate([scale, scale], -1).astype(np.float32)
    cos = np.cos(freqs).astype(np.float32)
    sin = np.sin(freqs).astype(np.float32)
    inv = (1.0 / scale).astype(np.float32)
    A_q = (cos * scale).T.copy()
    A_k = (cos * inv).T.copy()
    B_q = (sin * scale).T.copy(); B_q[0:32, :] *= -1.0
    B_k = (sin * inv).T.copy(); B_k[0:32, :] *= -1.0
    return A_q, B_q, A_k, B_k


def _local_masks():
    k = np.arange(2 * WIN)[:, None]
    q = np.arange(WIN)[None, :]
    m = ((k >= q) & (k <= q + WIN)).astype(np.float32)
    m0 = m.copy()
    m0[0:WIN, :] = 0.0
    return m.astype(ml_dtypes.bfloat16), m0.astype(ml_dtypes.bfloat16)


class PB:
    def __init__(self):
        self.nc = bass.Bass("TRN2", target_bir_lowering=False)
        self.ctx = contextlib.ExitStack()
        self.T = {}
        self.S = {}
        self.P = {}
        self.sems = {}
        self.cnt = {}
        self.seen = {}
        self.block = None
        self.dbg = set()
        self.stacks = {}

    def din(self, name, shape, dtype=F32):
        self.T[name] = self.nc.dram_tensor(name, list(shape), dtype, kind="ExternalInput")
        return self.T[name]

    def dout(self, name, shape, dtype=F32):
        self.T[name] = self.nc.dram_tensor(name, list(shape), dtype, kind="ExternalOutput")
        return self.T[name]

    def dscr(self, name, shape, dtype=F32):
        self.T[name] = self.nc.dram_tensor(name, list(shape), dtype)
        return self.T[name]

    _serial = 0

    def _uniq(self, name):
        PB._serial += 1
        return "%s__%d" % (name, PB._serial)

    def sbuf(self, name, shape, dtype, ph=None):
        stk = self.stacks[ph] if ph else self.ctx
        self.S[name] = stk.enter_context(self.nc.sbuf_tensor(self._uniq(name), list(shape), dtype))
        return self.S[name]

    def psum(self, name, shape, dtype=F32, ph=None):
        stk = self.stacks[ph] if ph else self.ctx
        self.P[name] = stk.enter_context(self.nc.psum_tensor(self._uniq(name), list(shape), dtype))
        return self.P[name]

    def open_phase(self, ph):
        self.stacks[ph] = contextlib.ExitStack()

    def close_phase(self, ph):
        self.stacks.pop(ph).close()

    def sem(self, name):
        if name not in self.sems:
            self.sems[name] = self.ctx.enter_context(self.nc.semaphore(name))
            self.cnt[name] = 0
        return self.sems[name]

    def tick(self, name, k=1):
        self.cnt[name] += k
        return self.cnt[name]

    def wait(self, ekey, eng, name, value=None):
        v = self.cnt[name] if value is None else value
        if v <= 0:
            return
        if self.seen.setdefault(ekey, {}).get(name, 0) >= v:
            return
        eng.wait_ge(self.sems[name], v)
        self.seen[ekey][name] = v

    def stream(self, name, nbuf=2):
        return _Stream(self, name, nbuf)

    def barrier(self, tag, drain=()):
        s = self.sem("barx")
        block = self.block
        def fy(e):
            for dn in drain:
                self.wait("y", e, dn, self.cnt[dn])
            e.nop().then_inc(s, 1)
        block.sync(fy)
        for attr in ("tensor", "vector", "scalar", "gpsimd"):
            def f(e, s=s):
                e.nop().then_inc(s, 1)
            getattr(block, attr)(f)
        self.tick("barx", 5)
        tgt = self.cnt["barx"]
        for ekey, attr in (("t", "tensor"), ("v", "vector"), ("s", "scalar"),
                           ("g", "gpsimd"), ("y", "sync")):
            def f(e, ekey=ekey, tgt=tgt):
                self.wait(ekey, e, "barx", tgt)
            getattr(block, attr)(f)


class _Stream:
    def __init__(self, pb, name, nbuf):
        self.pb = pb
        self.name = name
        self.nbuf = nbuf
        self.sems = [pb.sem(f"{name}{i}") for i in range(nbuf)]
        self.base = [pb.cnt[f"{name}{i}"] for i in range(nbuf)]
        self.nissued = 0

    def issue(self, eng, out, in_, **kw):
        i = self.nissued
        eng.dma_start(out=out, in_=in_, **kw).then_inc(self.sems[i % self.nbuf], 16)
        self.pb.tick(f"{self.name}{i % self.nbuf}", 16)
        self.nissued += 1
        return i

    def wait(self, ekey, eng, i):
        self.pb.wait(ekey, eng, f"{self.name}{i % self.nbuf}",
                     self.base[i % self.nbuf] + 16 * (i // self.nbuf + 1))


class Eng:

    def __init__(self, pb, ekey, raw, chained=True):
        self.pb = pb
        self.ekey = ekey
        self.raw = raw
        self.chained = chained
        self.chname = ekey + "ch"
        if chained:
            pb.sem(self.chname)

    def _pre(self):
        if self.chained:
            self.pb.wait(self.ekey, self.raw, self.chname)

    def __getattr__(self, name):
        fn = getattr(self.raw, name)
        if name in ("wait_ge", "nop", "dma_start"):
            return fn
        def wrapper(*a, **kw):
            self._pre()
            inst = fn(*a, **kw)
            if self.chained:
                inst.then_inc(self.pb.sems[self.chname], 1)
                self.pb.tick(self.chname)
            return inst
        return wrapper

    def wait(self, sem, v=None):
        self.pb.wait(self.ekey, self.raw, sem, v)

    def sig(self, sem, k=1):
        self._pre()
        self.raw.nop().then_inc(self.pb.sem(sem), k)
        self.pb.tick(sem, k)

    def dma(self, out, in_, sem, **kw):
        self._pre()
        self.raw.dma_start(out=out, in_=in_, **kw).then_inc(self.pb.sem(sem), 16)
        self.pb.tick(sem, 16)


def p0(pb):
    nc, block = pb.nc, pb.block
    S, T, P = pb.S, pb.T, pb.P

    pb.sbuf("p0_x", (128, 2 * D), F32, ph="p0")
    pb.sbuf("p0_xf", (128, 2 * D), F32, ph="p0")
    pb.sbuf("p0_xsq", (128, D), F32, ph="p0")
    pb.sbuf("p0_st", (128, 8), F32, ph="p0")
    pb.sbuf("p0_stage", (128, 1040), F32, ph="p0")
    pb.psum("ps_trf", (128, 512), F32, ph="p0")
    pb.psum("ps_trf2", (128, 512), F32, ph="p0")

    for s in ("wld", "xld", "cst", "mm", "ve", "se", "gd", "st", "cc",
              "p0c", "p0v", "p0s", "p0n", "p0e"):
        pb.sem(s)

    xs = pb.stream("xs", 2)

    @block.gpsimd
    def _(gr):
        g = Eng(pb, "g", gr)
        for dst, srcn in (("sb_identf", "identb"),
                          ("sb_onesb", "onesb"), ("sb_gates", "gates"),
                          ("sb_onesf", "onesb"), ("sb_bmapPf", "bmapP")):
            g.dma(S[dst][:, :], T[srcn][:, :], "cst")

    @block.sync
    def _(y_raw):
        y = Eng(pb, "y", y_raw, chained=False)
        for i in range(NT):
            if i >= 2:
                pb.wait("y", y_raw, "p0c", i - 1)
            xs.issue(y_raw, S["p0_x"][:, (i % 2) * D:(i % 2) * D + D],
                     T["x"][i * 128:(i + 1) * 128, :])

    @block.scalar
    def _(sr):
        s = Eng(pb, "s", sr)
        def ship(j):
            s.wait("ve", j + 1)
            s.dma(bass.AP(T["xinnerT"], SLAB + j * 128,
                          [[DOMF, 128], [128 * DOMF, 8], [1, 128]]),
                  S["p0_stage"][:, 0:1024].rearrange("p (f t) -> p f t", f=8),
                  "st")
        for i in range(NT):
            s.wait("p0v", i + 1)
            s.activation(S["p0_st"][:, 6:7], S["p0_st"][:, 2:3], AF.Sqrt)
            s.sig("p0s")
            if i >= 1 and i - 1 < NTO:
                ship(i - 1)

    @block.vector
    def _(vr):
        v = Eng(pb, "v", vr)
        for i in range(NT + 1):
            if i < NT:
                v.wait("cst", 80)
                xs.wait("v", vr, i)
                if i >= 2:
                    v.wait("mm", 8 * (i - 1))
                xt = S["p0_x"][:, (i % 2) * D:(i % 2) * D + D]
                st = S["p0_st"]
                v.reduce_sum(st[:, 0:1], xt, axis=AXX)
                v.tensor_scalar_mul(st[:, 0:1], st[:, 0:1], 1.0 / D)
                v.tensor_tensor(st[:, 4:5], st[:, 0:1], st[:, 0:1], op=ALU.mult)
                v.tensor_tensor(S["p0_xsq"][:, :], xt, xt, op=ALU.mult)
                v.reduce_sum(st[:, 1:2], S["p0_xsq"][:, :], axis=AXX)
                v.tensor_scalar_mul(st[:, 1:2], st[:, 1:2], 1.0 / D)
                v.tensor_tensor(st[:, 1:2], st[:, 1:2], st[:, 4:5], op=ALU.subtract)
                v.tensor_scalar_add(st[:, 2:3], st[:, 1:2], 1e-5)
                v.sig("p0v")
                v.wait("p0s", i + 1)
                v.reciprocal(st[:, 3:4], st[:, 6:7])
                v.tensor_scalar(S["p0_xf"][:, (i % 2) * D:(i % 2) * D + D], xt,
                                st[:, 0:1], st[:, 3:4], op0=ALU.subtract, op1=ALU.mult)
                v.sig("p0n")
                v.sig("p0c")
            if i >= 1:
                j = i - 1
                v.wait("mm", 8 * i)
                if 1 <= j < NTO:
                    v.wait("st", 16 * j)
                for fg in range(2):
                    ps = P["ps_trf"] if fg == 0 else P["ps_trf2"]
                    dst = S["xnT"] if j < NTO else S["xnTo"]
                    jj = j if j < NTO else j - NTO
                    for f in range(4):
                        ft = fg * 4 + f
                        v.tensor_copy(dst[:, ft * HALF + jj * 128: ft * HALF + (jj + 1) * 128],
                                      ps[:, f * 128:(f + 1) * 128])
                        if j < NTO:
                            v.tensor_copy(S["p0_stage"][:, ft * 128:(ft + 1) * 128],
                                          ps[:, f * 128:(f + 1) * 128])
                    v.sig("p0e")
                if j < NTO:
                    v.sig("ve")

    @block.tensor
    def _(tr):
        t = Eng(pb, "t", tr, chained=False)
        t.wait("cst", 80)
        for i in range(NT):
            t.wait("p0n", i + 1)
            if i >= 1:
                t.wait("p0e", 2 * i)
            xf = S["p0_xf"][:, (i % 2) * D:(i % 2) * D + D]
            for fg in range(2):
                ps = P["ps_trf"] if fg == 0 else P["ps_trf2"]
                for f in range(4):
                    ft = fg * 4 + f
                    inst = tr.matmul(ps[:, f * 128:(f + 1) * 128],
                                     xf[:, ft * 128:(ft + 1) * 128],
                                     S["sb_identf"][:, :], start=True, stop=True,
                                     is_transpose=True)
                    inst.then_inc(pb.sems["mm"], 1)
                    pb.tick("mm")

    for ci, (coff, clen) in enumerate(((0, 129), (129, 129))):
        def slab_v(vr, coff=coff, clen=clen):
            v = Eng(pb, "v", vr)
            v.wait("mm", pb.cnt["mm"])
            v.wait("cst", 80)
            v.wait("st", pb.cnt["st"])
            lo = HALF - SLAB + coff
            for f in range(8):
                v.tensor_scalar_mul(S["p0_stage"][:, f * clen:(f + 1) * clen],
                                    S["xnTo"][:, f * HALF + lo: f * HALF + lo + clen],
                                    S["sb_gates"][:, 0:1])
            v.sig("ve")
        block.vector(slab_v)

        def slab_s(sr, coff=coff, clen=clen):
            s = Eng(pb, "s", sr)
            s.wait("ve", pb.cnt["ve"])
            s.dma(bass.AP(T["xinnerT"], coff,
                          [[DOMF, 128], [128 * DOMF, 8], [1, clen]]),
                  S["p0_stage"][:, 0:8 * clen].rearrange("p (f t) -> p f t", f=8),
                  "st")
        block.scalar(slab_s)

    pb.barrier("p0", drain=("st",))

    if "xnT" in pb.dbg:
        dd = pb.dout("d_xnT", (128, 8 * HALF), BF16)
        ddo = pb.dout("d_xnTo", (128, 8 * HALF), BF16)
        def dumpf(gr, dd=dd, ddo=ddo):
            g = Eng(pb, "g", gr)
            g.dma(dd[:, :], S["xnT"][:, :], "gd")
            g.dma(ddo[:, :], S["xnTo"][:, :], "gd")
        block.gpsimd(dumpf)
    if "xinnerT" in pb.dbg:
        dd2 = pb.dout("d_xinnerT", (D, DOMF), F32)
        def dumpf2(gr, dd2=dd2):
            g = Eng(pb, "g", gr)
            g.wait("st", pb.cnt["st"])
            g.dma(dd2[:, :], T["xinnerT"][:, :], "gd")
        block.gpsimd(dumpf2)


def p1(pb):
    nc, block = pb.nc, pb.block
    S, T, P = pb.S, pb.T, pb.P
    CH = 512
    NCH = HALF // CH

    pb.open_phase("p1a")
    pb.sbuf("wkv", (128, 2, 8, D), BF16, ph="p1a")
    pb.sbuf("kt_sb", (128, 2 * D), BF16, ph="p1a")
    pb.sbuf("v_aug", (128, 2 * 1040), BF16, ph="p1a")
    pb.psum("pka", (128, 512), F32, ph="p1a")
    pb.psum("pkb", (128, 512), F32, ph="p1a")
    pb.psum("pva", (128, 512), F32, ph="p1a")
    pb.psum("pvb", (128, 512), F32, ph="p1a")
    for g4 in range(4):
        pb.psum(f"pkv{g4}", (128, 260), F32, ph="p1a")
    for s in ("p1w", "p1k", "p1e", "p1f", "sa", "dd"):
        pb.sem(s)

    @block.gpsimd
    def _(gr):
        g = Eng(pb, "g", gr)
        g.dma(S["wkv"][:, 0, :, :], T["Wk"][:, :].rearrange("(kt p) m -> p kt m", p=128), "p1w")
        g.dma(S["wkv"][:, 1, :, :], T["Wv"][:, :].rearrange("(kt p) m -> p kt m", p=128), "p1w")
        g.dma(S["sb_bkbv"][0:1, :], T["bvec"][2:4, :], "p1w")

    mm0 = [None]

    @block.tensor
    def _(tr):
        t = Eng(pb, "t", tr, chained=False)
        t.wait("p1w", 48)
        t.wait("barx", 5)
        mm0[0] = pb.cnt["mm"]
        for i in range(NT):
            if i >= 1:
                t.wait("p1k", i)
                t.wait("p1e", i)
            for half_ in range(2):
                ps = P["pka"] if half_ == 0 else P["pkb"]
                xsrc = S["xnT"] if i < NTO else S["xnTo"]
                ii = i if i < NTO else i - NTO
                for kt in range(8):
                    tr.matmul(ps[:, :], xsrc[:, kt * HALF + ii * 128:kt * HALF + (ii + 1) * 128],
                              S["wkv"][:, 0, kt, half_ * 512:half_ * 512 + 512],
                              start=(kt == 0), stop=False, skip_group_check=True)
                inst = tr.matmul(ps[:, :], S["sb_onesb"][0:1, 0:128],
                                 S["sb_bkbv"][0:1, half_ * 512:half_ * 512 + 512],
                                 start=False, stop=True, skip_group_check=True)
                inst.then_inc(pb.sems["mm"], 1); pb.tick("mm")
            for half_ in range(2):
                ps = P["pva"] if half_ == 0 else P["pvb"]
                for kt in range(8):
                    tr.matmul(ps[:, :], xsrc[:, kt * HALF + ii * 128:kt * HALF + (ii + 1) * 128],
                              S["wkv"][:, 1, kt, half_ * 512:half_ * 512 + 512],
                              start=(kt == 0), stop=False, skip_group_check=True)
                inst = tr.matmul(ps[:, :], S["sb_onesb"][0:1, 0:128],
                                 S["sb_bkbv"][0:1, D + half_ * 512:D + half_ * 512 + 512],
                                 start=False, stop=True, skip_group_check=True)
                inst.then_inc(pb.sems["mm"], 1); pb.tick("mm")
            t.wait("p1k", i + 1)
            t.wait("p1e", i + 1)
            ko = (i % 2) * D
            vo = (i % 2) * 1040
            for h in range(16):
                ps = P[f"pkv{h // 4}"]
                inst = tr.matmul(ps[0:64, 65 * (h % 4):65 * (h % 4) + 65],
                                 S["kt_sb"][:, ko + 64 * h:ko + 64 * h + 64],
                                 S["v_aug"][:, vo + 65 * h:vo + 65 * h + 65],
                                 start=(i == 0 and h % 4 == 0),
                                 stop=(i == NT - 1 and h % 4 == 3),
                                 skip_group_check=True)
            inst.then_inc(pb.sems["p1f"], 1); pb.tick("p1f")

    @block.scalar
    def _(sr):
        s = Eng(pb, "s", sr)
        for i in range(NT):
            ko = (i % 2) * D
            if i >= 2:
                s.wait("p1f", i - 1)
            s.wait("mm", mm0[0] + 4 * i + 2)
            s.activation(S["kt_sb"][:, ko:ko + 512], P["pka"][:, :], AF.Sigmoid)
            s.activation(S["kt_sb"][:, ko + 512:ko + 1024], P["pkb"][:, :], AF.Sigmoid)
            s.sig("p1k")

    @block.vector
    def _(vr):
        v = Eng(pb, "v", vr)
        v.memset(S["v_aug"][:, :], 1.0)
        for i in range(NT):
            vo = (i % 2) * 1040
            if i >= 2:
                v.wait("p1f", i - 1)
            v.wait("mm", mm0[0] + 4 * i + 4)
            v.tensor_copy(S["v_aug"][:, vo:vo + 520].rearrange("p (h m) -> p h m", m=65)[:, :, 0:64],
                          P["pva"][:, :].rearrange("p (h m) -> p h m", m=64))
            v.tensor_copy(S["v_aug"][:, vo + 520:vo + 1040].rearrange("p (h m) -> p h m", m=65)[:, :, 0:64],
                          P["pvb"][:, :].rearrange("p (h m) -> p h m", m=64))
            v.sig("p1e")

    @block.vector
    def _(vr):
        v = Eng(pb, "v", vr)
        v.wait("p1f", NT)
        v.tensor_copy(S["kv_sb"][0:64, 0:260], P["pkv0"][0:64, :])
        v.tensor_copy(S["kv_sb"][0:64, 260:520], P["pkv1"][0:64, :])
        v.tensor_copy(S["kv_sb2"][0:64, 0:260], P["pkv2"][0:64, :])
        v.tensor_copy(S["kv_sb2"][0:64, 260:520], P["pkv3"][0:64, :])
        v.tensor_copy(S["kv_sb"][64:128, :], S["kv_sb"][0:64, :])
        v.tensor_copy(S["kv_sb2"][64:128, :], S["kv_sb2"][0:64, :])
        v.sig("ve")

    pb.barrier("p1a")
    pb.close_phase("p1a")
    pb.close_phase("xoth")

    pb.open_phase("p1b")
    pb.sbuf("whq", (128, 3, 8, D), BF16, ph="p1b")
    pb.sbuf("bcols", (128, 64), F32, ph="p1b")
    pb.sbuf("hbcol", (128, 8), F32, ph="p1b")
    pb.sbuf("qc", (128, 8, CH), BF16, ph="p1b")
    pb.sbuf("hc", (128, 8, CH), BF16, ph="p1b")
    pb.sbuf("attnc", (128, 8, CH), BF16, ph="p1b")
    pb.sbuf("attn2c", (128, 8, CH), BF16, ph="p1b")
    pb.sbuf("gatec", (128, 8, CH), BF16, ph="p1b")
    pb.sbuf("den_pack", (128, 4096), F32, ph="p1b")
    pb.sbuf("den_sb", (16, CH), F32, ph="p1b")
    pb.sbuf("recipb", (16, CH), F32, ph="p1b")
    pb.sbuf("lnrow", (128, 4 * CH), F32, ph="p1b")
    pb.sbuf("mursb", (128, 2 * CH), F32, ph="p1b")
    pb.sbuf("tmpf", (128, CH), F32, ph="p1b")
    pb.sbuf("astage", (128, 8, CH), F32, ph="p1b")
    NPP = 6
    for k in range(NPP):
        pb.psum(f"pp{k}", (128, 512), F32, ph="p1b")
    for k in range(NPP):
        pb.sem(f"f{k}")
    psched = [0] * NPP

    def t_use(tr, k):
        pb.wait("t", tr, f"f{k}", psched[k])
        psched[k] += 1

    def done(eng_wrap, k):
        eng_wrap.sig(f"f{k}")

    @block.gpsimd
    def _(gr):
        g = Eng(pb, "g", gr)
        g.wait("barx", 10)
        g.dma(S["whq"][:, 0, :, :], T["Wh"][:, :].rearrange("(kt p) m -> p kt m", p=128), "p1w")
        g.dma(S["whq"][:, 1, :, :], T["Wq"][:, :].rearrange("(kt p) m -> p kt m", p=128), "p1w")
        g.dma(S["whq"][:, 2, :, :], T["Wo0"][:, :].rearrange("(kt p) m -> p kt m", p=128), "p1w")

    @block.sync
    def _(y):
        pb.wait("y", y, "barx", 10)
        y.dma_start(out=bass.AP(S["bcols"], 0, [[64, 128], [8, 8], [1, 8]]),
                    in_=bass.AP(T["bvecT"], 0, [[8, 128], [1024, 8], [1, 8]])
                    ).then_inc(pb.sem("sa"), 16)
        pb.tick("sa", 16)
    sa_bcols = pb.cnt["sa"]

    @block.vector
    def _(vr):
        v = Eng(pb, "v", vr)
        v.wait("sa", 16)
        bc = S["bcols"][:, :].rearrange("p (mt j) -> p mt j", j=8)
        v.tensor_tensor(S["hbcol"][:, :], bc[:, :, 0], bc[:, :, 4], op=ALU.add)
        v.sig("ve")
    ve_hb = pb.cnt["ve"]

    for c in range(NCH):
        c0 = c * CH

        qmm = []
        hmm = []
        @block.tensor
        def _(tr, qmm=qmm, hmm=hmm):
            pb.wait("t", tr, "p1w", pb.cnt["p1w"])
            for which, wsl, rec in (("q", 1, qmm), ("h", 0, hmm)):
                for mt in range(8):
                    k = mt % 2
                    t_use(tr, k)
                    ps = P[f"pp{k}"]
                    for kt in range(8):
                        inst = tr.matmul(ps[:, :], S["whq"][:, wsl, kt, mt * 128:(mt + 1) * 128],
                                         S["xnT"][:, kt * HALF + c0:kt * HALF + c0 + CH],
                                         start=(kt == 0), stop=(kt == 7), skip_group_check=True)
                    inst.then_inc(pb.sems["mm"], 1); pb.tick("mm")
                    rec.append((pb.cnt["mm"], k))

        @block.scalar
        def _(sr, qmm=qmm, hmm=hmm):
            s = Eng(pb, "s", sr)
            s.wait("sa", sa_bcols)
            for mt in range(8):
                mmv, k = qmm[mt]
                s.wait("mm", mmv)
                s.activation(S["qc"][:, mt, :], P[f"pp{k}"][:, :], AF.Sigmoid,
                             bias=S["bcols"][:, mt * 8 + 1:mt * 8 + 2])
                done(s, k)
            s.wait("ve", ve_hb)
            for mt in range(8):
                mmv, k = hmm[mt]
                s.wait("mm", mmv)
                s.activation(S["hc"][:, mt, :], P[f"pp{k}"][:, :], AF.Identity,
                             bias=S["hbcol"][:, mt:mt + 1])
                done(s, k)
            s.sig("p1k")

        ndmm = []
        @block.tensor
        def _(tr, ndmm=ndmm):
            t = Eng(pb, "t", tr, chained=False)
            t.wait("p1k", pb.cnt["p1k"])
            for h in range(16):
                k = 2 + h % 4
                t_use(tr, k)
                kvs = S["kv_sb"] if h < 8 else S["kv_sb2"]
                b = 64 * (h % 2)
                inst = tr.matmul(P[f"pp{k}"][0:65, :], kvs[b:b + 64, 65 * (h % 8):65 * (h % 8) + 65],
                                 S["qc"][b:b + 64, h // 2, :],
                                 start=True, stop=True, skip_group_check=True)
                inst.then_inc(pb.sems["mm"], 1); pb.tick("mm")
                ndmm.append((pb.cnt["mm"], k))

        @block.vector
        def _(vr, ndmm=ndmm):
            v = Eng(pb, "v", vr)
            for h in range(16):
                mmv, k = ndmm[h]
                v.wait("mm", mmv)
                ps = P[f"pp{k}"]
                v.tensor_copy(S["den_pack"][64 * (h % 2):64 * (h % 2) + 1,
                                            (h // 2) * CH:(h // 2) * CH + CH],
                              ps[64:65, :])
                v.tensor_copy(S["attnc"][64 * (h % 2):64 * (h % 2) + 64, h // 2, :],
                              ps[0:64, :])
                done(v, k)
            v.sig("ve")

        @block.sync
        def _(y):
            pb.wait("y", y, "ve", pb.cnt["ve"])
            for bq in range(2):
                y.dma_start(out=S["den_sb"][8 * bq:8 * bq + 8, :],
                            in_=S["den_pack"][64 * bq:64 * bq + 1, :]
                            ).then_inc(pb.sem("sa"), 16)
                pb.tick("sa", 16)

        @block.vector
        def _(vr):
            v = Eng(pb, "v", vr)
            v.wait("sa", pb.cnt["sa"])
            v.reciprocal(S["recipb"][:, :], S["den_sb"][:, :])
            v.sig("ve")

        bcmm = []
        @block.tensor
        def _(tr, bcmm=bcmm):
            pb.wait("t", tr, "ve", pb.cnt["ve"])
            for mt in range(8):
                k = mt % 2
                t_use(tr, k)
                inst = tr.matmul(P[f"pp{k}"][:, :], S["sb_bmapPf"][:, mt * 128:(mt + 1) * 128],
                                 S["recipb"][:, :], start=True, stop=True,
                                 skip_group_check=True)
                inst.then_inc(pb.sems["mm"], 1); pb.tick("mm")
                bcmm.append((pb.cnt["mm"], k))

        @block.vector
        def _(vr, bcmm=bcmm):
            v = Eng(pb, "v", vr)
            for mt in range(8):
                mmv, k = bcmm[mt]
                v.wait("mm", mmv)
                v.tensor_tensor(S["attnc"][:, mt, :], S["attnc"][:, mt, :],
                                P[f"pp{k}"][:, :], op=ALU.mult)
                done(v, k)
            v.sig("ve")

        @block.scalar
        def _(sr):
            s = Eng(pb, "s", sr)
            s.wait("ve", pb.cnt["ve"])
            s.activation(S["attn2c"][:, :, :], S["attnc"][:, :, :], AF.Square)
            s.sig("se")

        stmm = []
        @block.tensor
        def _(tr, stmm=stmm):
            pb.wait("t", tr, "se", pb.cnt["se"])
            for which, srcn in ((2, "attnc"), (3, "attn2c")):
                t_use(tr, which)
                for kt in range(8):
                    inst = tr.matmul(P[f"pp{which}"][0:1, :], S["sb_onesb"][:, 0:1],
                                     S[srcn][:, kt, :], start=(kt == 0), stop=(kt == 7),
                                     skip_group_check=True)
                inst.then_inc(pb.sems["mm"], 1); pb.tick("mm")
                stmm.append(pb.cnt["mm"])

        @block.vector
        def _(vr, stmm=stmm):
            v = Eng(pb, "v", vr)
            v.wait("mm", stmm[-1])
            L = S["lnrow"]
            v.tensor_scalar_mul(L[0:1, 0:CH], P["pp2"][0:1, :], 1.0 / D)
            v.tensor_scalar_mul(L[0:1, CH:2 * CH], P["pp3"][0:1, :], 1.0 / D)
            done(v, 2); done(v, 3)
            v.tensor_tensor(L[0:1, 2 * CH:3 * CH], L[0:1, 0:CH], L[0:1, 0:CH], op=ALU.mult)
            v.tensor_tensor(L[0:1, CH:2 * CH], L[0:1, CH:2 * CH], L[0:1, 2 * CH:3 * CH], op=ALU.subtract)
            v.tensor_scalar_add(L[0:1, CH:2 * CH], L[0:1, CH:2 * CH], 1e-5)
            v.sig("ve")

        @block.scalar
        def _(sr):
            s = Eng(pb, "s", sr)
            s.wait("ve", pb.cnt["ve"])
            s.activation(S["lnrow"][0:1, 3 * CH:4 * CH], S["lnrow"][0:1, CH:2 * CH], AF.Sqrt)
            s.sig("se")

        @block.vector
        def _(vr):
            v = Eng(pb, "v", vr)
            v.wait("se", pb.cnt["se"])
            L = S["lnrow"]
            v.reciprocal(L[0:1, 2 * CH:3 * CH], L[0:1, 3 * CH:4 * CH])
            v.tensor_tensor(L[0:1, 0:CH], L[0:1, 0:CH], L[0:1, 2 * CH:3 * CH], op=ALU.mult)
            v.sig("ve")

        mursmm = []
        @block.tensor
        def _(tr, mursmm=mursmm):
            pb.wait("t", tr, "ve", pb.cnt["ve"])
            for which, off in ((4, 0), (5, 2 * CH)):
                t_use(tr, which)
                inst = tr.matmul(P[f"pp{which}"][:, :], S["sb_onesf"][0:1, :],
                                 S["lnrow"][0:1, off:off + CH], start=True, stop=True,
                                 skip_group_check=True)
                inst.then_inc(pb.sems["mm"], 1); pb.tick("mm")
                mursmm.append(pb.cnt["mm"])

        @block.vector
        def _(vr, mursmm=mursmm):
            v = Eng(pb, "v", vr)
            v.wait("mm", mursmm[-1])
            v.wait("sa", sa_bcols)
            v.tensor_copy(S["mursb"][:, 0:CH], P["pp4"][:, :])
            v.tensor_copy(S["mursb"][:, CH:2 * CH], P["pp5"][:, :])
            done(v, 4); done(v, 5)
            for mt in range(8):
                v.tensor_tensor(S["tmpf"][:, :], S["attnc"][:, mt, :],
                                S["mursb"][:, CH:2 * CH], op=ALU.mult)
                v.tensor_tensor(S["tmpf"][:, :], S["tmpf"][:, :],
                                S["mursb"][:, 0:CH], op=ALU.subtract)
                v.tensor_scalar(S["tmpf"][:, :], S["tmpf"][:, :],
                                S["bcols"][:, mt * 8 + 5:mt * 8 + 6],
                                S["bcols"][:, mt * 8 + 6:mt * 8 + 7],
                                op0=ALU.mult, op1=ALU.add)
                v.tensor_tensor(S["gatec"][:, mt, :], S["tmpf"][:, :],
                                S["hc"][:, mt, :], op=ALU.mult)
            v.sig("ve")

        womm = []
        @block.tensor
        def _(tr, womm=womm):
            pb.wait("t", tr, "ve", pb.cnt["ve"])
            if c >= 1:
                pb.wait("t", tr, "sa", pb.cnt["sa"])
            for mt in range(8):
                k = mt % 2
                t_use(tr, k)
                for kt in range(8):
                    inst = tr.matmul(P[f"pp{k}"][:, :], S["whq"][:, 2, kt, mt * 128:(mt + 1) * 128],
                                     S["gatec"][:, kt, :], start=(kt == 0), stop=(kt == 7),
                                     skip_group_check=True)
                inst.then_inc(pb.sems["mm"], 1); pb.tick("mm")
                womm.append((pb.cnt["mm"], k))

        @block.scalar
        def _(sr, womm=womm):
            s = Eng(pb, "s", sr)
            for mt in range(8):
                mmv, k = womm[mt]
                s.wait("mm", mmv)
                s.activation(S["astage"][:, mt, :], P[f"pp{k}"][:, :], AF.Relu,
                             bias=S["bcols"][:, mt * 8 + 7:mt * 8 + 8])
                done(s, k)
            s.sig("se")

        @block.sync
        def _(y):
            pb.wait("y", y, "se", pb.cnt["se"])
            y.dma_start(out=bass.AP(T["A_T"], c0, [[HALF, 128], [128 * HALF, 8], [1, CH]]),
                        in_=S["astage"][:, :, :]).then_inc(pb.sem("sa"), 16)
            pb.tick("sa", 16)

    pb.barrier("p1", drain=("sa",))
    pb.close_phase("p1b")

    if "A_T" in pb.dbg:
        dd = pb.dout("d_A_T", (D, HALF), F32)
        def dumpf(gr, dd=dd):
            g = Eng(pb, "g", gr)
            g.dma(dd[:, :], T["A_T"][:, :], "gd")
        block.gpsimd(dumpf)


P2_LAYERS = FFN_DEPTH
HCH = 11


def p2_exchange(pb, st):
    nc, block = pb.nc, pb.block
    S, T = pb.S, pb.T
    groups = [[0, 1], [2, 3], [4, 5], [6, 7]]

    @block.gpsimd
    def _(gr):
        g = Eng(pb, "g", gr)
        g.wait("xw", pb.cnt["xw"])
        g.dma(T["slab_own"][:, :],
              bass.AP(T["xinnerT"], DOMF - SLAB, [[DOMF, D], [1, SLAB]]), "gd")
        g.wait("gd", pb.cnt["gd"])
        gr.collective_compute(
            "AllGather", ALU.bypass, replica_groups=groups,
            ins=[T["slab_own"][:, :].opt()], outs=[T["slab_all"][:, :].opt()],
        ).then_inc(pb.sem("cc"), 1)
        pb.tick("cc")
        g.wait("cc", pb.cnt["cc"])
        g.wait("shc", st["shc"])
        g.dma(S["shad"][:, :, 0:SLAB],
              bass.AP(T["slab_all"], 0, [[SLAB, 128], [128 * SLAB, 8], [1, SLAB]]), "gd")

    @block.vector
    def _(vr):
        v = Eng(pb, "v", vr)
        v.wait("gd", pb.cnt["gd"])
        v.wait("xw2", st["xw2"])
        v.tensor_scalar_mul(S["xch"][:, :, 0:SLAB], S["shad"][:, :, 0:SLAB],
                            S["sb_gates"][:, 0:1])
        v.sig("ve")
        v.sig("shc"); st["shc"] += 1

    @block.scalar
    def _(sr):
        s = Eng(pb, "s", sr)
        s.wait("ve", pb.cnt["ve"])
        s.dma(bass.AP(T["xinnerT"], 0, [[DOMF, 128], [128 * DOMF, 8], [1, SLAB]]),
              S["xch"][:, :, 0:SLAB], "xw")
        s.sig("xw2"); st["xw2"] += 1


def p2_attn(pb, l, st):
    nc, block = pb.nc, pb.block
    S, T, P = pb.S, pb.T, pb.P

    @block.gpsimd
    def _(gr):
        g = Eng(pb, "g", gr)
        g.dma(S["wqkv"][:, :, :], T["Wqkv"][l].rearrange("(kt p) m -> p kt m", p=128), "wld")
        g.dma(S["wof"][:, :, :], T["Wo_f"][l].rearrange("(kt p) m -> p kt m", p=128), "wld")
        g.dma(S["qkcols"][:, :], T["qsksc"][l], "wld")
        g.dma(S["rott"][:, :], T["rot"][:, :], "wld")
        g.dma(S["s_maskP"][:, :, :], T["maskP"][:, :, :], "wld")
        g.dma(S["s_maskF"][:, :, :], T["maskF"][:, :, :], "wld")
        g.dma(S["s_two2f"][:, :], T["two2f"][:, :], "wld")
        g.dma(S["s_two2fT"][:, :], T["two2fT"][:, :], "wld")
    wld_l = pb.cnt["wld"]

    @block.vector
    def _(vr):
        v = Eng(pb, "v", vr)
        v.wait("wld", wld_l)
        Lr = S["nrow"]
        v.tensor_tensor(Lr[0:64, 0:1], S["qkcols"][0:64, 0:1], S["qkcols"][0:64, 1:2], op=ALU.mult)
        v.tensor_scalar(Lr[0:64, 1:2], Lr[0:64, 0:1], -1.0, None, op0=ALU.mult)
        v.tensor_tensor(Lr[0:64, 0:1], Lr[0:64, 0:1], Lr[0:64, 1:2], op=ALU.max)
        v.tensor_copy(Lr[0:32, 2:3], Lr[32:64, 0:1])
        v.tensor_tensor(Lr[0:32, 0:1], Lr[0:32, 0:1], Lr[0:32, 2:3], op=ALU.max)
        v.sig("ve")

    @block.sync
    def _(y):
        pb.wait("y", y, "ve", pb.cnt["ve"])
        y.dma_start(out=S["nrow"][0:1, 32:64], in_=S["nrow"][0:32, 0:1]
                    ).then_inc(pb.sem("sa"), 16)
        pb.tick("sa", 16)

    @block.vector
    def _(vr):
        v = Eng(pb, "v", vr)
        v.wait("sa", pb.cnt["sa"])
        v.reduce_max(S["nrow"][0:1, 0:1], S["nrow"][0:1, 32:64], axis=AXX)
        v.tensor_scalar(S["nrow"][0:1, 1:2], S["nrow"][0:1, 0:1], -8.0, None, op0=ALU.mult)
        v.sig("ve")

    @block.tensor
    def _(tr):
        pb.wait("t", tr, "ve", pb.cnt["ve"])
        pb.wait("t", tr, "pne", st["pne"])
        inst = tr.matmul(P["pe5"][:, 0:1], S["sb_onesf"][0:1, :],
                         S["nrow"][0:1, 1:2], start=True, stop=True, skip_group_check=True)
        inst.then_inc(pb.sems["mm"], 1); pb.tick("mm")

    @block.vector
    def _(vr):
        v = Eng(pb, "v", vr)
        v.wait("mm", pb.cnt["mm"])
        v.tensor_copy(S["negc"][:, :], P["pe5"][:, 0:1])
        v.sig("pne"); st["pne"] += 1

    for seg in range(-1, 4):
        p2_attn_seg(pb, l, seg, wld_l, st)


def p2_attn_seg(pb, l, seg, wld_l, st):
    nc, block = pb.nc, pb.block
    S, T, P = pb.S, pb.T, pb.P
    if seg < 0:
        ntok, xlo = WIN, 1
        kslot = [2]
        tiles = list(range(8, 16))
    else:
        ntok, xlo = 2 * WIN, SLAB + seg * 512
        kslot = [(seg * 2) % 3, (seg * 2 + 1) % 3]
        tiles = list(range(16))
    nt_t = ntok // 128

    @block.sync
    def _(y):
        pb.wait("y", y, "shc", st["shc"])
        pb.wait("y", y, "xw2", st["xw2"])
        pb.wait("y", y, "xw", pb.cnt["xw"])
        y.dma_start(out=S["xch"][:, :, 0:ntok + 1],
                    in_=bass.AP(T["xinnerT"], xlo - 1,
                                [[DOMF, 128], [128 * DOMF, 8], [1, ntok + 1]])
                    ).then_inc(pb.sem("xr"), 16)
        pb.tick("xr", 16)

    @block.vector
    def _(vr):
        v = Eng(pb, "v", vr)
        v.wait("xr", pb.cnt["xr"])
        v.tensor_copy(S["shad"][:, :, 0:ntok + 1], S["xch"][:, :, 0:ntok + 1])
        v.sig("shr")
    shr_now = pb.cnt["shr"]

    n_tl = len(tiles)
    qmm = {}
    nrmm = {}
    bcmm = {}

    @block.tensor
    def _(tr):
        pb.wait("t", tr, "wld", wld_l)
        pb.wait("t", tr, "shr", shr_now)
        for i in range(n_tl + 2):
            if i >= 2:
                j = i - 2
                pb.wait("t", tr, "nr", st["nr_b"] + j + 1)
                pb.wait("t", tr, "fb%d" % (j % 2), st["fb"][j % 2]); st["fb"][j % 2] += 1
                pbc = P["pg7"] if j % 2 == 0 else P["ph8"]
                inst = tr.matmul(pbc[:, 0:ntok], S["s_two2f"][0:2, :],
                                 S["rn2"][0:2, (j % 2) * 512:(j % 2) * 512 + ntok],
                                 start=True, stop=True, skip_group_check=True)
                inst.then_inc(pb.sems["mm"], 1); pb.tick("mm")
                bcmm[j] = pb.cnt["mm"]
            if i >= 1 and i - 1 < n_tl:
                j = i - 1
                pb.wait("t", tr, "sq", st["sq_b"] + j + 1)
                pb.wait("t", tr, "fn%d" % (j % 2), st["fn"][j % 2]); st["fn"][j % 2] += 1
                pn = P["pe5"] if j % 2 == 0 else P["pf6"]
                inst = tr.matmul(pn[0:2, 0:ntok], S["s_two2fT"][:, 0:2],
                                 S["sqs"][:, (j % 2) * 512:(j % 2) * 512 + ntok],
                                 start=True, stop=True, skip_group_check=True)
                inst.then_inc(pb.sems["mm"], 1); pb.tick("mm")
                nrmm[j] = pb.cnt["mm"]
            if i < n_tl:
                mt = tiles[i]
                k = i % 2
                pb.wait("t", tr, "fq%d" % k, st["fq"][k]); st["fq"][k] += 1
                ps = P["pqa"] if k == 0 else P["pqb"]
                for kt in range(8):
                    off = 1 if kt < 4 else 0
                    inst = tr.matmul(ps[:, 0:ntok], S["wqkv"][:, kt, mt * 128:(mt + 1) * 128],
                                     S["shad"][:, kt, off:off + ntok],
                                     start=(kt == 0), stop=(kt == 7), skip_group_check=True)
                inst.then_inc(pb.sems["mm"], 1); pb.tick("mm")
                qmm[i] = pb.cnt["mm"]

    @block.scalar
    def _(sr):
        s = Eng(pb, "s", sr)
        for i in range(n_tl):
            s.wait("mm", qmm[i])
            if i >= 2:
                s.wait("mm", nrmm[i - 2])
            s.activation(S["sqs"][:, (i % 2) * 512:(i % 2) * 512 + ntok],
                         (P["pqa"] if i % 2 == 0 else P["pqb"])[:, 0:ntok], AF.Square)
            s.sig("sq")
            s.wait("mm", nrmm[i])
            if i >= 2:
                s.wait("nr", st["nr_b"] + i - 1)
            s.activation(S["rn1"][0:2, (i % 2) * 512:(i % 2) * 512 + ntok],
                         (P["pe5"] if i % 2 == 0 else P["pf6"])[0:2, 0:ntok], AF.Sqrt)
            s.sig("sn")
    st["sq_b"] += n_tl

    @block.vector
    def _(vr):
        v = Eng(pb, "v", vr)
        for i in range(n_tl):
            mt = tiles[i]
            is_q = mt < 8
            k = i % 2
            sl = k * 512
            v.wait("sn", st["sn_b"] + i + 1)
            v.tensor_scalar_max(S["rn1"][0:2, sl:sl + ntok], S["rn1"][0:2, sl:sl + ntok], 1e-12)
            v.reciprocal(S["rn2"][0:2, sl:sl + ntok], S["rn1"][0:2, sl:sl + ntok])
            v.sig("nr")
            v.sig("fn%d" % k)
            v.wait("mm", bcmm[i])
            v.tensor_copy(S["rnb"][:, 0:ntok], (P["pg7"] if k == 0 else P["ph8"])[:, 0:ntok])
            v.sig("fb%d" % k)
            col = 0 if is_q else 1
            ps = P["pqa"] if k == 0 else P["pqb"]
            v.scalar_tensor_tensor(S["t1f"][:, 0:ntok], ps[:, 0:ntok],
                                   S["qkcols"][:, col:col + 1], S["rnb"][:, 0:ntok],
                                   op0=ALU.mult, op1=ALU.mult)
            for hw in range(4):
                swp = (hw // 2) * 64 + (1 - hw % 2) * 32
                dwp = (hw // 2) * 64 + (hw % 2) * 32
                v.scalar_tensor_tensor(S["t3f"][dwp:dwp + 32, 0:ntok],
                                       ps[swp:swp + 32, 0:ntok],
                                       S["qkcols"][swp:swp + 32, col:col + 1],
                                       S["rnb"][swp:swp + 32, 0:ntok],
                                       op0=ALU.mult, op1=ALU.mult)
            v.sig("fq%d" % k)
            tb = (0 if is_q else 2 * DOMF) + xlo
            bb = (DOMF if is_q else 3 * DOMF) + xlo
            v.tensor_tensor(S["t2f"][:, 0:ntok], S["t1f"][:, 0:ntok],
                            S["rott"][:, tb:tb + ntok], op=ALU.mult)
            v.tensor_tensor(S["t3f"][:, 0:ntok], S["t3f"][:, 0:ntok],
                            S["rott"][:, bb:bb + ntok], op=ALU.mult)
            if is_q:
                v.tensor_tensor(S["qpb"][:, mt, 0:ntok], S["t2f"][:, 0:ntok],
                                S["t3f"][:, 0:ntok], op=ALU.add)
            else:
                mtk = mt - 8
                for w in range(len(kslot)):
                    v.tensor_tensor(S["kring"][:, kslot[w], mtk, :],
                                    S["t2f"][:, w * WIN:(w + 1) * WIN],
                                    S["t3f"][:, w * WIN:(w + 1) * WIN], op=ALU.add)
            v.sig("qkr")
    st["sn_b"] += n_tl
    st["nr_b"] += n_tl

    vmm = {}
    @block.tensor
    def _(tr):
        for tc in range(nt_t):
            for nh in range(2):
                k = nh
                pb.wait("t", tr, "fv%d" % k, st["fv"][k]); st["fv"][k] += 1
                ps = P["pvA"] if k == 0 else P["pvB"]
                for kt in range(8):
                    off = 1 if kt < 4 else 0
                    inst = tr.matmul(ps[:, :],
                                     S["shad"][:, kt, off + tc * 128:off + tc * 128 + 128],
                                     S["wqkv"][:, kt, 2048 + nh * 512:2048 + nh * 512 + 512],
                                     start=(kt == 0), stop=(kt == 7), skip_group_check=True)
                inst.then_inc(pb.sems["mm"], 1); pb.tick("mm")
                vmm[(tc, nh)] = pb.cnt["mm"]

    @block.vector
    def _(vr):
        v = Eng(pb, "v", vr)
        for tc in range(nt_t):
            ws = kslot[0] if seg < 0 else kslot[tc // 2]
            tt = tc if seg < 0 else tc % 2
            for nh in range(2):
                v.wait("mm", vmm[(tc, nh)])
                ps = P["pvA"] if nh == 0 else P["pvB"]
                v.tensor_copy(
                    S["vring"][:, ws, tt, nh * 520:nh * 520 + 520].rearrange(
                        "p (h m) -> p h m", m=65)[:, :, 0:64],
                    ps[:, :].rearrange("p (h m) -> p h m", m=64))
                v.sig("fv%d" % nh)
        v.sig("vkr")

    if seg < 0:
        @block.vector
        def _(vr):
            v = Eng(pb, "v", vr)
            v.wait("qkr", pb.cnt["qkr"])
            v.wait("vkr", pb.cnt["vkr"])
            v.sig("shc")
        st["shc"] += 1
        return

    qkr_now = pb.cnt["qkr"]
    vkr_now = pb.cnt["vkr"]
    for w in range(2):
        wslot = kslot[w]
        pslot = (wslot - 1) % 3
        first = (seg == 0 and w == 0)
        qoff = w * WIN
        scmm = {}
        vomm = {}
        PSLOT = {0: ("pg7", 0), 1: ("ph8", 0), 2: ("pg7", 256), 3: ("ph8", 256)}

        def emit_vmm(tr, h):
            pb.wait("t", tr, "pex", st["pex_b"] + h + 1)
            nmp, coff = PSLOT[h % 4]
            pb.wait("t", tr, "fo%d" % (h % 4), st["fo"][h % 4]); st["fo"][h % 4] += 1
            for kc in range(4):
                sl = pslot if kc < 2 else wslot
                inst = tr.matmul(P[nmp][0:65, coff:coff + WIN],
                                 S["vring"][:, sl, kc % 2, 65 * h:65 * h + 65],
                                 S["probs"][:, (h % 2) * 4 + kc, :],
                                 start=(kc == 0), stop=(kc == 3), skip_group_check=True)
            inst.then_inc(pb.sems["pvm"], 1); pb.tick("pvm")
            vomm[h] = pb.cnt["pvm"]

        @block.tensor
        def _(tr):
            pb.wait("t", tr, "qkr", qkr_now)
            pb.wait("t", tr, "vkr", vkr_now)
            for h in range(16):
                for kc in range(4):
                    k = (h * 4 + kc) % 2
                    sl = pslot if kc < 2 else wslot
                    pb.wait("t", tr, "fs%d" % k, st["fs"][k]); st["fs"][k] += 1
                    b = 64 * (h % 2)
                    pn = P["pe5"] if k == 0 else P["pf6"]
                    inst = tr.matmul(pn[:, 0:WIN],
                                     S["kring"][b:b + 64, sl, h // 2,
                                                (kc % 2) * 128:(kc % 2) * 128 + 128],
                                     S["qpb"][b:b + 64, h // 2, qoff:qoff + WIN],
                                     start=True, stop=True, skip_group_check=True)
                    inst.then_inc(pb.sems["mm"], 1); pb.tick("mm")
                    scmm[(h, kc)] = pb.cnt["mm"]
                if h >= 2:
                    emit_vmm(tr, h - 2)
            emit_vmm(tr, 14)
            emit_vmm(tr, 15)

        @block.vector
        def _(vr):
            v = Eng(pb, "v", vr)
            def evac(h):
                nmp, coff = PSLOT[h % 4]
                v.wait("pvm", vomm[h])
                v.tensor_copy(S["den_pack"][64 * (h % 2):64 * (h % 2) + 1,
                                            (h // 2) * WIN:(h // 2) * WIN + WIN],
                              P[nmp][64:65, coff:coff + WIN])
                v.tensor_copy(S["oall"][64 * (h % 2):64 * (h % 2) + 64, h // 2, :],
                              P[nmp][0:64, coff:coff + WIN])
                v.sig("fo%d" % (h % 4))
            for h in range(16):
                if h >= 2:
                    v.wait("pex", st["pex_b"] + h - 1)
                for kc in range(4):
                    k = (h * 4 + kc) % 2
                    v.wait("mm", scmm[(h, kc)])
                    mk = S["s_maskF"] if first else S["s_maskP"]
                    v.scalar_tensor_tensor(S["scf"][:, (h % 2) * 4 + kc, :],
                                           (P["pe5"] if k == 0 else P["pf6"])[:, 0:WIN],
                                           8.0, mk[:, kc, :], op0=ALU.mult, op1=ALU.add)
                    v.sig("fs%d" % k)
                v.sig("sce")
                if h >= 2:
                    evac(h - 2)
            evac(14)
            evac(15)
            v.sig("ve")

        @block.scalar
        def _(sr):
            s = Eng(pb, "s", sr)
            s.wait("pne", st["pne"])
            for h in range(16):
                s.wait("sce", st["sce_b"] + h + 1)
                if h >= 2:
                    s.wait("pvm", st["pvm_b"] + h - 1)
                s.activation(S["probs"][:, (h % 2) * 4:(h % 2) * 4 + 4, :],
                             S["scf"][:, (h % 2) * 4:(h % 2) * 4 + 4, :], AF.Exp,
                             bias=S["negc"][:, 0:1])
                s.sig("pex")
        st["sce_b"] += 16

        st["pex_b"] += 16

        st["pvm_b"] += 16

        @block.sync
        def _(y):
            pb.wait("y", y, "ve", pb.cnt["ve"])
            for bq in range(2):
                y.dma_start(out=S["den_sb"][8 * bq:8 * bq + 8, 0:WIN],
                            in_=S["den_pack"][64 * bq:64 * bq + 1, 0:8 * WIN]
                            ).then_inc(pb.sem("sa"), 16)
                pb.tick("sa", 16)

        @block.vector
        def _(vr):
            v = Eng(pb, "v", vr)
            v.wait("sa", pb.cnt["sa"])
            v.reciprocal(S["recipb"][:, 0:WIN], S["den_sb"][:, 0:WIN])
            v.sig("ve")

        rbmm = {}
        @block.tensor
        def _(tr):
            pb.wait("t", tr, "ve", pb.cnt["ve"])
            for mt in range(8):
                k = mt % 2
                pb.wait("t", tr, "fq%d" % k, st["fq"][k]); st["fq"][k] += 1
                ps = P["pqa"] if k == 0 else P["pqb"]
                inst = tr.matmul(ps[:, 0:WIN],
                                 S["sb_bmapPf"][:, mt * 128:(mt + 1) * 128],
                                 S["recipb"][:, 0:WIN], start=True, stop=True,
                                 skip_group_check=True)
                inst.then_inc(pb.sems["mm"], 1); pb.tick("mm")
                rbmm[mt] = pb.cnt["mm"]

        @block.vector
        def _(vr):
            v = Eng(pb, "v", vr)
            for mt in range(8):
                k = mt % 2
                v.wait("mm", rbmm[mt])
                v.tensor_copy(S["rbc2"][:, :], (P["pqa"] if k == 0 else P["pqb"])[:, 0:WIN])
                v.sig("fq%d" % k)
                v.tensor_tensor(S["concat"][:, mt, qoff:qoff + WIN],
                                S["oall"][:, mt, :], S["rbc2"][:, :], op=ALU.mult)
            v.sig("cc2")

    cc2_now = pb.cnt["cc2"]
    wommv = {}
    @block.tensor
    def _(tr):
        pb.wait("t", tr, "cc2", cc2_now)
        for mt in range(8):
            k = mt % 2
            pb.wait("t", tr, "fv%d" % k, st["fv"][k]); st["fv"][k] += 1
            ps = P["pvA"] if k == 0 else P["pvB"]
            for kt in range(8):
                inst = tr.matmul(ps[:, :], S["wof"][:, kt, mt * 128:(mt + 1) * 128],
                                 S["concat"][:, kt, :],
                                 start=(kt == 0), stop=(kt == 7), skip_group_check=True)
            inst.then_inc(pb.sems["mm"], 1); pb.tick("mm")
            wommv[mt] = pb.cnt["mm"]

    @block.vector
    def _(vr):
        v = Eng(pb, "v", vr)
        for mt in range(8):
            k = mt % 2
            v.wait("mm", wommv[mt])
            v.tensor_tensor(S["xch"][:, mt, 1:513],
                            (P["pvA"] if k == 0 else P["pvB"])[:, :],
                            S["xch"][:, mt, 1:513], op=ALU.add)
            v.sig("fv%d" % k)
        v.sig("shc"); st["shc"] += 1
        v.sig("ve")

    @block.scalar
    def _(sr):
        s = Eng(pb, "s", sr)
        s.wait("ve", pb.cnt["ve"])
        s.dma(bass.AP(T["xinnerT"], xlo, [[DOMF, 128], [128 * DOMF, 8], [1, 512]]),
              S["xch"][:, :, 1:513], "xw")
        s.sig("xw2"); st["xw2"] += 1


def p2_ffn(pb, l, st):
    nc, block = pb.nc, pb.block
    S, T, P = pb.S, pb.T, pb.P

    @block.gpsimd
    def _(gr):
        g = Eng(pb, "g", gr)
        g.dma(S["w1s"][:, :, :], T["W1"][l].rearrange("(kt p) m -> p kt m", p=128), "wld")
        g.dma(S["w2s"][:, 0:21, :], T["W2"][l][0:2688, :].rearrange("(kt p) m -> p kt m", p=128), "wld")
        g.dma(S["w2s"][0:42, 21, :], T["W2"][l][2688:2730, :], "wld")
        g.dma(S["s_b1c"][:, :], T["b1c"][l], "wld")
        g.dma(S["s_b2c"][:, :], T["b2c"][l], "wld")
        g.dma(S["s_lnc"][:, :], T["lnpc"][l], "wld")
    wld_l = pb.cnt["wld"]

    for tc in range(4):
        xlo = SLAB + tc * 512

        @block.sync
        def _(y):
            pb.wait("y", y, "shc", st["shc"])
            pb.wait("y", y, "xw2", st["xw2"])
            pb.wait("y", y, "xw", pb.cnt["xw"])
            y.dma_start(out=S["xch"][:, :, 0:513],
                        in_=bass.AP(T["xinnerT"], xlo - 1,
                                    [[DOMF, 128], [128 * DOMF, 8], [1, 513]])
                        ).then_inc(pb.sem("xr"), 16)
            pb.tick("xr", 16)

        @block.vector
        def _(vr):
            v = Eng(pb, "v", vr)
            v.wait("xr", pb.cnt["xr"])
            v.tensor_copy(S["shad"][:, :, 0:513], S["xch"][:, :, 0:513])
            v.sig("shr")
        shr_now = pb.cnt["shr"]

        @block.scalar
        def _(sr):
            s = Eng(pb, "s", sr)
            s.wait("shr", shr_now)
            s.wait("hgf", st["hgf"])
            for kt in range(8):
                off = 1 if kt < 4 else 0
                s.activation(S["hg"][:, 4 + kt, :], S["shad"][:, kt, off:off + 512], AF.Square)
            s.sig("se")

        stm = {}
        @block.tensor
        def _(tr):
            pb.wait("t", tr, "wld", wld_l)
            pb.wait("t", tr, "se", pb.cnt["se"])
            pb.wait("t", tr, "shr", shr_now)
            pb.wait("t", tr, "fst", st["fst"])
            for which in range(2):
                ps = P["pa00"] if which == 0 else P["pa10"]
                for kt in range(8):
                    off = 1 if kt < 4 else 0
                    rhs = (S["shad"][:, kt, off:off + 512] if which == 0
                           else S["hg"][:, 4 + kt, :])
                    inst = tr.matmul(ps[0:1, :], S["sb_onesb"][:, 0:1], rhs,
                                     start=(kt == 0), stop=(kt == 7), skip_group_check=True)
                inst.then_inc(pb.sems["mm"], 1); pb.tick("mm")
                stm[which] = pb.cnt["mm"]

        frow = S["hg"][:, 0:4, :].bitcast(F32)

        @block.vector
        def _(vr):
            v = Eng(pb, "v", vr)
            v.wait("mm", stm[1])
            v.tensor_scalar_mul(frow[0:1, 0, 0:256], P["pa00"][0:1, 0:256], 1.0 / D)
            v.tensor_scalar_mul(frow[0:1, 1, 0:256], P["pa00"][0:1, 256:512], 1.0 / D)
            v.tensor_scalar_mul(frow[0:1, 2, 0:256], P["pa10"][0:1, 0:256], 1.0 / D)
            v.tensor_scalar_mul(frow[0:1, 3, 0:256], P["pa10"][0:1, 256:512], 1.0 / D)
            v.sig("fst"); st["fst"] += 1
            v.tensor_tensor(S["t1f"][0:1, 0:256], frow[0:1, 0, 0:256], frow[0:1, 0, 0:256], op=ALU.mult)
            v.tensor_tensor(S["t1f"][0:1, 256:512], frow[0:1, 1, 0:256], frow[0:1, 1, 0:256], op=ALU.mult)
            v.tensor_tensor(frow[0:1, 2, 0:256], frow[0:1, 2, 0:256], S["t1f"][0:1, 0:256], op=ALU.subtract)
            v.tensor_tensor(frow[0:1, 3, 0:256], frow[0:1, 3, 0:256], S["t1f"][0:1, 256:512], op=ALU.subtract)
            v.tensor_scalar_add(frow[0:1, 2, 0:256], frow[0:1, 2, 0:256], 1e-5)
            v.tensor_scalar_add(frow[0:1, 3, 0:256], frow[0:1, 3, 0:256], 1e-5)
            v.sig("ve")

        @block.scalar
        def _(sr):
            s = Eng(pb, "s", sr)
            s.wait("ve", pb.cnt["ve"])
            s.activation(S["t1f"][0:1, 0:512], frow[0:1, 2:4, :].rearrange("p a b -> p (a b)"), AF.Sqrt)
            s.sig("se")

        @block.vector
        def _(vr):
            v = Eng(pb, "v", vr)
            v.wait("se", pb.cnt["se"])
            v.reciprocal(S["t2f"][0:1, 0:512], S["t1f"][0:1, 0:512])
            v.tensor_tensor(S["t1f"][0:1, 0:256], frow[0:1, 0, 0:256], S["t2f"][0:1, 0:256], op=ALU.mult)
            v.tensor_tensor(S["t1f"][0:1, 256:512], frow[0:1, 1, 0:256], S["t2f"][0:1, 256:512], op=ALU.mult)
            v.sig("ve")

        bcm = {}
        @block.tensor
        def _(tr):
            pb.wait("t", tr, "ve", pb.cnt["ve"])
            pb.wait("t", tr, "fbc", st["fbc"])
            for which, srcrow in ((0, "t1f"), (1, "t2f")):
                ps = P["pg00"] if which == 0 else P["pg10"]
                inst = tr.matmul(ps[:, :], S["sb_onesf"][0:1, :],
                                 S[srcrow][0:1, 0:512], start=True, stop=True,
                                 skip_group_check=True)
                inst.then_inc(pb.sems["mm"], 1); pb.tick("mm")
                bcm[which] = pb.cnt["mm"]

        @block.vector
        def _(vr):
            v = Eng(pb, "v", vr)
            v.wait("mm", bcm[1])
            v.tensor_copy(S["t3f"][:, 0:512], P["pg00"][:, :])
            for kt in range(8):
                off = 1 if kt < 4 else 0
                v.tensor_tensor(S["t1f"][:, 0:512], S["shad"][:, kt, off:off + 512],
                                P["pg10"][:, :], op=ALU.mult)
                v.tensor_tensor(S["t1f"][:, 0:512], S["t1f"][:, 0:512], S["t3f"][:, 0:512],
                                op=ALU.subtract)
                v.tensor_scalar(S["xn2"][:, kt, :], S["t1f"][:, 0:512],
                                S["s_lnc"][:, kt * 2:kt * 2 + 1], S["s_lnc"][:, kt * 2 + 1:kt * 2 + 2],
                                op0=ALU.mult, op1=ALU.add)
            v.sig("fbc"); st["fbc"] += 1
            v.sig("xn2r")
            v.sig("shc"); st["shc"] += 1

        xn2_now = pb.cnt["xn2r"]

        for hch in range(HCH):
            at0 = 2 * hch
            amm = {}
            @block.tensor
            def _(tr, hch=hch, at0=at0, amm=amm):
                pb.wait("t", tr, "xn2r", xn2_now)
                kk = hch % 2
                pb.wait("t", tr, "fpa%d" % kk, st["fpa"][kk]); st["fpa"][kk] += 1
                for gate in range(2):
                    for sub in range(2):
                        nrow_ = 128 if (hch < 10 or sub == 0) else 42
                        c0 = (2730 if gate else 0) + hch * 256 + sub * 128
                        ps = P["p%s%d%d" % ("g" if gate else "a", kk, sub)]
                        for kt in range(8):
                            inst = tr.matmul(ps[0:nrow_, :], S["w1s"][:, kt, c0:c0 + nrow_],
                                             S["xn2"][:, kt, :],
                                             start=(kt == 0), stop=(kt == 7), skip_group_check=True)
                        inst.then_inc(pb.sems["mm"], 1); pb.tick("mm")
                amm[0] = pb.cnt["mm"]

            @block.scalar
            def _(sr, hch=hch, at0=at0, amm=amm):
                s = Eng(pb, "s", sr)
                kk = hch % 2
                s.wait("mm", amm[0])
                for sub in range(2):
                    nrow_ = 128 if (hch < 10 or sub == 0) else 42
                    s.activation(S["absc"][0:nrow_, 4 * kk + sub, :], P["pa%d%d" % (kk, sub)][0:nrow_, :],
                                 AF.Identity, bias=S["s_b1c"][0:nrow_, at0 + sub:at0 + sub + 1])
                    s.activation(S["absc"][0:nrow_, 4 * kk + 2 + sub, :], P["pg%d%d" % (kk, sub)][0:nrow_, :],
                                 AF.Gelu, bias=S["s_b1c"][0:nrow_, 22 + at0 + sub:22 + at0 + sub + 1])
                s.sig("se")

            @block.vector
            def _(vr, hch=hch, amm=amm):
                v = Eng(pb, "v", vr)
                kk = hch % 2
                v.wait("se", pb.cnt["se"])
                for sub in range(2):
                    nrow_ = 128 if (hch < 10 or sub == 0) else 42
                    v.tensor_tensor(S["hg"][0:nrow_, 2 * hch + sub, :],
                                    S["absc"][0:nrow_, 4 * kk + sub, :],
                                    S["absc"][0:nrow_, 4 * kk + 2 + sub, :], op=ALU.mult)
                v.sig("fpa%d" % kk)
                v.sig("hgr")
        hgr_now = pb.cnt["hgr"]

        w2m = {}
        @block.tensor
        def _(tr):
            pb.wait("t", tr, "hgr", hgr_now)
            for mt in range(8):
                kk = mt % 2
                pb.wait("t", tr, "fpa%d" % kk, st["fpa"][kk]); st["fpa"][kk] += 1
                ps = P["pa%d0" % kk]
                for kt in range(22):
                    nrow_ = 128 if kt < 21 else 42
                    inst = tr.matmul(ps[:, :], S["w2s"][0:nrow_, kt, mt * 128:(mt + 1) * 128],
                                     S["hg"][0:nrow_, kt, :],
                                     start=(kt == 0), stop=(kt == 21), skip_group_check=True)
                inst.then_inc(pb.sems["mm"], 1); pb.tick("mm")
                w2m[mt] = pb.cnt["mm"]

        @block.vector
        def _(vr):
            v = Eng(pb, "v", vr)
            for mt in range(8):
                kk = mt % 2
                v.wait("mm", w2m[mt])
                v.scalar_tensor_tensor(S["xch"][:, mt, 1:513], P["pa%d0" % kk][:, :],
                                       S["s_b2c"][:, mt:mt + 1], S["xch"][:, mt, 1:513],
                                       op0=ALU.add, op1=ALU.add)
                v.sig("fpa%d" % kk)
            v.sig("ve")
            v.sig("hgf"); st["hgf"] += 1

        @block.scalar
        def _(sr, xlo=xlo):
            s = Eng(pb, "s", sr)
            s.wait("ve", pb.cnt["ve"])
            s.dma(bass.AP(T["xinnerT"], xlo, [[DOMF, 128], [128 * DOMF, 8], [1, 512]]),
                  S["xch"][:, :, 1:513], "xw")
            s.sig("xw2"); st["xw2"] += 1


def p2(pb):
    nc, block = pb.nc, pb.block
    S, T = pb.S, pb.T

    pb.open_phase("p2")
    pb.sbuf("xch", (128, 8, 513), F32, ph="p2")
    pb.sbuf("shad", (128, 8, 513), BF16, ph="p2")
    pb.sbuf("nrow", (128, 64), F32, ph="p2")
    pb.sbuf("negc", (128, 1), F32, ph="p2")
    pb.sbuf("qkcols", (128, 2), F32, ph="p2")
    pb.sbuf("t1f", (128, 512), F32, ph="p2")
    pb.sbuf("t2f", (128, 512), F32, ph="p2")
    pb.sbuf("t3f", (128, 512), F32, ph="p2")
    for s in ("xw", "xw2", "xr", "shr", "shc", "sq", "sn", "nr", "qkr", "vkr",
              "sce", "pex", "pvm", "cc2", "pne", "hgr", "hgf", "xn2r",
              "fst", "fbc"):
        pb.sem(s)
    for k in range(2):
        pb.sem("fq%d" % k); pb.sem("fn%d" % k); pb.sem("fb%d" % k)
        pb.sem("fv%d" % k); pb.sem("fs%d" % k); pb.sem("fpa%d" % k)
    for k in range(4):
        pb.sem("fo%d" % k)

    st = {"pne": 0, "shc": 0, "xw2": 0, "sq_b": 0, "sn_b": 0, "nr_b": 0,
          "sce_b": 0, "pex_b": 0, "pvm_b": 0, "hgf": 0, "fst": 0, "fbc": 0,
          "fq": [0, 0], "fn": [0, 0], "fb": [0, 0], "fv": [0, 0], "fs": [0, 0],
          "fo": [0, 0, 0, 0], "fpa": [0, 0]}

    for l in range(P2_LAYERS):
        if l > 0:
            p2_exchange(pb, st)
        pb.open_phase("p2a")
        pb.sbuf("s_maskP", (128, 4, WIN), F32, ph="p2a")
        pb.sbuf("s_maskF", (128, 4, WIN), F32, ph="p2a")
        pb.sbuf("s_two2f", (2, 128), F32, ph="p2a")
        pb.sbuf("s_two2fT", (128, 2), BF16, ph="p2a")
        pb.sbuf("wqkv", (128, 8, 3 * D), BF16, ph="p2a")
        pb.sbuf("wof", (128, 8, D), BF16, ph="p2a")
        pb.sbuf("rott", (128, 4 * DOMF), BF16, ph="p2a")
        pb.sbuf("kring", (128, 3, 8, WIN), BF16, ph="p2a")
        pb.sbuf("vring", (128, 3, 2, 1040), BF16, ph="p2a")
        pb.sbuf("qpb", (128, 8, 512), BF16, ph="p2a")
        pb.sbuf("probs", (128, 8, WIN), BF16, ph="p2a")
        pb.sbuf("scf", (128, 8, WIN), F32, ph="p2a")
        pb.sbuf("oall", (128, 8, WIN), BF16, ph="p2a")
        pb.sbuf("concat", (128, 8, 512), BF16, ph="p2a")
        pb.sbuf("sqs", (128, 1024), BF16, ph="p2a")
        pb.sbuf("rn1", (2, 1024), F32, ph="p2a")
        pb.sbuf("rn2", (2, 1024), F32, ph="p2a")
        pb.sbuf("rnb", (128, 512), F32, ph="p2a")
        pb.sbuf("recipb", (16, WIN), F32, ph="p2a")
        pb.sbuf("den_sb", (16, WIN), F32, ph="p2a")
        pb.sbuf("den_pack", (128, 2048), F32, ph="p2a")
        pb.sbuf("rbc2", (128, WIN), F32, ph="p2a")
        pb.psum("pqa", (128, 512), F32, ph="p2a")
        pb.psum("pqb", (128, 512), F32, ph="p2a")
        pb.psum("pvA", (128, 512), F32, ph="p2a")
        pb.psum("pvB", (128, 512), F32, ph="p2a")
        pb.psum("pe5", (128, 512), F32, ph="p2a")
        pb.psum("pf6", (128, 512), F32, ph="p2a")
        pb.psum("pg7", (128, 512), F32, ph="p2a")
        pb.psum("ph8", (128, 512), F32, ph="p2a")

        @block.vector
        def _(vr):
            v = Eng(pb, "v", vr)
            v.memset(S["vring"][:, :, :, :], 0.0)
            v.memset(S["vring"][:, :, :, :].rearrange("p s t (h m) -> p s t h m", m=65)[:, :, :, :, 64:65], 1.0)
            v.memset(S["kring"][:, :, :, :], 0.0)
            v.sig("vkr")
            v.sig("qkr")

        p2_attn(pb, l, st)
        pb.barrier("p2a%d" % l, drain=("xw", "xr", "wld", "gd", "sa"))
        pb.close_phase("p2a")

        p2_exchange(pb, st)
        pb.open_phase("p2f")
        pb.sbuf("w1s", (128, 8, W1C), BF16, ph="p2f")
        pb.sbuf("w2s", (128, 22, D), BF16, ph="p2f")
        pb.sbuf("hg", (128, 22, 512), BF16, ph="p2f")
        pb.sbuf("xn2", (128, 8, 512), BF16, ph="p2f")
        pb.sbuf("absc", (128, 8, 512), BF16, ph="p2f")
        pb.sbuf("s_b1c", (128, 44), F32, ph="p2f")
        pb.sbuf("s_b2c", (128, 8), F32, ph="p2f")
        pb.sbuf("s_lnc", (128, 16), F32, ph="p2f")
        pb.psum("pa00", (128, 512), F32, ph="p2f")
        pb.psum("pa01", (128, 512), F32, ph="p2f")
        pb.psum("pa10", (128, 512), F32, ph="p2f")
        pb.psum("pa11", (128, 512), F32, ph="p2f")
        pb.psum("pg00", (128, 512), F32, ph="p2f")
        pb.psum("pg01", (128, 512), F32, ph="p2f")
        pb.psum("pg10", (128, 512), F32, ph="p2f")
        pb.psum("pg11", (128, 512), F32, ph="p2f")

        p2_ffn(pb, l, st)
        pb.barrier("p2f%d" % l, drain=("xw", "xr", "wld", "gd"))
        pb.close_phase("p2f")

    pb.close_phase("p2")

    if "xinner_final" in pb.dbg:
        dd = pb.dout("d_xinner", (D, DOMF), F32)
        def dumpf(gr, dd=dd):
            g = Eng(pb, "g", gr)
            g.wait("xw", pb.cnt["xw"])
            g.dma(dd[:, :], T["xinnerT"][:, :], "gd")
        block.gpsimd(dumpf)


def p3(pb):
    nc, block = pb.nc, pb.block
    S, T, P = pb.S, pb.T, pb.P
    groups = [[0, 1], [2, 3], [4, 5], [6, 7]]

    pb.open_phase("p3")
    pb.sbuf("p3x", (128, 2, D), F32, ph="p3")
    pb.sbuf("p3af", (128, 2, 2, D), F32, ph="p3")
    pb.sbuf("p3sum", (128, 2, D), F32, ph="p3")
    pb.sbuf("p3x1", (128, 2, D), F32, ph="p3")
    pb.sbuf("p3xn", (128, 2, D), BF16, ph="p3")
    pb.sbuf("p3sq", (128, D), F32, ph="p3")
    pb.sbuf("p3st", (128, 8), F32, ph="p3")
    pb.psum("p3ta", (128, 512), F32, ph="p3")
    pb.psum("p3tb", (128, 512), F32, ph="p3")
    for s in ("p0n", "p0v", "p0s", "p0e"):
        pb.sem(s)
    xs3 = pb.stream("xs", 2)
    af3 = pb.stream("af3", 4)

    mmb = [0]
    st0 = pb.cnt["st"]
    n0 = pb.cnt["p0n"]
    v0 = pb.cnt["p0v"]
    s0 = pb.cnt["p0s"]
    e0 = pb.cnt["p0e"]

    @block.sync
    def _(y_raw):
        for i in range(NTO):
            if i >= 2:
                pb.wait("y", y_raw, "p0e", e0 + i - 1)
            xs3.issue(y_raw, S["p3x"][:, i % 2, :], T["x"][i * 128:(i + 1) * 128, :])
            if i >= 2:
                pb.wait("y", y_raw, "p0n", n0 + i - 1)
            af3.issue(y_raw, S["p3af"][:, i % 2, 0, :].rearrange("p (f c) -> p f c", f=8),
                      bass.AP(T["A_T"], i * 128, [[HALF, 128], [128 * HALF, 8], [1, 128]]))
            af3.issue(y_raw, S["p3af"][:, i % 2, 1, :].rearrange("p (f c) -> p f c", f=8),
                      bass.AP(T["xinnerT"], SLAB + i * 128, [[DOMF, 128], [128 * DOMF, 8], [1, 128]]))

    @block.tensor
    def _(tr):
        t = Eng(pb, "t", tr, chained=False)
        mmb[0] = pb.cnt["mm"]
        for i in range(NTO):
            t.wait("p0n", n0 + i + 1)
            if i >= 1:
                t.wait("p0e", e0 + i)
            for fg in range(2):
                ps = P["p3ta"] if fg == 0 else P["p3tb"]
                for f in range(4):
                    ft = fg * 4 + f
                    inst = tr.matmul(ps[:, f * 128:(f + 1) * 128],
                                     S["p3sum"][:, i % 2, ft * 128:(ft + 1) * 128],
                                     S["sb_identf"][:, :], start=True, stop=True,
                                     is_transpose=True)
            inst.then_inc(pb.sems["mm"], 1)
            pb.tick("mm")

    @block.vector
    def _(vr):
        v = Eng(pb, "v", vr)
        for i in range(NTO):
            af3.wait("v", vr, 2 * i)
            af3.wait("v", vr, 2 * i + 1)
            v.tensor_tensor(S["p3sum"][:, i % 2, :], S["p3af"][:, i % 2, 0, :],
                            S["p3af"][:, i % 2, 1, :], op=ALU.add)
            v.sig("p0n")
            v.wait("mm", mmb[0] + i + 1)
            xs3.wait("v", vr, i)
            if i >= 2:
                v.wait("st", st0 + 32 * (i - 1))
            xt = S["p3x"][:, i % 2, :]
            x1 = S["p3x1"][:, i % 2, :]
            for fg in range(2):
                ps = P["p3ta"] if fg == 0 else P["p3tb"]
                for f in range(4):
                    ft = fg * 4 + f
                    v.tensor_tensor(x1[:, ft * 128:(ft + 1) * 128], ps[:, f * 128:(f + 1) * 128],
                                    xt[:, ft * 128:(ft + 1) * 128], op=ALU.add)
            st = S["p3st"]
            v.reduce_sum(st[:, 0:1], x1, axis=AXX)
            v.tensor_scalar_mul(st[:, 0:1], st[:, 0:1], 1.0 / D)
            v.tensor_tensor(st[:, 4:5], st[:, 0:1], st[:, 0:1], op=ALU.mult)
            v.tensor_tensor(S["p3sq"][:, :], x1, x1, op=ALU.mult)
            v.reduce_sum(st[:, 1:2], S["p3sq"][:, :], axis=AXX)
            v.tensor_scalar_mul(st[:, 1:2], st[:, 1:2], 1.0 / D)
            v.tensor_tensor(st[:, 1:2], st[:, 1:2], st[:, 4:5], op=ALU.subtract)
            v.tensor_scalar_add(st[:, 2:3], st[:, 1:2], 1e-5)
            v.sig("p0v")
            v.wait("p0s", s0 + i + 1)
            v.reciprocal(st[:, 3:4], st[:, 6:7])
            v.tensor_scalar(S["p3xn"][:, i % 2, :], x1, st[:, 0:1], st[:, 3:4],
                            op0=ALU.subtract, op1=ALU.mult)
            v.sig("p0e")
            v.sig("ve")

    ve0 = pb.cnt["ve"] - NTO

    @block.scalar
    def _(sr):
        s = Eng(pb, "s", sr)
        for i in range(NTO + 1):
            if i < NTO:
                s.wait("p0v", v0 + i + 1)
                s.activation(S["p3st"][:, 6:7], S["p3st"][:, 2:3], AF.Sqrt)
                s.sig("p0s")
            if i >= 1:
                j = i - 1
                s.wait("ve", ve0 + j + 1)
                s.dma(T["x1_tm"][j * 128:(j + 1) * 128, :], S["p3x1"][:, j % 2, :], "st")
                s.dma(T["xn1_own"][j * 128:(j + 1) * 128, :], S["p3xn"][:, j % 2, :], "st")

    @block.gpsimd
    def _(gr):
        g = Eng(pb, "g", gr)
        g.wait("st", pb.cnt["st"])
        gr.collective_compute(
            "AllGather", ALU.bypass, replica_groups=groups,
            ins=[T["xn1_own"][:, :].opt()], outs=[T["xn1_all"][:, :].opt()],
        ).then_inc(pb.sem("cc"), 1)
        pb.tick("cc")

    pb.barrier("p3", drain=("st", "cc"))
    pb.close_phase("p3")

    if "p3" in pb.dbg:
        dd = pb.dout("d_xn1all", (N, D), BF16)
        dd2 = pb.dout("d_x1tm", (HALF, D), F32)
        def dumpf(gr, dd=dd, dd2=dd2):
            g = Eng(pb, "g", gr)
            g.dma(dd[:, :], T["xn1_all"][:, :], "gd")
            g.dma(dd2[:, :], T["x1_tm"][:, :], "gd")
        block.gpsimd(dumpf)


def p4_gate(pb):
    nc, block = pb.nc, pb.block
    S, T, P = pb.S, pb.T, pb.P
    pb.open_phase("p4g")
    pb.sbuf("xn1T", (128, 8, N), BF16, ph="p4g")
    pb.psum("pga", (128, 512), F32, ph="p4g")
    pb.psum("pgb", (128, 512), F32, ph="p4g")
    xt4 = pb.stream("xt4", 2)

    @block.gpsimd
    def _(gr):
        g = Eng(pb, "g", gr)
        g.dma(S["gw4"][:, :, :], T["gate_w"][:, :].rearrange("(kt p) e -> p kt e", p=128), "wld")
        g.dma(S["s_triu"][:, :], T["triu128"][:, :], "wld")
        g.dma(S["s_iotasel"][:, :], T["iotasel"][:, :], "wld")
        g.dma(S["s_iotacol"][:, :], T["iotacol"][:, :], "wld")
    wld4 = pb.cnt["wld"]

    @block.sync
    def _(y):
        pb.wait("y", y, "cc", pb.cnt["cc"])
        for tt4 in range(4):
            for ft in range(8):
                i = tt4 * 8 + ft
                if i >= 2:
                    xt4.wait("y", y, i - 2)
                xt4.issue(y, S["xn1T"][:, ft, tt4 * 1024:(tt4 + 1) * 1024],
                          T["xn1_all"][tt4 * 1024:(tt4 + 1) * 1024, ft * 128:(ft + 1) * 128],
                          transpose=True)

    mmg = [0]
    sce0 = pb.cnt["sce"]

    @block.tensor
    def _(tr):
        t = Eng(pb, "t", tr, chained=False)
        t.wait("wld", wld4)
        mmg[0] = pb.cnt["mm"]
        for tt in range(NT):
            xt4.wait("t", tr, 8 * (tt // 8) + 6)
            xt4.wait("t", tr, 8 * (tt // 8) + 7)
            if tt >= 2:
                t.wait("sce", sce0 + tt - 1)
            ps = P["pga"] if tt % 2 == 0 else P["pgb"]
            for kt in range(8):
                inst = tr.matmul(ps[:, 0:8], S["xn1T"][:, kt, tt * 128:(tt + 1) * 128],
                                 S["gw4"][:, kt, :], start=(kt == 0), stop=(kt == 7),
                                 skip_group_check=True)
            inst.then_inc(pb.sems["mm"], 1); pb.tick("mm")

    @block.vector
    def _(vr):
        v = Eng(pb, "v", vr)
        for tt in range(NT):
            v.wait("mm", mmg[0] + tt + 1)
            ps = P["pga"] if tt % 2 == 0 else P["pgb"]
            v.tensor_copy(S["probs4"][:, :, tt:tt + 1], ps[:, 0:8].rearrange("p (e o) -> p e o", o=1))
            v.sig("sce")

    pb.barrier("p4g", drain=("xt40", "xt41", "wld"))
    pb.close_phase("p4g")


def p4_select(pb):
    nc, block = pb.nc, pb.block
    S, T, P = pb.S, pb.T, pb.P
    pb.open_phase("p4b")
    pb.psum("pb4a", (128, 512), F32, ph="p4b")
    pb.psum("pb4b", (128, 512), F32, ph="p4b")
    pb.psum("pb4c", (128, 512), F32, ph="p4b")

    @block.scalar
    def _(sr):
        s = Eng(pb, "s", sr)
        s.activation(S["exp4"][:, :, :], S["probs4"][:, :, :], AF.Exp)
        s.sig("se")

    @block.vector
    def _(vr):
        v = Eng(pb, "v", vr)
        v.wait("se", pb.cnt["se"])
        v.reduce_sum(S["den4"][:, :], S["exp4"][:, :, :].rearrange("p e t -> p t e"), axis=AXX)
        v.reciprocal(S["den4"][:, :], S["den4"][:, :])
        for e in range(N_EXP):
            v.tensor_tensor(S["probs4"][:, e, :], S["exp4"][:, e, :], S["den4"][:, :], op=ALU.mult)
        br = S["brow"]
        v.memset(br[0:1, 0:8], 0.0)
        v.memset(br[0:1, 8:16], 1.0)
        v.memset(br[0:1, 16:24], 0.5)
        v.sig("pvm")

    for it in range(30):
        @block.tensor
        def _(tr):
            t = Eng(pb, "t", tr, chained=False)
            t.wait("pvm", pb.cnt["pvm"])
            inst = tr.matmul(P["pb4a"][:, 0:8], S["sb_onesf"][0:1, :], S["brow"][0:1, 16:24],
                             start=True, stop=True, skip_group_check=True)
            inst.then_inc(pb.sems["mm"], 1); pb.tick("mm")

        @block.vector
        def _(vr):
            v = Eng(pb, "v", vr)
            v.wait("mm", pb.cnt["mm"])
            v.tensor_copy(S["taubc"][:, :], P["pb4a"][:, 0:8])
            for e in range(N_EXP):
                v.tensor_scalar(S["cmp4"][:, e, :], S["probs4"][:, e, :],
                                S["taubc"][:, e:e + 1], None, op0=ALU.is_ge)
            v.reduce_sum(S["cnt4"][:, :], S["cmp4"][:, :, :], axis=AXX)
            v.sig("pvm")

        @block.tensor
        def _(tr):
            t = Eng(pb, "t", tr, chained=False)
            t.wait("pvm", pb.cnt["pvm"])
            inst = tr.matmul(P["pb4b"][0:1, 0:8], S["sb_onesf"][:, 0:1], S["cnt4"][:, :],
                             start=True, stop=True, skip_group_check=True)
            inst.then_inc(pb.sems["mm"], 1); pb.tick("mm")

        @block.vector
        def _(vr):
            v = Eng(pb, "v", vr)
            v.wait("mm", pb.cnt["mm"])
            br = S["brow"]
            v.tensor_copy(br[0:1, 24:32], P["pb4b"][0:1, 0:8])
            v.tensor_scalar(br[0:1, 32:40], br[0:1, 24:32], float(MOE_K), None, op0=ALU.is_ge)
            v.tensor_tensor(br[0:1, 40:48], br[0:1, 16:24], br[0:1, 0:8], op=ALU.subtract)
            v.tensor_tensor(br[0:1, 48:56], br[0:1, 40:48], br[0:1, 32:40], op=ALU.mult)
            v.tensor_tensor(br[0:1, 0:8], br[0:1, 0:8], br[0:1, 48:56], op=ALU.add)
            v.tensor_scalar(br[0:1, 56:64], br[0:1, 32:40], -1.0, 1.0, op0=ALU.mult, op1=ALU.add)
            v.tensor_tensor(br[0:1, 40:48], br[0:1, 16:24], br[0:1, 8:16], op=ALU.subtract)
            v.tensor_tensor(br[0:1, 48:56], br[0:1, 40:48], br[0:1, 56:64], op=ALU.mult)
            v.tensor_tensor(br[0:1, 8:16], br[0:1, 8:16], br[0:1, 48:56], op=ALU.add)
            v.tensor_tensor(br[0:1, 16:24], br[0:1, 0:8], br[0:1, 8:16], op=ALU.add)
            v.tensor_scalar_mul(br[0:1, 16:24], br[0:1, 16:24], 0.5)
            v.sig("pvm")

    @block.tensor
    def _(tr):
        t = Eng(pb, "t", tr, chained=False)
        t.wait("pvm", pb.cnt["pvm"])
        inst = tr.matmul(P["pb4a"][:, 0:8], S["sb_onesf"][0:1, :], S["brow"][0:1, 0:8],
                         start=True, stop=True, skip_group_check=True)
        inst.then_inc(pb.sems["mm"], 1); pb.tick("mm")

    @block.vector
    def _(vr):
        v = Eng(pb, "v", vr)
        v.wait("mm", pb.cnt["mm"])
        v.tensor_copy(S["taubc"][:, :], P["pb4a"][:, 0:8])
        for e in range(N_EXP):
            v.tensor_scalar(S["maskf"][:, e, :], S["probs4"][:, e, :],
                            S["taubc"][:, e:e + 1], None, op0=ALU.is_ge)
        v.tensor_tensor(S["exp4"][:, :, :], S["probs4"][:, :, :], S["maskf"][:, :, :], op=ALU.mult)
        for j in range(4):
            v.tensor_scalar_mul(S["t4a"][:, :], S["exp4"][:, j, :], S["sb_gates"][:, 1:2])
            v.tensor_scalar_mul(S["t4b"][:, :], S["exp4"][:, j + 4, :], S["sb_gates"][:, 0:1])
            v.tensor_tensor(S["gmine"][:, j, :], S["t4a"][:, :], S["t4b"][:, :], op=ALU.add)
        v.tensor_copy(S["gmb"][:, :, :], S["gmine"][:, :, :])
        v.sig("pvm")

    for e in range(N_EXP):
        @block.tensor
        def _(tr, e=e):
            t = Eng(pb, "t", tr, chained=False)
            t.wait("pvm", pb.cnt["pvm"])
            if e >= 1:
                t.wait("sce", pb.cnt["sce"])
            tr.matmul(P["pb4a"][:, 0:32], S["s_triu"][:, :], S["maskf"][:, e, :],
                      start=True, stop=True, skip_group_check=True)
            inst = tr.matmul(P["pb4b"][0:1, 0:32], S["sb_onesf"][:, 0:1], S["maskf"][:, e, :],
                             start=True, stop=True, skip_group_check=True)
            inst.then_inc(pb.sems["mm"], 1); pb.tick("mm")

        @block.vector
        def _(vr, e=e):
            v = Eng(pb, "v", vr)
            v.wait("mm", pb.cnt["mm"])
            sr = S["srow"]
            v.tensor_copy(sr[0:1, 0, :], P["pb4b"][0:1, 0:32])
            for k, sh in enumerate((1, 2, 4, 8, 16)):
                src, dst = k % 2, (k + 1) % 2
                v.tensor_copy(sr[0:1, dst, 0:sh], sr[0:1, src, 0:sh])
                v.tensor_tensor(sr[0:1, dst, sh:32], sr[0:1, src, sh:32],
                                sr[0:1, src, 0:32 - sh], op=ALU.add)
            v.memset(sr[0:1, 2, 0:1], 0.0)
            v.tensor_copy(sr[0:1, 2, 1:32], sr[0:1, 1, 0:31])
            v.tensor_copy(S["cwt"][:, :], P["pb4a"][:, 0:32])
            v.sig("pne")

        @block.tensor
        def _(tr, e=e):
            t = Eng(pb, "t", tr, chained=False)
            t.wait("pne", pb.cnt["pne"])
            inst = tr.matmul(P["pb4c"][:, 0:32], S["sb_onesf"][0:1, :], S["srow"][0:1, 2, :],
                             start=True, stop=True, skip_group_check=True)
            inst.then_inc(pb.sems["mm"], 1); pb.tick("mm")

        @block.vector
        def _(vr, e=e):
            v = Eng(pb, "v", vr)
            v.wait("mm", pb.cnt["mm"])
            v.tensor_tensor(S["cumm"][:, e, :], S["cwt"][:, :], P["pb4c"][:, 0:32], op=ALU.add)
            v.tensor_tensor(S["cumm"][:, e, :], S["cumm"][:, e, :], S["maskf"][:, e, :], op=ALU.mult)
            v.sig("sce")

    @block.vector
    def _(vr):
        v = Eng(pb, "v", vr)
        v.wait("sce", pb.cnt["sce"])
        for j in range(4):
            v.tensor_scalar_mul(S["t4a"][:, :], S["cumm"][:, j, :], S["sb_gates"][:, 1:2])
            v.tensor_scalar_mul(S["t4b"][:, :], S["cumm"][:, j + 4, :], S["sb_gates"][:, 0:1])
            v.tensor_tensor(S["cmine"][:, j, :], S["t4a"][:, :], S["t4b"][:, :], op=ALU.add)
        v.sig("pvm")

    pb.barrier("p4b")
    pb.close_phase("p4b")


def p4_experts(pb):
    nc, block = pb.nc, pb.block
    S, T, P = pb.S, pb.T, pb.P
    pb.open_phase("p4x")
    pb.sbuf("xn1_tm", (128, NT, D), BF16, ph="p4x")
    pb.sbuf("hgeT", (128, 32, MOE_K), BF16, ph="p4x")
    pb.sbuf("xinT", (128, 8, MOE_K), BF16, ph="p4x")
    pb.sbuf("w1s4", (128, 2, 8, 128), BF16, ph="p4x")
    pb.sbuf("w2s4", (128, 2, D), BF16, ph="p4x")
    pb.sbuf("Ptg", (128, 2, 512), BF16, ph="p4x")
    pb.sbuf("grow4", (1, MOE_K), F32, ph="p4x")
    pb.sbuf("gcols", (128, 8), F32, ph="p4x")
    pb.sbuf("h2st", (128, 8, D), BF16, ph="p4x")
    for i in range(8):
        pb.psum(f"px{i}", (128, 512), F32, ph="p4x")
    w14 = pb.stream("w14", 2)
    w24 = pb.stream("w24", 2)

    @block.gpsimd
    def _(gr):
        g = Eng(pb, "g", gr)
        g.wait("cc", pb.cnt["cc"])
        g.dma(S["xn1_tm"][:, :, :], T["xn1_all"][:, :].rearrange("(tt p) d -> p tt d", p=128), "xld")
    xld4 = pb.cnt["xld"]

    for j in range(4):
        st_j = {}

        @block.tensor
        def _(tr, j=j, st_j=st_j):
            t = Eng(pb, "t", tr, chained=False)
            t.wait("xld", xld4)
            t.wait("pvm", pb.cnt["pvm"])
            if j >= 1:
                t.wait("sce", pb.cnt["sce"])
            st_j["mm0"] = pb.cnt["mm"]
            st_j["pt0"] = pb.cnt["hgr"]
            st_j["g4x0"] = pb.cnt["hgf"]
            st_j["g4e0"] = pb.cnt["xn2r"]
            for nh in range(2):
                if nh == 1:
                    t.wait("xn2r", st_j["g4e0"] + 1)
                for tt in range(NT):
                    i = nh * NT + tt
                    t.wait("hgr", st_j["pt0"] + i + 1)
                    for fc in range(8):
                        inst = tr.matmul(P[f"px{fc}"][:, :],
                                         S["xn1_tm"][:, tt, fc * 128:(fc + 1) * 128],
                                         S["Ptg"][:, i % 2, :],
                                         start=(tt == 0), stop=(tt == NT - 1),
                                         skip_group_check=True)
                    inst.then_inc(pb.sems["mm"], 1); pb.tick("mm")

        @block.tensor
        def _(tr, j=j, st_j=st_j):
            t = Eng(pb, "t", tr, chained=False)
            st_j["gm0"] = pb.cnt["mm"]
            st_j["gmm0"] = pb.cnt["sq"]
            gpt0 = st_j["pt0"] + 64
            for nh in range(2):
                for tt in range(NT):
                    i = nh * NT + tt
                    t.wait("hgr", gpt0 + i + 1)
                    bank = P["px0"] if nh == 0 else P["px1"]
                    inst = tr.matmul(bank[0:1, 0:512],
                                     S["gmb"][:, j, tt:tt + 1], S["Ptg"][:, i % 2, :],
                                     start=(tt == 0), stop=(tt == NT - 1),
                                     skip_group_check=True)
                    inst.then_inc(pb.sems["sq"], 1); pb.tick("sq")

        @block.vector
        def _(vr, j=j, st_j=st_j):
            v = Eng(pb, "v", vr)
            for nh in range(2):
                for tt in range(NT):
                    i = nh * NT + tt
                    if i >= 2:
                        v.wait("mm", st_j["mm0"] + (i - 1))
                    v.tensor_scalar(S["Ptg"][:, i % 2, :], S["s_iotasel"][:, nh * 512:nh * 512 + 512],
                                    S["cmine"][:, j, tt:tt + 1], 0.0,
                                    op0=ALU.subtract, op1=ALU.is_equal)
                    v.sig("hgr")
                v.wait("mm", st_j["mm0"] + NT * (nh + 1))
                for fc in range(8):
                    v.tensor_copy(S["xinT"][:, fc, nh * 512:nh * 512 + 512], P[f"px{fc}"][:, :])
                v.sig("xn2r")
            for nh in range(2):
                for tt in range(NT):
                    i = nh * NT + tt
                    if i >= 2:
                        v.wait("sq", st_j["gmm0"] + (i - 1))
                    else:
                        v.wait("mm", st_j["mm0"] + 64)
                    v.tensor_scalar(S["Ptg"][:, i % 2, :], S["s_iotasel"][:, nh * 512:nh * 512 + 512],
                                    S["cmine"][:, j, tt:tt + 1], 0.0,
                                    op0=ALU.subtract, op1=ALU.is_equal)
                    v.sig("hgr")
            v.wait("sq", st_j["gmm0"] + 64)
            v.tensor_copy(S["grow4"][0:1, 0:512], P["px0"][0:1, 0:512])
            v.tensor_copy(S["grow4"][0:1, 512:1024], P["px1"][0:1, 0:512])
            v.sig("xn2r")

        @block.tensor
        def _(tr, j=j, st_j=st_j):
            t = Eng(pb, "t", tr, chained=False)
            t.wait("xn2r", pb.cnt["xn2r"])
            for c in range(8):
                inst = tr.matmul(P["px2"][:, c:c + 1], S["grow4"][0:1, c * 128:(c + 1) * 128],
                                 S["sb_identf"][0:1, 0:1], start=(c == 0), stop=(c == 7),
                                 is_transpose=True, skip_group_check=True)
            inst.then_inc(pb.sems["mm"], 1); pb.tick("mm")

        @block.vector
        def _(vr, j=j):
            v = Eng(pb, "v", vr)
            v.wait("mm", pb.cnt["mm"])
            v.tensor_copy(S["gcols"][:, :], P["px2"][:, 0:8])
            v.sig("xn2r")

        h140 = pb.cnt["nr"]

        @block.gpsimd
        def _(gr, j=j, h140=h140):
            g = Eng(pb, "g", gr)
            for ht in range(32):
                if ht >= 2:
                    pb.wait("g", gr, "nr", h140 + 2 * (ht - 1))
                w14.issue(gr, S["w1s4"][:, ht % 2, :, :],
                          T["moe_w1"][j, :, ht * 128:(ht + 1) * 128]
                          .rearrange("(kt p) c -> p kt c", p=128))

        @block.tensor
        def _(tr, j=j, st_j=st_j):
            t = Eng(pb, "t", tr, chained=False)
            t.wait("xn2r", pb.cnt["xn2r"])
            st_j["h10"] = pb.cnt["mm"]
            st_j["h1e0"] = pb.cnt["nr"]
            w0 = w14.nissued - 32
            for ht in range(32):
                w14.wait("t", tr, w0 + ht)
                if ht >= 2:
                    t.wait("nr", st_j["h1e0"] + 2 * (ht - 1))
                for nh in range(2):
                    bank = P[f"px{2 + (ht % 2) * 2 + nh}"]
                    for kt in range(8):
                        inst = tr.matmul(bank[:, :], S["w1s4"][:, ht % 2, kt, :],
                                         S["xinT"][:, kt, nh * 512:nh * 512 + 512],
                                         start=(kt == 0), stop=(kt == 7), skip_group_check=True)
                    inst.then_inc(pb.sems["mm"], 1); pb.tick("mm")

        @block.scalar
        def _(sr, j=j, st_j=st_j):
            s = Eng(pb, "s", sr)
            for ht in range(32):
                s.wait("mm", st_j["h10"] + 2 * ht + 2)
                for nh in range(2):
                    bank = P[f"px{2 + (ht % 2) * 2 + nh}"]
                    s.activation(S["hgeT"][:, ht, nh * 512:nh * 512 + 512], bank[:, :],
                                 AF.Gelu_apprx_tanh)
                s.sig("nr", 2)

        h240 = pb.cnt["pex"]

        @block.gpsimd
        def _(gr, j=j, h240=h240):
            g = Eng(pb, "g", gr)
            for g2 in range(2):
                for kt in range(32):
                    i = g2 * 32 + kt
                    if i >= 2:
                        pb.wait("g", gr, "pex", h240 + (i - 1) * 8)
                    w24.issue(gr, S["w2s4"][:, i % 2, :],
                              T["moe_w2"][j, kt * 128:(kt + 1) * 128, :])

        @block.tensor
        def _(tr, j=j, st_j=st_j):
            t = Eng(pb, "t", tr, chained=False)
            t.wait("nr", pb.cnt["nr"])
            st_j["h20"] = pb.cnt["mm"]
            st_j["pex0"] = pb.cnt["pex"]
            st_j["h2e0"] = pb.cnt["fst"]
            w0 = w24.nissued - 64
            for g2 in range(2):
                if g2 == 1:
                    t.wait("fst", st_j["h2e0"] + 8)
                for kt in range(32):
                    i = g2 * 32 + kt
                    w24.wait("t", tr, w0 + i)
                    for stl in range(4):
                        st_ = g2 * 4 + stl
                        for nh in range(2):
                            bank = P[f"px{stl * 2 + nh}"]
                            inst = tr.matmul(bank[:, :], S["hgeT"][:, kt, st_ * 128:(st_ + 1) * 128],
                                             S["w2s4"][:, i % 2, nh * 512:nh * 512 + 512],
                                             start=(kt == 0), stop=(kt == 31), skip_group_check=True)
                    inst.then_inc(pb.sems["pex"], 8); pb.tick("pex", 8)

        @block.vector
        def _(vr, j=j, st_j=st_j):
            v = Eng(pb, "v", vr)
            v.wait("st", pb.cnt["st"])
            for g2 in range(2):
                v.wait("pex", st_j["pex0"] + 256 * (g2 + 1))
                for stl in range(4):
                    st_ = g2 * 4 + stl
                    for nh in range(2):
                        bank = P[f"px{stl * 2 + nh}"]
                        v.tensor_scalar_mul(S["h2st"][:, st_, nh * 512:nh * 512 + 512],
                                            bank[:, :], S["gcols"][:, st_:st_ + 1])
                v.sig("fst", 8)

        @block.scalar
        def _(sr, j=j):
            s = Eng(pb, "s", sr)
            s.wait("fst", pb.cnt["fst"])
            s.dma(T["h2w_own"][j * MOE_K:(j + 1) * MOE_K, :].rearrange("(st p) d -> p st d", p=128),
                  S["h2st"][:, :, :], "st")
            s.sig("sce")

    @block.gpsimd
    def _(gr):
        g = Eng(pb, "g", gr)
        g.wait("st", pb.cnt["st"])
        gr.collective_compute(
            "AllGather", ALU.bypass, replica_groups=[[0, 1], [2, 3], [4, 5], [6, 7]],
            ins=[T["h2w_own"][:, :].opt()], outs=[T["h2w_all"][:, :].opt()],
        ).then_inc(pb.sem("cc"), 1)
        pb.tick("cc")

    pb.barrier("p4x", drain=("st", "cc", "xld", "w140", "w141", "w240", "w241"))
    pb.close_phase("p4x")


def p4_scatter(pb):
    nc, block = pb.nc, pb.block
    S, T, P = pb.S, pb.T, pb.P
    pb.open_phase("p4s")
    pb.sbuf("fst", (128, 2, 8, D), BF16, ph="p4s")
    pb.sbuf("cumbc8", (128, 8, 512), F32, ph="p4s")
    pb.sbuf("Psel", (128, 2, 512), BF16, ph="p4s")
    pb.sbuf("F1st", (128, 8, 512), BF16, ph="p4s")
    pb.sbuf("cmow", (128, 8, 16), F32, ph="p4s")
    pb.sbuf("c4row", (1, 512), F32, ph="p4s")
    for i in range(8):
        pb.psum(f"ps{i}", (128, 512), F32, ph="p4s")
    h2l = pb.stream("h2l", 2)

    @block.vector
    def _(vr):
        v = Eng(pb, "v", vr)
        for e in range(N_EXP):
            v.tensor_scalar_mul(S["t4a"][:, 0:16], S["cumm"][:, e, 0:16], S["sb_gates"][:, 1:2])
            v.tensor_scalar_mul(S["t4b"][:, 0:16], S["cumm"][:, e, 16:32], S["sb_gates"][:, 0:1])
            v.tensor_tensor(S["cmow"][:, e, :], S["t4a"][:, 0:16], S["t4b"][:, 0:16], op=ALU.add)
        v.sig("fbc")

    s4m0 = pb.cnt["fq0"]

    @block.gpsimd
    def _(gr):
        g = Eng(pb, "g", gr)
        g.wait("cc", pb.cnt["cc"])
        for tc in range(4):
            for e in range(N_EXP):
                i = tc * N_EXP + e
                if i >= 2:
                    pb.wait("g", gr, "fq0", s4m0 + (i - 1) * 64)
                h2l.issue(gr, S["fst"][:, i % 2, :, :],
                          T["h2w_all"][e * MOE_K:(e + 1) * MOE_K, :]
                          .rearrange("(st p) d -> p st d", p=128))
    hl0 = h2l.nissued - 32

    for tc in range(4):
        st_t = {}

        @block.tensor
        def _(tr, tc=tc, st_t=st_t):
            t = Eng(pb, "t", tr, chained=False)
            t.wait("fbc", pb.cnt["fbc"])
            if tc >= 1:
                t.wait("fn0", pb.cnt["fn0"])
            st_t["mm0"] = pb.cnt["mm"]
            st_t["s4r0"] = pb.cnt["shc"]
            st_t["bc40"] = pb.cnt["fb0"]
            for e in range(N_EXP):
                if e >= 1:
                    t.wait("shc", st_t["s4r0"] + 2 * e - 1)
                for c2 in range(4):
                    inst = tr.matmul(P["ps0"][0:1, c2 * 128:(c2 + 1) * 128],
                                     S["cmow"][:, e, tc * 4 + c2: tc * 4 + c2 + 1],
                                     S["sb_identf"][:, :], start=(c2 == 0), stop=(c2 == 3),
                                     is_transpose=True, skip_group_check=True)
                inst.then_inc(pb.sems["mm"], 1); pb.tick("mm")
                t.wait("shc", st_t["s4r0"] + 2 * e + 1)
                inst = tr.matmul(P["ps1"][:, :], S["sb_onesf"][0:1, :], S["c4row"][0:1, :],
                                 start=True, stop=True, skip_group_check=True)
                inst.then_inc(pb.sems["fb0"], 1); pb.tick("fb0")

        @block.vector
        def _(vr, tc=tc, st_t=st_t):
            v = Eng(pb, "v", vr)
            for e in range(N_EXP):
                v.wait("mm", st_t["mm0"] + e + 1)
                v.tensor_copy(S["c4row"][0:1, :], P["ps0"][0:1, :])
                v.sig("shc")
                v.wait("fb0", st_t["bc40"] + e + 1)
                v.tensor_copy(S["cumbc8"][:, e, :], P["ps1"][:, :])
                v.sig("shc")

        st_b = {}

        @block.tensor
        def _(tr, tc=tc, st_b=st_b, st_t=st_t):
            t = Eng(pb, "t", tr, chained=False)
            t.wait("shc", st_t["s4r0"] + 16)
            st_b["sp0"] = pb.cnt["xw2"]
            for e in range(N_EXP):
                h2l.wait("t", tr, hl0 + tc * N_EXP + e)
                for st_ in range(8):
                    i = e * 8 + st_
                    t.wait("xw2", st_b["sp0"] + i + 1)
                    for ft in range(8):
                        inst = tr.matmul(P[f"ps{ft}"][:, :],
                                         S["fst"][:, e % 2, st_, ft * 128:(ft + 1) * 128],
                                         S["Psel"][:, i % 2, :],
                                         start=(e == 0 and st_ == 0), stop=(e == 7 and st_ == 7),
                                         skip_group_check=True)
                    inst.then_inc(pb.sems["fq0"], 8); pb.tick("fq0", 8)

        @block.vector
        def _(vr, tc=tc, st_b=st_b):
            v = Eng(pb, "v", vr)
            sm0 = pb.cnt["fq0"] - 512
            for e in range(N_EXP):
                for st_ in range(8):
                    i = e * 8 + st_
                    if i >= 2:
                        v.wait("fq0", sm0 + 8 * (i - 1))
                    v.tensor_scalar(S["Psel"][:, i % 2, :], S["cumbc8"][:, e, :],
                                    S["s_iotacol"][:, 0:1], float(st_ * 128),
                                    op0=ALU.subtract, op1=ALU.is_equal)
                    v.sig("xw2")
            v.wait("fq0", pb.cnt["fq0"])
            v.wait("st", pb.cnt["st"])
            for ft in range(8):
                v.tensor_copy(S["F1st"][:, ft, :], P[f"ps{ft}"][:, :])
            v.sig("ve")

        @block.scalar
        def _(sr, tc=tc):
            s = Eng(pb, "s", sr)
            s.wait("ve", pb.cnt["ve"])
            s.dma(bass.AP(T["F1T"], tc * 512, [[HALF, 128], [128 * HALF, 8], [1, 512]]),
                  S["F1st"][:, :, :], "st")
            s.sig("fn0")

    pb.barrier("p4s", drain=("st", "h2l0", "h2l1"))
    pb.close_phase("p4s")


def p4(pb):
    nc, block = pb.nc, pb.block
    S, T = pb.S, pb.T
    pb.open_phase("p4")
    pb.sbuf("probs4", (128, N_EXP, NT), F32, ph="p4")
    pb.sbuf("exp4", (128, N_EXP, NT), F32, ph="p4")
    pb.sbuf("den4", (128, NT), F32, ph="p4")
    pb.sbuf("cumm", (128, N_EXP, NT), F32, ph="p4")
    pb.sbuf("maskf", (128, N_EXP, NT), F32, ph="p4")
    pb.sbuf("cmp4", (128, N_EXP, NT), BF16, ph="p4")
    pb.sbuf("cnt4", (128, N_EXP), F32, ph="p4")
    pb.sbuf("brow", (1, 64), F32, ph="p4")
    pb.sbuf("taubc", (128, N_EXP), F32, ph="p4")
    pb.sbuf("gmine", (128, 4, NT), F32, ph="p4")
    pb.sbuf("gmb", (128, 4, NT), BF16, ph="p4")
    pb.sbuf("cmine", (128, 4, NT), F32, ph="p4")
    pb.sbuf("srow", (1, 3, NT), F32, ph="p4")
    pb.sbuf("cwt", (128, NT), F32, ph="p4")
    pb.sbuf("t4a", (128, NT), F32, ph="p4")
    pb.sbuf("t4b", (128, NT), F32, ph="p4")
    pb.sbuf("gw4", (128, 8, N_EXP), BF16, ph="p4")
    pb.sbuf("s_triu", (128, 128), F32, ph="p4")
    pb.sbuf("s_iotasel", (128, MOE_K), F32, ph="p4")
    pb.sbuf("s_iotacol", (128, 1), F32, ph="p4")
    for s in ("sce", "pvm", "pne", "hgr", "hgf", "xn2r", "sq", "sn", "nr", "pex",
              "fst", "fbc", "shc", "xw2", "fq0", "fn0", "fb0"):
        pb.sem(s)

    p4_gate(pb)
    p4_select(pb)
    p4_experts(pb)
    p4_scatter(pb)
    pb.close_phase("p4")

    if "p4" in pb.dbg:
        dd = pb.dout("d_F1T", (D, HALF), BF16)
        def dumpf(gr, dd=dd):
            g = Eng(pb, "g", gr)
            g.dma(dd[:, :], T["F1T"][:, :], "gd")
        block.gpsimd(dumpf)


def p5_prep(pb):
    nc, block = pb.nc, pb.block
    S, T, P = pb.S, pb.T, pb.P
    pb.open_phase("p5x")
    pb.sbuf("wkv5", (128, 2, 8, D), BF16, ph="p5x")
    pb.sbuf("wi1sb", (128, 8, 512), BF16, ph="p5x")
    pb.sbuf("wi2sb", (128, 4, 1), BF16, ph="p5x")
    pb.sbuf("himp5", (128, 4, 512), BF16, ph="p5x")
    pb.sbuf("improw", (1, 512), F32, ph="p5x")
    pb.sbuf("bvbc", (128, D), F32, ph="p5x")
    pb.sbuf("brsrc", (1, D), F32, ph="p5x")
    pb.sbuf("wi2r", (1, 512), BF16, ph="p5x")
    pb.sbuf("xc5", (128, 2, 8, 512), BF16, ph="p5x")

    @block.gpsimd
    def _(gr):
        g = Eng(pb, "g", gr)
        g.dma(S["wkv5"][:, 0, :, :], T["wk1"][:, :].rearrange("(kt p) m -> p kt m", p=128), "wld")
        g.dma(S["wkv5"][:, 1, :, :], T["wv1"][:, :].rearrange("(kt p) m -> p kt m", p=128), "wld")
        g.dma(S["wi1sb"][:, :, :], T["wi1"][:, :].rearrange("(kt p) m -> p kt m", p=128), "wld")
        g.dma(S["wi2r"][0:1, :], T["wi2"][:, :], "wld")
        g.wait("wld", pb.cnt["wld"])
        for mt in range(4):
            g.dma(S["wi2sb"][:, mt, 0:1], S["wi2r"][0:1, mt * 128:(mt + 1) * 128], "wld")
        g.dma(S["s_t2f"][:, :], T["two2f128"][:, :], "wld")
        g.dma(S["b5c"][:, :, :],
              bass.AP(T["b1vecT"], 0, [[8, 128], [1024, 8], [1, 8]]), "wld")
        g.dma(S["brsrc"][0:1, :], T["b1vec"][2:3, :], "wld")
    wld5 = pb.cnt["wld"]

    def xchunk_loader(stream, mm_sem, mm_base, per_ch):
        @block.sync
        def _(y):
            pb.wait("y", y, "cc", pb.cnt["cc"])
            s0 = stream.nissued
            for ch in range(8):
                if ch >= 2:
                    pb.wait("y", y, mm_sem, mm_base[0] + per_ch * (ch - 1))
                else:
                    pb.wait("y", y, mm_sem, mm_base[0])
                for ft in range(8):
                    i = s0 + ch * 8 + ft
                    if ch * 8 + ft >= 2:
                        stream.wait("y", y, i - 2)
                    stream.issue(y, S["xc5"][:, ch % 2, ft, :],
                                 T["xn1_all"][ch * 512:(ch + 1) * 512, ft * 128:(ft + 1) * 128],
                                 transpose=True)

    @block.tensor
    def _(tr):
        t = Eng(pb, "t", tr, chained=False)
        t.wait("wld", wld5)
        dd0 = pb.cnt["dd"]
        for nh in range(2):
            if nh:
                t.wait("dd", dd0 + 1)
            inst = tr.matmul(P["pq0"][:, :], S["sb_onesf"][0:1, :],
                             S["brsrc"][0:1, nh * 512:nh * 512 + 512],
                             start=True, stop=True, skip_group_check=True)
            inst.then_inc(pb.sems["mm"], 1); pb.tick("mm")

    @block.vector
    def _(vr):
        v = Eng(pb, "v", vr)
        mm0 = pb.cnt["mm"] - 2
        for nh in range(2):
            v.wait("mm", mm0 + nh + 1)
            v.tensor_copy(S["bvbc"][:, nh * 512:nh * 512 + 512], P["pq0"][:, :])
            v.sig("dd")
        v.memset(S["kn2"][:, :], -1e30)
        v.memset(S["vt5"][:, :, :].rearrange("p t (h m) -> p t h m", m=65)[:, :, :, 64:65], 1.0)
        v.sig("dd")

    mmi = [0, 0]
    xs_i = pb.stream("xt4", 2)

    @block.tensor
    def _(tr):
        t = Eng(pb, "t", tr, chained=False)
        t.wait("wld", wld5)
        t.wait("dd", pb.cnt["dd"])
        mmi[0] = pb.cnt["mm"]
        mmi[1] = pb.cnt["vkr"]
        p1k0 = pb.cnt["p1k"]
        qk0 = pb.cnt["qkr"]
        x0 = 0
        for ch in range(8):
            xs_i.wait("t", tr, x0 + ch * 8 + 6)
            xs_i.wait("t", tr, x0 + ch * 8 + 7)
            for mt in range(4):
                if ch >= 1:
                    t.wait("p1k", p1k0 + 4 * (ch - 1) + mt + 1)
                bank = P[f"pq{mt}"]
                for kt in range(8):
                    inst = tr.matmul(bank[:, :], S["wi1sb"][:, kt, mt * 128:(mt + 1) * 128],
                                     S["xc5"][:, ch % 2, kt, :],
                                     start=(kt == 0), stop=(kt == 7), skip_group_check=True)
                inst.then_inc(pb.sems["mm"], 1); pb.tick("mm")
            t.wait("p1k", p1k0 + 4 * (ch + 1))
            if ch >= 1:
                t.wait("qkr", qk0 + ch)
            for mt in range(4):
                inst = tr.matmul(P["pq4"][0:1, 0:512], S["wi2sb"][:, mt, :],
                                 S["himp5"][:, mt, :], start=(mt == 0), stop=(mt == 3),
                                 skip_group_check=True)
            inst.then_inc(pb.sems["mm"], 1); pb.tick("mm")
            if ch >= 1:
                for c in range(4):
                    inst = tr.matmul(P["pq5"][:, (ch - 1) * 4 + c:(ch - 1) * 4 + c + 1],
                                     S["improw"][0:1, c * 128:(c + 1) * 128],
                                     S["sb_identf"][0:1, 0:1], start=(ch == 1 and c == 0),
                                     stop=False, is_transpose=True, skip_group_check=True)
                inst.then_inc(pb.sems["vkr"], 1); pb.tick("vkr")
        t.wait("qkr", qk0 + 8)
        for c in range(4):
            inst = tr.matmul(P["pq5"][:, 28 + c:28 + c + 1],
                             S["improw"][0:1, c * 128:(c + 1) * 128],
                             S["sb_identf"][0:1, 0:1], start=False,
                             stop=(c == 3), is_transpose=True, skip_group_check=True)
        inst.then_inc(pb.sems["vkr"], 1); pb.tick("vkr")

    xchunk_loader(xs_i, "mm", mmi, 5)

    @block.scalar
    def _(sr):
        s = Eng(pb, "s", sr)
        for ch in range(8):
            for mt in range(4):
                s.wait("mm", mmi[0] + 5 * ch + mt + 1)
                bank = P[f"pq{mt}"]
                s.activation(S["himp5"][:, mt, :], bank[:, :], AF.Relu,
                             bias=S["b5c"][:, mt, 4:5])
                s.sig("p1k")

    @block.vector
    def _(vr):
        v = Eng(pb, "v", vr)
        vk0 = mmi[1]
        for ch in range(8):
            v.wait("mm", mmi[0] + 5 * (ch + 1))
            if ch >= 1:
                v.wait("vkr", vk0 + ch)
            v.tensor_scalar(S["improw"][0:1, :], P["pq4"][0:1, 0:512],
                            S["b5c"][0:1, 0, 5:6], None, op0=ALU.add)
            v.sig("qkr")

    @block.vector
    def _(vr):
        v = Eng(pb, "v", vr)
        v.wait("vkr", pb.cnt["vkr"])
        v.tensor_copy(S["imp_tm"][:, :], P["pq5"][:, 0:32])
        c = S["colsc"]
        v.tensor_reduce(c[:, 0:1], S["imp_tm"][:, :], axis=AXX, op=ALU.min)
        v.reduce_max(c[:, 1:2], S["imp_tm"][:, :], axis=AXX)
        for col in range(2):
            op = ALU.min if col == 0 else ALU.max
            v.tensor_copy(c[0:64, 2:3], c[64:128, col:col + 1])
            v.tensor_tensor(c[0:64, col:col + 1], c[0:64, col:col + 1], c[0:64, 2:3], op=op)
            v.tensor_copy(c[0:32, 2:3], c[32:64, col:col + 1])
            v.tensor_tensor(c[0:32, col:col + 1], c[0:32, col:col + 1], c[0:32, 2:3], op=op)
        v.sig("shr")

    @block.tensor
    def _(tr):
        t = Eng(pb, "t", tr, chained=False)
        t.wait("shr", pb.cnt["shr"])
        for col in range(2):
            inst = tr.matmul(P["pq6"][0:1, col * 32:col * 32 + 32],
                             S["colsc"][0:32, col:col + 1], S["sb_identf"][0:32, 0:32],
                             start=(col == 0), stop=(col == 1), is_transpose=True,
                             skip_group_check=True)
        inst.then_inc(pb.sems["mm"], 1); pb.tick("mm")

    @block.vector
    def _(vr):
        v = Eng(pb, "v", vr)
        v.wait("mm", pb.cnt["mm"])
        br = S["brow5"]
        v.tensor_reduce(br[0:1, 0:1], P["pq6"][0:1, 0:32], axis=AXX, op=ALU.min)
        v.reduce_max(br[0:1, 1:2], P["pq6"][0:1, 32:64], axis=AXX)
        v.tensor_scalar_add(br[0:1, 1:2], br[0:1, 1:2], 1.0)
        v.tensor_tensor(br[0:1, 2:3], br[0:1, 0:1], br[0:1, 1:2], op=ALU.add)
        v.tensor_scalar_mul(br[0:1, 2:3], br[0:1, 2:3], 0.5)
        v.sig("pvm")

    for it in range(36):
        final = it == 35

        @block.tensor
        def _(tr, final=final):
            t = Eng(pb, "t", tr, chained=False)
            t.wait("pvm", pb.cnt["pvm"])
            src = S["brow5"][0:1, 0:1] if final else S["brow5"][0:1, 2:3]
            inst = tr.matmul(P["pq7"][:, 0:1], S["sb_onesf"][0:1, :], src,
                             start=True, stop=True, skip_group_check=True)
            inst.then_inc(pb.sems["mm"], 1); pb.tick("mm")

        @block.vector
        def _(vr, final=final):
            v = Eng(pb, "v", vr)
            v.wait("mm", pb.cnt["mm"])
            v.tensor_copy(S["tbc5"][:, :], P["pq7"][:, 0:1])
            if final:
                v.tensor_scalar(S["kmb5"][:, :], S["imp_tm"][:, :], S["tbc5"][:, 0:1],
                                None, op0=ALU.is_ge)
                v.tensor_scalar(S["kmb5"][:, :], S["kmb5"][:, :], -NEG_BIG, NEG_BIG,
                                op0=ALU.mult, op1=ALU.add)
                v.sig("pvm")
            else:
                v.tensor_scalar(S["cmpi"][:, :], S["imp_tm"][:, :], S["tbc5"][:, 0:1],
                                None, op0=ALU.is_ge)
                v.reduce_sum(S["cnti"][:, :], S["cmpi"][:, :], axis=AXX)
                v.sig("pvm")

        if not final:
            @block.tensor
            def _(tr):
                t = Eng(pb, "t", tr, chained=False)
                t.wait("pvm", pb.cnt["pvm"])
                inst = tr.matmul(P["pq7"][0:1, 32:33], S["sb_onesf"][:, 0:1], S["cnti"][:, :],
                                 start=True, stop=True, skip_group_check=True)
                inst.then_inc(pb.sems["mm"], 1); pb.tick("mm")

            @block.vector
            def _(vr):
                v = Eng(pb, "v", vr)
                v.wait("mm", pb.cnt["mm"])
                br = S["brow5"]
                v.tensor_copy(br[0:1, 3:4], P["pq7"][0:1, 32:33])
                v.tensor_scalar(br[0:1, 4:5], br[0:1, 3:4], float(L1_K), None, op0=ALU.is_ge)
                v.tensor_tensor(br[0:1, 5:6], br[0:1, 2:3], br[0:1, 0:1], op=ALU.subtract)
                v.tensor_tensor(br[0:1, 6:7], br[0:1, 5:6], br[0:1, 4:5], op=ALU.mult)
                v.tensor_tensor(br[0:1, 0:1], br[0:1, 0:1], br[0:1, 6:7], op=ALU.add)
                v.tensor_scalar(br[0:1, 7:8], br[0:1, 4:5], -1.0, 1.0, op0=ALU.mult, op1=ALU.add)
                v.tensor_tensor(br[0:1, 5:6], br[0:1, 2:3], br[0:1, 1:2], op=ALU.subtract)
                v.tensor_tensor(br[0:1, 6:7], br[0:1, 5:6], br[0:1, 7:8], op=ALU.mult)
                v.tensor_tensor(br[0:1, 1:2], br[0:1, 1:2], br[0:1, 6:7], op=ALU.add)
                v.tensor_tensor(br[0:1, 2:3], br[0:1, 0:1], br[0:1, 1:2], op=ALU.add)
                v.tensor_scalar_mul(br[0:1, 2:3], br[0:1, 2:3], 0.5)
                v.sig("pvm")

    mmk = [0]
    xs_k = pb.stream("xt4", 2)

    @block.tensor
    def _(tr):
        t = Eng(pb, "t", tr, chained=False)
        mmk[0] = pb.cnt["mm"]
        pk0 = pb.cnt["p1k"]
        fpa0b = pb.cnt["fpa0"]
        x0 = 0
        for ch in range(8):
            xs_k.wait("t", tr, x0 + ch * 8 + 6)
            xs_k.wait("t", tr, x0 + ch * 8 + 7)
            for mt in range(8):
                i = ch * 8 + mt
                if i >= 2:
                    t.wait("p1k", pk0 + 2 * (i - 1))
                bank = P[f"pq{i % 2}"]
                for kt in range(8):
                    inst = tr.matmul(bank[:, :], S["wkv5"][:, 0, kt, mt * 128:(mt + 1) * 128],
                                     S["xc5"][:, ch % 2, kt, :],
                                     start=(kt == 0), stop=(kt == 7), skip_group_check=True)
                inst.then_inc(pb.sems["mm"], 1); pb.tick("mm")
                t.wait("p1k", pk0 + 2 * i + 2)
                if i >= 1:
                    t.wait("fpa0", fpa0b + i)
                inst = tr.matmul(P["pq2"][:, :], S["s_t2f"][:, :], S["qsq5"][:, :],
                                 start=True, stop=True, skip_group_check=True)
                inst.then_inc(pb.sems["p1f"], 1); pb.tick("p1f")

    xchunk_loader(xs_k, "mm", mmk, 8)

    @block.scalar
    def _(sr):
        s = Eng(pb, "s", sr)
        pf0 = pb.cnt["p1f"] - 64
        for ch in range(8):
            for mt in range(8):
                i = ch * 8 + mt
                s.wait("mm", mmk[0] + i + 1)
                bank = P[f"pq{i % 2}"]
                s.activation(S["kp5"][:, mt, ch * 512:ch * 512 + 512], bank[:, :],
                             AF.Identity, bias=S["b5c"][:, mt, 1:2])
                s.sig("p1k")
                if i >= 1:
                    s.wait("p1f", pf0 + i)
                s.activation(S["qsq5"][:, :], bank[:, :], AF.Square,
                             bias=S["b5c"][:, mt, 1:2])
                s.sig("p1k")

    @block.vector
    def _(vr):
        v = Eng(pb, "v", vr)
        pf0 = pb.cnt["p1f"] - 64
        for ch in range(8):
            for mt in range(8):
                i = ch * 8 + mt
                v.wait("p1f", pf0 + i + 1)
                v.reduce_max(S["colsc"][:, 3:4], P["pq2"][:, :], axis=AXX)
                v.tensor_tensor(S["kn2"][:, mt:mt + 1], S["kn2"][:, mt:mt + 1],
                                S["colsc"][:, 3:4], op=ALU.max)
                v.sig("fpa0")

    mmv = [0]
    xs_v = pb.stream("xt4", 2)

    @block.tensor
    def _(tr):
        t = Eng(pb, "t", tr, chained=False)
        t.wait("dd", pb.cnt["dd"])
        mmv[0] = pb.cnt["mm"]
        fo0 = pb.cnt["fo0"]
        x0 = 0
        for ch in range(8):
            xs_v.wait("t", tr, x0 + ch * 8 + 6)
            xs_v.wait("t", tr, x0 + ch * 8 + 7)
            for twc in range(4):
                tt = ch * 4 + twc
                for nh in range(2):
                    i = tt * 2 + nh
                    if i >= 2:
                        t.wait("fo0", fo0 + i - 1)
                    bank = P[f"pq{3 + i % 2}"]
                    for kt in range(8):
                        inst = tr.matmul(bank[:, :], S["xc5"][:, ch % 2, kt, twc * 128:(twc + 1) * 128],
                                         S["wkv5"][:, 1, kt, nh * 512:nh * 512 + 512],
                                         start=(kt == 0), stop=(kt == 7), skip_group_check=True)
                    inst.then_inc(pb.sems["mm"], 1); pb.tick("mm")

    xchunk_loader(xs_v, "mm", mmv, 8)

    @block.vector
    def _(vr):
        v = Eng(pb, "v", vr)
        for tt in range(NT):
            for nh in range(2):
                i = tt * 2 + nh
                v.wait("mm", mmv[0] + i + 1)
                bank = P[f"pq{3 + i % 2}"]
                dst = S["vt5"][:, tt, :].rearrange("p (h m) -> p h m", m=65)[:, nh * 8:nh * 8 + 8, 0:64]
                v.tensor_tensor(dst, bank[:, :].rearrange("p (h m) -> p h m", m=64),
                                S["bvbc"][:, nh * 512:nh * 512 + 512].rearrange("p (h m) -> p h m", m=64),
                                op=ALU.add)
                v.sig("fo0")

    pb.barrier("p5x", drain=("wld", "xt40", "xt41"))
    pb.close_phase("p5x")


def p5_qc(pb, qc):
    nc, block = pb.nc, pb.block
    S, T, P = pb.S, pb.T, pb.P
    xq5 = pb.stream("xs", 2)
    wq5s = pb.stream("w14", 2)

    @block.gpsimd
    def _(gr):
        g = Eng(pb, "g", gr)
        p1kb = pb.cnt["p1k"]
        for mt in range(8):
            if mt >= 2:
                pb.wait("g", gr, "p1k", p1kb + 2 * (mt - 1))
            else:
                pb.wait("g", gr, "p1k", p1kb)
            wq5s.issue(gr, S["wqst"][:, mt % 2, :, :],
                       T["wq1"][:, mt * 128:(mt + 1) * 128].rearrange("(kt p) c -> p kt c", p=128))

    @block.sync
    def _(y):
        pb.wait("y", y, "p1e", pb.cnt["p1e"])
        for ft in range(8):
            if ft >= 2:
                xq5.wait("y", y, xq5.nissued - 2)
            xq5.issue(y, S["scrA"][:, ft, :],
                      T["xn1_own"][qc * 512:(qc + 1) * 512, ft * 128:(ft + 1) * 128],
                      transpose=True)
    xq50 = xq5.nissued - 8

    mmq = [0]

    @block.tensor
    def _(tr):
        t = Eng(pb, "t", tr, chained=False)
        xq5.wait("t", tr, xq50 + 6)
        xq5.wait("t", tr, xq50 + 7)
        t.wait("fs0", pb.cnt["fs0"])
        mmq[0] = pb.cnt["mm"]
        pk0 = pb.cnt["p1k"]
        fpa0q = pb.cnt["fpa0"]
        for mt in range(8):
            if mt >= 2:
                t.wait("p1k", pk0 + 2 * (mt - 1))
            bank = P[f"pq{6 + mt % 2}"]
            wq5s.wait("t", tr, wq5s.nissued - 8 + mt)
            for kt in range(8):
                inst = tr.matmul(bank[:, :], S["wqst"][:, mt % 2, kt, :],
                                 S["scrA"][:, kt, :], start=(kt == 0), stop=(kt == 7),
                                 skip_group_check=True)
            inst.then_inc(pb.sems["mm"], 1); pb.tick("mm")
            t.wait("p1k", pk0 + 2 * mt + 2)
            inst = tr.matmul(P["pq2"][:, :], S["s_t2f"][:, :], S["qsq5"][:, :],
                             start=True, stop=True, skip_group_check=True)
            inst.then_inc(pb.sems["p1f"], 1); pb.tick("p1f")
            if mt < 7:
                t.wait("fpa0", fpa0q + mt + 1)

    @block.scalar
    def _(sr):
        s = Eng(pb, "s", sr)
        pf0q = pb.cnt["p1f"] - 8
        for mt in range(8):
            s.wait("mm", mmq[0] + mt + 1)
            bank = P[f"pq{6 + mt % 2}"]
            s.activation(S["qc5"][:, mt, :], bank[:, :], AF.Identity,
                         bias=S["b5c"][:, mt, 0:1])
            s.sig("p1k")
            if mt >= 1:
                s.wait("p1f", pf0q + mt)
            s.activation(S["qsq5"][:, :], bank[:, :], AF.Square, bias=S["b5c"][:, mt, 0:1])
            s.sig("p1k")

    @block.vector
    def _(vr):
        v = Eng(pb, "v", vr)
        pf0 = pb.cnt["p1f"] - 8
        v.memset(S["qn2"][:, :], -1e30)
        for mt in range(8):
            v.wait("p1f", pf0 + mt + 1)
            v.reduce_max(S["colsc"][:, 3:4], P["pq2"][:, :], axis=AXX)
            v.tensor_tensor(S["qn2"][:, mt:mt + 1], S["qn2"][:, mt:mt + 1],
                            S["colsc"][:, 3:4], op=ALU.max)
            v.sig("fpa0")
        v.tensor_tensor(S["cm5"][:, 0:8], S["qn2"][:, :], S["kn2"][:, :], op=ALU.mult)
        v.sig("sn")

    @block.scalar
    def _(sr):
        s = Eng(pb, "s", sr)
        s.wait("sn", pb.cnt["sn"])
        s.activation(S["cm5"][:, 8:16], S["cm5"][:, 0:8], AF.Sqrt, scale=0.015625)
        s.sig("sn")

    @block.tensor
    def _(tr):
        t = Eng(pb, "t", tr, chained=False)
        t.wait("sn", pb.cnt["sn"])
        for r in range(2):
            inst = tr.matmul(P["pq2"][:, r * 8:r * 8 + 8], S["sb_onesf"][64 * r:64 * r + 1, :],
                             S["cm5"][64 * r:64 * r + 1, 8:16], start=(r == 0), stop=(r == 1),
                             skip_group_check=True)
        inst.then_inc(pb.sems["mm"], 1); pb.tick("mm")

    @block.vector
    def _(vr):
        v = Eng(pb, "v", vr)
        v.wait("mm", pb.cnt["mm"])
        v.tensor_copy(S["cbc5"][:, :], P["pq2"][:, 0:16])
        for h in range(HEADS):
            v.tensor_scalar(S["kmbc"][:, :, h], S["kmb5"][:, :],
                            S["cbc5"][:, 8 * (h % 2) + h // 2: 8 * (h % 2) + h // 2 + 1],
                            None, op0=ALU.subtract)
        v.sig("shr")

    st5 = {"mm0": 0, "pk0": 0, "fv0": 0, "qk0": 0, "fb0": 0, "vk0": 0, "p1e0": 0}

    @block.tensor
    def _(tr):
        t = Eng(pb, "t", tr, chained=False)
        t.wait("shr", pb.cnt["shr"])
        st5["mm0"] = pb.cnt["mm"]
        st5["pk0"] = pb.cnt["p1k"]
        st5["fv0"] = pb.cnt["fv1"]
        st5["qk0"] = pb.cnt["qkr"]
        st5["fb0"] = pb.cnt["fb1"]
        st5["vk0"] = pb.cnt["vkr"]
        st5["p1e0"] = pb.cnt["p1e"]
        for h in range(HEADS):
            r = 64 * (h % 2)
            f = h // 2
            if h >= 2:
                t.wait("p1e", st5["p1e0"] + h - 1)
            nb = P[f"pq{4 + h % 2}"]
            for kt in range(NT + 2):
                if kt < NT:
                    if kt >= 4:
                        t.wait("p1k", st5["pk0"] + h * NT + kt - 3)
                    sb = P[f"pq{kt % 4}"]
                    inst = tr.matmul(sb[:, :], S["kp5"][r:r + 64, f, kt * 128:(kt + 1) * 128],
                                     S["qc5"][r:r + 64, f, :], start=True, stop=True,
                                     skip_group_check=True)
                    inst.then_inc(pb.sems["mm"], 1); pb.tick("mm")
                k2 = kt - 2
                if k2 >= 0:
                    t.wait("p1k", st5["pk0"] + h * NT + k2 + 1)
                    inst = tr.matmul(nb[0:65, :], S["vt5"][:, k2, 65 * h:65 * h + 65],
                                     S["scrA"][:, k2 % 8, :], start=(k2 == 0), stop=(k2 == NT - 1),
                                     skip_group_check=True)
                    inst.then_inc(pb.sems["fv1"], 1); pb.tick("fv1")
            t.wait("qkr", st5["qk0"] + h + 1)
            inst = tr.matmul(P["pq6"][:, :], S["sb_onesf"][64:65, :], S["dsb5"][64:65, 512:1024],
                             start=True, stop=True, skip_group_check=True)
            inst.then_inc(pb.sems["fb1"], 1); pb.tick("fb1")

    @block.scalar
    def _(sr):
        s = Eng(pb, "s", sr)
        for h in range(HEADS):
            for kt in range(NT):
                s.wait("mm", st5["mm0"] + h * NT + kt + 1)
                if h * NT + kt >= 8:
                    s.wait("fv1", st5["fv0"] + h * NT + kt - 7)
                sb = P[f"pq{kt % 4}"]
                s.activation(S["scrA"][:, kt % 8, :], sb[:, :], AF.Exp,
                             bias=S["kmbc"][:, kt, h:h + 1], scale=0.125)
                s.sig("p1k")

    @block.vector
    def _(vr):
        v = Eng(pb, "v", vr)
        for h in range(HEADS):
            nb = P[f"pq{4 + h % 2}"]
            v.wait("fv1", st5["fv0"] + NT * (h + 1))
            if h >= 1:
                v.wait("fb1", st5["fb0"] + h)
            v.tensor_copy(S["dsb5"][64:65, 0:512], nb[64:65, :])
            v.reciprocal(S["dsb5"][64:65, 512:1024], S["dsb5"][64:65, 0:512])
            v.sig("qkr")
            v.wait("fb1", st5["fb0"] + h + 1)
            v.tensor_copy(S["rbc5"][:, :], P["pq6"][:, :])
            v.sig("vkr")
            r = 64 * (h % 2)
            f = h // 2
            v.tensor_tensor(S["conc5"][r:r + 64, f, :], nb[0:64, :], S["rbc5"][r:r + 64, :],
                            op=ALU.mult)
            v.sig("p1e")

    mmw = [0]

    @block.sync
    def _(y):
        fs0s = pb.cnt["fs0"]
        for tt in range(4):
            pb.wait("y", y, "fs0", fs0s + tt)
            tile = qc * 4 + tt
            y.dma_start(out=S["x1c5"][:, :], in_=T["x1_tm"][tile * 128:(tile + 1) * 128, :]
                        ).then_inc(pb.sem("xr"), 16)
            pb.tick("xr", 16)
            y.dma_start(out=S["f1b5"][:, :], in_=T["F1T"][:, tile * 128:(tile + 1) * 128],
                        transpose=True).then_inc(pb.sem("xr"), 16)
            pb.tick("xr", 16)

    xr0 = pb.cnt["xr"] - 128

    @block.tensor
    def _(tr):
        t = Eng(pb, "t", tr, chained=False)
        t.wait("p1e", pb.cnt["p1e"])
        t.wait("wld", pb.cnt["wld"])
        mmw[0] = pb.cnt["mm"]
        fs00 = pb.cnt["fs0"]
        for tt in range(4):
            if tt >= 1:
                t.wait("fs0", fs00 + tt)
            for nh in range(2):
                bank = P[f"pq{nh}"]
                for kt in range(8):
                    inst = tr.matmul(bank[:, :], S["conc5"][:, kt, tt * 128:(tt + 1) * 128],
                                     S["wo5"][:, kt, nh * 512:nh * 512 + 512],
                                     start=(kt == 0), stop=(kt == 7), skip_group_check=True)
                inst.then_inc(pb.sems["mm"], 1); pb.tick("mm")

    @block.scalar
    def _(sr):
        s = Eng(pb, "s", sr)
        fs0b = pb.cnt["fs0"]
        for tt in range(4):
            s.wait("xr", xr0 + 32 * tt + 32)
            if tt >= 1:
                s.wait("fs0", fs0b + tt)
            s.activation(S["f1f5"][:, :], S["f1b5"][:, :], AF.Identity)
            s.sig("fs1")
            s.wait("fs0", fs0b + tt + 1)
            s.dma(T["out"][(qc * 4 + tt) * 128:(qc * 4 + tt + 1) * 128, :],
                  S["outst"][:, :], "so")

    @block.vector
    def _(vr):
        v = Eng(pb, "v", vr)
        fs10 = pb.cnt["fs1"] - 4
        so0 = pb.cnt["so"] - 64
        for tt in range(4):
            v.wait("mm", mmw[0] + 2 * tt + 2)
            v.wait("fs1", fs10 + tt + 1)
            v.wait("so", so0 + 16 * tt)
            for nh in range(2):
                bank = P[f"pq{nh}"]
                sl = slice(nh * 512, nh * 512 + 512)
                v.tensor_tensor(S["outst"][:, sl], bank[:, :], S["x1c5"][:, sl], op=ALU.add)
                v.tensor_tensor(S["outst"][:, sl], S["outst"][:, sl], S["borow"][:, sl], op=ALU.add)
                v.tensor_tensor(S["outst"][:, sl], S["outst"][:, sl], S["f1f5"][:, sl], op=ALU.add)
            v.sig("fs0")


def p5(pb):
    nc, block = pb.nc, pb.block
    S, T, P = pb.S, pb.T, pb.P
    pb.open_phase("p5")
    pb.sbuf("kp5", (128, 8, N), BF16, ph="p5")
    pb.sbuf("vt5", (128, NT, 1040), BF16, ph="p5")
    pb.sbuf("b5c", (128, 8, 8), F32, ph="p5")
    pb.sbuf("s_t2f", (128, 128), F32, ph="p5")
    pb.sbuf("kmb5", (128, 32), F32, ph="p5")
    pb.sbuf("imp_tm", (128, 32), F32, ph="p5")
    pb.sbuf("qn2", (128, 8), F32, ph="p5")
    pb.sbuf("kn2", (128, 8), F32, ph="p5")
    pb.sbuf("cm5", (128, 16), F32, ph="p5")
    pb.sbuf("cbc5", (128, 16), F32, ph="p5")
    pb.sbuf("colsc", (128, 4), F32, ph="p5")
    pb.sbuf("brow5", (1, 16), F32, ph="p5")
    pb.sbuf("tbc5", (128, 1), F32, ph="p5")
    pb.sbuf("cmpi", (128, 32), BF16, ph="p5")
    pb.sbuf("cnti", (128, 1), F32, ph="p5")
    pb.sbuf("qsq5", (128, 512), F32, ph="p5")
    for i in range(8):
        pb.psum(f"pq{i}", (128, 512), F32, ph="p5")
    for s in ("so", "fv1", "fb1", "fs1"):
        pb.sem(s)

    p5_prep(pb)

    pb.open_phase("p5m")
    pb.sbuf("wqst", (128, 2, 8, 128), BF16, ph="p5m")
    pb.sbuf("wo5", (128, 8, D), BF16, ph="p5m")
    pb.sbuf("kmbc", (128, 32, HEADS), F32, ph="p5m")
    pb.sbuf("scrA", (128, 8, 512), BF16, ph="p5m")
    pb.sbuf("qc5", (128, 8, 512), BF16, ph="p5m")
    pb.sbuf("conc5", (128, 8, 512), BF16, ph="p5m")
    pb.sbuf("dsb5", (128, 1024), F32, ph="p5m")
    pb.sbuf("rbc5", (128, 512), F32, ph="p5m")
    pb.sbuf("borow", (128, D), F32, ph="p5m")
    pb.sbuf("x1c5", (128, D), F32, ph="p5m")
    pb.sbuf("f1b5", (128, D), BF16, ph="p5m")
    pb.sbuf("f1f5", (128, D), F32, ph="p5m")
    pb.sbuf("outst", (128, D), F32, ph="p5m")

    @block.gpsimd
    def _(gr):
        g = Eng(pb, "g", gr)
        g.dma(S["wo5"][:, :, :], T["wo1"][:, :].rearrange("(kt p) m -> p kt m", p=128), "wld")

    @block.gpsimd
    def _(gr):
        g = Eng(pb, "g", gr)
        g.dma(S["dsb5"][0:1, :], T["b1vec"][3:4, :], "xld")

    @block.tensor
    def _(tr):
        t = Eng(pb, "t", tr, chained=False)
        t.wait("xld", pb.cnt["xld"])
        dd0b = pb.cnt["dd"]
        for nh in range(2):
            if nh:
                t.wait("dd", dd0b + 1)
            inst = tr.matmul(P["pq3"][:, :], S["sb_onesf"][0:1, :],
                             S["dsb5"][0:1, nh * 512:nh * 512 + 512],
                             start=True, stop=True, skip_group_check=True)
            inst.then_inc(pb.sems["mm"], 1); pb.tick("mm")

    @block.vector
    def _(vr):
        v = Eng(pb, "v", vr)
        mm0 = pb.cnt["mm"] - 2
        for nh in range(2):
            v.wait("mm", mm0 + nh + 1)
            v.tensor_copy(S["borow"][:, nh * 512:nh * 512 + 512], P["pq3"][:, :])
            v.sig("dd")

    for qc in range(4):
        p5_qc(pb, qc)

    @block.sync
    def _(y):
        y.wait_ge(pb.sems["so"], pb.cnt["so"])

    pb.close_phase("p5m")
    pb.close_phase("p5")


def finish_placeholder(pb):
    block = pb.block
    S, T = pb.S, pb.T
    pb.open_phase("fin")
    pb.sbuf("zout", (128, D), F32, ph="fin")
    pb.sem("so")

    @block.vector
    def _(vr):
        v = Eng(pb, "v", vr)
        v.memset(S["zout"][:, :], 0.0)
        v.sig("ve")

    @block.sync
    def _(y):
        pb.wait("y", y, "ve", pb.cnt["ve"])
        pb.wait("y", y, "gd", pb.cnt["gd"])
        for i in range(NTO):
            y.dma_start(out=T["out"][i * 128:(i + 1) * 128, :],
                        in_=S["zout"][:, :]).then_inc(pb.sem("so"), 16); pb.tick("so", 16)
        y.wait_ge(pb.sem("so"), pb.cnt["so"])
    pb.close_phase("fin")


def build_program(dbg=()):
    pb = PB()
    pb.dbg = set(dbg)
    nc = pb.nc

    pb.din("x", (N, D))
    pb.din("Wh", (D, D)); pb.din("Wq", (D, D)); pb.din("Wk", (D, D)); pb.din("Wv", (D, D))
    pb.din("Wo0", (D, D))
    pb.din("bvec", (8, D))
    pb.din("Wqkv", (FFN_DEPTH, D, 3 * D))
    pb.din("Wo_f", (FFN_DEPTH, D, D))
    pb.din("W1", (FFN_DEPTH, D, W1C))
    pb.din("W2", (FFN_DEPTH, DIM_INNER, D))
    pb.din("wq1", (D, D)); pb.din("wk1", (D, D)); pb.din("wv1", (D, D)); pb.din("wo1", (D, D))
    pb.din("b1vec", (8, D)); pb.din("b1vecT", (D, 8))
    pb.din("wi1", (D, 512)); pb.din("wi2", (1, 512))
    pb.din("gate_w", (D, N_EXP))
    pb.din("moe_w1", (4, D, 4 * D))
    pb.din("moe_w2", (4, 4 * D, D))
    pb.din("identb", (128, 128), BF16)
    pb.din("rot", (128, 4 * DOMF), BF16)
    pb.din("maskP", (128, 4, WIN))
    pb.din("maskF", (128, 4, WIN))
    pb.din("two2f", (2, 128))
    pb.din("two2fT", (128, 2), BF16)
    pb.din("qsksc", (FFN_DEPTH, 128, 2))
    pb.din("lnpc", (FFN_DEPTH, 128, 16))
    pb.din("b1c", (FFN_DEPTH, 128, 44))
    pb.din("b2c", (FFN_DEPTH, 128, 8))
    pb.din("onesb", (128, 128), BF16)
    pb.din("triu128", (128, 128))
    pb.din("iotasel", (128, MOE_K))
    pb.din("iotacol", (128, 1))
    pb.din("two2f128", (128, 128))
    pb.din("offs", (1, 8), I32)
    pb.din("gates", (128, 8))

    pb.dout("out", (HALF, D))

    pb.dscr("xinnerT", (D, DOMF))
    pb.dscr("A_T", (D, HALF))
    pb.dscr("x1_tm", (HALF, D))
    pb.dscr("F1T", (D, HALF), BF16)
    pb.dscr("xn1_own", (HALF, D), BF16)
    pb.dscr("xn1_all", (N, D), BF16)
    pb.dscr("slab_own", (SLAB, D), BF16)
    pb.dscr("slab_all", (2 * SLAB, D), BF16)
    pb.dscr("h2w_own", (4 * MOE_K, D), BF16)
    pb.dscr("h2w_all", (8 * MOE_K, D), BF16)

    pb.din("bmapP", (16, D), BF16)
    pb.din("bvecT", (D, 8))

    pb.sbuf("sb_identf", (128, 128), F32)
    pb.sbuf("sb_gates", (128, 8), F32)
    pb.sbuf("sb_onesf", (128, 128), F32)
    pb.open_phase("g2")
    pb.sbuf("sb_onesb", (128, 128), BF16, ph="g2")
    pb.sbuf("sb_bmapPf", (16, D), F32, ph="g2")

    pb.block = pb.ctx.enter_context(nc.Block())

    pb.open_phase("x01")
    pb.sbuf("xnT", (128, 8 * HALF), BF16, ph="x01")
    pb.open_phase("p1p")
    pb.sbuf("sb_bkbv", (1, 2 * D), BF16, ph="p1p")
    pb.sbuf("kv_sb", (128, 520), BF16, ph="p1p")
    pb.sbuf("kv_sb2", (128, 520), BF16, ph="p1p")
    pb.open_phase("xoth")
    pb.sbuf("xnTo", (128, 8 * HALF), BF16, ph="xoth")
    pb.open_phase("p0")
    p0(pb)
    pb.close_phase("p0")
    if STOP_AFTER != "P0":
        p1(pb)
    else:
        pb.close_phase("xoth")
    pb.close_phase("p1p")
    pb.close_phase("x01")
    if STOP_AFTER not in ("P0", "P1"):
        p2(pb)
    pb.close_phase("g2")
    if STOP_AFTER not in ("P0", "P1", "P2"):
        p3(pb)
    if STOP_AFTER not in ("P0", "P1", "P2", "P3"):
        p4(pb)
    if STOP_AFTER is None:
        p5(pb)
    else:
        finish_placeholder(pb)

    pb.ctx.close()
    return pb


def _prep_inputs(inputs):
    perm = _sigma_perm()
    A_q, B_q, A_k, B_k = _rotary_tables()
    identb = np.eye(128, dtype=np.float32).astype(ml_dtypes.bfloat16)
    onesb = np.ones((128, 128), dtype=np.float32).astype(ml_dtypes.bfloat16)
    bmapP = np.zeros((HEADS, D), dtype=np.float32)
    for r in range(HEADS):
        h = 2 * (r % 8) + r // 8
        bmapP[r, h * DH:(h + 1) * DH] = 1.0
    bmapP = bmapP.astype(ml_dtypes.bfloat16)
    triu128 = np.triu(np.ones((128, 128), dtype=np.float32))
    iotasel = np.broadcast_to(np.arange(1, MOE_K + 1, dtype=np.float32)[None, :],
                              (128, MOE_K)).copy()
    iotacol = np.arange(1, 129, dtype=np.float32)[:, None].copy()
    two2f128 = np.zeros((128, 128), np.float32)
    two2f128[0:64, 0] = 1.0
    two2f128[64:128, 64] = 1.0

    x_all = np.asarray(inputs["L0_attn_x"], dtype=np.float32)
    bvec = np.stack([
        np.asarray(inputs["L0_attn_bh"])[perm],
        np.asarray(inputs["L0_attn_bq"])[perm],
        np.asarray(inputs["L0_attn_bk"])[perm],
        np.asarray(inputs["L0_attn_bv"])[perm],
        np.asarray(inputs["L0_attn_beta"])[perm],
        np.asarray(inputs["L0_attn_ln_w"])[perm],
        np.asarray(inputs["L0_attn_ln_b"])[perm],
        np.asarray(inputs["L0_attn_bo"]),
    ]).astype(np.float32)
    b1vec = np.zeros((8, D), dtype=np.float32)
    b1vec[0, :] = np.asarray(inputs["L1_attn_bq"])
    b1vec[1, :] = np.asarray(inputs["L1_attn_bk"])
    b1vec[2, :] = np.asarray(inputs["L1_attn_bv"])
    b1vec[3, :] = np.asarray(inputs["L1_attn_bo"])
    b1vec[4, 0:512] = np.asarray(inputs["L1_attn_bi1"])
    b1vec[5, 0:1] = np.asarray(inputs["L1_attn_bi2"])
    qs_ = np.asarray(inputs["L0_ffn_q_scale"]).astype(np.float32)
    ks_ = np.asarray(inputs["L0_ffn_k_scale"]).astype(np.float32)
    qsksc = np.zeros((FFN_DEPTH, 128, 2), np.float32)
    for l in range(FFN_DEPTH):
        qsksc[l, 0:64, 0] = qs_[l]; qsksc[l, 64:128, 0] = qs_[l]
        qsksc[l, 0:64, 1] = ks_[l]; qsksc[l, 64:128, 1] = ks_[l]
    lng = np.asarray(inputs["L0_ffn_ln_g"]).astype(np.float32)
    lnb_ = np.asarray(inputs["L0_ffn_ln_b"]).astype(np.float32)
    lnpc = np.zeros((FFN_DEPTH, 128, 16), np.float32)
    for l in range(FFN_DEPTH):
        for mt in range(8):
            lnpc[l, :, 2 * mt] = lng[l, mt * 128:(mt + 1) * 128]
            lnpc[l, :, 2 * mt + 1] = lnb_[l, mt * 128:(mt + 1) * 128]
    b1_ = np.asarray(inputs["L0_ffn_b1"]).astype(np.float32)
    b1c = np.zeros((FFN_DEPTH, 128, 44), np.float32)
    for l in range(FFN_DEPTH):
        for part, base in ((0, 0), (22, 2730)):
            for t_ in range(22):
                nrow = 128 if t_ < 21 else 42
                b1c[l, 0:nrow, part + t_] = b1_[l, base + t_ * 128: base + t_ * 128 + nrow]
    b2_ = np.asarray(inputs["L0_ffn_b2"]).astype(np.float32)
    b2c = np.zeros((FFN_DEPTH, 128, 8), np.float32)
    for l in range(FFN_DEPTH):
        for mt in range(8):
            b2c[l, :, mt] = b2_[l, mt * 128:(mt + 1) * 128]
    two2f = np.zeros((2, 128), np.float32)
    two2f[0, 0:64] = 1.0; two2f[1, 64:128] = 1.0
    two2fT = two2f.T.copy().astype(ml_dtypes.bfloat16)
    kk_ = np.arange(2 * WIN)[:, None]
    qq_ = np.arange(WIN)[None, :]
    band = ((kk_ >= qq_) & (kk_ <= qq_ + WIN))
    maskP_add = np.where(band, 0.0, NEG_BIG).astype(np.float32)
    maskF_add = maskP_add.copy()
    maskF_add[0:WIN, :] = NEG_BIG
    maskP_P = maskP_add.reshape(4, 128, WIN).transpose(1, 0, 2).copy()
    maskF_P = maskF_add.reshape(4, 128, WIN).transpose(1, 0, 2).copy()
    rots = [A_q, B_q, A_k, B_k]

    in_maps = []
    for core in range(8):
        b, half = core // 2, core % 2
        own_lo = half * HALF
        x_rel = np.concatenate([x_all[b, own_lo:own_lo + HALF],
                                x_all[b, HALF - own_lo:HALF - own_lo + HALF]], axis=0)
        lo = own_lo - SLAB
        idx = np.clip(np.arange(lo, own_lo + HALF), 0, N - 1)
        rot = np.concatenate([np.vstack([t[:, idx], t[:, idx]]) for t in rots],
                             axis=1).astype(ml_dtypes.bfloat16)
        offs = np.zeros((1, 8), dtype=np.int32)
        offs[0, 0] = HALF - own_lo
        gates = np.zeros((128, 8), dtype=np.float32)
        gates[:, 0] = float(half)
        gates[:, 1] = 1.0 - float(half)
        m = {
            "x": np.ascontiguousarray(x_rel),
            "Wh": np.ascontiguousarray(np.asarray(inputs["L0_attn_Wh"])[:, perm]),
            "Wq": np.ascontiguousarray(np.asarray(inputs["L0_attn_Wq"])[:, perm]),
            "Wk": np.ascontiguousarray(np.asarray(inputs["L0_attn_Wk"])[:, perm]),
            "Wv": np.ascontiguousarray(np.asarray(inputs["L0_attn_Wv"])[:, perm]),
            "Wo0": np.ascontiguousarray(np.asarray(inputs["L0_attn_Wo"])[perm, :]),
            "bvec": bvec,
            "Wqkv": np.asarray(inputs["L0_ffn_Wqkv"]),
            "Wo_f": np.asarray(inputs["L0_ffn_Wo"]),
            "W1": np.asarray(inputs["L0_ffn_W1"]),
            "W2": np.asarray(inputs["L0_ffn_W2"]),
            "qsksc": qsksc, "lnpc": lnpc, "b1c": b1c, "b2c": b2c,
            "two2f": two2f, "two2fT": two2fT,
            "maskP": maskP_P, "maskF": (maskF_P if half == 0 else maskP_P),
            "wq1": np.ascontiguousarray(np.asarray(inputs["L1_attn_wq"]).T),
            "wk1": np.ascontiguousarray(np.asarray(inputs["L1_attn_wk"]).T),
            "wv1": np.ascontiguousarray(np.asarray(inputs["L1_attn_wv"]).T),
            "wo1": np.ascontiguousarray(np.asarray(inputs["L1_attn_wo"]).T),
            "b1vec": b1vec,
            "wi1": np.ascontiguousarray(np.asarray(inputs["L1_attn_wi1"]).T),
            "wi2": np.ascontiguousarray(np.asarray(inputs["L1_attn_wi2"])),
            "gate_w": np.asarray(inputs["L1_ffn_gate_w"]),
            "moe_w1": np.asarray(inputs["L1_ffn_w1"])[half * 4:half * 4 + 4],
            "moe_w2": np.asarray(inputs["L1_ffn_w2"])[half * 4:half * 4 + 4],
            "identb": identb,
            "rot": rot,
            "onesb": onesb, "bmapP": bmapP,
            "bvecT": np.ascontiguousarray(bvec.T),
            "b1vecT": np.ascontiguousarray(b1vec.T),
            "triu128": triu128, "iotasel": iotasel, "iotacol": iotacol,
            "two2f128": two2f128,
            "offs": offs, "gates": gates,
        }
        in_maps.append(m)
    return in_maps


def kernel(**inputs):
    in_maps = _prep_inputs(inputs)
    pb = build_program(dbg=DEBUG_DUMPS)
    res = run_bass_kernel_spmd(pb.nc, in_maps, core_ids=list(range(8)))
    out = np.zeros((B, N, D), dtype=np.float32)
    for core in range(8):
        b, half = core // 2, core % 2
        out[b, half * HALF:(half + 1) * HALF, :] = res.results[core]["out"]
    if DEBUG_DUMPS:
        kernel._dbg = res.results
    return out


# version 02
# speedup vs baseline: 1.0094x; 1.0094x over previous
import contextlib
import numpy as np
import ml_dtypes
import concourse.bass as bass
import concourse.mybir as mybir
from concourse.bass_utils import run_bass_kernel_spmd

F32 = mybir.dt.float32
BF16 = mybir.dt.bfloat16
I32 = mybir.dt.int32
AF = mybir.ActivationFunctionType
ALU = mybir.AluOpType
AXX = mybir.AxisListType.X

B, N, D = 4, 4096, 1024
HALF = 2048
HEADS, DH = 16, 64
WIN = 256
FFN_DEPTH = 6
DIM_INNER = 2730
W1C = 2 * DIM_INNER
SLAB = 258
DOMF = HALF + SLAB
N_EXP = 8
MOE_K = 1024
L1_K = 1228
NEG_BIG = -30000.0
NT = N // 128
NTO = HALF // 128

DEBUG_DUMPS = ()
STOP_AFTER = None
SCOPES = False


def _sigma_perm():
    p = np.zeros(D, dtype=np.int64)
    for h in range(HEADS):
        for d in range(DH):
            p[h * DH + d] = d * HEADS + h
    return p


def _rotary_tables():
    half = np.arange(0, DH, 2, dtype=np.float32)
    inv_freq = (1.0 / 10000.0 ** (half / DH)).astype(np.float32)
    t = np.arange(N, dtype=np.float32)
    freqs = t[:, None] * inv_freq[None, :]
    freqs = np.concatenate([freqs, freqs], -1)
    base_scale = ((half + 0.4 * DH) / (1.4 * DH)).astype(np.float32)
    power = (t - N // 2) / (WIN // 2)
    scale = base_scale[None, :] ** power[:, None]
    scale = np.concatenate([scale, scale], -1).astype(np.float32)
    cos = np.cos(freqs).astype(np.float32)
    sin = np.sin(freqs).astype(np.float32)
    inv = (1.0 / scale).astype(np.float32)
    A_q = (cos * scale).T.copy()
    A_k = (cos * inv).T.copy()
    B_q = (sin * scale).T.copy(); B_q[0:32, :] *= -1.0
    B_k = (sin * inv).T.copy(); B_k[0:32, :] *= -1.0
    return A_q, B_q, A_k, B_k


def _local_masks():
    k = np.arange(2 * WIN)[:, None]
    q = np.arange(WIN)[None, :]
    m = ((k >= q) & (k <= q + WIN)).astype(np.float32)
    m0 = m.copy()
    m0[0:WIN, :] = 0.0
    return m.astype(ml_dtypes.bfloat16), m0.astype(ml_dtypes.bfloat16)


class PB:
    def __init__(self):
        self.nc = bass.Bass("TRN2", target_bir_lowering=False)
        self.ctx = contextlib.ExitStack()
        self.T = {}
        self.S = {}
        self.P = {}
        self.sems = {}
        self.cnt = {}
        self.seen = {}
        self.block = None
        self.dbg = set()
        self.stacks = {}

    def din(self, name, shape, dtype=F32):
        self.T[name] = self.nc.dram_tensor(name, list(shape), dtype, kind="ExternalInput")
        return self.T[name]

    def dout(self, name, shape, dtype=F32):
        self.T[name] = self.nc.dram_tensor(name, list(shape), dtype, kind="ExternalOutput")
        return self.T[name]

    def dscr(self, name, shape, dtype=F32):
        self.T[name] = self.nc.dram_tensor(name, list(shape), dtype)
        return self.T[name]

    _serial = 0

    def _uniq(self, name):
        PB._serial += 1
        return "%s__%d" % (name, PB._serial)

    def sbuf(self, name, shape, dtype, ph=None):
        stk = self.stacks[ph] if ph else self.ctx
        self.S[name] = stk.enter_context(self.nc.sbuf_tensor(self._uniq(name), list(shape), dtype))
        return self.S[name]

    def psum(self, name, shape, dtype=F32, ph=None):
        stk = self.stacks[ph] if ph else self.ctx
        self.P[name] = stk.enter_context(self.nc.psum_tensor(self._uniq(name), list(shape), dtype))
        return self.P[name]

    def open_phase(self, ph):
        self.stacks[ph] = contextlib.ExitStack()

    def close_phase(self, ph):
        self.stacks.pop(ph).close()

    def sem(self, name):
        if name not in self.sems:
            self.sems[name] = self.ctx.enter_context(self.nc.semaphore(name))
            self.cnt[name] = 0
        return self.sems[name]

    def tick(self, name, k=1):
        self.cnt[name] += k
        return self.cnt[name]

    def wait(self, ekey, eng, name, value=None):
        v = self.cnt[name] if value is None else value
        if v <= 0:
            return
        if self.seen.setdefault(ekey, {}).get(name, 0) >= v:
            return
        eng.wait_ge(self.sems[name], v)
        self.seen[ekey][name] = v

    def stream(self, name, nbuf=2):
        return _Stream(self, name, nbuf)

    def barrier(self, tag, drain=()):
        s = self.sem("barx")
        block = self.block
        def fy(e):
            for dn in drain:
                self.wait("y", e, dn, self.cnt[dn])
            e.nop().then_inc(s, 1)
        block.sync(fy)
        for attr in ("tensor", "vector", "scalar", "gpsimd"):
            def f(e, s=s):
                e.nop().then_inc(s, 1)
            getattr(block, attr)(f)
        self.tick("barx", 5)
        tgt = self.cnt["barx"]
        for ekey, attr in (("t", "tensor"), ("v", "vector"), ("s", "scalar"),
                           ("g", "gpsimd"), ("y", "sync")):
            def f(e, ekey=ekey, tgt=tgt):
                self.wait(ekey, e, "barx", tgt)
            getattr(block, attr)(f)


class _Stream:
    def __init__(self, pb, name, nbuf):
        self.pb = pb
        self.name = name
        self.nbuf = nbuf
        self.sems = [pb.sem(f"{name}{i}") for i in range(nbuf)]
        self.base = [pb.cnt[f"{name}{i}"] for i in range(nbuf)]
        self.nissued = 0

    def issue(self, eng, out, in_, **kw):
        i = self.nissued
        eng.dma_start(out=out, in_=in_, **kw).then_inc(self.sems[i % self.nbuf], 16)
        self.pb.tick(f"{self.name}{i % self.nbuf}", 16)
        self.nissued += 1
        return i

    def wait(self, ekey, eng, i):
        self.pb.wait(ekey, eng, f"{self.name}{i % self.nbuf}",
                     self.base[i % self.nbuf] + 16 * (i // self.nbuf + 1))


class Eng:

    def __init__(self, pb, ekey, raw, chained=True):
        self.pb = pb
        self.ekey = ekey
        self.raw = raw
        self.chained = chained
        self.chname = ekey + "ch"
        if chained:
            pb.sem(self.chname)

    def _pre(self):
        if self.chained:
            self.pb.wait(self.ekey, self.raw, self.chname)

    def __getattr__(self, name):
        fn = getattr(self.raw, name)
        if name in ("wait_ge", "nop", "dma_start"):
            return fn
        def wrapper(*a, **kw):
            self._pre()
            inst = fn(*a, **kw)
            if self.chained:
                inst.then_inc(self.pb.sems[self.chname], 1)
                self.pb.tick(self.chname)
            return inst
        return wrapper

    def wait(self, sem, v=None):
        self.pb.wait(self.ekey, self.raw, sem, v)

    def sig(self, sem, k=1):
        self._pre()
        self.raw.nop().then_inc(self.pb.sem(sem), k)
        self.pb.tick(sem, k)

    def dma(self, out, in_, sem, **kw):
        self._pre()
        self.raw.dma_start(out=out, in_=in_, **kw).then_inc(self.pb.sem(sem), 16)
        self.pb.tick(sem, 16)


def p0(pb):
    nc, block = pb.nc, pb.block
    S, T, P = pb.S, pb.T, pb.P

    pb.sbuf("p0_x", (128, 2 * D), F32, ph="p0")
    pb.sbuf("p0_xf", (128, 2 * D), F32, ph="p0")
    pb.sbuf("p0_xsq", (128, D), F32, ph="p0")
    pb.sbuf("p0_st", (128, 8), F32, ph="p0")
    pb.sbuf("p0_stage", (128, 1040), F32, ph="p0")
    pb.psum("ps_trf", (128, 512), F32, ph="p0")
    pb.psum("ps_trf2", (128, 512), F32, ph="p0")

    for s in ("wld", "xld", "cst", "mm", "ve", "se", "gd", "st", "cc",
              "p0c", "p0v", "p0s", "p0n", "p0e"):
        pb.sem(s)

    xs = pb.stream("xs", 2)

    @block.gpsimd
    def _(gr):
        g = Eng(pb, "g", gr)
        for dst, srcn in (("sb_identf", "identb"),
                          ("sb_onesb", "onesb"), ("sb_gates", "gates"),
                          ("sb_onesf", "onesb"), ("sb_bmapPf", "bmapP"),
                          ("sb_bc0f", "bc0f"), ("sb_bc64f", "bc64f")):
            g.dma(S[dst][:, :], T[srcn][:, :], "cst")

    @block.sync
    def _(y_raw):
        y = Eng(pb, "y", y_raw, chained=False)
        for i in range(NT):
            if i >= 2:
                pb.wait("y", y_raw, "p0c", i - 1)
            xs.issue(y_raw, S["p0_x"][:, (i % 2) * D:(i % 2) * D + D],
                     T["x"][i * 128:(i + 1) * 128, :])

    @block.scalar
    def _(sr):
        s = Eng(pb, "s", sr)
        def ship(j):
            s.wait("ve", j + 1)
            s.dma(bass.AP(T["xinnerT"], SLAB + j * 128,
                          [[DOMF, 128], [128 * DOMF, 8], [1, 128]]),
                  S["p0_stage"][:, 0:1024].rearrange("p (f t) -> p f t", f=8),
                  "st")
        for i in range(NT):
            s.wait("p0v", i + 1)
            s.activation(S["p0_st"][:, 6:7], S["p0_st"][:, 2:3], AF.Sqrt)
            s.sig("p0s")
            if i >= 1 and i - 1 < NTO:
                ship(i - 1)

    @block.vector
    def _(vr):
        v = Eng(pb, "v", vr)
        for i in range(NT + 1):
            if i < NT:
                v.wait("cst", 112)
                xs.wait("v", vr, i)
                if i >= 2:
                    v.wait("mm", 8 * (i - 1))
                xt = S["p0_x"][:, (i % 2) * D:(i % 2) * D + D]
                st = S["p0_st"]
                v.reduce_sum(st[:, 0:1], xt, axis=AXX)
                v.tensor_scalar_mul(st[:, 0:1], st[:, 0:1], 1.0 / D)
                v.tensor_tensor(st[:, 4:5], st[:, 0:1], st[:, 0:1], op=ALU.mult)
                v.tensor_tensor(S["p0_xsq"][:, :], xt, xt, op=ALU.mult)
                v.reduce_sum(st[:, 1:2], S["p0_xsq"][:, :], axis=AXX)
                v.tensor_scalar_mul(st[:, 1:2], st[:, 1:2], 1.0 / D)
                v.tensor_tensor(st[:, 1:2], st[:, 1:2], st[:, 4:5], op=ALU.subtract)
                v.tensor_scalar_add(st[:, 2:3], st[:, 1:2], 1e-5)
                v.sig("p0v")
                v.wait("p0s", i + 1)
                v.reciprocal(st[:, 3:4], st[:, 6:7])
                v.tensor_scalar(S["p0_xf"][:, (i % 2) * D:(i % 2) * D + D], xt,
                                st[:, 0:1], st[:, 3:4], op0=ALU.subtract, op1=ALU.mult)
                v.sig("p0n")
                v.sig("p0c")
            if i >= 1:
                j = i - 1
                v.wait("mm", 8 * i)
                if 1 <= j < NTO:
                    v.wait("st", 16 * j)
                for fg in range(2):
                    ps = P["ps_trf"] if fg == 0 else P["ps_trf2"]
                    dst = S["xnT"] if j < NTO else S["xnTo"]
                    jj = j if j < NTO else j - NTO
                    for f in range(4):
                        ft = fg * 4 + f
                        v.tensor_copy(dst[:, ft * HALF + jj * 128: ft * HALF + (jj + 1) * 128],
                                      ps[:, f * 128:(f + 1) * 128])
                        if j < NTO:
                            v.tensor_copy(S["p0_stage"][:, ft * 128:(ft + 1) * 128],
                                          ps[:, f * 128:(f + 1) * 128])
                    v.sig("p0e")
                if j < NTO:
                    v.sig("ve")

    @block.tensor
    def _(tr):
        t = Eng(pb, "t", tr, chained=False)
        t.wait("cst", 112)
        for i in range(NT):
            t.wait("p0n", i + 1)
            if i >= 1:
                t.wait("p0e", 2 * i)
            xf = S["p0_xf"][:, (i % 2) * D:(i % 2) * D + D]
            for fg in range(2):
                ps = P["ps_trf"] if fg == 0 else P["ps_trf2"]
                for f in range(4):
                    ft = fg * 4 + f
                    inst = tr.matmul(ps[:, f * 128:(f + 1) * 128],
                                     xf[:, ft * 128:(ft + 1) * 128],
                                     S["sb_identf"][:, :], start=True, stop=True,
                                     is_transpose=True)
                    inst.then_inc(pb.sems["mm"], 1)
                    pb.tick("mm")

    for ci, (coff, clen) in enumerate(((0, 129), (129, 129))):
        def slab_v(vr, coff=coff, clen=clen):
            v = Eng(pb, "v", vr)
            v.wait("mm", pb.cnt["mm"])
            v.wait("cst", 112)
            v.wait("st", pb.cnt["st"])
            lo = HALF - SLAB + coff
            for f in range(8):
                v.tensor_scalar_mul(S["p0_stage"][:, f * clen:(f + 1) * clen],
                                    S["xnTo"][:, f * HALF + lo: f * HALF + lo + clen],
                                    S["sb_gates"][:, 0:1])
            v.sig("ve")
        block.vector(slab_v)

        def slab_s(sr, coff=coff, clen=clen):
            s = Eng(pb, "s", sr)
            s.wait("ve", pb.cnt["ve"])
            s.dma(bass.AP(T["xinnerT"], coff,
                          [[DOMF, 128], [128 * DOMF, 8], [1, clen]]),
                  S["p0_stage"][:, 0:8 * clen].rearrange("p (f t) -> p f t", f=8),
                  "st")
        block.scalar(slab_s)

    pb.barrier("p0", drain=("st",))

    if "xnT" in pb.dbg:
        dd = pb.dout("d_xnT", (128, 8 * HALF), BF16)
        ddo = pb.dout("d_xnTo", (128, 8 * HALF), BF16)
        def dumpf(gr, dd=dd, ddo=ddo):
            g = Eng(pb, "g", gr)
            g.dma(dd[:, :], S["xnT"][:, :], "gd")
            g.dma(ddo[:, :], S["xnTo"][:, :], "gd")
        block.gpsimd(dumpf)
    if "xinnerT" in pb.dbg:
        dd2 = pb.dout("d_xinnerT", (D, DOMF), F32)
        def dumpf2(gr, dd2=dd2):
            g = Eng(pb, "g", gr)
            g.wait("st", pb.cnt["st"])
            g.dma(dd2[:, :], T["xinnerT"][:, :], "gd")
        block.gpsimd(dumpf2)


def p1(pb):
    nc, block = pb.nc, pb.block
    S, T, P = pb.S, pb.T, pb.P
    CH = 512
    NCH = HALF // CH

    pb.open_phase("p1a")
    pb.sbuf("wkv", (128, 2, 8, D), BF16, ph="p1a")
    pb.sbuf("kt_sb", (128, 2 * D), BF16, ph="p1a")
    pb.sbuf("v_aug", (128, 2 * 1040), BF16, ph="p1a")
    pb.psum("pka", (128, 512), F32, ph="p1a")
    pb.psum("pkb", (128, 512), F32, ph="p1a")
    pb.psum("pva", (128, 512), F32, ph="p1a")
    pb.psum("pvb", (128, 512), F32, ph="p1a")
    for g4 in range(4):
        pb.psum(f"pkv{g4}", (128, 260), F32, ph="p1a")
    for s in ("p1w", "p1k", "p1e", "p1f", "sa", "dd"):
        pb.sem(s)

    @block.gpsimd
    def _(gr):
        g = Eng(pb, "g", gr)
        g.dma(S["wkv"][:, 0, :, :], T["Wk"][:, :].rearrange("(kt p) m -> p kt m", p=128), "p1w")
        g.dma(S["wkv"][:, 1, :, :], T["Wv"][:, :].rearrange("(kt p) m -> p kt m", p=128), "p1w")
        g.dma(S["sb_bkbv"][0:1, :], T["bvec"][2:4, :], "p1w")

    mm0 = [None]

    @block.tensor
    def _(tr):
        t = Eng(pb, "t", tr, chained=False)
        t.wait("p1w", 48)
        t.wait("barx", 5)
        mm0[0] = pb.cnt["mm"]
        for i in range(NT):
            if i >= 1:
                t.wait("p1k", i)
                t.wait("p1e", i)
            for half_ in range(2):
                ps = P["pka"] if half_ == 0 else P["pkb"]
                xsrc = S["xnT"] if i < NTO else S["xnTo"]
                ii = i if i < NTO else i - NTO
                for kt in range(8):
                    tr.matmul(ps[:, :], xsrc[:, kt * HALF + ii * 128:kt * HALF + (ii + 1) * 128],
                              S["wkv"][:, 0, kt, half_ * 512:half_ * 512 + 512],
                              start=(kt == 0), stop=False, skip_group_check=True)
                inst = tr.matmul(ps[:, :], S["sb_onesb"][0:1, 0:128],
                                 S["sb_bkbv"][0:1, half_ * 512:half_ * 512 + 512],
                                 start=False, stop=True, skip_group_check=True)
                inst.then_inc(pb.sems["mm"], 1); pb.tick("mm")
            for half_ in range(2):
                ps = P["pva"] if half_ == 0 else P["pvb"]
                for kt in range(8):
                    tr.matmul(ps[:, :], xsrc[:, kt * HALF + ii * 128:kt * HALF + (ii + 1) * 128],
                              S["wkv"][:, 1, kt, half_ * 512:half_ * 512 + 512],
                              start=(kt == 0), stop=False, skip_group_check=True)
                inst = tr.matmul(ps[:, :], S["sb_onesb"][0:1, 0:128],
                                 S["sb_bkbv"][0:1, D + half_ * 512:D + half_ * 512 + 512],
                                 start=False, stop=True, skip_group_check=True)
                inst.then_inc(pb.sems["mm"], 1); pb.tick("mm")
            t.wait("p1k", i + 1)
            t.wait("p1e", i + 1)
            ko = (i % 2) * D
            vo = (i % 2) * 1040
            for h in range(16):
                ps = P[f"pkv{h // 4}"]
                inst = tr.matmul(ps[0:64, 65 * (h % 4):65 * (h % 4) + 65],
                                 S["kt_sb"][:, ko + 64 * h:ko + 64 * h + 64],
                                 S["v_aug"][:, vo + 65 * h:vo + 65 * h + 65],
                                 start=(i == 0 and h % 4 == 0),
                                 stop=(i == NT - 1 and h % 4 == 3),
                                 skip_group_check=True)
            inst.then_inc(pb.sems["p1f"], 1); pb.tick("p1f")

    @block.scalar
    def _(sr):
        s = Eng(pb, "s", sr)
        for i in range(NT):
            ko = (i % 2) * D
            if i >= 2:
                s.wait("p1f", i - 1)
            s.wait("mm", mm0[0] + 4 * i + 2)
            s.activation(S["kt_sb"][:, ko:ko + 512], P["pka"][:, :], AF.Sigmoid)
            s.activation(S["kt_sb"][:, ko + 512:ko + 1024], P["pkb"][:, :], AF.Sigmoid)
            s.sig("p1k")

    @block.vector
    def _(vr):
        v = Eng(pb, "v", vr)
        v.memset(S["v_aug"][:, :], 1.0)
        for i in range(NT):
            vo = (i % 2) * 1040
            if i >= 2:
                v.wait("p1f", i - 1)
            v.wait("mm", mm0[0] + 4 * i + 4)
            v.tensor_copy(S["v_aug"][:, vo:vo + 520].rearrange("p (h m) -> p h m", m=65)[:, :, 0:64],
                          P["pva"][:, :].rearrange("p (h m) -> p h m", m=64))
            v.tensor_copy(S["v_aug"][:, vo + 520:vo + 1040].rearrange("p (h m) -> p h m", m=65)[:, :, 0:64],
                          P["pvb"][:, :].rearrange("p (h m) -> p h m", m=64))
            v.sig("p1e")

    @block.vector
    def _(vr):
        v = Eng(pb, "v", vr)
        v.wait("p1f", NT)
        v.tensor_copy(S["kv_sb"][0:64, 0:260], P["pkv0"][0:64, :])
        v.tensor_copy(S["kv_sb"][0:64, 260:520], P["pkv1"][0:64, :])
        v.tensor_copy(S["kv_sb2"][0:64, 0:260], P["pkv2"][0:64, :])
        v.tensor_copy(S["kv_sb2"][0:64, 260:520], P["pkv3"][0:64, :])
        v.tensor_copy(S["kv_sb"][64:128, :], S["kv_sb"][0:64, :])
        v.tensor_copy(S["kv_sb2"][64:128, :], S["kv_sb2"][0:64, :])
        v.sig("ve")

    pb.barrier("p1a")
    pb.close_phase("p1a")
    pb.close_phase("xoth")

    pb.open_phase("p1b")
    pb.sbuf("whq", (128, 3, 8, D), BF16, ph="p1b")
    pb.sbuf("bcols", (128, 64), F32, ph="p1b")
    pb.sbuf("hbcol", (128, 8), F32, ph="p1b")
    pb.sbuf("qc", (128, 8, CH), BF16, ph="p1b")
    pb.sbuf("hc", (128, 8, CH), BF16, ph="p1b")
    pb.sbuf("attnc", (128, 8, CH), BF16, ph="p1b")
    pb.sbuf("attn2c", (128, 8, CH), BF16, ph="p1b")
    pb.sbuf("gatec", (128, 8, CH), BF16, ph="p1b")
    pb.sbuf("den_pack", (128, 4096), F32, ph="p1b")
    pb.sbuf("den_sb", (128, CH), F32, ph="p1b")
    pb.sbuf("recipb", (128, CH), F32, ph="p1b")
    pb.sbuf("lnrow", (128, 4 * CH), F32, ph="p1b")
    pb.sbuf("mursb", (128, 2 * CH), F32, ph="p1b")

    def p1b_memset(vr):
        v = Eng(pb, "v", vr)
        v.memset(S["den_sb"][:, :], 0.0)
        v.memset(S["recipb"][:, :], 0.0)
        v.memset(S["lnrow"][:, :], 0.0)
        v.sig("ve")
    block.vector(p1b_memset)
    pb.sbuf("tmpf", (128, CH), F32, ph="p1b")
    pb.sbuf("astage", (128, 8, CH), F32, ph="p1b")
    NPP = 6
    for k in range(NPP):
        pb.psum(f"pp{k}", (128, 512), F32, ph="p1b")
    for k in range(NPP):
        pb.sem(f"f{k}")
    psched = [0] * NPP

    def t_use(tr, k):
        pb.wait("t", tr, f"f{k}", psched[k])
        psched[k] += 1

    def done(eng_wrap, k):
        eng_wrap.sig(f"f{k}")

    @block.gpsimd
    def _(gr):
        g = Eng(pb, "g", gr)
        g.wait("barx", 10)
        g.dma(S["whq"][:, 0, :, :], T["Wh"][:, :].rearrange("(kt p) m -> p kt m", p=128), "p1w")
        g.dma(S["whq"][:, 1, :, :], T["Wq"][:, :].rearrange("(kt p) m -> p kt m", p=128), "p1w")
        g.dma(S["whq"][:, 2, :, :], T["Wo0"][:, :].rearrange("(kt p) m -> p kt m", p=128), "p1w")

    @block.sync
    def _(y):
        pb.wait("y", y, "barx", 10)
        y.dma_start(out=bass.AP(S["bcols"], 0, [[64, 128], [8, 8], [1, 8]]),
                    in_=bass.AP(T["bvecT"], 0, [[8, 128], [1024, 8], [1, 8]])
                    ).then_inc(pb.sem("sa"), 16)
        pb.tick("sa", 16)
    sa_bcols = pb.cnt["sa"]

    @block.vector
    def _(vr):
        v = Eng(pb, "v", vr)
        v.wait("sa", 16)
        bc = S["bcols"][:, :].rearrange("p (mt j) -> p mt j", j=8)
        v.tensor_tensor(S["hbcol"][:, :], bc[:, :, 0], bc[:, :, 4], op=ALU.add)
        v.sig("ve")
    ve_hb = pb.cnt["ve"]

    for c in range(NCH):
        c0 = c * CH

        qmm = []
        hmm = []
        @block.tensor
        def _(tr, qmm=qmm, hmm=hmm):
            pb.wait("t", tr, "p1w", pb.cnt["p1w"])
            for which, wsl, rec in (("q", 1, qmm), ("h", 0, hmm)):
                for mt in range(8):
                    k = mt % 2
                    t_use(tr, k)
                    ps = P[f"pp{k}"]
                    for kt in range(8):
                        inst = tr.matmul(ps[:, :], S["whq"][:, wsl, kt, mt * 128:(mt + 1) * 128],
                                         S["xnT"][:, kt * HALF + c0:kt * HALF + c0 + CH],
                                         start=(kt == 0), stop=(kt == 7), skip_group_check=True)
                    inst.then_inc(pb.sems["mm"], 1); pb.tick("mm")
                    rec.append((pb.cnt["mm"], k))

        @block.scalar
        def _(sr, qmm=qmm, hmm=hmm):
            s = Eng(pb, "s", sr)
            s.wait("sa", sa_bcols)
            for mt in range(8):
                mmv, k = qmm[mt]
                s.wait("mm", mmv)
                s.activation(S["qc"][:, mt, :], P[f"pp{k}"][:, :], AF.Sigmoid,
                             bias=S["bcols"][:, mt * 8 + 1:mt * 8 + 2])
                done(s, k)
            s.wait("ve", ve_hb)
            for mt in range(8):
                mmv, k = hmm[mt]
                s.wait("mm", mmv)
                s.activation(S["hc"][:, mt, :], P[f"pp{k}"][:, :], AF.Identity,
                             bias=S["hbcol"][:, mt:mt + 1])
                done(s, k)
            s.sig("p1k")

        ndmm = []
        @block.tensor
        def _(tr, ndmm=ndmm):
            t = Eng(pb, "t", tr, chained=False)
            t.wait("p1k", pb.cnt["p1k"])
            for h in range(16):
                k = 2 + h % 4
                t_use(tr, k)
                kvs = S["kv_sb"] if h < 8 else S["kv_sb2"]
                b = 64 * (h % 2)
                inst = tr.matmul(P[f"pp{k}"][0:65, :], kvs[b:b + 64, 65 * (h % 8):65 * (h % 8) + 65],
                                 S["qc"][b:b + 64, h // 2, :],
                                 start=True, stop=True, skip_group_check=True)
                inst.then_inc(pb.sems["mm"], 1); pb.tick("mm")
                ndmm.append((pb.cnt["mm"], k))

        @block.vector
        def _(vr, ndmm=ndmm):
            v = Eng(pb, "v", vr)
            for h in range(16):
                mmv, k = ndmm[h]
                v.wait("mm", mmv)
                ps = P[f"pp{k}"]
                v.tensor_copy(S["den_pack"][64 * (h % 2):64 * (h % 2) + 1,
                                            (h // 2) * CH:(h // 2) * CH + CH],
                              ps[64:65, :])
                v.tensor_copy(S["attnc"][64 * (h % 2):64 * (h % 2) + 64, h // 2, :],
                              ps[0:64, :])
                done(v, k)
            v.sig("ve")

        @block.sync
        def _(y):
            pb.wait("y", y, "ve", pb.cnt["ve"])
            for bq in range(2):
                y.dma_start(out=S["den_sb"][8 * bq:8 * bq + 8, :],
                            in_=S["den_pack"][64 * bq:64 * bq + 1, :]
                            ).then_inc(pb.sem("sa"), 16)
                pb.tick("sa", 16)

        @block.vector
        def _(vr):
            v = Eng(pb, "v", vr)
            v.wait("sa", pb.cnt["sa"])
            v.reciprocal(S["recipb"][0:16, :], S["den_sb"][0:16, :])
            v.sig("ve")

        bcmm = []
        @block.tensor
        def _(tr, bcmm=bcmm):
            pb.wait("t", tr, "ve", pb.cnt["ve"])
            for mt in range(8):
                k = mt % 2
                t_use(tr, k)
                inst = tr.matmul(P[f"pp{k}"][:, :], S["sb_bmapPf"][:, mt * 128:(mt + 1) * 128],
                                 S["recipb"][:, :], start=True, stop=True,
                                 skip_group_check=True)
                inst.then_inc(pb.sems["mm"], 1); pb.tick("mm")
                bcmm.append((pb.cnt["mm"], k))

        @block.vector
        def _(vr, bcmm=bcmm):
            v = Eng(pb, "v", vr)
            for mt in range(8):
                mmv, k = bcmm[mt]
                v.wait("mm", mmv)
                v.tensor_tensor(S["attnc"][:, mt, :], S["attnc"][:, mt, :],
                                P[f"pp{k}"][:, :], op=ALU.mult)
                done(v, k)
            v.sig("ve")

        @block.scalar
        def _(sr):
            s = Eng(pb, "s", sr)
            s.wait("ve", pb.cnt["ve"])
            s.activation(S["attn2c"][:, :, :], S["attnc"][:, :, :], AF.Square)
            s.sig("se")

        stmm = []
        @block.tensor
        def _(tr, stmm=stmm):
            pb.wait("t", tr, "se", pb.cnt["se"])
            for which, srcn in ((2, "attnc"), (3, "attn2c")):
                t_use(tr, which)
                for kt in range(8):
                    inst = tr.matmul(P[f"pp{which}"][0:1, :], S["sb_onesb"][:, 0:1],
                                     S[srcn][:, kt, :], start=(kt == 0), stop=(kt == 7),
                                     skip_group_check=True)
                inst.then_inc(pb.sems["mm"], 1); pb.tick("mm")
                stmm.append(pb.cnt["mm"])

        @block.vector
        def _(vr, stmm=stmm):
            v = Eng(pb, "v", vr)
            v.wait("mm", stmm[-1])
            L = S["lnrow"]
            v.tensor_scalar_mul(L[0:1, 0:CH], P["pp2"][0:1, :], 1.0 / D)
            v.tensor_scalar_mul(L[0:1, CH:2 * CH], P["pp3"][0:1, :], 1.0 / D)
            done(v, 2); done(v, 3)
            v.tensor_tensor(L[0:1, 2 * CH:3 * CH], L[0:1, 0:CH], L[0:1, 0:CH], op=ALU.mult)
            v.tensor_tensor(L[0:1, CH:2 * CH], L[0:1, CH:2 * CH], L[0:1, 2 * CH:3 * CH], op=ALU.subtract)
            v.tensor_scalar_add(L[0:1, CH:2 * CH], L[0:1, CH:2 * CH], 1e-5)
            v.sig("ve")

        @block.scalar
        def _(sr):
            s = Eng(pb, "s", sr)
            s.wait("ve", pb.cnt["ve"])
            s.activation(S["lnrow"][0:1, 3 * CH:4 * CH], S["lnrow"][0:1, CH:2 * CH], AF.Sqrt)
            s.sig("se")

        @block.vector
        def _(vr):
            v = Eng(pb, "v", vr)
            v.wait("se", pb.cnt["se"])
            L = S["lnrow"]
            v.reciprocal(L[0:1, 2 * CH:3 * CH], L[0:1, 3 * CH:4 * CH])
            v.tensor_tensor(L[0:1, 0:CH], L[0:1, 0:CH], L[0:1, 2 * CH:3 * CH], op=ALU.mult)
            v.sig("ve")

        mursmm = []
        @block.tensor
        def _(tr, mursmm=mursmm):
            pb.wait("t", tr, "ve", pb.cnt["ve"])
            for which, off in ((4, 0), (5, 2 * CH)):
                t_use(tr, which)
                inst = tr.matmul(P[f"pp{which}"][:, :], S["sb_bc0f"][:, :],
                                 S["lnrow"][:, off:off + CH], start=True, stop=True,
                                 skip_group_check=True)
                inst.then_inc(pb.sems["mm"], 1); pb.tick("mm")
                mursmm.append(pb.cnt["mm"])

        @block.vector
        def _(vr, mursmm=mursmm):
            v = Eng(pb, "v", vr)
            v.wait("mm", mursmm[-1])
            v.wait("sa", sa_bcols)
            v.tensor_copy(S["mursb"][:, 0:CH], P["pp4"][:, :])
            v.tensor_copy(S["mursb"][:, CH:2 * CH], P["pp5"][:, :])
            done(v, 4); done(v, 5)
            for mt in range(8):
                v.tensor_tensor(S["tmpf"][:, :], S["attnc"][:, mt, :],
                                S["mursb"][:, CH:2 * CH], op=ALU.mult)
                v.tensor_tensor(S["tmpf"][:, :], S["tmpf"][:, :],
                                S["mursb"][:, 0:CH], op=ALU.subtract)
                v.tensor_scalar(S["tmpf"][:, :], S["tmpf"][:, :],
                                S["bcols"][:, mt * 8 + 5:mt * 8 + 6],
                                S["bcols"][:, mt * 8 + 6:mt * 8 + 7],
                                op0=ALU.mult, op1=ALU.add)
                v.tensor_tensor(S["gatec"][:, mt, :], S["tmpf"][:, :],
                                S["hc"][:, mt, :], op=ALU.mult)
            v.sig("ve")

        womm = []
        @block.tensor
        def _(tr, womm=womm):
            pb.wait("t", tr, "ve", pb.cnt["ve"])
            if c >= 1:
                pb.wait("t", tr, "sa", pb.cnt["sa"])
            for mt in range(8):
                k = mt % 2
                t_use(tr, k)
                for kt in range(8):
                    inst = tr.matmul(P[f"pp{k}"][:, :], S["whq"][:, 2, kt, mt * 128:(mt + 1) * 128],
                                     S["gatec"][:, kt, :], start=(kt == 0), stop=(kt == 7),
                                     skip_group_check=True)
                inst.then_inc(pb.sems["mm"], 1); pb.tick("mm")
                womm.append((pb.cnt["mm"], k))

        @block.scalar
        def _(sr, womm=womm):
            s = Eng(pb, "s", sr)
            for mt in range(8):
                mmv, k = womm[mt]
                s.wait("mm", mmv)
                s.activation(S["astage"][:, mt, :], P[f"pp{k}"][:, :], AF.Relu,
                             bias=S["bcols"][:, mt * 8 + 7:mt * 8 + 8])
                done(s, k)
            s.sig("se")

        @block.sync
        def _(y):
            pb.wait("y", y, "se", pb.cnt["se"])
            y.dma_start(out=bass.AP(T["A_T"], c0, [[HALF, 128], [128 * HALF, 8], [1, CH]]),
                        in_=S["astage"][:, :, :]).then_inc(pb.sem("sa"), 16)
            pb.tick("sa", 16)

    pb.barrier("p1", drain=("sa",))
    pb.close_phase("p1b")

    if "A_T" in pb.dbg:
        dd = pb.dout("d_A_T", (D, HALF), F32)
        def dumpf(gr, dd=dd):
            g = Eng(pb, "g", gr)
            g.dma(dd[:, :], T["A_T"][:, :], "gd")
        block.gpsimd(dumpf)


P2_LAYERS = FFN_DEPTH
HCH = 11


def p2_exchange(pb, st):
    nc, block = pb.nc, pb.block
    S, T = pb.S, pb.T
    groups = [[0, 1], [2, 3], [4, 5], [6, 7]]

    @block.gpsimd
    def _(gr):
        g = Eng(pb, "g", gr)
        g.wait("xw", pb.cnt["xw"])
        g.dma(T["slab_own"][:, :],
              bass.AP(T["xinnerT"], DOMF - SLAB, [[DOMF, D], [1, SLAB]]), "gd")
        g.wait("gd", pb.cnt["gd"])
        gr.collective_compute(
            "AllGather", ALU.bypass, replica_groups=groups,
            ins=[T["slab_own"][:, :].opt()], outs=[T["slab_all"][:, :].opt()],
        ).then_inc(pb.sem("cc"), 1)
        pb.tick("cc")
        g.wait("cc", pb.cnt["cc"])
        g.wait("shc", st["shc"])
        g.dma(S["shad"][:, :, 0:SLAB],
              bass.AP(T["slab_all"], 0, [[SLAB, 128], [128 * SLAB, 8], [1, SLAB]]), "gd")

    @block.vector
    def _(vr):
        v = Eng(pb, "v", vr)
        v.wait("gd", pb.cnt["gd"])
        v.wait("xw2", st["xw2"])
        v.tensor_scalar_mul(S["xch"][:, :, 0:SLAB], S["shad"][:, :, 0:SLAB],
                            S["sb_gates"][:, 0:1])
        v.sig("ve")
        v.sig("shc"); st["shc"] += 1

    @block.scalar
    def _(sr):
        s = Eng(pb, "s", sr)
        s.wait("ve", pb.cnt["ve"])
        s.dma(bass.AP(T["xinnerT"], 0, [[DOMF, 128], [128 * DOMF, 8], [1, SLAB]]),
              S["xch"][:, :, 0:SLAB], "xw")
        s.sig("xw2"); st["xw2"] += 1


def p2_attn(pb, l, st):
    nc, block = pb.nc, pb.block
    S, T, P = pb.S, pb.T, pb.P

    @block.gpsimd
    def _(gr):
        g = Eng(pb, "g", gr)
        g.dma(S["wqkv"][:, :, :], T["Wqkv"][l].rearrange("(kt p) m -> p kt m", p=128), "wld")
        g.dma(S["wof"][:, :, :], T["Wo_f"][l].rearrange("(kt p) m -> p kt m", p=128), "wld")
        g.dma(S["qkcols"][:, :], T["qsksc"][l], "wld")
        g.dma(S["rott"][:, :], T["rot"][:, :], "wld")
        g.dma(S["s_maskP"][:, :, :], T["maskP"][:, :, :], "wld")
        g.dma(S["s_maskF"][:, :, :], T["maskF"][:, :, :], "wld")
        g.dma(S["s_two2f"][:, :], T["two2f"][:, :], "wld")
        g.dma(S["s_two2fT"][:, :], T["two2fT"][:, :], "wld")
    wld_l = pb.cnt["wld"]

    @block.vector
    def _(vr):
        v = Eng(pb, "v", vr)
        v.wait("wld", wld_l)
        Lr = S["nrow"]
        v.tensor_tensor(Lr[0:64, 0:1], S["qkcols"][0:64, 0:1], S["qkcols"][0:64, 1:2], op=ALU.mult)
        v.tensor_scalar(Lr[0:64, 1:2], Lr[0:64, 0:1], -1.0, None, op0=ALU.mult)
        v.tensor_tensor(Lr[0:64, 0:1], Lr[0:64, 0:1], Lr[0:64, 1:2], op=ALU.max)
        v.tensor_copy(Lr[0:32, 2:3], Lr[32:64, 0:1])
        v.tensor_tensor(Lr[0:32, 0:1], Lr[0:32, 0:1], Lr[0:32, 2:3], op=ALU.max)
        v.sig("ve")

    @block.sync
    def _(y):
        pb.wait("y", y, "ve", pb.cnt["ve"])
        y.dma_start(out=S["nrow"][0:1, 32:64], in_=S["nrow"][0:32, 0:1]
                    ).then_inc(pb.sem("sa"), 16)
        pb.tick("sa", 16)

    @block.vector
    def _(vr):
        v = Eng(pb, "v", vr)
        v.wait("sa", pb.cnt["sa"])
        v.reduce_max(S["nrow"][0:1, 0:1], S["nrow"][0:1, 32:64], axis=AXX)
        v.tensor_scalar(S["nrow"][0:1, 1:2], S["nrow"][0:1, 0:1], -8.0, None, op0=ALU.mult)
        v.sig("ve")

    @block.tensor
    def _(tr):
        pb.wait("t", tr, "ve", pb.cnt["ve"])
        pb.wait("t", tr, "pne", st["pne"])
        inst = tr.matmul(P["pe5"][:, 0:1], S["sb_bc0f"][:, :],
                         S["nrow"][:, 1:2], start=True, stop=True, skip_group_check=True)
        inst.then_inc(pb.sems["mm"], 1); pb.tick("mm")

    @block.vector
    def _(vr):
        v = Eng(pb, "v", vr)
        v.wait("mm", pb.cnt["mm"])
        v.tensor_copy(S["negc"][:, :], P["pe5"][:, 0:1])
        v.sig("pne"); st["pne"] += 1

    for seg in range(-1, 4):
        p2_attn_seg(pb, l, seg, wld_l, st)


def p2_attn_seg(pb, l, seg, wld_l, st):
    nc, block = pb.nc, pb.block
    S, T, P = pb.S, pb.T, pb.P
    if seg < 0:
        ntok, xlo = WIN, 1
        kslot = [2]
        tiles = list(range(8, 16))
    else:
        ntok, xlo = 2 * WIN, SLAB + seg * 512
        kslot = [(seg * 2) % 3, (seg * 2 + 1) % 3]
        tiles = list(range(16))
    nt_t = ntok // 128

    @block.sync
    def _(y):
        pb.wait("y", y, "shc", st["shc"])
        pb.wait("y", y, "xw2", st["xw2"])
        pb.wait("y", y, "xw", pb.cnt["xw"])
        y.dma_start(out=S["xch"][:, :, 0:ntok + 1],
                    in_=bass.AP(T["xinnerT"], xlo - 1,
                                [[DOMF, 128], [128 * DOMF, 8], [1, ntok + 1]])
                    ).then_inc(pb.sem("xr"), 16)
        pb.tick("xr", 16)

    @block.vector
    def _(vr):
        v = Eng(pb, "v", vr)
        v.wait("xr", pb.cnt["xr"])
        v.tensor_copy(S["shad"][:, :, 0:ntok + 1], S["xch"][:, :, 0:ntok + 1])
        v.sig("shr")
    shr_now = pb.cnt["shr"]

    n_tl = len(tiles)
    qmm = {}
    nrmm = {}
    bcmm = {}

    @block.tensor
    def _(tr):
        pb.wait("t", tr, "wld", wld_l)
        pb.wait("t", tr, "shr", shr_now)
        for i in range(n_tl + 2):
            if i >= 2:
                j = i - 2
                pb.wait("t", tr, "nr", st["nr_b"] + j + 1)
                pb.wait("t", tr, "fb%d" % (j % 2), st["fb"][j % 2]); st["fb"][j % 2] += 1
                pbc = P["pg7"] if j % 2 == 0 else P["ph8"]
                inst = tr.matmul(pbc[:, 0:ntok], S["s_two2f"][0:2, :],
                                 S["rn2"][0:2, (j % 2) * 512:(j % 2) * 512 + ntok],
                                 start=True, stop=True, skip_group_check=True)
                inst.then_inc(pb.sems["mm"], 1); pb.tick("mm")
                bcmm[j] = pb.cnt["mm"]
            if i >= 1 and i - 1 < n_tl:
                j = i - 1
                pb.wait("t", tr, "sq", st["sq_b"] + j + 1)
                pb.wait("t", tr, "fn%d" % (j % 2), st["fn"][j % 2]); st["fn"][j % 2] += 1
                pn = P["pe5"] if j % 2 == 0 else P["pf6"]
                inst = tr.matmul(pn[0:2, 0:ntok], S["s_two2fT"][:, 0:2],
                                 S["sqs"][:, (j % 2) * 512:(j % 2) * 512 + ntok],
                                 start=True, stop=True, skip_group_check=True)
                inst.then_inc(pb.sems["mm"], 1); pb.tick("mm")
                nrmm[j] = pb.cnt["mm"]
            if i < n_tl:
                mt = tiles[i]
                k = i % 2
                pb.wait("t", tr, "fq%d" % k, st["fq"][k]); st["fq"][k] += 1
                ps = P["pqa"] if k == 0 else P["pqb"]
                for kt in range(8):
                    off = 1 if kt < 4 else 0
                    inst = tr.matmul(ps[:, 0:ntok], S["wqkv"][:, kt, mt * 128:(mt + 1) * 128],
                                     S["shad"][:, kt, off:off + ntok],
                                     start=(kt == 0), stop=(kt == 7), skip_group_check=True)
                inst.then_inc(pb.sems["mm"], 1); pb.tick("mm")
                qmm[i] = pb.cnt["mm"]

    @block.scalar
    def _(sr):
        s = Eng(pb, "s", sr)
        for i in range(n_tl):
            s.wait("mm", qmm[i])
            if i >= 2:
                s.wait("mm", nrmm[i - 2])
            s.activation(S["sqs"][:, (i % 2) * 512:(i % 2) * 512 + ntok],
                         (P["pqa"] if i % 2 == 0 else P["pqb"])[:, 0:ntok], AF.Square)
            s.sig("sq")
            s.wait("mm", nrmm[i])
            if i >= 2:
                s.wait("nr", st["nr_b"] + i - 1)
            s.activation(S["rn1"][0:2, (i % 2) * 512:(i % 2) * 512 + ntok],
                         (P["pe5"] if i % 2 == 0 else P["pf6"])[0:2, 0:ntok], AF.Sqrt)
            s.sig("sn")
    st["sq_b"] += n_tl

    @block.vector
    def _(vr):
        v = Eng(pb, "v", vr)
        for i in range(n_tl):
            mt = tiles[i]
            is_q = mt < 8
            k = i % 2
            sl = k * 512
            v.wait("sn", st["sn_b"] + i + 1)
            v.tensor_scalar_max(S["rn1"][0:2, sl:sl + ntok], S["rn1"][0:2, sl:sl + ntok], 1e-12)
            v.reciprocal(S["rn2"][0:2, sl:sl + ntok], S["rn1"][0:2, sl:sl + ntok])
            v.sig("nr")
            v.sig("fn%d" % k)
            v.wait("mm", bcmm[i])
            v.tensor_copy(S["rnb"][:, 0:ntok], (P["pg7"] if k == 0 else P["ph8"])[:, 0:ntok])
            v.sig("fb%d" % k)
            col = 0 if is_q else 1
            ps = P["pqa"] if k == 0 else P["pqb"]
            v.scalar_tensor_tensor(S["t1f"][:, 0:ntok], ps[:, 0:ntok],
                                   S["qkcols"][:, col:col + 1], S["rnb"][:, 0:ntok],
                                   op0=ALU.mult, op1=ALU.mult)
            for hw in range(4):
                swp = (hw // 2) * 64 + (1 - hw % 2) * 32
                dwp = (hw // 2) * 64 + (hw % 2) * 32
                v.scalar_tensor_tensor(S["t3f"][dwp:dwp + 32, 0:ntok],
                                       ps[swp:swp + 32, 0:ntok],
                                       S["qkcols"][swp:swp + 32, col:col + 1],
                                       S["rnb"][swp:swp + 32, 0:ntok],
                                       op0=ALU.mult, op1=ALU.mult)
            v.sig("fq%d" % k)
            tb = (0 if is_q else 2 * DOMF) + xlo
            bb = (DOMF if is_q else 3 * DOMF) + xlo
            v.tensor_tensor(S["t2f"][:, 0:ntok], S["t1f"][:, 0:ntok],
                            S["rott"][:, tb:tb + ntok], op=ALU.mult)
            v.tensor_tensor(S["t3f"][:, 0:ntok], S["t3f"][:, 0:ntok],
                            S["rott"][:, bb:bb + ntok], op=ALU.mult)
            if is_q:
                v.tensor_tensor(S["qpb"][:, mt, 0:ntok], S["t2f"][:, 0:ntok],
                                S["t3f"][:, 0:ntok], op=ALU.add)
            else:
                mtk = mt - 8
                for w in range(len(kslot)):
                    v.tensor_tensor(S["kring"][:, kslot[w], mtk, :],
                                    S["t2f"][:, w * WIN:(w + 1) * WIN],
                                    S["t3f"][:, w * WIN:(w + 1) * WIN], op=ALU.add)
            v.sig("qkr")
    st["sn_b"] += n_tl
    st["nr_b"] += n_tl

    vmm = {}
    @block.tensor
    def _(tr):
        for tc in range(nt_t):
            for nh in range(2):
                k = nh
                pb.wait("t", tr, "fv%d" % k, st["fv"][k]); st["fv"][k] += 1
                ps = P["pvA"] if k == 0 else P["pvB"]
                for kt in range(8):
                    off = 1 if kt < 4 else 0
                    inst = tr.matmul(ps[:, :],
                                     S["shad"][:, kt, off + tc * 128:off + tc * 128 + 128],
                                     S["wqkv"][:, kt, 2048 + nh * 512:2048 + nh * 512 + 512],
                                     start=(kt == 0), stop=(kt == 7), skip_group_check=True)
                inst.then_inc(pb.sems["mm"], 1); pb.tick("mm")
                vmm[(tc, nh)] = pb.cnt["mm"]

    @block.vector
    def _(vr):
        v = Eng(pb, "v", vr)
        for tc in range(nt_t):
            ws = kslot[0] if seg < 0 else kslot[tc // 2]
            tt = tc if seg < 0 else tc % 2
            for nh in range(2):
                v.wait("mm", vmm[(tc, nh)])
                ps = P["pvA"] if nh == 0 else P["pvB"]
                v.tensor_copy(
                    S["vring"][:, ws, tt, nh * 520:nh * 520 + 520].rearrange(
                        "p (h m) -> p h m", m=65)[:, :, 0:64],
                    ps[:, :].rearrange("p (h m) -> p h m", m=64))
                v.sig("fv%d" % nh)
        v.sig("vkr")

    if seg < 0:
        @block.vector
        def _(vr):
            v = Eng(pb, "v", vr)
            v.wait("qkr", pb.cnt["qkr"])
            v.wait("vkr", pb.cnt["vkr"])
            v.sig("shc")
        st["shc"] += 1
        return

    qkr_now = pb.cnt["qkr"]
    vkr_now = pb.cnt["vkr"]
    for w in range(2):
        wslot = kslot[w]
        pslot = (wslot - 1) % 3
        first = (seg == 0 and w == 0)
        qoff = w * WIN
        scmm = {}
        vomm = {}
        PSLOT = {0: ("pg7", 0), 1: ("ph8", 0), 2: ("pg7", 256), 3: ("ph8", 256)}

        def emit_vmm(tr, h):
            pb.wait("t", tr, "pex", st["pex_b"] + h + 1)
            nmp, coff = PSLOT[h % 4]
            pb.wait("t", tr, "fo%d" % (h % 4), st["fo"][h % 4]); st["fo"][h % 4] += 1
            for kc in range(4):
                sl = pslot if kc < 2 else wslot
                inst = tr.matmul(P[nmp][0:65, coff:coff + WIN],
                                 S["vring"][:, sl, kc % 2, 65 * h:65 * h + 65],
                                 S["probs"][:, (h % 2) * 4 + kc, :],
                                 start=(kc == 0), stop=(kc == 3), skip_group_check=True)
            inst.then_inc(pb.sems["pvm"], 1); pb.tick("pvm")
            vomm[h] = pb.cnt["pvm"]

        @block.tensor
        def _(tr):
            pb.wait("t", tr, "qkr", qkr_now)
            pb.wait("t", tr, "vkr", vkr_now)
            for h in range(16):
                for kc in range(4):
                    k = (h * 4 + kc) % 2
                    sl = pslot if kc < 2 else wslot
                    pb.wait("t", tr, "fs%d" % k, st["fs"][k]); st["fs"][k] += 1
                    b = 64 * (h % 2)
                    pn = P["pe5"] if k == 0 else P["pf6"]
                    inst = tr.matmul(pn[:, 0:WIN],
                                     S["kring"][b:b + 64, sl, h // 2,
                                                (kc % 2) * 128:(kc % 2) * 128 + 128],
                                     S["qpb"][b:b + 64, h // 2, qoff:qoff + WIN],
                                     start=True, stop=True, skip_group_check=True)
                    inst.then_inc(pb.sems["mm"], 1); pb.tick("mm")
                    scmm[(h, kc)] = pb.cnt["mm"]
                if h >= 2:
                    emit_vmm(tr, h - 2)
            emit_vmm(tr, 14)
            emit_vmm(tr, 15)

        @block.vector
        def _(vr):
            v = Eng(pb, "v", vr)
            def evac(h):
                nmp, coff = PSLOT[h % 4]
                v.wait("pvm", vomm[h])
                v.tensor_copy(S["den_pack"][64 * (h % 2):64 * (h % 2) + 1,
                                            (h // 2) * WIN:(h // 2) * WIN + WIN],
                              P[nmp][64:65, coff:coff + WIN])
                v.tensor_copy(S["oall"][64 * (h % 2):64 * (h % 2) + 64, h // 2, :],
                              P[nmp][0:64, coff:coff + WIN])
                v.sig("fo%d" % (h % 4))
            for h in range(16):
                if h >= 2:
                    v.wait("pex", st["pex_b"] + h - 1)
                for kc in range(4):
                    k = (h * 4 + kc) % 2
                    v.wait("mm", scmm[(h, kc)])
                    mk = S["s_maskF"] if first else S["s_maskP"]
                    v.scalar_tensor_tensor(S["scf"][:, (h % 2) * 4 + kc, :],
                                           (P["pe5"] if k == 0 else P["pf6"])[:, 0:WIN],
                                           8.0, mk[:, kc, :], op0=ALU.mult, op1=ALU.add)
                    v.sig("fs%d" % k)
                v.sig("sce")
                if h >= 2:
                    evac(h - 2)
            evac(14)
            evac(15)
            v.sig("ve")

        @block.scalar
        def _(sr):
            s = Eng(pb, "s", sr)
            s.wait("pne", st["pne"])
            for h in range(16):
                s.wait("sce", st["sce_b"] + h + 1)
                if h >= 2:
                    s.wait("pvm", st["pvm_b"] + h - 1)
                s.activation(S["probs"][:, (h % 2) * 4:(h % 2) * 4 + 4, :],
                             S["scf"][:, (h % 2) * 4:(h % 2) * 4 + 4, :], AF.Exp,
                             bias=S["negc"][:, 0:1])
                s.sig("pex")
        st["sce_b"] += 16

        st["pex_b"] += 16

        st["pvm_b"] += 16

        @block.sync
        def _(y):
            pb.wait("y", y, "ve", pb.cnt["ve"])
            for bq in range(2):
                y.dma_start(out=S["den_sb"][8 * bq:8 * bq + 8, 0:WIN],
                            in_=S["den_pack"][64 * bq:64 * bq + 1, 0:8 * WIN]
                            ).then_inc(pb.sem("sa"), 16)
                pb.tick("sa", 16)

        @block.vector
        def _(vr):
            v = Eng(pb, "v", vr)
            v.wait("sa", pb.cnt["sa"])
            v.reciprocal(S["recipb"][0:16, 0:WIN], S["den_sb"][0:16, 0:WIN])
            v.sig("ve")

        rbmm = {}
        @block.tensor
        def _(tr):
            pb.wait("t", tr, "ve", pb.cnt["ve"])
            for mt in range(8):
                k = mt % 2
                pb.wait("t", tr, "fq%d" % k, st["fq"][k]); st["fq"][k] += 1
                ps = P["pqa"] if k == 0 else P["pqb"]
                inst = tr.matmul(ps[:, 0:WIN],
                                 S["sb_bmapPf"][:, mt * 128:(mt + 1) * 128],
                                 S["recipb"][:, 0:WIN], start=True, stop=True,
                                 skip_group_check=True)
                inst.then_inc(pb.sems["mm"], 1); pb.tick("mm")
                rbmm[mt] = pb.cnt["mm"]

        @block.vector
        def _(vr):
            v = Eng(pb, "v", vr)
            for mt in range(8):
                k = mt % 2
                v.wait("mm", rbmm[mt])
                v.tensor_copy(S["rbc2"][:, :], (P["pqa"] if k == 0 else P["pqb"])[:, 0:WIN])
                v.sig("fq%d" % k)
                v.tensor_tensor(S["concat"][:, mt, qoff:qoff + WIN],
                                S["oall"][:, mt, :], S["rbc2"][:, :], op=ALU.mult)
            v.sig("cc2")

    cc2_now = pb.cnt["cc2"]
    wommv = {}
    @block.tensor
    def _(tr):
        pb.wait("t", tr, "cc2", cc2_now)
        for mt in range(8):
            k = mt % 2
            pb.wait("t", tr, "fv%d" % k, st["fv"][k]); st["fv"][k] += 1
            ps = P["pvA"] if k == 0 else P["pvB"]
            for kt in range(8):
                inst = tr.matmul(ps[:, :], S["wof"][:, kt, mt * 128:(mt + 1) * 128],
                                 S["concat"][:, kt, :],
                                 start=(kt == 0), stop=(kt == 7), skip_group_check=True)
            inst.then_inc(pb.sems["mm"], 1); pb.tick("mm")
            wommv[mt] = pb.cnt["mm"]

    @block.vector
    def _(vr):
        v = Eng(pb, "v", vr)
        for mt in range(8):
            k = mt % 2
            v.wait("mm", wommv[mt])
            v.tensor_tensor(S["xch"][:, mt, 1:513],
                            (P["pvA"] if k == 0 else P["pvB"])[:, :],
                            S["xch"][:, mt, 1:513], op=ALU.add)
            v.sig("fv%d" % k)
        v.sig("shc"); st["shc"] += 1
        v.sig("ve")

    @block.scalar
    def _(sr):
        s = Eng(pb, "s", sr)
        s.wait("ve", pb.cnt["ve"])
        s.dma(bass.AP(T["xinnerT"], xlo, [[DOMF, 128], [128 * DOMF, 8], [1, 512]]),
              S["xch"][:, :, 1:513], "xw")
        s.sig("xw2"); st["xw2"] += 1


def p2_ffn(pb, l, st):
    nc, block = pb.nc, pb.block
    S, T, P = pb.S, pb.T, pb.P

    @block.gpsimd
    def _(gr):
        g = Eng(pb, "g", gr)
        g.dma(S["w1s"][:, :, :], T["W1"][l].rearrange("(kt p) m -> p kt m", p=128), "wld")
        g.dma(S["w2s"][:, 0:21, :], T["W2"][l][0:2688, :].rearrange("(kt p) m -> p kt m", p=128), "wld")
        g.dma(S["w2s"][0:42, 21, :], T["W2"][l][2688:2730, :], "wld")
        g.dma(S["s_b1c"][:, :], T["b1c"][l], "wld")
        g.dma(S["s_b2c"][:, :], T["b2c"][l], "wld")
        g.dma(S["s_lnc"][:, :], T["lnpc"][l], "wld")
    wld_l = pb.cnt["wld"]

    for tc in range(4):
        xlo = SLAB + tc * 512

        @block.sync
        def _(y):
            pb.wait("y", y, "shc", st["shc"])
            pb.wait("y", y, "xw2", st["xw2"])
            pb.wait("y", y, "xw", pb.cnt["xw"])
            y.dma_start(out=S["xch"][:, :, 0:513],
                        in_=bass.AP(T["xinnerT"], xlo - 1,
                                    [[DOMF, 128], [128 * DOMF, 8], [1, 513]])
                        ).then_inc(pb.sem("xr"), 16)
            pb.tick("xr", 16)

        @block.vector
        def _(vr):
            v = Eng(pb, "v", vr)
            v.wait("xr", pb.cnt["xr"])
            v.tensor_copy(S["shad"][:, :, 0:513], S["xch"][:, :, 0:513])
            v.sig("shr")
        shr_now = pb.cnt["shr"]

        @block.scalar
        def _(sr):
            s = Eng(pb, "s", sr)
            s.wait("shr", shr_now)
            s.wait("hgf", st["hgf"])
            for kt in range(8):
                off = 1 if kt < 4 else 0
                s.activation(S["hg"][:, 4 + kt, :], S["shad"][:, kt, off:off + 512], AF.Square)
            s.sig("se")

        stm = {}
        @block.tensor
        def _(tr):
            pb.wait("t", tr, "wld", wld_l)
            pb.wait("t", tr, "se", pb.cnt["se"])
            pb.wait("t", tr, "shr", shr_now)
            pb.wait("t", tr, "fst", st["fst"])
            for which in range(2):
                ps = P["pa00"] if which == 0 else P["pa10"]
                for kt in range(8):
                    off = 1 if kt < 4 else 0
                    rhs = (S["shad"][:, kt, off:off + 512] if which == 0
                           else S["hg"][:, 4 + kt, :])
                    inst = tr.matmul(ps[0:1, :], S["sb_onesb"][:, 0:1], rhs,
                                     start=(kt == 0), stop=(kt == 7), skip_group_check=True)
                inst.then_inc(pb.sems["mm"], 1); pb.tick("mm")
                stm[which] = pb.cnt["mm"]

        frow = S["hg"][:, 0:4, :].bitcast(F32)

        @block.vector
        def _(vr):
            v = Eng(pb, "v", vr)
            v.wait("mm", stm[1])
            v.tensor_scalar_mul(frow[0:1, 0, 0:256], P["pa00"][0:1, 0:256], 1.0 / D)
            v.tensor_scalar_mul(frow[0:1, 1, 0:256], P["pa00"][0:1, 256:512], 1.0 / D)
            v.tensor_scalar_mul(frow[0:1, 2, 0:256], P["pa10"][0:1, 0:256], 1.0 / D)
            v.tensor_scalar_mul(frow[0:1, 3, 0:256], P["pa10"][0:1, 256:512], 1.0 / D)
            v.sig("fst"); st["fst"] += 1
            v.tensor_tensor(S["t1f"][0:1, 0:256], frow[0:1, 0, 0:256], frow[0:1, 0, 0:256], op=ALU.mult)
            v.tensor_tensor(S["t1f"][0:1, 256:512], frow[0:1, 1, 0:256], frow[0:1, 1, 0:256], op=ALU.mult)
            v.tensor_tensor(frow[0:1, 2, 0:256], frow[0:1, 2, 0:256], S["t1f"][0:1, 0:256], op=ALU.subtract)
            v.tensor_tensor(frow[0:1, 3, 0:256], frow[0:1, 3, 0:256], S["t1f"][0:1, 256:512], op=ALU.subtract)
            v.tensor_scalar_add(frow[0:1, 2, 0:256], frow[0:1, 2, 0:256], 1e-5)
            v.tensor_scalar_add(frow[0:1, 3, 0:256], frow[0:1, 3, 0:256], 1e-5)
            v.sig("ve")

        @block.scalar
        def _(sr):
            s = Eng(pb, "s", sr)
            s.wait("ve", pb.cnt["ve"])
            s.activation(S["t1f"][0:1, 0:512], frow[0:1, 2:4, :].rearrange("p a b -> p (a b)"), AF.Sqrt)
            s.sig("se")

        @block.vector
        def _(vr):
            v = Eng(pb, "v", vr)
            v.wait("se", pb.cnt["se"])
            v.reciprocal(S["t2f"][0:1, 0:512], S["t1f"][0:1, 0:512])
            v.tensor_tensor(S["t1f"][0:1, 0:256], frow[0:1, 0, 0:256], S["t2f"][0:1, 0:256], op=ALU.mult)
            v.tensor_tensor(S["t1f"][0:1, 256:512], frow[0:1, 1, 0:256], S["t2f"][0:1, 256:512], op=ALU.mult)
            v.sig("ve")

        bcm = {}
        @block.tensor
        def _(tr):
            pb.wait("t", tr, "ve", pb.cnt["ve"])
            pb.wait("t", tr, "fbc", st["fbc"])
            for which, srcrow in ((0, "t1f"), (1, "t2f")):
                ps = P["pg00"] if which == 0 else P["pg10"]
                inst = tr.matmul(ps[:, :], S["sb_bc0f"][:, :],
                                 S[srcrow][:, 0:512], start=True, stop=True,
                                 skip_group_check=True)
                inst.then_inc(pb.sems["mm"], 1); pb.tick("mm")
                bcm[which] = pb.cnt["mm"]

        @block.vector
        def _(vr):
            v = Eng(pb, "v", vr)
            v.wait("mm", bcm[1])
            v.tensor_copy(S["t3f"][:, 0:512], P["pg00"][:, :])
            for kt in range(8):
                off = 1 if kt < 4 else 0
                v.tensor_tensor(S["t1f"][:, 0:512], S["shad"][:, kt, off:off + 512],
                                P["pg10"][:, :], op=ALU.mult)
                v.tensor_tensor(S["t1f"][:, 0:512], S["t1f"][:, 0:512], S["t3f"][:, 0:512],
                                op=ALU.subtract)
                v.tensor_scalar(S["xn2"][:, kt, :], S["t1f"][:, 0:512],
                                S["s_lnc"][:, kt * 2:kt * 2 + 1], S["s_lnc"][:, kt * 2 + 1:kt * 2 + 2],
                                op0=ALU.mult, op1=ALU.add)
            v.sig("fbc"); st["fbc"] += 1
            v.sig("xn2r")
            v.sig("shc"); st["shc"] += 1

        xn2_now = pb.cnt["xn2r"]

        for hch in range(HCH):
            at0 = 2 * hch
            amm = {}
            @block.tensor
            def _(tr, hch=hch, at0=at0, amm=amm):
                pb.wait("t", tr, "xn2r", xn2_now)
                kk = hch % 2
                pb.wait("t", tr, "fpa%d" % kk, st["fpa"][kk]); st["fpa"][kk] += 1
                for gate in range(2):
                    for sub in range(2):
                        nrow_ = 128 if (hch < 10 or sub == 0) else 42
                        c0 = (2730 if gate else 0) + hch * 256 + sub * 128
                        ps = P["p%s%d%d" % ("g" if gate else "a", kk, sub)]
                        for kt in range(8):
                            inst = tr.matmul(ps[0:nrow_, :], S["w1s"][:, kt, c0:c0 + nrow_],
                                             S["xn2"][:, kt, :],
                                             start=(kt == 0), stop=(kt == 7), skip_group_check=True)
                        inst.then_inc(pb.sems["mm"], 1); pb.tick("mm")
                amm[0] = pb.cnt["mm"]

            @block.scalar
            def _(sr, hch=hch, at0=at0, amm=amm):
                s = Eng(pb, "s", sr)
                kk = hch % 2
                s.wait("mm", amm[0])
                for sub in range(2):
                    nrow_ = 128 if (hch < 10 or sub == 0) else 42
                    s.activation(S["absc"][0:nrow_, 4 * kk + sub, :], P["pa%d%d" % (kk, sub)][0:nrow_, :],
                                 AF.Identity, bias=S["s_b1c"][0:nrow_, at0 + sub:at0 + sub + 1])
                    s.activation(S["absc"][0:nrow_, 4 * kk + 2 + sub, :], P["pg%d%d" % (kk, sub)][0:nrow_, :],
                                 AF.Gelu, bias=S["s_b1c"][0:nrow_, 22 + at0 + sub:22 + at0 + sub + 1])
                s.sig("se")

            @block.vector
            def _(vr, hch=hch, amm=amm):
                v = Eng(pb, "v", vr)
                kk = hch % 2
                v.wait("se", pb.cnt["se"])
                for sub in range(2):
                    nrow_ = 128 if (hch < 10 or sub == 0) else 42
                    v.tensor_tensor(S["hg"][0:nrow_, 2 * hch + sub, :],
                                    S["absc"][0:nrow_, 4 * kk + sub, :],
                                    S["absc"][0:nrow_, 4 * kk + 2 + sub, :], op=ALU.mult)
                v.sig("fpa%d" % kk)
                v.sig("hgr")
        hgr_now = pb.cnt["hgr"]

        w2m = {}
        @block.tensor
        def _(tr):
            pb.wait("t", tr, "hgr", hgr_now)
            for mt in range(8):
                kk = mt % 2
                pb.wait("t", tr, "fpa%d" % kk, st["fpa"][kk]); st["fpa"][kk] += 1
                ps = P["pa%d0" % kk]
                for kt in range(22):
                    nrow_ = 128 if kt < 21 else 42
                    inst = tr.matmul(ps[:, :], S["w2s"][0:nrow_, kt, mt * 128:(mt + 1) * 128],
                                     S["hg"][0:nrow_, kt, :],
                                     start=(kt == 0), stop=(kt == 21), skip_group_check=True)
                inst.then_inc(pb.sems["mm"], 1); pb.tick("mm")
                w2m[mt] = pb.cnt["mm"]

        @block.vector
        def _(vr):
            v = Eng(pb, "v", vr)
            for mt in range(8):
                kk = mt % 2
                v.wait("mm", w2m[mt])
                v.scalar_tensor_tensor(S["xch"][:, mt, 1:513], P["pa%d0" % kk][:, :],
                                       S["s_b2c"][:, mt:mt + 1], S["xch"][:, mt, 1:513],
                                       op0=ALU.add, op1=ALU.add)
                v.sig("fpa%d" % kk)
            v.sig("ve")
            v.sig("hgf"); st["hgf"] += 1

        @block.scalar
        def _(sr, xlo=xlo):
            s = Eng(pb, "s", sr)
            s.wait("ve", pb.cnt["ve"])
            s.dma(bass.AP(T["xinnerT"], xlo, [[DOMF, 128], [128 * DOMF, 8], [1, 512]]),
                  S["xch"][:, :, 1:513], "xw")
            s.sig("xw2"); st["xw2"] += 1


def p2(pb):
    nc, block = pb.nc, pb.block
    S, T = pb.S, pb.T

    pb.open_phase("p2")
    pb.sbuf("xch", (128, 8, 513), F32, ph="p2")
    pb.sbuf("shad", (128, 8, 513), BF16, ph="p2")
    pb.sbuf("nrow", (128, 64), F32, ph="p2")
    pb.sbuf("negc", (128, 1), F32, ph="p2")
    pb.sbuf("qkcols", (128, 2), F32, ph="p2")
    pb.sbuf("t1f", (128, 512), F32, ph="p2")
    pb.sbuf("t2f", (128, 512), F32, ph="p2")
    pb.sbuf("t3f", (128, 512), F32, ph="p2")
    for s in ("xw", "xw2", "xr", "shr", "shc", "sq", "sn", "nr", "qkr", "vkr",
              "sce", "pex", "pvm", "cc2", "pne", "hgr", "hgf", "xn2r",
              "fst", "fbc"):
        pb.sem(s)
    for k in range(2):
        pb.sem("fq%d" % k); pb.sem("fn%d" % k); pb.sem("fb%d" % k)
        pb.sem("fv%d" % k); pb.sem("fs%d" % k); pb.sem("fpa%d" % k)
    for k in range(4):
        pb.sem("fo%d" % k)

    def p2_nrow_ms(vr):
        v = Eng(pb, "v", vr)
        v.memset(S["nrow"][:, :], 0.0)
        v.sig("ve")
    block.vector(p2_nrow_ms)

    st = {"pne": 0, "shc": 0, "xw2": 0, "sq_b": 0, "sn_b": 0, "nr_b": 0,
          "sce_b": 0, "pex_b": 0, "pvm_b": 0, "hgf": 0, "fst": 0, "fbc": 0,
          "fq": [0, 0], "fn": [0, 0], "fb": [0, 0], "fv": [0, 0], "fs": [0, 0],
          "fo": [0, 0, 0, 0], "fpa": [0, 0]}

    for l in range(P2_LAYERS):
        if l > 0:
            p2_exchange(pb, st)
        pb.open_phase("p2a")
        pb.sbuf("s_maskP", (128, 4, WIN), F32, ph="p2a")
        pb.sbuf("s_maskF", (128, 4, WIN), F32, ph="p2a")
        pb.sbuf("s_two2f", (2, 128), F32, ph="p2a")
        pb.sbuf("s_two2fT", (128, 2), BF16, ph="p2a")
        pb.sbuf("wqkv", (128, 8, 3 * D), BF16, ph="p2a")
        pb.sbuf("wof", (128, 8, D), BF16, ph="p2a")
        pb.sbuf("rott", (128, 4 * DOMF), BF16, ph="p2a")
        pb.sbuf("kring", (128, 3, 8, WIN), BF16, ph="p2a")
        pb.sbuf("vring", (128, 3, 2, 1040), BF16, ph="p2a")
        pb.sbuf("qpb", (128, 8, 512), BF16, ph="p2a")
        pb.sbuf("probs", (128, 8, WIN), BF16, ph="p2a")
        pb.sbuf("scf", (128, 8, WIN), F32, ph="p2a")
        pb.sbuf("oall", (128, 8, WIN), BF16, ph="p2a")
        pb.sbuf("concat", (128, 8, 512), BF16, ph="p2a")
        pb.sbuf("sqs", (128, 1024), BF16, ph="p2a")
        pb.sbuf("rn1", (2, 1024), F32, ph="p2a")
        pb.sbuf("rn2", (2, 1024), F32, ph="p2a")
        pb.sbuf("rnb", (128, 512), F32, ph="p2a")
        pb.sbuf("recipb", (128, WIN), F32, ph="p2a")
        pb.sbuf("den_sb", (128, WIN), F32, ph="p2a")
        pb.sbuf("den_pack", (128, 2048), F32, ph="p2a")
        pb.sbuf("rbc2", (128, WIN), F32, ph="p2a")
        pb.psum("pqa", (128, 512), F32, ph="p2a")
        pb.psum("pqb", (128, 512), F32, ph="p2a")
        pb.psum("pvA", (128, 512), F32, ph="p2a")
        pb.psum("pvB", (128, 512), F32, ph="p2a")
        pb.psum("pe5", (128, 512), F32, ph="p2a")
        pb.psum("pf6", (128, 512), F32, ph="p2a")
        pb.psum("pg7", (128, 512), F32, ph="p2a")
        pb.psum("ph8", (128, 512), F32, ph="p2a")

        @block.vector
        def _(vr):
            v = Eng(pb, "v", vr)
            v.memset(S["vring"][:, :, :, :], 0.0)
            v.memset(S["vring"][:, :, :, :].rearrange("p s t (h m) -> p s t h m", m=65)[:, :, :, :, 64:65], 1.0)
            v.memset(S["kring"][:, :, :, :], 0.0)
            v.memset(S["den_sb"][:, :], 0.0)
            v.memset(S["recipb"][:, :], 0.0)
            v.sig("ve")
            v.sig("vkr")
            v.sig("qkr")

        p2_attn(pb, l, st)
        pb.barrier("p2a%d" % l, drain=("xw", "xr", "wld", "gd", "sa"))
        pb.close_phase("p2a")

        p2_exchange(pb, st)
        pb.open_phase("p2f")
        pb.sbuf("w1s", (128, 8, W1C), BF16, ph="p2f")
        pb.sbuf("w2s", (128, 22, D), BF16, ph="p2f")
        pb.sbuf("hg", (128, 22, 512), BF16, ph="p2f")
        pb.sbuf("xn2", (128, 8, 512), BF16, ph="p2f")
        pb.sbuf("absc", (128, 8, 512), BF16, ph="p2f")
        pb.sbuf("s_b1c", (128, 44), F32, ph="p2f")
        pb.sbuf("s_b2c", (128, 8), F32, ph="p2f")
        pb.sbuf("s_lnc", (128, 16), F32, ph="p2f")
        pb.psum("pa00", (128, 512), F32, ph="p2f")
        pb.psum("pa01", (128, 512), F32, ph="p2f")
        pb.psum("pa10", (128, 512), F32, ph="p2f")
        pb.psum("pa11", (128, 512), F32, ph="p2f")
        pb.psum("pg00", (128, 512), F32, ph="p2f")
        pb.psum("pg01", (128, 512), F32, ph="p2f")
        pb.psum("pg10", (128, 512), F32, ph="p2f")
        pb.psum("pg11", (128, 512), F32, ph="p2f")

        p2_ffn(pb, l, st)
        pb.barrier("p2f%d" % l, drain=("xw", "xr", "wld", "gd"))
        pb.close_phase("p2f")

    pb.close_phase("p2")

    if "xinner_final" in pb.dbg:
        dd = pb.dout("d_xinner", (D, DOMF), F32)
        def dumpf(gr, dd=dd):
            g = Eng(pb, "g", gr)
            g.wait("xw", pb.cnt["xw"])
            g.dma(dd[:, :], T["xinnerT"][:, :], "gd")
        block.gpsimd(dumpf)


def p3(pb):
    nc, block = pb.nc, pb.block
    S, T, P = pb.S, pb.T, pb.P
    groups = [[0, 1], [2, 3], [4, 5], [6, 7]]

    pb.open_phase("p3")
    pb.sbuf("p3x", (128, 2, D), F32, ph="p3")
    pb.sbuf("p3af", (128, 2, 2, D), F32, ph="p3")
    pb.sbuf("p3sum", (128, 2, D), F32, ph="p3")
    pb.sbuf("p3x1", (128, 2, D), F32, ph="p3")
    pb.sbuf("p3xn", (128, 2, D), BF16, ph="p3")
    pb.sbuf("p3sq", (128, D), F32, ph="p3")
    pb.sbuf("p3st", (128, 8), F32, ph="p3")
    pb.psum("p3ta", (128, 512), F32, ph="p3")
    pb.psum("p3tb", (128, 512), F32, ph="p3")
    for s in ("p0n", "p0v", "p0s", "p0e"):
        pb.sem(s)
    xs3 = pb.stream("xs", 2)
    af3 = pb.stream("af3", 4)

    mmb = [0]
    st0 = pb.cnt["st"]
    n0 = pb.cnt["p0n"]
    v0 = pb.cnt["p0v"]
    s0 = pb.cnt["p0s"]
    e0 = pb.cnt["p0e"]

    @block.sync
    def _(y_raw):
        for i in range(NTO):
            if i >= 2:
                pb.wait("y", y_raw, "p0e", e0 + i - 1)
            xs3.issue(y_raw, S["p3x"][:, i % 2, :], T["x"][i * 128:(i + 1) * 128, :])
            if i >= 2:
                pb.wait("y", y_raw, "p0n", n0 + i - 1)
            af3.issue(y_raw, S["p3af"][:, i % 2, 0, :].rearrange("p (f c) -> p f c", f=8),
                      bass.AP(T["A_T"], i * 128, [[HALF, 128], [128 * HALF, 8], [1, 128]]))
            af3.issue(y_raw, S["p3af"][:, i % 2, 1, :].rearrange("p (f c) -> p f c", f=8),
                      bass.AP(T["xinnerT"], SLAB + i * 128, [[DOMF, 128], [128 * DOMF, 8], [1, 128]]))

    @block.tensor
    def _(tr):
        t = Eng(pb, "t", tr, chained=False)
        mmb[0] = pb.cnt["mm"]
        for i in range(NTO):
            t.wait("p0n", n0 + i + 1)
            if i >= 1:
                t.wait("p0e", e0 + i)
            for fg in range(2):
                ps = P["p3ta"] if fg == 0 else P["p3tb"]
                for f in range(4):
                    ft = fg * 4 + f
                    inst = tr.matmul(ps[:, f * 128:(f + 1) * 128],
                                     S["p3sum"][:, i % 2, ft * 128:(ft + 1) * 128],
                                     S["sb_identf"][:, :], start=True, stop=True,
                                     is_transpose=True)
            inst.then_inc(pb.sems["mm"], 1)
            pb.tick("mm")

    @block.vector
    def _(vr):
        v = Eng(pb, "v", vr)
        for i in range(NTO):
            af3.wait("v", vr, 2 * i)
            af3.wait("v", vr, 2 * i + 1)
            v.tensor_tensor(S["p3sum"][:, i % 2, :], S["p3af"][:, i % 2, 0, :],
                            S["p3af"][:, i % 2, 1, :], op=ALU.add)
            v.sig("p0n")
            v.wait("mm", mmb[0] + i + 1)
            xs3.wait("v", vr, i)
            if i >= 2:
                v.wait("st", st0 + 32 * (i - 1))
            xt = S["p3x"][:, i % 2, :]
            x1 = S["p3x1"][:, i % 2, :]
            for fg in range(2):
                ps = P["p3ta"] if fg == 0 else P["p3tb"]
                for f in range(4):
                    ft = fg * 4 + f
                    v.tensor_tensor(x1[:, ft * 128:(ft + 1) * 128], ps[:, f * 128:(f + 1) * 128],
                                    xt[:, ft * 128:(ft + 1) * 128], op=ALU.add)
            st = S["p3st"]
            v.reduce_sum(st[:, 0:1], x1, axis=AXX)
            v.tensor_scalar_mul(st[:, 0:1], st[:, 0:1], 1.0 / D)
            v.tensor_tensor(st[:, 4:5], st[:, 0:1], st[:, 0:1], op=ALU.mult)
            v.tensor_tensor(S["p3sq"][:, :], x1, x1, op=ALU.mult)
            v.reduce_sum(st[:, 1:2], S["p3sq"][:, :], axis=AXX)
            v.tensor_scalar_mul(st[:, 1:2], st[:, 1:2], 1.0 / D)
            v.tensor_tensor(st[:, 1:2], st[:, 1:2], st[:, 4:5], op=ALU.subtract)
            v.tensor_scalar_add(st[:, 2:3], st[:, 1:2], 1e-5)
            v.sig("p0v")
            v.wait("p0s", s0 + i + 1)
            v.reciprocal(st[:, 3:4], st[:, 6:7])
            v.tensor_scalar(S["p3xn"][:, i % 2, :], x1, st[:, 0:1], st[:, 3:4],
                            op0=ALU.subtract, op1=ALU.mult)
            v.sig("p0e")
            v.sig("ve")

    ve0 = pb.cnt["ve"] - NTO

    @block.scalar
    def _(sr):
        s = Eng(pb, "s", sr)
        for i in range(NTO + 1):
            if i < NTO:
                s.wait("p0v", v0 + i + 1)
                s.activation(S["p3st"][:, 6:7], S["p3st"][:, 2:3], AF.Sqrt)
                s.sig("p0s")
            if i >= 1:
                j = i - 1
                s.wait("ve", ve0 + j + 1)
                s.dma(T["x1_tm"][j * 128:(j + 1) * 128, :], S["p3x1"][:, j % 2, :], "st")
                s.dma(T["xn1_own"][j * 128:(j + 1) * 128, :], S["p3xn"][:, j % 2, :], "st")

    @block.gpsimd
    def _(gr):
        g = Eng(pb, "g", gr)
        g.wait("st", pb.cnt["st"])
        gr.collective_compute(
            "AllGather", ALU.bypass, replica_groups=groups,
            ins=[T["xn1_own"][:, :].opt()], outs=[T["xn1_all"][:, :].opt()],
        ).then_inc(pb.sem("cc"), 1)
        pb.tick("cc")

    pb.barrier("p3", drain=("st", "cc"))
    pb.close_phase("p3")

    if "p3" in pb.dbg:
        dd = pb.dout("d_xn1all", (N, D), BF16)
        dd2 = pb.dout("d_x1tm", (HALF, D), F32)
        def dumpf(gr, dd=dd, dd2=dd2):
            g = Eng(pb, "g", gr)
            g.dma(dd[:, :], T["xn1_all"][:, :], "gd")
            g.dma(dd2[:, :], T["x1_tm"][:, :], "gd")
        block.gpsimd(dumpf)


def p4_gate(pb):
    nc, block = pb.nc, pb.block
    S, T, P = pb.S, pb.T, pb.P
    pb.open_phase("p4g")
    pb.sbuf("xn1T", (128, 8, N), BF16, ph="p4g")
    pb.psum("pga", (128, 512), F32, ph="p4g")
    pb.psum("pgb", (128, 512), F32, ph="p4g")
    xt4 = pb.stream("xt4", 2)

    @block.gpsimd
    def _(gr):
        g = Eng(pb, "g", gr)
        g.dma(S["gw4"][:, :, :], T["gate_w"][:, :].rearrange("(kt p) e -> p kt e", p=128), "wld")
        g.dma(S["s_triu"][:, :], T["triu128"][:, :], "wld")
        g.dma(S["s_iotasel"][:, :], T["iotasel"][:, :], "wld")
        g.dma(S["s_iotacol"][:, :], T["iotacol"][:, :], "wld")
    wld4 = pb.cnt["wld"]

    @block.sync
    def _(y):
        pb.wait("y", y, "cc", pb.cnt["cc"])
        for tt4 in range(4):
            for ft in range(8):
                i = tt4 * 8 + ft
                if i >= 2:
                    xt4.wait("y", y, i - 2)
                xt4.issue(y, S["xn1T"][:, ft, tt4 * 1024:(tt4 + 1) * 1024],
                          T["xn1_all"][tt4 * 1024:(tt4 + 1) * 1024, ft * 128:(ft + 1) * 128],
                          transpose=True)

    mmg = [0]
    sce0 = pb.cnt["sce"]

    @block.tensor
    def _(tr):
        t = Eng(pb, "t", tr, chained=False)
        t.wait("wld", wld4)
        mmg[0] = pb.cnt["mm"]
        for tt in range(NT):
            xt4.wait("t", tr, 8 * (tt // 8) + 6)
            xt4.wait("t", tr, 8 * (tt // 8) + 7)
            if tt >= 2:
                t.wait("sce", sce0 + tt - 1)
            ps = P["pga"] if tt % 2 == 0 else P["pgb"]
            for kt in range(8):
                inst = tr.matmul(ps[:, 0:8], S["xn1T"][:, kt, tt * 128:(tt + 1) * 128],
                                 S["gw4"][:, kt, :], start=(kt == 0), stop=(kt == 7),
                                 skip_group_check=True)
            inst.then_inc(pb.sems["mm"], 1); pb.tick("mm")

    @block.vector
    def _(vr):
        v = Eng(pb, "v", vr)
        for tt in range(NT):
            v.wait("mm", mmg[0] + tt + 1)
            ps = P["pga"] if tt % 2 == 0 else P["pgb"]
            v.tensor_copy(S["probs4"][:, :, tt:tt + 1], ps[:, 0:8].rearrange("p (e o) -> p e o", o=1))
            v.sig("sce")

    pb.barrier("p4g", drain=("xt40", "xt41", "wld"))
    pb.close_phase("p4g")


def p4_select(pb):
    nc, block = pb.nc, pb.block
    S, T, P = pb.S, pb.T, pb.P
    pb.open_phase("p4b")
    pb.psum("pb4a", (128, 512), F32, ph="p4b")
    pb.psum("pb4b", (128, 512), F32, ph="p4b")
    pb.psum("pb4c", (128, 512), F32, ph="p4b")

    @block.scalar
    def _(sr):
        s = Eng(pb, "s", sr)
        s.activation(S["exp4"][:, :, :], S["probs4"][:, :, :], AF.Exp)
        s.sig("se")

    @block.vector
    def _(vr):
        v = Eng(pb, "v", vr)
        v.wait("se", pb.cnt["se"])
        v.reduce_sum(S["den4"][:, :], S["exp4"][:, :, :].rearrange("p e t -> p t e"), axis=AXX)
        v.reciprocal(S["den4"][:, :], S["den4"][:, :])
        for e in range(N_EXP):
            v.tensor_tensor(S["probs4"][:, e, :], S["exp4"][:, e, :], S["den4"][:, :], op=ALU.mult)
        br = S["brow"]
        v.memset(br[:, :], 0.0)
        v.memset(S["srow"][:, :, :], 0.0)
        v.memset(br[0:1, 8:16], 1.0)
        v.memset(br[0:1, 16:24], 0.5)
        v.sig("pvm")

    for it in range(30):
        @block.tensor
        def _(tr):
            t = Eng(pb, "t", tr, chained=False)
            t.wait("pvm", pb.cnt["pvm"])
            inst = tr.matmul(P["pb4a"][:, 0:8], S["sb_bc0f"][:, :], S["brow"][:, 16:24],
                             start=True, stop=True, skip_group_check=True)
            inst.then_inc(pb.sems["mm"], 1); pb.tick("mm")

        @block.vector
        def _(vr):
            v = Eng(pb, "v", vr)
            v.wait("mm", pb.cnt["mm"])
            v.tensor_copy(S["taubc"][:, :], P["pb4a"][:, 0:8])
            for e in range(N_EXP):
                v.tensor_scalar(S["cmp4"][:, e, :], S["probs4"][:, e, :],
                                S["taubc"][:, e:e + 1], None, op0=ALU.is_ge)
            v.reduce_sum(S["cnt4"][:, :], S["cmp4"][:, :, :], axis=AXX)
            v.sig("pvm")

        @block.tensor
        def _(tr):
            t = Eng(pb, "t", tr, chained=False)
            t.wait("pvm", pb.cnt["pvm"])
            inst = tr.matmul(P["pb4b"][0:1, 0:8], S["sb_onesf"][:, 0:1], S["cnt4"][:, :],
                             start=True, stop=True, skip_group_check=True)
            inst.then_inc(pb.sems["mm"], 1); pb.tick("mm")

        @block.vector
        def _(vr):
            v = Eng(pb, "v", vr)
            v.wait("mm", pb.cnt["mm"])
            br = S["brow"]
            v.tensor_copy(br[0:1, 24:32], P["pb4b"][0:1, 0:8])
            v.tensor_scalar(br[0:1, 32:40], br[0:1, 24:32], float(MOE_K), None, op0=ALU.is_ge)
            v.tensor_tensor(br[0:1, 40:48], br[0:1, 16:24], br[0:1, 0:8], op=ALU.subtract)
            v.tensor_tensor(br[0:1, 48:56], br[0:1, 40:48], br[0:1, 32:40], op=ALU.mult)
            v.tensor_tensor(br[0:1, 0:8], br[0:1, 0:8], br[0:1, 48:56], op=ALU.add)
            v.tensor_scalar(br[0:1, 56:64], br[0:1, 32:40], -1.0, 1.0, op0=ALU.mult, op1=ALU.add)
            v.tensor_tensor(br[0:1, 40:48], br[0:1, 16:24], br[0:1, 8:16], op=ALU.subtract)
            v.tensor_tensor(br[0:1, 48:56], br[0:1, 40:48], br[0:1, 56:64], op=ALU.mult)
            v.tensor_tensor(br[0:1, 8:16], br[0:1, 8:16], br[0:1, 48:56], op=ALU.add)
            v.tensor_tensor(br[0:1, 16:24], br[0:1, 0:8], br[0:1, 8:16], op=ALU.add)
            v.tensor_scalar_mul(br[0:1, 16:24], br[0:1, 16:24], 0.5)
            v.sig("pvm")

    @block.tensor
    def _(tr):
        t = Eng(pb, "t", tr, chained=False)
        t.wait("pvm", pb.cnt["pvm"])
        inst = tr.matmul(P["pb4a"][:, 0:8], S["sb_bc0f"][:, :], S["brow"][:, 0:8],
                         start=True, stop=True, skip_group_check=True)
        inst.then_inc(pb.sems["mm"], 1); pb.tick("mm")

    @block.vector
    def _(vr):
        v = Eng(pb, "v", vr)
        v.wait("mm", pb.cnt["mm"])
        v.tensor_copy(S["taubc"][:, :], P["pb4a"][:, 0:8])
        for e in range(N_EXP):
            v.tensor_scalar(S["maskf"][:, e, :], S["probs4"][:, e, :],
                            S["taubc"][:, e:e + 1], None, op0=ALU.is_ge)
        v.tensor_tensor(S["exp4"][:, :, :], S["probs4"][:, :, :], S["maskf"][:, :, :], op=ALU.mult)
        for j in range(4):
            v.tensor_scalar_mul(S["t4a"][:, :], S["exp4"][:, j, :], S["sb_gates"][:, 1:2])
            v.tensor_scalar_mul(S["t4b"][:, :], S["exp4"][:, j + 4, :], S["sb_gates"][:, 0:1])
            v.tensor_tensor(S["gmine"][:, j, :], S["t4a"][:, :], S["t4b"][:, :], op=ALU.add)
        v.tensor_copy(S["gmb"][:, :, :], S["gmine"][:, :, :])
        v.sig("pvm")

    for e in range(N_EXP):
        @block.tensor
        def _(tr, e=e):
            t = Eng(pb, "t", tr, chained=False)
            t.wait("pvm", pb.cnt["pvm"])
            if e >= 1:
                t.wait("sce", pb.cnt["sce"])
            tr.matmul(P["pb4a"][:, 0:32], S["s_triu"][:, :], S["maskf"][:, e, :],
                      start=True, stop=True, skip_group_check=True)
            inst = tr.matmul(P["pb4b"][0:1, 0:32], S["sb_onesf"][:, 0:1], S["maskf"][:, e, :],
                             start=True, stop=True, skip_group_check=True)
            inst.then_inc(pb.sems["mm"], 1); pb.tick("mm")

        @block.vector
        def _(vr, e=e):
            v = Eng(pb, "v", vr)
            v.wait("mm", pb.cnt["mm"])
            sr = S["srow"]
            v.tensor_copy(sr[0:1, 0, :], P["pb4b"][0:1, 0:32])
            for k, sh in enumerate((1, 2, 4, 8, 16)):
                src, dst = k % 2, (k + 1) % 2
                v.tensor_copy(sr[0:1, dst, 0:sh], sr[0:1, src, 0:sh])
                v.tensor_tensor(sr[0:1, dst, sh:32], sr[0:1, src, sh:32],
                                sr[0:1, src, 0:32 - sh], op=ALU.add)
            v.memset(sr[0:1, 2, 0:1], 0.0)
            v.tensor_copy(sr[0:1, 2, 1:32], sr[0:1, 1, 0:31])
            v.tensor_copy(S["cwt"][:, :], P["pb4a"][:, 0:32])
            v.sig("pne")

        @block.tensor
        def _(tr, e=e):
            t = Eng(pb, "t", tr, chained=False)
            t.wait("pne", pb.cnt["pne"])
            inst = tr.matmul(P["pb4c"][:, 0:32], S["sb_bc0f"][:, :], S["srow"][:, 2, :],
                             start=True, stop=True, skip_group_check=True)
            inst.then_inc(pb.sems["mm"], 1); pb.tick("mm")

        @block.vector
        def _(vr, e=e):
            v = Eng(pb, "v", vr)
            v.wait("mm", pb.cnt["mm"])
            v.tensor_tensor(S["cumm"][:, e, :], S["cwt"][:, :], P["pb4c"][:, 0:32], op=ALU.add)
            v.tensor_tensor(S["cumm"][:, e, :], S["cumm"][:, e, :], S["maskf"][:, e, :], op=ALU.mult)
            v.sig("sce")

    @block.vector
    def _(vr):
        v = Eng(pb, "v", vr)
        v.wait("sce", pb.cnt["sce"])
        for j in range(4):
            v.tensor_scalar_mul(S["t4a"][:, :], S["cumm"][:, j, :], S["sb_gates"][:, 1:2])
            v.tensor_scalar_mul(S["t4b"][:, :], S["cumm"][:, j + 4, :], S["sb_gates"][:, 0:1])
            v.tensor_tensor(S["cmine"][:, j, :], S["t4a"][:, :], S["t4b"][:, :], op=ALU.add)
        v.sig("pvm")

    pb.barrier("p4b")
    pb.close_phase("p4b")


def p4_experts(pb):
    nc, block = pb.nc, pb.block
    S, T, P = pb.S, pb.T, pb.P
    pb.open_phase("p4x")
    pb.sbuf("xn1_tm", (128, NT, D), BF16, ph="p4x")
    pb.sbuf("hgeT", (128, 32, MOE_K), BF16, ph="p4x")
    pb.sbuf("xinT", (128, 8, MOE_K), BF16, ph="p4x")
    pb.sbuf("w1s4", (128, 2, 8, 128), BF16, ph="p4x")
    pb.sbuf("w2s4", (128, 2, D), BF16, ph="p4x")
    pb.sbuf("Ptg", (128, 2, 512), BF16, ph="p4x")
    pb.sbuf("grow4", (1, MOE_K), F32, ph="p4x")
    pb.sbuf("gcols", (128, 8), F32, ph="p4x")
    pb.sbuf("h2st", (128, 8, D), BF16, ph="p4x")
    for i in range(8):
        pb.psum(f"px{i}", (128, 512), F32, ph="p4x")
    w14 = pb.stream("w14", 2)
    w24 = pb.stream("w24", 2)

    @block.gpsimd
    def _(gr):
        g = Eng(pb, "g", gr)
        g.wait("cc", pb.cnt["cc"])
        g.dma(S["xn1_tm"][:, :, :], T["xn1_all"][:, :].rearrange("(tt p) d -> p tt d", p=128), "xld")
    xld4 = pb.cnt["xld"]

    for j in range(4):
        st_j = {}

        @block.tensor
        def _(tr, j=j, st_j=st_j):
            t = Eng(pb, "t", tr, chained=False)
            t.wait("xld", xld4)
            t.wait("pvm", pb.cnt["pvm"])
            if j >= 1:
                t.wait("sce", pb.cnt["sce"])
            st_j["mm0"] = pb.cnt["mm"]
            st_j["pt0"] = pb.cnt["hgr"]
            st_j["g4x0"] = pb.cnt["hgf"]
            st_j["g4e0"] = pb.cnt["xn2r"]
            for nh in range(2):
                if nh == 1:
                    t.wait("xn2r", st_j["g4e0"] + 1)
                for tt in range(NT):
                    i = nh * NT + tt
                    t.wait("hgr", st_j["pt0"] + i + 1)
                    for fc in range(8):
                        inst = tr.matmul(P[f"px{fc}"][:, :],
                                         S["xn1_tm"][:, tt, fc * 128:(fc + 1) * 128],
                                         S["Ptg"][:, i % 2, :],
                                         start=(tt == 0), stop=(tt == NT - 1),
                                         skip_group_check=True)
                    inst.then_inc(pb.sems["mm"], 1); pb.tick("mm")

        @block.tensor
        def _(tr, j=j, st_j=st_j):
            t = Eng(pb, "t", tr, chained=False)
            st_j["gm0"] = pb.cnt["mm"]
            st_j["gmm0"] = pb.cnt["sq"]
            gpt0 = st_j["pt0"] + 64
            for nh in range(2):
                for tt in range(NT):
                    i = nh * NT + tt
                    t.wait("hgr", gpt0 + i + 1)
                    bank = P["px0"] if nh == 0 else P["px1"]
                    inst = tr.matmul(bank[0:1, 0:512],
                                     S["gmb"][:, j, tt:tt + 1], S["Ptg"][:, i % 2, :],
                                     start=(tt == 0), stop=(tt == NT - 1),
                                     skip_group_check=True)
                    inst.then_inc(pb.sems["sq"], 1); pb.tick("sq")

        @block.vector
        def _(vr, j=j, st_j=st_j):
            v = Eng(pb, "v", vr)
            for nh in range(2):
                for tt in range(NT):
                    i = nh * NT + tt
                    if i >= 2:
                        v.wait("mm", st_j["mm0"] + (i - 1))
                    v.tensor_scalar(S["Ptg"][:, i % 2, :], S["s_iotasel"][:, nh * 512:nh * 512 + 512],
                                    S["cmine"][:, j, tt:tt + 1], 0.0,
                                    op0=ALU.subtract, op1=ALU.is_equal)
                    v.sig("hgr")
                v.wait("mm", st_j["mm0"] + NT * (nh + 1))
                for fc in range(8):
                    v.tensor_copy(S["xinT"][:, fc, nh * 512:nh * 512 + 512], P[f"px{fc}"][:, :])
                v.sig("xn2r")
            for nh in range(2):
                for tt in range(NT):
                    i = nh * NT + tt
                    if i >= 2:
                        v.wait("sq", st_j["gmm0"] + (i - 1))
                    else:
                        v.wait("mm", st_j["mm0"] + 64)
                    v.tensor_scalar(S["Ptg"][:, i % 2, :], S["s_iotasel"][:, nh * 512:nh * 512 + 512],
                                    S["cmine"][:, j, tt:tt + 1], 0.0,
                                    op0=ALU.subtract, op1=ALU.is_equal)
                    v.sig("hgr")
            v.wait("sq", st_j["gmm0"] + 64)
            v.tensor_copy(S["grow4"][0:1, 0:512], P["px0"][0:1, 0:512])
            v.tensor_copy(S["grow4"][0:1, 512:1024], P["px1"][0:1, 0:512])
            v.sig("xn2r")

        @block.tensor
        def _(tr, j=j, st_j=st_j):
            t = Eng(pb, "t", tr, chained=False)
            t.wait("xn2r", pb.cnt["xn2r"])
            for c in range(8):
                inst = tr.matmul(P["px2"][:, c:c + 1], S["grow4"][0:1, c * 128:(c + 1) * 128],
                                 S["sb_identf"][0:1, 0:1], start=(c == 0), stop=(c == 7),
                                 is_transpose=True, skip_group_check=True)
            inst.then_inc(pb.sems["mm"], 1); pb.tick("mm")

        @block.vector
        def _(vr, j=j):
            v = Eng(pb, "v", vr)
            v.wait("mm", pb.cnt["mm"])
            v.tensor_copy(S["gcols"][:, :], P["px2"][:, 0:8])
            v.sig("xn2r")

        h140 = pb.cnt["nr"]

        @block.gpsimd
        def _(gr, j=j, h140=h140):
            g = Eng(pb, "g", gr)
            for ht in range(32):
                if ht >= 2:
                    pb.wait("g", gr, "nr", h140 + 2 * (ht - 1))
                w14.issue(gr, S["w1s4"][:, ht % 2, :, :],
                          T["moe_w1"][j, :, ht * 128:(ht + 1) * 128]
                          .rearrange("(kt p) c -> p kt c", p=128))

        @block.tensor
        def _(tr, j=j, st_j=st_j):
            t = Eng(pb, "t", tr, chained=False)
            t.wait("xn2r", pb.cnt["xn2r"])
            st_j["h10"] = pb.cnt["mm"]
            st_j["h1e0"] = pb.cnt["nr"]
            w0 = w14.nissued - 32
            for ht in range(32):
                w14.wait("t", tr, w0 + ht)
                if ht >= 2:
                    t.wait("nr", st_j["h1e0"] + 2 * (ht - 1))
                for nh in range(2):
                    bank = P[f"px{2 + (ht % 2) * 2 + nh}"]
                    for kt in range(8):
                        inst = tr.matmul(bank[:, :], S["w1s4"][:, ht % 2, kt, :],
                                         S["xinT"][:, kt, nh * 512:nh * 512 + 512],
                                         start=(kt == 0), stop=(kt == 7), skip_group_check=True)
                    inst.then_inc(pb.sems["mm"], 1); pb.tick("mm")

        @block.scalar
        def _(sr, j=j, st_j=st_j):
            s = Eng(pb, "s", sr)
            for ht in range(32):
                s.wait("mm", st_j["h10"] + 2 * ht + 2)
                for nh in range(2):
                    bank = P[f"px{2 + (ht % 2) * 2 + nh}"]
                    s.activation(S["hgeT"][:, ht, nh * 512:nh * 512 + 512], bank[:, :],
                                 AF.Gelu_apprx_tanh)
                s.sig("nr", 2)

        h240 = pb.cnt["pex"]

        @block.gpsimd
        def _(gr, j=j, h240=h240):
            g = Eng(pb, "g", gr)
            for g2 in range(2):
                for kt in range(32):
                    i = g2 * 32 + kt
                    if i >= 2:
                        pb.wait("g", gr, "pex", h240 + (i - 1) * 8)
                    w24.issue(gr, S["w2s4"][:, i % 2, :],
                              T["moe_w2"][j, kt * 128:(kt + 1) * 128, :])

        @block.tensor
        def _(tr, j=j, st_j=st_j):
            t = Eng(pb, "t", tr, chained=False)
            t.wait("nr", pb.cnt["nr"])
            st_j["h20"] = pb.cnt["mm"]
            st_j["pex0"] = pb.cnt["pex"]
            st_j["h2e0"] = pb.cnt["fst"]
            w0 = w24.nissued - 64
            for g2 in range(2):
                if g2 == 1:
                    t.wait("fst", st_j["h2e0"] + 8)
                for kt in range(32):
                    i = g2 * 32 + kt
                    w24.wait("t", tr, w0 + i)
                    for stl in range(4):
                        st_ = g2 * 4 + stl
                        for nh in range(2):
                            bank = P[f"px{stl * 2 + nh}"]
                            inst = tr.matmul(bank[:, :], S["hgeT"][:, kt, st_ * 128:(st_ + 1) * 128],
                                             S["w2s4"][:, i % 2, nh * 512:nh * 512 + 512],
                                             start=(kt == 0), stop=(kt == 31), skip_group_check=True)
                    inst.then_inc(pb.sems["pex"], 8); pb.tick("pex", 8)

        @block.vector
        def _(vr, j=j, st_j=st_j):
            v = Eng(pb, "v", vr)
            v.wait("st", pb.cnt["st"])
            for g2 in range(2):
                v.wait("pex", st_j["pex0"] + 256 * (g2 + 1))
                for stl in range(4):
                    st_ = g2 * 4 + stl
                    for nh in range(2):
                        bank = P[f"px{stl * 2 + nh}"]
                        v.tensor_scalar_mul(S["h2st"][:, st_, nh * 512:nh * 512 + 512],
                                            bank[:, :], S["gcols"][:, st_:st_ + 1])
                v.sig("fst", 8)

        @block.scalar
        def _(sr, j=j):
            s = Eng(pb, "s", sr)
            s.wait("fst", pb.cnt["fst"])
            s.dma(T["h2w_own"][j * MOE_K:(j + 1) * MOE_K, :].rearrange("(st p) d -> p st d", p=128),
                  S["h2st"][:, :, :], "st")
            s.sig("sce")

    @block.gpsimd
    def _(gr):
        g = Eng(pb, "g", gr)
        g.wait("st", pb.cnt["st"])
        gr.collective_compute(
            "AllGather", ALU.bypass, replica_groups=[[0, 1], [2, 3], [4, 5], [6, 7]],
            ins=[T["h2w_own"][:, :].opt()], outs=[T["h2w_all"][:, :].opt()],
        ).then_inc(pb.sem("cc"), 1)
        pb.tick("cc")

    pb.barrier("p4x", drain=("st", "cc", "xld", "w140", "w141", "w240", "w241"))
    pb.close_phase("p4x")


def p4_scatter(pb):
    nc, block = pb.nc, pb.block
    S, T, P = pb.S, pb.T, pb.P
    pb.open_phase("p4s")
    pb.sbuf("fst", (128, 2, 8, D), BF16, ph="p4s")
    pb.sbuf("cumbc8", (128, 8, 512), F32, ph="p4s")
    pb.sbuf("Psel", (128, 2, 512), BF16, ph="p4s")
    pb.sbuf("F1st", (128, 8, 512), BF16, ph="p4s")
    pb.sbuf("cmow", (128, 8, 16), F32, ph="p4s")
    pb.sbuf("c4row", (128, 512), F32, ph="p4s")
    for i in range(8):
        pb.psum(f"ps{i}", (128, 512), F32, ph="p4s")
    h2l = pb.stream("h2l", 2)

    @block.vector
    def _(vr):
        v = Eng(pb, "v", vr)
        v.memset(S["c4row"][:, :], 0.0)
        for e in range(N_EXP):
            v.tensor_scalar_mul(S["t4a"][:, 0:16], S["cumm"][:, e, 0:16], S["sb_gates"][:, 1:2])
            v.tensor_scalar_mul(S["t4b"][:, 0:16], S["cumm"][:, e, 16:32], S["sb_gates"][:, 0:1])
            v.tensor_tensor(S["cmow"][:, e, :], S["t4a"][:, 0:16], S["t4b"][:, 0:16], op=ALU.add)
        v.sig("fbc")

    s4m0 = pb.cnt["fq0"]

    @block.gpsimd
    def _(gr):
        g = Eng(pb, "g", gr)
        g.wait("cc", pb.cnt["cc"])
        for tc in range(4):
            for e in range(N_EXP):
                i = tc * N_EXP + e
                if i >= 2:
                    pb.wait("g", gr, "fq0", s4m0 + (i - 1) * 64)
                h2l.issue(gr, S["fst"][:, i % 2, :, :],
                          T["h2w_all"][e * MOE_K:(e + 1) * MOE_K, :]
                          .rearrange("(st p) d -> p st d", p=128))
    hl0 = h2l.nissued - 32

    for tc in range(4):
        st_t = {}

        @block.tensor
        def _(tr, tc=tc, st_t=st_t):
            t = Eng(pb, "t", tr, chained=False)
            t.wait("fbc", pb.cnt["fbc"])
            if tc >= 1:
                t.wait("fn0", pb.cnt["fn0"])
            st_t["mm0"] = pb.cnt["mm"]
            st_t["s4r0"] = pb.cnt["shc"]
            st_t["bc40"] = pb.cnt["fb0"]
            for e in range(N_EXP):
                if e >= 1:
                    t.wait("shc", st_t["s4r0"] + 2 * e - 1)
                for c2 in range(4):
                    inst = tr.matmul(P["ps0"][0:1, c2 * 128:(c2 + 1) * 128],
                                     S["cmow"][:, e, tc * 4 + c2: tc * 4 + c2 + 1],
                                     S["sb_identf"][:, :], start=(c2 == 0), stop=(c2 == 3),
                                     is_transpose=True, skip_group_check=True)
                inst.then_inc(pb.sems["mm"], 1); pb.tick("mm")
                t.wait("shc", st_t["s4r0"] + 2 * e + 1)
                inst = tr.matmul(P["ps1"][:, :], S["sb_bc0f"][:, :], S["c4row"][:, :],
                                 start=True, stop=True, skip_group_check=True)
                inst.then_inc(pb.sems["fb0"], 1); pb.tick("fb0")

        @block.vector
        def _(vr, tc=tc, st_t=st_t):
            v = Eng(pb, "v", vr)
            for e in range(N_EXP):
                v.wait("mm", st_t["mm0"] + e + 1)
                v.tensor_copy(S["c4row"][0:1, :], P["ps0"][0:1, :])
                v.sig("shc")
                v.wait("fb0", st_t["bc40"] + e + 1)
                v.tensor_copy(S["cumbc8"][:, e, :], P["ps1"][:, :])
                v.sig("shc")

        st_b = {}

        @block.tensor
        def _(tr, tc=tc, st_b=st_b, st_t=st_t):
            t = Eng(pb, "t", tr, chained=False)
            t.wait("shc", st_t["s4r0"] + 16)
            st_b["sp0"] = pb.cnt["xw2"]
            for e in range(N_EXP):
                h2l.wait("t", tr, hl0 + tc * N_EXP + e)
                for st_ in range(8):
                    i = e * 8 + st_
                    t.wait("xw2", st_b["sp0"] + i + 1)
                    for ft in range(8):
                        inst = tr.matmul(P[f"ps{ft}"][:, :],
                                         S["fst"][:, e % 2, st_, ft * 128:(ft + 1) * 128],
                                         S["Psel"][:, i % 2, :],
                                         start=(e == 0 and st_ == 0), stop=(e == 7 and st_ == 7),
                                         skip_group_check=True)
                    inst.then_inc(pb.sems["fq0"], 8); pb.tick("fq0", 8)

        @block.vector
        def _(vr, tc=tc, st_b=st_b):
            v = Eng(pb, "v", vr)
            sm0 = pb.cnt["fq0"] - 512
            for e in range(N_EXP):
                for st_ in range(8):
                    i = e * 8 + st_
                    if i >= 2:
                        v.wait("fq0", sm0 + 8 * (i - 1))
                    v.tensor_scalar(S["Psel"][:, i % 2, :], S["cumbc8"][:, e, :],
                                    S["s_iotacol"][:, 0:1], float(st_ * 128),
                                    op0=ALU.subtract, op1=ALU.is_equal)
                    v.sig("xw2")
            v.wait("fq0", pb.cnt["fq0"])
            v.wait("st", pb.cnt["st"])
            for ft in range(8):
                v.tensor_copy(S["F1st"][:, ft, :], P[f"ps{ft}"][:, :])
            v.sig("ve")

        @block.scalar
        def _(sr, tc=tc):
            s = Eng(pb, "s", sr)
            s.wait("ve", pb.cnt["ve"])
            s.dma(bass.AP(T["F1T"], tc * 512, [[HALF, 128], [128 * HALF, 8], [1, 512]]),
                  S["F1st"][:, :, :], "st")
            s.sig("fn0")

    pb.barrier("p4s", drain=("st", "h2l0", "h2l1"))
    pb.close_phase("p4s")


def p4(pb):
    nc, block = pb.nc, pb.block
    S, T = pb.S, pb.T
    pb.open_phase("p4")
    pb.sbuf("probs4", (128, N_EXP, NT), F32, ph="p4")
    pb.sbuf("exp4", (128, N_EXP, NT), F32, ph="p4")
    pb.sbuf("den4", (128, NT), F32, ph="p4")
    pb.sbuf("cumm", (128, N_EXP, NT), F32, ph="p4")
    pb.sbuf("maskf", (128, N_EXP, NT), F32, ph="p4")
    pb.sbuf("cmp4", (128, N_EXP, NT), BF16, ph="p4")
    pb.sbuf("cnt4", (128, N_EXP), F32, ph="p4")
    pb.sbuf("brow", (128, 64), F32, ph="p4")
    pb.sbuf("taubc", (128, N_EXP), F32, ph="p4")
    pb.sbuf("gmine", (128, 4, NT), F32, ph="p4")
    pb.sbuf("gmb", (128, 4, NT), BF16, ph="p4")
    pb.sbuf("cmine", (128, 4, NT), F32, ph="p4")
    pb.sbuf("srow", (128, 3, NT), F32, ph="p4")
    pb.sbuf("cwt", (128, NT), F32, ph="p4")
    pb.sbuf("t4a", (128, NT), F32, ph="p4")
    pb.sbuf("t4b", (128, NT), F32, ph="p4")
    pb.sbuf("gw4", (128, 8, N_EXP), BF16, ph="p4")
    pb.sbuf("s_triu", (128, 128), F32, ph="p4")
    pb.sbuf("s_iotasel", (128, MOE_K), F32, ph="p4")
    pb.sbuf("s_iotacol", (128, 1), F32, ph="p4")
    for s in ("sce", "pvm", "pne", "hgr", "hgf", "xn2r", "sq", "sn", "nr", "pex",
              "fst", "fbc", "shc", "xw2", "fq0", "fn0", "fb0"):
        pb.sem(s)

    p4_gate(pb)
    p4_select(pb)
    p4_experts(pb)
    p4_scatter(pb)
    pb.close_phase("p4")

    if "p4" in pb.dbg:
        dd = pb.dout("d_F1T", (D, HALF), BF16)
        def dumpf(gr, dd=dd):
            g = Eng(pb, "g", gr)
            g.dma(dd[:, :], T["F1T"][:, :], "gd")
        block.gpsimd(dumpf)


def p5_prep(pb):
    nc, block = pb.nc, pb.block
    S, T, P = pb.S, pb.T, pb.P
    pb.open_phase("p5x")
    pb.sbuf("wkv5", (128, 2, 8, D), BF16, ph="p5x")
    pb.sbuf("wi1sb", (128, 8, 512), BF16, ph="p5x")
    pb.sbuf("wi2sb", (128, 4, 1), BF16, ph="p5x")
    pb.sbuf("himp5", (128, 4, 512), BF16, ph="p5x")
    pb.sbuf("improw", (1, 512), F32, ph="p5x")
    pb.sbuf("bvbc", (128, D), F32, ph="p5x")
    pb.sbuf("brsrc", (1, D), F32, ph="p5x")
    pb.sbuf("wi2r", (1, 512), BF16, ph="p5x")
    pb.sbuf("xc5", (128, 2, 8, 512), BF16, ph="p5x")

    @block.gpsimd
    def _(gr):
        g = Eng(pb, "g", gr)
        g.dma(S["wkv5"][:, 0, :, :], T["wk1"][:, :].rearrange("(kt p) m -> p kt m", p=128), "wld")
        g.dma(S["wkv5"][:, 1, :, :], T["wv1"][:, :].rearrange("(kt p) m -> p kt m", p=128), "wld")
        g.dma(S["wi1sb"][:, :, :], T["wi1"][:, :].rearrange("(kt p) m -> p kt m", p=128), "wld")
        g.dma(S["wi2r"][0:1, :], T["wi2"][:, :], "wld")
        g.wait("wld", pb.cnt["wld"])
        for mt in range(4):
            g.dma(S["wi2sb"][:, mt, 0:1], S["wi2r"][0:1, mt * 128:(mt + 1) * 128], "wld")
        g.dma(S["s_t2f"][:, :], T["two2f128"][:, :], "wld")
        g.dma(S["b5c"][:, :, :],
              bass.AP(T["b1vecT"], 0, [[8, 128], [1024, 8], [1, 8]]), "wld")
        g.dma(S["brsrc"][0:1, :], T["b1vec"][2:3, :], "wld")
    wld5 = pb.cnt["wld"]

    def xchunk_loader(stream, mm_sem, mm_base, per_ch):
        @block.sync
        def _(y):
            pb.wait("y", y, "cc", pb.cnt["cc"])
            s0 = stream.nissued
            for ch in range(8):
                if ch >= 2:
                    pb.wait("y", y, mm_sem, mm_base[0] + per_ch * (ch - 1))
                else:
                    pb.wait("y", y, mm_sem, mm_base[0])
                for ft in range(8):
                    i = s0 + ch * 8 + ft
                    if ch * 8 + ft >= 2:
                        stream.wait("y", y, i - 2)
                    stream.issue(y, S["xc5"][:, ch % 2, ft, :],
                                 T["xn1_all"][ch * 512:(ch + 1) * 512, ft * 128:(ft + 1) * 128],
                                 transpose=True)

    @block.tensor
    def _(tr):
        t = Eng(pb, "t", tr, chained=False)
        t.wait("wld", wld5)
        dd0 = pb.cnt["dd"]
        for nh in range(2):
            if nh:
                t.wait("dd", dd0 + 1)
            inst = tr.matmul(P["pq0"][:, :], S["sb_onesf"][0:1, :],
                             S["brsrc"][0:1, nh * 512:nh * 512 + 512],
                             start=True, stop=True, skip_group_check=True)
            inst.then_inc(pb.sems["mm"], 1); pb.tick("mm")

    @block.vector
    def _(vr):
        v = Eng(pb, "v", vr)
        mm0 = pb.cnt["mm"] - 2
        for nh in range(2):
            v.wait("mm", mm0 + nh + 1)
            v.tensor_copy(S["bvbc"][:, nh * 512:nh * 512 + 512], P["pq0"][:, :])
            v.sig("dd")
        v.memset(S["kn2"][:, :], -1e30)
        v.memset(S["vt5"][:, :, :].rearrange("p t (h m) -> p t h m", m=65)[:, :, :, 64:65], 1.0)
        v.sig("dd")

    mmi = [0, 0]
    xs_i = pb.stream("xt4", 2)

    @block.tensor
    def _(tr):
        t = Eng(pb, "t", tr, chained=False)
        t.wait("wld", wld5)
        t.wait("dd", pb.cnt["dd"])
        mmi[0] = pb.cnt["mm"]
        mmi[1] = pb.cnt["vkr"]
        p1k0 = pb.cnt["p1k"]
        qk0 = pb.cnt["qkr"]
        x0 = 0
        for ch in range(8):
            xs_i.wait("t", tr, x0 + ch * 8 + 6)
            xs_i.wait("t", tr, x0 + ch * 8 + 7)
            for mt in range(4):
                if ch >= 1:
                    t.wait("p1k", p1k0 + 4 * (ch - 1) + mt + 1)
                bank = P[f"pq{mt}"]
                for kt in range(8):
                    inst = tr.matmul(bank[:, :], S["wi1sb"][:, kt, mt * 128:(mt + 1) * 128],
                                     S["xc5"][:, ch % 2, kt, :],
                                     start=(kt == 0), stop=(kt == 7), skip_group_check=True)
                inst.then_inc(pb.sems["mm"], 1); pb.tick("mm")
            t.wait("p1k", p1k0 + 4 * (ch + 1))
            if ch >= 1:
                t.wait("qkr", qk0 + ch)
            for mt in range(4):
                inst = tr.matmul(P["pq4"][0:1, 0:512], S["wi2sb"][:, mt, :],
                                 S["himp5"][:, mt, :], start=(mt == 0), stop=(mt == 3),
                                 skip_group_check=True)
            inst.then_inc(pb.sems["mm"], 1); pb.tick("mm")
            if ch >= 1:
                for c in range(4):
                    inst = tr.matmul(P["pq5"][:, (ch - 1) * 4 + c:(ch - 1) * 4 + c + 1],
                                     S["improw"][0:1, c * 128:(c + 1) * 128],
                                     S["sb_identf"][0:1, 0:1], start=(ch == 1 and c == 0),
                                     stop=False, is_transpose=True, skip_group_check=True)
                inst.then_inc(pb.sems["vkr"], 1); pb.tick("vkr")
        t.wait("qkr", qk0 + 8)
        for c in range(4):
            inst = tr.matmul(P["pq5"][:, 28 + c:28 + c + 1],
                             S["improw"][0:1, c * 128:(c + 1) * 128],
                             S["sb_identf"][0:1, 0:1], start=False,
                             stop=(c == 3), is_transpose=True, skip_group_check=True)
        inst.then_inc(pb.sems["vkr"], 1); pb.tick("vkr")

    xchunk_loader(xs_i, "mm", mmi, 5)

    @block.scalar
    def _(sr):
        s = Eng(pb, "s", sr)
        for ch in range(8):
            for mt in range(4):
                s.wait("mm", mmi[0] + 5 * ch + mt + 1)
                bank = P[f"pq{mt}"]
                s.activation(S["himp5"][:, mt, :], bank[:, :], AF.Relu,
                             bias=S["b5c"][:, mt, 4:5])
                s.sig("p1k")

    @block.vector
    def _(vr):
        v = Eng(pb, "v", vr)
        vk0 = mmi[1]
        for ch in range(8):
            v.wait("mm", mmi[0] + 5 * (ch + 1))
            if ch >= 1:
                v.wait("vkr", vk0 + ch)
            v.tensor_scalar(S["improw"][0:1, :], P["pq4"][0:1, 0:512],
                            S["b5c"][0:1, 0, 5:6], None, op0=ALU.add)
            v.sig("qkr")

    @block.vector
    def _(vr):
        v = Eng(pb, "v", vr)
        v.wait("vkr", pb.cnt["vkr"])
        v.tensor_copy(S["imp_tm"][:, :], P["pq5"][:, 0:32])
        c = S["colsc"]
        v.tensor_reduce(c[:, 0:1], S["imp_tm"][:, :], axis=AXX, op=ALU.min)
        v.reduce_max(c[:, 1:2], S["imp_tm"][:, :], axis=AXX)
        for col in range(2):
            op = ALU.min if col == 0 else ALU.max
            v.tensor_copy(c[0:64, 2:3], c[64:128, col:col + 1])
            v.tensor_tensor(c[0:64, col:col + 1], c[0:64, col:col + 1], c[0:64, 2:3], op=op)
            v.tensor_copy(c[0:32, 2:3], c[32:64, col:col + 1])
            v.tensor_tensor(c[0:32, col:col + 1], c[0:32, col:col + 1], c[0:32, 2:3], op=op)
        v.sig("shr")

    @block.tensor
    def _(tr):
        t = Eng(pb, "t", tr, chained=False)
        t.wait("shr", pb.cnt["shr"])
        for col in range(2):
            inst = tr.matmul(P["pq6"][0:1, col * 32:col * 32 + 32],
                             S["colsc"][0:32, col:col + 1], S["sb_identf"][0:32, 0:32],
                             start=(col == 0), stop=(col == 1), is_transpose=True,
                             skip_group_check=True)
        inst.then_inc(pb.sems["mm"], 1); pb.tick("mm")

    @block.vector
    def _(vr):
        v = Eng(pb, "v", vr)
        v.wait("mm", pb.cnt["mm"])
        br = S["brow5"]
        v.memset(br[:, :], 0.0)
        v.tensor_reduce(br[0:1, 0:1], P["pq6"][0:1, 0:32], axis=AXX, op=ALU.min)
        v.reduce_max(br[0:1, 1:2], P["pq6"][0:1, 32:64], axis=AXX)
        v.tensor_scalar_add(br[0:1, 1:2], br[0:1, 1:2], 1.0)
        v.tensor_tensor(br[0:1, 2:3], br[0:1, 0:1], br[0:1, 1:2], op=ALU.add)
        v.tensor_scalar_mul(br[0:1, 2:3], br[0:1, 2:3], 0.5)
        v.sig("pvm")

    for it in range(36):
        final = it == 35

        @block.tensor
        def _(tr, final=final):
            t = Eng(pb, "t", tr, chained=False)
            t.wait("pvm", pb.cnt["pvm"])
            src = S["brow5"][:, 0:1] if final else S["brow5"][:, 2:3]
            inst = tr.matmul(P["pq7"][:, 0:1], S["sb_bc0f"][:, :], src,
                             start=True, stop=True, skip_group_check=True)
            inst.then_inc(pb.sems["mm"], 1); pb.tick("mm")

        @block.vector
        def _(vr, final=final):
            v = Eng(pb, "v", vr)
            v.wait("mm", pb.cnt["mm"])
            v.tensor_copy(S["tbc5"][:, :], P["pq7"][:, 0:1])
            if final:
                v.tensor_scalar(S["kmb5"][:, :], S["imp_tm"][:, :], S["tbc5"][:, 0:1],
                                None, op0=ALU.is_ge)
                v.tensor_scalar(S["kmb5"][:, :], S["kmb5"][:, :], -NEG_BIG, NEG_BIG,
                                op0=ALU.mult, op1=ALU.add)
                v.sig("pvm")
            else:
                v.tensor_scalar(S["cmpi"][:, :], S["imp_tm"][:, :], S["tbc5"][:, 0:1],
                                None, op0=ALU.is_ge)
                v.reduce_sum(S["cnti"][:, :], S["cmpi"][:, :], axis=AXX)
                v.sig("pvm")

        if not final:
            @block.tensor
            def _(tr):
                t = Eng(pb, "t", tr, chained=False)
                t.wait("pvm", pb.cnt["pvm"])
                inst = tr.matmul(P["pq7"][0:1, 32:33], S["sb_onesf"][:, 0:1], S["cnti"][:, :],
                                 start=True, stop=True, skip_group_check=True)
                inst.then_inc(pb.sems["mm"], 1); pb.tick("mm")

            @block.vector
            def _(vr):
                v = Eng(pb, "v", vr)
                v.wait("mm", pb.cnt["mm"])
                br = S["brow5"]
                v.tensor_copy(br[0:1, 3:4], P["pq7"][0:1, 32:33])
                v.tensor_scalar(br[0:1, 4:5], br[0:1, 3:4], float(L1_K), None, op0=ALU.is_ge)
                v.tensor_tensor(br[0:1, 5:6], br[0:1, 2:3], br[0:1, 0:1], op=ALU.subtract)
                v.tensor_tensor(br[0:1, 6:7], br[0:1, 5:6], br[0:1, 4:5], op=ALU.mult)
                v.tensor_tensor(br[0:1, 0:1], br[0:1, 0:1], br[0:1, 6:7], op=ALU.add)
                v.tensor_scalar(br[0:1, 7:8], br[0:1, 4:5], -1.0, 1.0, op0=ALU.mult, op1=ALU.add)
                v.tensor_tensor(br[0:1, 5:6], br[0:1, 2:3], br[0:1, 1:2], op=ALU.subtract)
                v.tensor_tensor(br[0:1, 6:7], br[0:1, 5:6], br[0:1, 7:8], op=ALU.mult)
                v.tensor_tensor(br[0:1, 1:2], br[0:1, 1:2], br[0:1, 6:7], op=ALU.add)
                v.tensor_tensor(br[0:1, 2:3], br[0:1, 0:1], br[0:1, 1:2], op=ALU.add)
                v.tensor_scalar_mul(br[0:1, 2:3], br[0:1, 2:3], 0.5)
                v.sig("pvm")

    mmk = [0]
    xs_k = pb.stream("xt4", 2)

    @block.tensor
    def _(tr):
        t = Eng(pb, "t", tr, chained=False)
        mmk[0] = pb.cnt["mm"]
        pk0 = pb.cnt["p1k"]
        fpa0b = pb.cnt["fpa0"]
        x0 = 0
        for ch in range(8):
            xs_k.wait("t", tr, x0 + ch * 8 + 6)
            xs_k.wait("t", tr, x0 + ch * 8 + 7)
            for mt in range(8):
                i = ch * 8 + mt
                if i >= 2:
                    t.wait("p1k", pk0 + 2 * (i - 1))
                bank = P[f"pq{i % 2}"]
                for kt in range(8):
                    inst = tr.matmul(bank[:, :], S["wkv5"][:, 0, kt, mt * 128:(mt + 1) * 128],
                                     S["xc5"][:, ch % 2, kt, :],
                                     start=(kt == 0), stop=(kt == 7), skip_group_check=True)
                inst.then_inc(pb.sems["mm"], 1); pb.tick("mm")
                t.wait("p1k", pk0 + 2 * i + 2)
                if i >= 1:
                    t.wait("fpa0", fpa0b + i)
                inst = tr.matmul(P["pq2"][:, :], S["s_t2f"][:, :], S["qsq5"][:, :],
                                 start=True, stop=True, skip_group_check=True)
                inst.then_inc(pb.sems["p1f"], 1); pb.tick("p1f")

    xchunk_loader(xs_k, "mm", mmk, 8)

    @block.scalar
    def _(sr):
        s = Eng(pb, "s", sr)
        pf0 = pb.cnt["p1f"] - 64
        for ch in range(8):
            for mt in range(8):
                i = ch * 8 + mt
                s.wait("mm", mmk[0] + i + 1)
                bank = P[f"pq{i % 2}"]
                s.activation(S["kp5"][:, mt, ch * 512:ch * 512 + 512], bank[:, :],
                             AF.Identity, bias=S["b5c"][:, mt, 1:2])
                s.sig("p1k")
                if i >= 1:
                    s.wait("p1f", pf0 + i)
                s.activation(S["qsq5"][:, :], bank[:, :], AF.Square,
                             bias=S["b5c"][:, mt, 1:2])
                s.sig("p1k")

    @block.vector
    def _(vr):
        v = Eng(pb, "v", vr)
        pf0 = pb.cnt["p1f"] - 64
        for ch in range(8):
            for mt in range(8):
                i = ch * 8 + mt
                v.wait("p1f", pf0 + i + 1)
                v.reduce_max(S["colsc"][:, 3:4], P["pq2"][:, :], axis=AXX)
                v.tensor_tensor(S["kn2"][:, mt:mt + 1], S["kn2"][:, mt:mt + 1],
                                S["colsc"][:, 3:4], op=ALU.max)
                v.sig("fpa0")

    mmv = [0]
    xs_v = pb.stream("xt4", 2)

    @block.tensor
    def _(tr):
        t = Eng(pb, "t", tr, chained=False)
        t.wait("dd", pb.cnt["dd"])
        mmv[0] = pb.cnt["mm"]
        fo0 = pb.cnt["fo0"]
        x0 = 0
        for ch in range(8):
            xs_v.wait("t", tr, x0 + ch * 8 + 6)
            xs_v.wait("t", tr, x0 + ch * 8 + 7)
            for twc in range(4):
                tt = ch * 4 + twc
                for nh in range(2):
                    i = tt * 2 + nh
                    if i >= 2:
                        t.wait("fo0", fo0 + i - 1)
                    bank = P[f"pq{3 + i % 2}"]
                    for kt in range(8):
                        inst = tr.matmul(bank[:, :], S["xc5"][:, ch % 2, kt, twc * 128:(twc + 1) * 128],
                                         S["wkv5"][:, 1, kt, nh * 512:nh * 512 + 512],
                                         start=(kt == 0), stop=(kt == 7), skip_group_check=True)
                    inst.then_inc(pb.sems["mm"], 1); pb.tick("mm")

    xchunk_loader(xs_v, "mm", mmv, 8)

    @block.vector
    def _(vr):
        v = Eng(pb, "v", vr)
        for tt in range(NT):
            for nh in range(2):
                i = tt * 2 + nh
                v.wait("mm", mmv[0] + i + 1)
                bank = P[f"pq{3 + i % 2}"]
                dst = S["vt5"][:, tt, :].rearrange("p (h m) -> p h m", m=65)[:, nh * 8:nh * 8 + 8, 0:64]
                v.tensor_tensor(dst, bank[:, :].rearrange("p (h m) -> p h m", m=64),
                                S["bvbc"][:, nh * 512:nh * 512 + 512].rearrange("p (h m) -> p h m", m=64),
                                op=ALU.add)
                v.sig("fo0")

    pb.barrier("p5x", drain=("wld", "xt40", "xt41"))
    pb.close_phase("p5x")


def p5_qc(pb, qc):
    nc, block = pb.nc, pb.block
    S, T, P = pb.S, pb.T, pb.P
    xq5 = pb.stream("xs", 2)
    wq5s = pb.stream("w14", 2)

    @block.gpsimd
    def _(gr):
        g = Eng(pb, "g", gr)
        p1kb = pb.cnt["p1k"]
        for mt in range(8):
            if mt >= 2:
                pb.wait("g", gr, "p1k", p1kb + 2 * (mt - 1))
            else:
                pb.wait("g", gr, "p1k", p1kb)
            wq5s.issue(gr, S["wqst"][:, mt % 2, :, :],
                       T["wq1"][:, mt * 128:(mt + 1) * 128].rearrange("(kt p) c -> p kt c", p=128))

    @block.sync
    def _(y):
        pb.wait("y", y, "p1e", pb.cnt["p1e"])
        for ft in range(8):
            if ft >= 2:
                xq5.wait("y", y, xq5.nissued - 2)
            xq5.issue(y, S["scrA"][:, ft, :],
                      T["xn1_own"][qc * 512:(qc + 1) * 512, ft * 128:(ft + 1) * 128],
                      transpose=True)
    xq50 = xq5.nissued - 8

    mmq = [0]

    @block.tensor
    def _(tr):
        t = Eng(pb, "t", tr, chained=False)
        xq5.wait("t", tr, xq50 + 6)
        xq5.wait("t", tr, xq50 + 7)
        t.wait("fs0", pb.cnt["fs0"])
        mmq[0] = pb.cnt["mm"]
        pk0 = pb.cnt["p1k"]
        fpa0q = pb.cnt["fpa0"]
        for mt in range(8):
            if mt >= 2:
                t.wait("p1k", pk0 + 2 * (mt - 1))
            bank = P[f"pq{6 + mt % 2}"]
            wq5s.wait("t", tr, wq5s.nissued - 8 + mt)
            for kt in range(8):
                inst = tr.matmul(bank[:, :], S["wqst"][:, mt % 2, kt, :],
                                 S["scrA"][:, kt, :], start=(kt == 0), stop=(kt == 7),
                                 skip_group_check=True)
            inst.then_inc(pb.sems["mm"], 1); pb.tick("mm")
            t.wait("p1k", pk0 + 2 * mt + 2)
            inst = tr.matmul(P["pq2"][:, :], S["s_t2f"][:, :], S["qsq5"][:, :],
                             start=True, stop=True, skip_group_check=True)
            inst.then_inc(pb.sems["p1f"], 1); pb.tick("p1f")
            if mt < 7:
                t.wait("fpa0", fpa0q + mt + 1)

    @block.scalar
    def _(sr):
        s = Eng(pb, "s", sr)
        pf0q = pb.cnt["p1f"] - 8
        for mt in range(8):
            s.wait("mm", mmq[0] + mt + 1)
            bank = P[f"pq{6 + mt % 2}"]
            s.activation(S["qc5"][:, mt, :], bank[:, :], AF.Identity,
                         bias=S["b5c"][:, mt, 0:1])
            s.sig("p1k")
            if mt >= 1:
                s.wait("p1f", pf0q + mt)
            s.activation(S["qsq5"][:, :], bank[:, :], AF.Square, bias=S["b5c"][:, mt, 0:1])
            s.sig("p1k")

    @block.vector
    def _(vr):
        v = Eng(pb, "v", vr)
        pf0 = pb.cnt["p1f"] - 8
        v.memset(S["qn2"][:, :], -1e30)
        for mt in range(8):
            v.wait("p1f", pf0 + mt + 1)
            v.reduce_max(S["colsc"][:, 3:4], P["pq2"][:, :], axis=AXX)
            v.tensor_tensor(S["qn2"][:, mt:mt + 1], S["qn2"][:, mt:mt + 1],
                            S["colsc"][:, 3:4], op=ALU.max)
            v.sig("fpa0")
        v.tensor_tensor(S["cm5"][:, 0:8], S["qn2"][:, :], S["kn2"][:, :], op=ALU.mult)
        v.sig("sn")

    @block.scalar
    def _(sr):
        s = Eng(pb, "s", sr)
        s.wait("sn", pb.cnt["sn"])
        s.activation(S["cm5"][:, 8:16], S["cm5"][:, 0:8], AF.Sqrt, scale=0.015625)
        s.sig("sn")

    @block.tensor
    def _(tr):
        t = Eng(pb, "t", tr, chained=False)
        t.wait("sn", pb.cnt["sn"])
        for r in range(2):
            inst = tr.matmul(P["pq2"][:, r * 8:r * 8 + 8], S["sb_onesf"][64 * r:64 * r + 1, :],
                             S["cm5"][64 * r:64 * r + 1, 8:16], start=(r == 0), stop=(r == 1),
                             skip_group_check=True)
        inst.then_inc(pb.sems["mm"], 1); pb.tick("mm")

    @block.vector
    def _(vr):
        v = Eng(pb, "v", vr)
        v.wait("mm", pb.cnt["mm"])
        v.tensor_copy(S["cbc5"][:, :], P["pq2"][:, 0:16])
        for h in range(HEADS):
            v.tensor_scalar(S["kmbc"][:, :, h], S["kmb5"][:, :],
                            S["cbc5"][:, 8 * (h % 2) + h // 2: 8 * (h % 2) + h // 2 + 1],
                            None, op0=ALU.subtract)
        v.sig("shr")

    st5 = {"mm0": 0, "pk0": 0, "fv0": 0, "qk0": 0, "fb0": 0, "vk0": 0, "p1e0": 0}

    @block.tensor
    def _(tr):
        t = Eng(pb, "t", tr, chained=False)
        t.wait("shr", pb.cnt["shr"])
        st5["mm0"] = pb.cnt["mm"]
        st5["pk0"] = pb.cnt["p1k"]
        st5["fv0"] = pb.cnt["fv1"]
        st5["qk0"] = pb.cnt["qkr"]
        st5["fb0"] = pb.cnt["fb1"]
        st5["vk0"] = pb.cnt["vkr"]
        st5["p1e0"] = pb.cnt["p1e"]
        st5["sq0"] = pb.cnt["sq"]
        for h in range(HEADS):
            r = 64 * (h % 2)
            f = h // 2
            if h >= 2:
                t.wait("p1e", st5["p1e0"] + h - 1)
            nb = P[f"pq{4 + h % 2}"]
            for g in range(9):
                if g < 8:
                    for kt4 in range(4):
                        kt = g * 4 + kt4
                        if kt >= 4:
                            t.wait("p1k", st5["pk0"] + h * NT + kt - 3)
                        sb = P[f"pq{kt % 4}"]
                        inst = tr.matmul(sb[:, :], S["kp5"][r:r + 64, f, kt * 128:(kt + 1) * 128],
                                         S["qc5"][r:r + 64, f, :], start=True, stop=True,
                                         skip_group_check=True)
                        inst.then_inc(pb.sems["mm"], 1); pb.tick("mm")
                if g >= 1:
                    for kt4 in range(4):
                        k2 = (g - 1) * 4 + kt4
                        t.wait("p1k", st5["pk0"] + h * NT + k2 + 1)
                        inst = tr.matmul(nb[0:65, :], S["vt5"][:, k2, 65 * h:65 * h + 65],
                                         S["scrA"][:, k2 % 8, :], start=(k2 == 0), stop=(k2 == NT - 1),
                                         skip_group_check=True)
                        inst.then_inc(pb.sems["fv1"], 1); pb.tick("fv1")
            t.wait("qkr", st5["qk0"] + h + 1)
            inst = tr.matmul(P["pq6"][:, :], S["sb_bc64f"][:, :], S["dsb5"][:, 512:1024],
                             start=True, stop=True, skip_group_check=True)
            inst.then_inc(pb.sems["fb1"], 1); pb.tick("fb1")

    @block.scalar
    def _(sr):
        s = Eng(pb, "s", sr)
        for h in range(HEADS):
            for kt in range(NT):
                s.wait("mm", st5["mm0"] + h * NT + kt + 1)
                if h * NT + kt >= 8:
                    s.wait("fv1", st5["fv0"] + h * NT + kt - 7)
                sb = P[f"pq{kt % 4}"]
                s.activation(S["scrA"][:, kt % 8, :], sb[:, :], AF.Exp,
                             bias=S["kmbc"][:, kt, h:h + 1], scale=0.125)
                s.sig("p1k")


    @block.vector
    def _(vr):
        v = Eng(pb, "v", vr)
        for h in range(HEADS):
            nb = P[f"pq{4 + h % 2}"]
            v.wait("fv1", st5["fv0"] + NT * (h + 1))
            if h >= 1:
                v.wait("fb1", st5["fb0"] + h)
            v.tensor_copy(S["dsb5"][64:65, 0:512], nb[64:65, :])
            v.reciprocal(S["dsb5"][64:65, 512:1024], S["dsb5"][64:65, 0:512])
            v.sig("qkr")
            v.wait("fb1", st5["fb0"] + h + 1)
            v.tensor_copy(S["rbc5"][:, :], P["pq6"][:, :])
            v.sig("vkr")
            r = 64 * (h % 2)
            f = h // 2
            v.tensor_tensor(S["conc5"][r:r + 64, f, :], nb[0:64, :], S["rbc5"][r:r + 64, :],
                            op=ALU.mult)
            v.sig("p1e")

    mmw = [0]

    @block.sync
    def _(y):
        fs0s = pb.cnt["fs0"]
        for tt in range(4):
            pb.wait("y", y, "fs0", fs0s + tt)
            tile = qc * 4 + tt
            y.dma_start(out=S["x1c5"][:, :], in_=T["x1_tm"][tile * 128:(tile + 1) * 128, :]
                        ).then_inc(pb.sem("xr"), 16)
            pb.tick("xr", 16)
            y.dma_start(out=S["f1b5"][:, :], in_=T["F1T"][:, tile * 128:(tile + 1) * 128],
                        transpose=True).then_inc(pb.sem("xr"), 16)
            pb.tick("xr", 16)

    xr0 = pb.cnt["xr"] - 128

    @block.tensor
    def _(tr):
        t = Eng(pb, "t", tr, chained=False)
        t.wait("p1e", pb.cnt["p1e"])
        t.wait("wld", pb.cnt["wld"])
        mmw[0] = pb.cnt["mm"]
        fs00 = pb.cnt["fs0"]
        for tt in range(4):
            if tt >= 1:
                t.wait("fs0", fs00 + tt)
            for nh in range(2):
                bank = P[f"pq{nh}"]
                for kt in range(8):
                    inst = tr.matmul(bank[:, :], S["conc5"][:, kt, tt * 128:(tt + 1) * 128],
                                     S["wo5"][:, kt, nh * 512:nh * 512 + 512],
                                     start=(kt == 0), stop=(kt == 7), skip_group_check=True)
                inst.then_inc(pb.sems["mm"], 1); pb.tick("mm")

    @block.scalar
    def _(sr):
        s = Eng(pb, "s", sr)
        fs0b = pb.cnt["fs0"]
        for tt in range(4):
            s.wait("xr", xr0 + 32 * tt + 32)
            if tt >= 1:
                s.wait("fs0", fs0b + tt)
            s.activation(S["f1f5"][:, :], S["f1b5"][:, :], AF.Identity)
            s.sig("fs1")
            s.wait("fs0", fs0b + tt + 1)
            s.dma(T["out"][(qc * 4 + tt) * 128:(qc * 4 + tt + 1) * 128, :],
                  S["outst"][:, :], "so")

    @block.vector
    def _(vr):
        v = Eng(pb, "v", vr)
        fs10 = pb.cnt["fs1"] - 4
        so0 = pb.cnt["so"] - 64
        for tt in range(4):
            v.wait("mm", mmw[0] + 2 * tt + 2)
            v.wait("fs1", fs10 + tt + 1)
            v.wait("so", so0 + 16 * tt)
            for nh in range(2):
                bank = P[f"pq{nh}"]
                sl = slice(nh * 512, nh * 512 + 512)
                v.tensor_tensor(S["outst"][:, sl], bank[:, :], S["x1c5"][:, sl], op=ALU.add)
                v.tensor_tensor(S["outst"][:, sl], S["outst"][:, sl], S["borow"][:, sl], op=ALU.add)
                v.tensor_tensor(S["outst"][:, sl], S["outst"][:, sl], S["f1f5"][:, sl], op=ALU.add)
            v.sig("fs0")


def p5(pb):
    nc, block = pb.nc, pb.block
    S, T, P = pb.S, pb.T, pb.P
    pb.open_phase("p5")
    pb.sbuf("kp5", (128, 8, N), BF16, ph="p5")
    pb.sbuf("vt5", (128, NT, 1040), BF16, ph="p5")
    pb.sbuf("b5c", (128, 8, 8), F32, ph="p5")
    pb.sbuf("s_t2f", (128, 128), F32, ph="p5")
    pb.sbuf("kmb5", (128, 32), F32, ph="p5")
    pb.sbuf("imp_tm", (128, 32), F32, ph="p5")
    pb.sbuf("qn2", (128, 8), F32, ph="p5")
    pb.sbuf("kn2", (128, 8), F32, ph="p5")
    pb.sbuf("cm5", (128, 16), F32, ph="p5")
    pb.sbuf("cbc5", (128, 16), F32, ph="p5")
    pb.sbuf("colsc", (128, 4), F32, ph="p5")
    pb.sbuf("brow5", (128, 16), F32, ph="p5")
    pb.sbuf("tbc5", (128, 1), F32, ph="p5")
    pb.sbuf("cmpi", (128, 32), BF16, ph="p5")
    pb.sbuf("cnti", (128, 1), F32, ph="p5")
    pb.sbuf("qsq5", (128, 512), F32, ph="p5")
    for i in range(8):
        pb.psum(f"pq{i}", (128, 512), F32, ph="p5")
    for s in ("so", "fv1", "fb1", "fs1"):
        pb.sem(s)

    p5_prep(pb)

    pb.open_phase("p5m")
    pb.sbuf("wqst", (128, 2, 8, 128), BF16, ph="p5m")
    pb.sbuf("wo5", (128, 8, D), BF16, ph="p5m")
    pb.sbuf("kmbc", (128, 32, HEADS), F32, ph="p5m")
    pb.sbuf("scrA", (128, 8, 512), BF16, ph="p5m")
    pb.sbuf("qc5", (128, 8, 512), BF16, ph="p5m")
    pb.sbuf("conc5", (128, 8, 512), BF16, ph="p5m")
    pb.sbuf("dsb5", (128, 1024), F32, ph="p5m")
    pb.sbuf("rbc5", (128, 512), F32, ph="p5m")
    pb.sbuf("borow", (128, D), F32, ph="p5m")
    pb.sbuf("x1c5", (128, D), F32, ph="p5m")
    pb.sbuf("f1b5", (128, D), BF16, ph="p5m")
    pb.sbuf("f1f5", (128, D), F32, ph="p5m")
    pb.sbuf("outst", (128, D), F32, ph="p5m")

    @block.gpsimd
    def _(gr):
        g = Eng(pb, "g", gr)
        g.dma(S["wo5"][:, :, :], T["wo1"][:, :].rearrange("(kt p) m -> p kt m", p=128), "wld")

    @block.gpsimd
    def _(gr):
        g = Eng(pb, "g", gr)
        g.wait("dd", pb.cnt["dd"] + 1)
        g.dma(S["dsb5"][0:1, :], T["b1vec"][3:4, :], "xld")

    @block.tensor
    def _(tr):
        t = Eng(pb, "t", tr, chained=False)
        t.wait("xld", pb.cnt["xld"])
        dd0b = pb.cnt["dd"]
        for nh in range(2):
            if nh:
                t.wait("dd", dd0b + 2)
            inst = tr.matmul(P["pq3"][:, :], S["sb_onesf"][0:1, :],
                             S["dsb5"][0:1, nh * 512:nh * 512 + 512],
                             start=True, stop=True, skip_group_check=True)
            inst.then_inc(pb.sems["mm"], 1); pb.tick("mm")

    @block.vector
    def _(vr):
        v = Eng(pb, "v", vr)
        v.memset(S["dsb5"][:, :], 0.0)
        v.sig("dd")
        mm0 = pb.cnt["mm"] - 2
        for nh in range(2):
            v.wait("mm", mm0 + nh + 1)
            v.tensor_copy(S["borow"][:, nh * 512:nh * 512 + 512], P["pq3"][:, :])
            v.sig("dd")

    for qc in range(4):
        p5_qc(pb, qc)

    @block.sync
    def _(y):
        y.wait_ge(pb.sems["so"], pb.cnt["so"])

    pb.close_phase("p5m")
    pb.close_phase("p5")


def finish_placeholder(pb):
    block = pb.block
    S, T = pb.S, pb.T
    pb.open_phase("fin")
    pb.sbuf("zout", (128, D), F32, ph="fin")
    pb.sem("so")

    @block.vector
    def _(vr):
        v = Eng(pb, "v", vr)
        v.memset(S["zout"][:, :], 0.0)
        v.sig("ve")

    @block.sync
    def _(y):
        pb.wait("y", y, "ve", pb.cnt["ve"])
        pb.wait("y", y, "gd", pb.cnt["gd"])
        for i in range(NTO):
            y.dma_start(out=T["out"][i * 128:(i + 1) * 128, :],
                        in_=S["zout"][:, :]).then_inc(pb.sem("so"), 16); pb.tick("so", 16)
        y.wait_ge(pb.sem("so"), pb.cnt["so"])
    pb.close_phase("fin")


def build_program(dbg=()):
    pb = PB()
    pb.dbg = set(dbg)
    nc = pb.nc

    pb.din("x", (N, D))
    pb.din("Wh", (D, D)); pb.din("Wq", (D, D)); pb.din("Wk", (D, D)); pb.din("Wv", (D, D))
    pb.din("Wo0", (D, D))
    pb.din("bvec", (8, D))
    pb.din("Wqkv", (FFN_DEPTH, D, 3 * D))
    pb.din("Wo_f", (FFN_DEPTH, D, D))
    pb.din("W1", (FFN_DEPTH, D, W1C))
    pb.din("W2", (FFN_DEPTH, DIM_INNER, D))
    pb.din("wq1", (D, D)); pb.din("wk1", (D, D)); pb.din("wv1", (D, D)); pb.din("wo1", (D, D))
    pb.din("b1vec", (8, D)); pb.din("b1vecT", (D, 8))
    pb.din("wi1", (D, 512)); pb.din("wi2", (1, 512))
    pb.din("gate_w", (D, N_EXP))
    pb.din("moe_w1", (4, D, 4 * D))
    pb.din("moe_w2", (4, 4 * D, D))
    pb.din("identb", (128, 128), BF16)
    pb.din("rot", (128, 4 * DOMF), BF16)
    pb.din("maskP", (128, 4, WIN))
    pb.din("maskF", (128, 4, WIN))
    pb.din("two2f", (2, 128))
    pb.din("two2fT", (128, 2), BF16)
    pb.din("qsksc", (FFN_DEPTH, 128, 2))
    pb.din("lnpc", (FFN_DEPTH, 128, 16))
    pb.din("b1c", (FFN_DEPTH, 128, 44))
    pb.din("b2c", (FFN_DEPTH, 128, 8))
    pb.din("onesb", (128, 128), BF16)
    pb.din("triu128", (128, 128))
    pb.din("iotasel", (128, MOE_K))
    pb.din("iotacol", (128, 1))
    pb.din("two2f128", (128, 128))
    pb.din("offs", (1, 8), I32)
    pb.din("gates", (128, 8))

    pb.dout("out", (HALF, D))

    pb.dscr("xinnerT", (D, DOMF))
    pb.dscr("A_T", (D, HALF))
    pb.dscr("x1_tm", (HALF, D))
    pb.dscr("F1T", (D, HALF), BF16)
    pb.dscr("xn1_own", (HALF, D), BF16)
    pb.dscr("xn1_all", (N, D), BF16)
    pb.dscr("slab_own", (SLAB, D), BF16)
    pb.dscr("slab_all", (2 * SLAB, D), BF16)
    pb.dscr("h2w_own", (4 * MOE_K, D), BF16)
    pb.dscr("h2w_all", (8 * MOE_K, D), BF16)

    pb.din("bmapP", (128, D), BF16)
    pb.din("bc0f", (128, 128))
    pb.din("bc64f", (128, 128))
    pb.din("bvecT", (D, 8))

    pb.sbuf("sb_identf", (128, 128), F32)
    pb.sbuf("sb_gates", (128, 8), F32)
    pb.sbuf("sb_onesf", (128, 128), F32)
    pb.sbuf("sb_bc0f", (128, 128), F32)
    pb.sbuf("sb_bc64f", (128, 128), F32)
    pb.open_phase("g2")
    pb.sbuf("sb_onesb", (128, 128), BF16, ph="g2")
    pb.sbuf("sb_bmapPf", (128, D), F32, ph="g2")

    pb.block = pb.ctx.enter_context(nc.Block())
    import contextlib as _cl
    scope = (lambda n: nc.named_scope(n)) if SCOPES else (lambda n: _cl.nullcontext())

    pb.open_phase("x01")
    pb.sbuf("xnT", (128, 8 * HALF), BF16, ph="x01")
    pb.open_phase("p1p")
    pb.sbuf("sb_bkbv", (1, 2 * D), BF16, ph="p1p")
    pb.sbuf("kv_sb", (128, 520), BF16, ph="p1p")
    pb.sbuf("kv_sb2", (128, 520), BF16, ph="p1p")
    pb.open_phase("xoth")
    pb.sbuf("xnTo", (128, 8 * HALF), BF16, ph="xoth")
    pb.open_phase("p0")
    with scope("p0"):
        p0(pb)
    pb.close_phase("p0")
    if STOP_AFTER != "P0":
        with scope("p1"):
            p1(pb)
    else:
        pb.close_phase("xoth")
    pb.close_phase("p1p")
    pb.close_phase("x01")
    if STOP_AFTER not in ("P0", "P1"):
        with scope("p2"):
            p2(pb)
    pb.close_phase("g2")
    if STOP_AFTER not in ("P0", "P1", "P2"):
        with scope("p3"):
            p3(pb)
    if STOP_AFTER not in ("P0", "P1", "P2", "P3"):
        with scope("p4"):
            p4(pb)
    if STOP_AFTER is None:
        with scope("p5"):
            p5(pb)
    else:
        finish_placeholder(pb)

    pb.ctx.close()
    return pb


def _prep_inputs(inputs):
    perm = _sigma_perm()
    A_q, B_q, A_k, B_k = _rotary_tables()
    identb = np.eye(128, dtype=np.float32).astype(ml_dtypes.bfloat16)
    onesb = np.ones((128, 128), dtype=np.float32).astype(ml_dtypes.bfloat16)
    bmapP = np.zeros((128, D), dtype=np.float32)
    for r in range(HEADS):
        h = 2 * (r % 8) + r // 8
        bmapP[r, h * DH:(h + 1) * DH] = 1.0
    bmapP = bmapP.astype(ml_dtypes.bfloat16)
    bc0f = np.zeros((128, 128), np.float32); bc0f[0, :] = 1.0
    bc64f = np.zeros((128, 128), np.float32); bc64f[64, :] = 1.0
    triu128 = np.triu(np.ones((128, 128), dtype=np.float32))
    iotasel = np.broadcast_to(np.arange(1, MOE_K + 1, dtype=np.float32)[None, :],
                              (128, MOE_K)).copy()
    iotacol = np.arange(1, 129, dtype=np.float32)[:, None].copy()
    two2f128 = np.zeros((128, 128), np.float32)
    two2f128[0:64, 0] = 1.0
    two2f128[64:128, 64] = 1.0

    x_all = np.asarray(inputs["L0_attn_x"], dtype=np.float32)
    bvec = np.stack([
        np.asarray(inputs["L0_attn_bh"])[perm],
        np.asarray(inputs["L0_attn_bq"])[perm],
        np.asarray(inputs["L0_attn_bk"])[perm],
        np.asarray(inputs["L0_attn_bv"])[perm],
        np.asarray(inputs["L0_attn_beta"])[perm],
        np.asarray(inputs["L0_attn_ln_w"])[perm],
        np.asarray(inputs["L0_attn_ln_b"])[perm],
        np.asarray(inputs["L0_attn_bo"]),
    ]).astype(np.float32)
    b1vec = np.zeros((8, D), dtype=np.float32)
    b1vec[0, :] = np.asarray(inputs["L1_attn_bq"])
    b1vec[1, :] = np.asarray(inputs["L1_attn_bk"])
    b1vec[2, :] = np.asarray(inputs["L1_attn_bv"])
    b1vec[3, :] = np.asarray(inputs["L1_attn_bo"])
    b1vec[4, 0:512] = np.asarray(inputs["L1_attn_bi1"])
    b1vec[5, 0:1] = np.asarray(inputs["L1_attn_bi2"])
    qs_ = np.asarray(inputs["L0_ffn_q_scale"]).astype(np.float32)
    ks_ = np.asarray(inputs["L0_ffn_k_scale"]).astype(np.float32)
    qsksc = np.zeros((FFN_DEPTH, 128, 2), np.float32)
    for l in range(FFN_DEPTH):
        qsksc[l, 0:64, 0] = qs_[l]; qsksc[l, 64:128, 0] = qs_[l]
        qsksc[l, 0:64, 1] = ks_[l]; qsksc[l, 64:128, 1] = ks_[l]
    lng = np.asarray(inputs["L0_ffn_ln_g"]).astype(np.float32)
    lnb_ = np.asarray(inputs["L0_ffn_ln_b"]).astype(np.float32)
    lnpc = np.zeros((FFN_DEPTH, 128, 16), np.float32)
    for l in range(FFN_DEPTH):
        for mt in range(8):
            lnpc[l, :, 2 * mt] = lng[l, mt * 128:(mt + 1) * 128]
            lnpc[l, :, 2 * mt + 1] = lnb_[l, mt * 128:(mt + 1) * 128]
    b1_ = np.asarray(inputs["L0_ffn_b1"]).astype(np.float32)
    b1c = np.zeros((FFN_DEPTH, 128, 44), np.float32)
    for l in range(FFN_DEPTH):
        for part, base in ((0, 0), (22, 2730)):
            for t_ in range(22):
                nrow = 128 if t_ < 21 else 42
                b1c[l, 0:nrow, part + t_] = b1_[l, base + t_ * 128: base + t_ * 128 + nrow]
    b2_ = np.asarray(inputs["L0_ffn_b2"]).astype(np.float32)
    b2c = np.zeros((FFN_DEPTH, 128, 8), np.float32)
    for l in range(FFN_DEPTH):
        for mt in range(8):
            b2c[l, :, mt] = b2_[l, mt * 128:(mt + 1) * 128]
    two2f = np.zeros((2, 128), np.float32)
    two2f[0, 0:64] = 1.0; two2f[1, 64:128] = 1.0
    two2fT = two2f.T.copy().astype(ml_dtypes.bfloat16)
    kk_ = np.arange(2 * WIN)[:, None]
    qq_ = np.arange(WIN)[None, :]
    band = ((kk_ >= qq_) & (kk_ <= qq_ + WIN))
    maskP_add = np.where(band, 0.0, NEG_BIG).astype(np.float32)
    maskF_add = maskP_add.copy()
    maskF_add[0:WIN, :] = NEG_BIG
    maskP_P = maskP_add.reshape(4, 128, WIN).transpose(1, 0, 2).copy()
    maskF_P = maskF_add.reshape(4, 128, WIN).transpose(1, 0, 2).copy()
    rots = [A_q, B_q, A_k, B_k]

    in_maps = []
    for core in range(8):
        b, half = core // 2, core % 2
        own_lo = half * HALF
        x_rel = np.concatenate([x_all[b, own_lo:own_lo + HALF],
                                x_all[b, HALF - own_lo:HALF - own_lo + HALF]], axis=0)
        lo = own_lo - SLAB
        idx = np.clip(np.arange(lo, own_lo + HALF), 0, N - 1)
        rot = np.concatenate([np.vstack([t[:, idx], t[:, idx]]) for t in rots],
                             axis=1).astype(ml_dtypes.bfloat16)
        offs = np.zeros((1, 8), dtype=np.int32)
        offs[0, 0] = HALF - own_lo
        gates = np.zeros((128, 8), dtype=np.float32)
        gates[:, 0] = float(half)
        gates[:, 1] = 1.0 - float(half)
        m = {
            "x": np.ascontiguousarray(x_rel),
            "Wh": np.ascontiguousarray(np.asarray(inputs["L0_attn_Wh"])[:, perm]),
            "Wq": np.ascontiguousarray(np.asarray(inputs["L0_attn_Wq"])[:, perm]),
            "Wk": np.ascontiguousarray(np.asarray(inputs["L0_attn_Wk"])[:, perm]),
            "Wv": np.ascontiguousarray(np.asarray(inputs["L0_attn_Wv"])[:, perm]),
            "Wo0": np.ascontiguousarray(np.asarray(inputs["L0_attn_Wo"])[perm, :]),
            "bvec": bvec,
            "Wqkv": np.asarray(inputs["L0_ffn_Wqkv"]),
            "Wo_f": np.asarray(inputs["L0_ffn_Wo"]),
            "W1": np.asarray(inputs["L0_ffn_W1"]),
            "W2": np.asarray(inputs["L0_ffn_W2"]),
            "qsksc": qsksc, "lnpc": lnpc, "b1c": b1c, "b2c": b2c,
            "two2f": two2f, "two2fT": two2fT,
            "maskP": maskP_P, "maskF": (maskF_P if half == 0 else maskP_P),
            "wq1": np.ascontiguousarray(np.asarray(inputs["L1_attn_wq"]).T),
            "wk1": np.ascontiguousarray(np.asarray(inputs["L1_attn_wk"]).T),
            "wv1": np.ascontiguousarray(np.asarray(inputs["L1_attn_wv"]).T),
            "wo1": np.ascontiguousarray(np.asarray(inputs["L1_attn_wo"]).T),
            "b1vec": b1vec,
            "wi1": np.ascontiguousarray(np.asarray(inputs["L1_attn_wi1"]).T),
            "wi2": np.ascontiguousarray(np.asarray(inputs["L1_attn_wi2"])),
            "gate_w": np.asarray(inputs["L1_ffn_gate_w"]),
            "moe_w1": np.asarray(inputs["L1_ffn_w1"])[half * 4:half * 4 + 4],
            "moe_w2": np.asarray(inputs["L1_ffn_w2"])[half * 4:half * 4 + 4],
            "identb": identb,
            "rot": rot,
            "onesb": onesb, "bmapP": bmapP, "bc0f": bc0f, "bc64f": bc64f,
            "bvecT": np.ascontiguousarray(bvec.T),
            "b1vecT": np.ascontiguousarray(b1vec.T),
            "triu128": triu128, "iotasel": iotasel, "iotacol": iotacol,
            "two2f128": two2f128,
            "offs": offs, "gates": gates,
        }
        in_maps.append(m)
    return in_maps


def kernel(**inputs):
    in_maps = _prep_inputs(inputs)
    pb = build_program(dbg=DEBUG_DUMPS)
    res = run_bass_kernel_spmd(pb.nc, in_maps, core_ids=list(range(8)))
    out = np.zeros((B, N, D), dtype=np.float32)
    for core in range(8):
        b, half = core // 2, core % 2
        out[b, half * HALF:(half + 1) * HALF, :] = res.results[core]["out"]
    if DEBUG_DUMPS:
        kernel._dbg = res.results
    return out


# version 03
# speedup vs baseline: 1.0792x; 1.0692x over previous
import contextlib
import numpy as np
import ml_dtypes
import concourse.bass as bass
import concourse.mybir as mybir
from concourse.bass_utils import run_bass_kernel_spmd

F32 = mybir.dt.float32
BF16 = mybir.dt.bfloat16
I32 = mybir.dt.int32
AF = mybir.ActivationFunctionType
ALU = mybir.AluOpType
AXX = mybir.AxisListType.X

B, N, D = 4, 4096, 1024
HALF = 2048
HEADS, DH = 16, 64
WIN = 256
FFN_DEPTH = 6
DIM_INNER = 2730
W1C = 2 * DIM_INNER
SLAB = 258
DOMF = HALF + SLAB
N_EXP = 8
MOE_K = 1024
L1_K = 1228
NEG_BIG = -30000.0
NT = N // 128
NTO = HALF // 128

DEBUG_DUMPS = ()
STOP_AFTER = None
SCOPES = False


def _sigma_perm():
    p = np.zeros(D, dtype=np.int64)
    for h in range(HEADS):
        for d in range(DH):
            p[h * DH + d] = d * HEADS + h
    return p


def _rotary_tables():
    half = np.arange(0, DH, 2, dtype=np.float32)
    inv_freq = (1.0 / 10000.0 ** (half / DH)).astype(np.float32)
    t = np.arange(N, dtype=np.float32)
    freqs = t[:, None] * inv_freq[None, :]
    freqs = np.concatenate([freqs, freqs], -1)
    base_scale = ((half + 0.4 * DH) / (1.4 * DH)).astype(np.float32)
    power = (t - N // 2) / (WIN // 2)
    scale = base_scale[None, :] ** power[:, None]
    scale = np.concatenate([scale, scale], -1).astype(np.float32)
    cos = np.cos(freqs).astype(np.float32)
    sin = np.sin(freqs).astype(np.float32)
    inv = (1.0 / scale).astype(np.float32)
    A_q = (cos * scale).T.copy()
    A_k = (cos * inv).T.copy()
    B_q = (sin * scale).T.copy(); B_q[0:32, :] *= -1.0
    B_k = (sin * inv).T.copy(); B_k[0:32, :] *= -1.0
    return A_q, B_q, A_k, B_k


def _local_masks():
    k = np.arange(2 * WIN)[:, None]
    q = np.arange(WIN)[None, :]
    m = ((k >= q) & (k <= q + WIN)).astype(np.float32)
    m0 = m.copy()
    m0[0:WIN, :] = 0.0
    return m.astype(ml_dtypes.bfloat16), m0.astype(ml_dtypes.bfloat16)


class PB:
    def __init__(self):
        self.nc = bass.Bass("TRN2", target_bir_lowering=False)
        self.ctx = contextlib.ExitStack()
        self.T = {}
        self.S = {}
        self.P = {}
        self.sems = {}
        self.cnt = {}
        self.seen = {}
        self.block = None
        self.dbg = set()
        self.stacks = {}

    def din(self, name, shape, dtype=F32):
        self.T[name] = self.nc.dram_tensor(name, list(shape), dtype, kind="ExternalInput")
        return self.T[name]

    def dout(self, name, shape, dtype=F32):
        self.T[name] = self.nc.dram_tensor(name, list(shape), dtype, kind="ExternalOutput")
        return self.T[name]

    def dscr(self, name, shape, dtype=F32):
        self.T[name] = self.nc.dram_tensor(name, list(shape), dtype)
        return self.T[name]

    _serial = 0

    def _uniq(self, name):
        PB._serial += 1
        return "%s__%d" % (name, PB._serial)

    def sbuf(self, name, shape, dtype, ph=None):
        stk = self.stacks[ph] if ph else self.ctx
        self.S[name] = stk.enter_context(self.nc.sbuf_tensor(self._uniq(name), list(shape), dtype))
        return self.S[name]

    def psum(self, name, shape, dtype=F32, ph=None):
        stk = self.stacks[ph] if ph else self.ctx
        self.P[name] = stk.enter_context(self.nc.psum_tensor(self._uniq(name), list(shape), dtype))
        return self.P[name]

    def open_phase(self, ph):
        self.stacks[ph] = contextlib.ExitStack()

    def close_phase(self, ph):
        self.stacks.pop(ph).close()

    def sem(self, name):
        if name not in self.sems:
            self.sems[name] = self.ctx.enter_context(self.nc.semaphore(name))
            self.cnt[name] = 0
        return self.sems[name]

    def tick(self, name, k=1):
        self.cnt[name] += k
        return self.cnt[name]

    def wait(self, ekey, eng, name, value=None):
        v = self.cnt[name] if value is None else value
        if v <= 0:
            return
        if self.seen.setdefault(ekey, {}).get(name, 0) >= v:
            return
        eng.wait_ge(self.sems[name], v)
        self.seen[ekey][name] = v

    def stream(self, name, nbuf=2):
        return _Stream(self, name, nbuf)

    def barrier(self, tag, drain=()):
        s = self.sem("barx")
        block = self.block
        def fy(e):
            for dn in drain:
                self.wait("y", e, dn, self.cnt[dn])
            e.nop().then_inc(s, 1)
        block.sync(fy)
        for attr in ("tensor", "vector", "scalar", "gpsimd"):
            def f(e, s=s):
                e.nop().then_inc(s, 1)
            getattr(block, attr)(f)
        self.tick("barx", 5)
        tgt = self.cnt["barx"]
        for ekey, attr in (("t", "tensor"), ("v", "vector"), ("s", "scalar"),
                           ("g", "gpsimd"), ("y", "sync")):
            def f(e, ekey=ekey, tgt=tgt):
                self.wait(ekey, e, "barx", tgt)
            getattr(block, attr)(f)


class _Stream:
    def __init__(self, pb, name, nbuf):
        self.pb = pb
        self.name = name
        self.nbuf = nbuf
        self.sems = [pb.sem(f"{name}{i}") for i in range(nbuf)]
        self.base = [pb.cnt[f"{name}{i}"] for i in range(nbuf)]
        self.nissued = 0

    def issue(self, eng, out, in_, **kw):
        i = self.nissued
        eng.dma_start(out=out, in_=in_, **kw).then_inc(self.sems[i % self.nbuf], 16)
        self.pb.tick(f"{self.name}{i % self.nbuf}", 16)
        self.nissued += 1
        return i

    def wait(self, ekey, eng, i):
        self.pb.wait(ekey, eng, f"{self.name}{i % self.nbuf}",
                     self.base[i % self.nbuf] + 16 * (i // self.nbuf + 1))


class Eng:

    def __init__(self, pb, ekey, raw, chained=True):
        self.pb = pb
        self.ekey = ekey
        self.raw = raw
        self.chained = chained
        self.chname = ekey + "ch"
        if chained:
            pb.sem(self.chname)

    def _pre(self):
        if self.chained:
            self.pb.wait(self.ekey, self.raw, self.chname)

    def __getattr__(self, name):
        fn = getattr(self.raw, name)
        if name in ("wait_ge", "nop", "dma_start"):
            return fn
        def wrapper(*a, **kw):
            self._pre()
            inst = fn(*a, **kw)
            if self.chained:
                inst.then_inc(self.pb.sems[self.chname], 1)
                self.pb.tick(self.chname)
            return inst
        return wrapper

    def wait(self, sem, v=None):
        self.pb.wait(self.ekey, self.raw, sem, v)

    def sig(self, sem, k=1):
        self._pre()
        self.raw.nop().then_inc(self.pb.sem(sem), k)
        self.pb.tick(sem, k)

    def dma(self, out, in_, sem, **kw):
        self._pre()
        self.raw.dma_start(out=out, in_=in_, **kw).then_inc(self.pb.sem(sem), 16)
        self.pb.tick(sem, 16)


def p0(pb):
    nc, block = pb.nc, pb.block
    S, T, P = pb.S, pb.T, pb.P

    pb.sbuf("p0_x", (128, 2 * D), F32, ph="p0")
    pb.sbuf("p0_xf", (128, 2 * D), F32, ph="p0")
    pb.sbuf("p0_xsq", (128, D), F32, ph="p0")
    pb.sbuf("p0_st", (128, 8), F32, ph="p0")
    pb.sbuf("p0_stage", (128, 1040), F32, ph="p0")
    pb.psum("ps_trf", (128, 512), F32, ph="p0")
    pb.psum("ps_trf2", (128, 512), F32, ph="p0")

    for s in ("wld", "xld", "cst", "mm", "ve", "se", "gd", "st", "cc",
              "p0c", "p0v", "p0s", "p0n", "p0e"):
        pb.sem(s)

    xs = pb.stream("xs", 2)

    @block.gpsimd
    def _(gr):
        g = Eng(pb, "g", gr)
        for dst, srcn in (("sb_identf", "identb"),
                          ("sb_onesb", "onesb"), ("sb_gates", "gates"),
                          ("sb_onesf", "onesb"), ("sb_bmapPf", "bmapP"),
                          ("sb_bc0f", "bc0f"), ("sb_bc64f", "bc64f")):
            g.dma(S[dst][:, :], T[srcn][:, :], "cst")

    @block.sync
    def _(y_raw):
        y = Eng(pb, "y", y_raw, chained=False)
        for i in range(NT):
            if i >= 2:
                pb.wait("y", y_raw, "p0c", i - 1)
            xs.issue(y_raw, S["p0_x"][:, (i % 2) * D:(i % 2) * D + D],
                     T["x"][i * 128:(i + 1) * 128, :])

    @block.scalar
    def _(sr):
        s = Eng(pb, "s", sr)
        def ship(j):
            s.wait("ve", j + 1)
            s.dma(bass.AP(T["xinnerT"], SLAB + j * 128,
                          [[DOMF, 128], [128 * DOMF, 8], [1, 128]]),
                  S["p0_stage"][:, 0:1024].rearrange("p (f t) -> p f t", f=8),
                  "st")
        for i in range(NT):
            s.wait("p0v", i + 1)
            s.activation(S["p0_st"][:, 6:7], S["p0_st"][:, 2:3], AF.Sqrt)
            s.sig("p0s")
            if i >= 1 and i - 1 < NTO:
                ship(i - 1)

    @block.vector
    def _(vr):
        v = Eng(pb, "v", vr)
        for i in range(NT + 1):
            if i < NT:
                v.wait("cst", 112)
                xs.wait("v", vr, i)
                if i >= 2:
                    v.wait("mm", 8 * (i - 1))
                xt = S["p0_x"][:, (i % 2) * D:(i % 2) * D + D]
                st = S["p0_st"]
                v.reduce_sum(st[:, 0:1], xt, axis=AXX)
                v.tensor_scalar_mul(st[:, 0:1], st[:, 0:1], 1.0 / D)
                v.tensor_tensor(st[:, 4:5], st[:, 0:1], st[:, 0:1], op=ALU.mult)
                v.tensor_tensor(S["p0_xsq"][:, :], xt, xt, op=ALU.mult)
                v.reduce_sum(st[:, 1:2], S["p0_xsq"][:, :], axis=AXX)
                v.tensor_scalar_mul(st[:, 1:2], st[:, 1:2], 1.0 / D)
                v.tensor_tensor(st[:, 1:2], st[:, 1:2], st[:, 4:5], op=ALU.subtract)
                v.tensor_scalar_add(st[:, 2:3], st[:, 1:2], 1e-5)
                v.sig("p0v")
                v.wait("p0s", i + 1)
                v.reciprocal(st[:, 3:4], st[:, 6:7])
                v.tensor_scalar(S["p0_xf"][:, (i % 2) * D:(i % 2) * D + D], xt,
                                st[:, 0:1], st[:, 3:4], op0=ALU.subtract, op1=ALU.mult)
                v.sig("p0n")
                v.sig("p0c")
            if i >= 1:
                j = i - 1
                v.wait("mm", 8 * i)
                if 1 <= j < NTO:
                    v.wait("st", 16 * j)
                for fg in range(2):
                    ps = P["ps_trf"] if fg == 0 else P["ps_trf2"]
                    dst = S["xnT"] if j < NTO else S["xnTo"]
                    jj = j if j < NTO else j - NTO
                    for f in range(4):
                        ft = fg * 4 + f
                        v.tensor_copy(dst[:, ft * HALF + jj * 128: ft * HALF + (jj + 1) * 128],
                                      ps[:, f * 128:(f + 1) * 128])
                        if j < NTO:
                            v.tensor_copy(S["p0_stage"][:, ft * 128:(ft + 1) * 128],
                                          ps[:, f * 128:(f + 1) * 128])
                    v.sig("p0e")
                if j < NTO:
                    v.sig("ve")

    @block.tensor
    def _(tr):
        t = Eng(pb, "t", tr, chained=False)
        t.wait("cst", 112)
        for i in range(NT):
            t.wait("p0n", i + 1)
            if i >= 1:
                t.wait("p0e", 2 * i)
            xf = S["p0_xf"][:, (i % 2) * D:(i % 2) * D + D]
            for fg in range(2):
                ps = P["ps_trf"] if fg == 0 else P["ps_trf2"]
                for f in range(4):
                    ft = fg * 4 + f
                    inst = tr.matmul(ps[:, f * 128:(f + 1) * 128],
                                     xf[:, ft * 128:(ft + 1) * 128],
                                     S["sb_identf"][:, :], start=True, stop=True,
                                     is_transpose=True)
                    inst.then_inc(pb.sems["mm"], 1)
                    pb.tick("mm")

    for ci, (coff, clen) in enumerate(((0, 129), (129, 129))):
        def slab_v(vr, coff=coff, clen=clen):
            v = Eng(pb, "v", vr)
            v.wait("mm", pb.cnt["mm"])
            v.wait("cst", 112)
            v.wait("st", pb.cnt["st"])
            lo = HALF - SLAB + coff
            for f in range(8):
                v.tensor_scalar_mul(S["p0_stage"][:, f * clen:(f + 1) * clen],
                                    S["xnTo"][:, f * HALF + lo: f * HALF + lo + clen],
                                    S["sb_gates"][:, 0:1])
            v.sig("ve")
        block.vector(slab_v)

        def slab_s(sr, coff=coff, clen=clen):
            s = Eng(pb, "s", sr)
            s.wait("ve", pb.cnt["ve"])
            s.dma(bass.AP(T["xinnerT"], coff,
                          [[DOMF, 128], [128 * DOMF, 8], [1, clen]]),
                  S["p0_stage"][:, 0:8 * clen].rearrange("p (f t) -> p f t", f=8),
                  "st")
        block.scalar(slab_s)

    pb.barrier("p0", drain=("st",))

    if "xnT" in pb.dbg:
        dd = pb.dout("d_xnT", (128, 8 * HALF), BF16)
        ddo = pb.dout("d_xnTo", (128, 8 * HALF), BF16)
        def dumpf(gr, dd=dd, ddo=ddo):
            g = Eng(pb, "g", gr)
            g.dma(dd[:, :], S["xnT"][:, :], "gd")
            g.dma(ddo[:, :], S["xnTo"][:, :], "gd")
        block.gpsimd(dumpf)
    if "xinnerT" in pb.dbg:
        dd2 = pb.dout("d_xinnerT", (D, DOMF), F32)
        def dumpf2(gr, dd2=dd2):
            g = Eng(pb, "g", gr)
            g.wait("st", pb.cnt["st"])
            g.dma(dd2[:, :], T["xinnerT"][:, :], "gd")
        block.gpsimd(dumpf2)


def p1(pb):
    nc, block = pb.nc, pb.block
    S, T, P = pb.S, pb.T, pb.P
    CH = 512
    NCH = HALF // CH

    pb.open_phase("p1a")
    pb.sbuf("wkv", (128, 2, 8, D), BF16, ph="p1a")
    pb.sbuf("kt_sb", (128, 2 * D), BF16, ph="p1a")
    pb.sbuf("v_aug", (128, 2 * 1040), BF16, ph="p1a")
    pb.psum("pka", (128, 512), F32, ph="p1a")
    pb.psum("pkb", (128, 512), F32, ph="p1a")
    pb.psum("pva", (128, 512), F32, ph="p1a")
    pb.psum("pvb", (128, 512), F32, ph="p1a")
    for g4 in range(4):
        pb.psum(f"pkv{g4}", (128, 260), F32, ph="p1a")
    for s in ("p1w", "p1k", "p1e", "p1f", "sa", "dd"):
        pb.sem(s)

    @block.gpsimd
    def _(gr):
        g = Eng(pb, "g", gr)
        g.dma(S["wkv"][:, 0, :, :], T["Wk"][:, :].rearrange("(kt p) m -> p kt m", p=128), "p1w")
        g.dma(S["wkv"][:, 1, :, :], T["Wv"][:, :].rearrange("(kt p) m -> p kt m", p=128), "p1w")
        g.dma(S["sb_bkbv"][0:1, :], T["bvec"][2:4, :], "p1w")

    mm0 = [None]

    @block.tensor
    def _(tr):
        t = Eng(pb, "t", tr, chained=False)
        t.wait("p1w", 48)
        t.wait("barx", 5)
        mm0[0] = pb.cnt["mm"]
        for i in range(NT):
            if i >= 1:
                t.wait("p1k", i)
                t.wait("p1e", i)
            for half_ in range(2):
                ps = P["pka"] if half_ == 0 else P["pkb"]
                xsrc = S["xnT"] if i < NTO else S["xnTo"]
                ii = i if i < NTO else i - NTO
                for kt in range(8):
                    tr.matmul(ps[:, :], xsrc[:, kt * HALF + ii * 128:kt * HALF + (ii + 1) * 128],
                              S["wkv"][:, 0, kt, half_ * 512:half_ * 512 + 512],
                              start=(kt == 0), stop=False, skip_group_check=True)
                inst = tr.matmul(ps[:, :], S["sb_onesb"][0:1, 0:128],
                                 S["sb_bkbv"][0:1, half_ * 512:half_ * 512 + 512],
                                 start=False, stop=True, skip_group_check=True)
                inst.then_inc(pb.sems["mm"], 1); pb.tick("mm")
            for half_ in range(2):
                ps = P["pva"] if half_ == 0 else P["pvb"]
                for kt in range(8):
                    tr.matmul(ps[:, :], xsrc[:, kt * HALF + ii * 128:kt * HALF + (ii + 1) * 128],
                              S["wkv"][:, 1, kt, half_ * 512:half_ * 512 + 512],
                              start=(kt == 0), stop=False, skip_group_check=True)
                inst = tr.matmul(ps[:, :], S["sb_onesb"][0:1, 0:128],
                                 S["sb_bkbv"][0:1, D + half_ * 512:D + half_ * 512 + 512],
                                 start=False, stop=True, skip_group_check=True)
                inst.then_inc(pb.sems["mm"], 1); pb.tick("mm")
            t.wait("p1k", i + 1)
            t.wait("p1e", i + 1)
            ko = (i % 2) * D
            vo = (i % 2) * 1040
            for h in range(16):
                ps = P[f"pkv{h // 4}"]
                inst = tr.matmul(ps[0:64, 65 * (h % 4):65 * (h % 4) + 65],
                                 S["kt_sb"][:, ko + 64 * h:ko + 64 * h + 64],
                                 S["v_aug"][:, vo + 65 * h:vo + 65 * h + 65],
                                 start=(i == 0 and h % 4 == 0),
                                 stop=(i == NT - 1 and h % 4 == 3),
                                 skip_group_check=True)
            inst.then_inc(pb.sems["p1f"], 1); pb.tick("p1f")

    @block.scalar
    def _(sr):
        s = Eng(pb, "s", sr)
        for i in range(NT):
            ko = (i % 2) * D
            if i >= 2:
                s.wait("p1f", i - 1)
            s.wait("mm", mm0[0] + 4 * i + 2)
            s.activation(S["kt_sb"][:, ko:ko + 512], P["pka"][:, :], AF.Sigmoid)
            s.activation(S["kt_sb"][:, ko + 512:ko + 1024], P["pkb"][:, :], AF.Sigmoid)
            s.sig("p1k")

    @block.vector
    def _(vr):
        v = Eng(pb, "v", vr)
        v.memset(S["v_aug"][:, :], 1.0)
        for i in range(NT):
            vo = (i % 2) * 1040
            if i >= 2:
                v.wait("p1f", i - 1)
            v.wait("mm", mm0[0] + 4 * i + 4)
            v.tensor_copy(S["v_aug"][:, vo:vo + 520].rearrange("p (h m) -> p h m", m=65)[:, :, 0:64],
                          P["pva"][:, :].rearrange("p (h m) -> p h m", m=64))
            v.tensor_copy(S["v_aug"][:, vo + 520:vo + 1040].rearrange("p (h m) -> p h m", m=65)[:, :, 0:64],
                          P["pvb"][:, :].rearrange("p (h m) -> p h m", m=64))
            v.sig("p1e")

    @block.vector
    def _(vr):
        v = Eng(pb, "v", vr)
        v.wait("p1f", NT)
        v.tensor_copy(S["kv_sb"][0:64, 0:260], P["pkv0"][0:64, :])
        v.tensor_copy(S["kv_sb"][0:64, 260:520], P["pkv1"][0:64, :])
        v.tensor_copy(S["kv_sb2"][0:64, 0:260], P["pkv2"][0:64, :])
        v.tensor_copy(S["kv_sb2"][0:64, 260:520], P["pkv3"][0:64, :])
        v.tensor_copy(S["kv_sb"][64:128, :], S["kv_sb"][0:64, :])
        v.tensor_copy(S["kv_sb2"][64:128, :], S["kv_sb2"][0:64, :])
        v.sig("ve")

    pb.barrier("p1a")
    pb.close_phase("p1a")
    pb.close_phase("xoth")

    pb.open_phase("p1b")
    pb.sbuf("whq", (128, 3, 8, D), BF16, ph="p1b")
    pb.sbuf("bcols", (128, 64), F32, ph="p1b")
    pb.sbuf("hbcol", (128, 8), F32, ph="p1b")
    pb.sbuf("qc", (128, 8, CH), BF16, ph="p1b")
    pb.sbuf("hc", (128, 8, CH), BF16, ph="p1b")
    pb.sbuf("attnc", (128, 8, CH), BF16, ph="p1b")
    pb.sbuf("attn2c", (128, 8, CH), BF16, ph="p1b")
    pb.sbuf("gatec", (128, 8, CH), BF16, ph="p1b")
    pb.sbuf("den_pack", (128, 4096), F32, ph="p1b")
    pb.sbuf("den_sb", (128, CH), F32, ph="p1b")
    pb.sbuf("recipb", (128, CH), F32, ph="p1b")
    pb.sbuf("lnrow", (128, 4 * CH), F32, ph="p1b")
    pb.sbuf("mursb", (128, 2 * CH), F32, ph="p1b")

    def p1b_memset(vr):
        v = Eng(pb, "v", vr)
        v.memset(S["den_sb"][:, :], 0.0)
        v.memset(S["recipb"][:, :], 0.0)
        v.memset(S["lnrow"][:, :], 0.0)
        v.sig("ve")
    block.vector(p1b_memset)
    pb.sbuf("tmpf", (128, CH), F32, ph="p1b")
    pb.sbuf("astage", (128, 8, CH), F32, ph="p1b")
    NPP = 6
    for k in range(NPP):
        pb.psum(f"pp{k}", (128, 512), F32, ph="p1b")
    for k in range(NPP):
        pb.sem(f"f{k}")
    psched = [0] * NPP

    def t_use(tr, k):
        pb.wait("t", tr, f"f{k}", psched[k])
        psched[k] += 1

    def done(eng_wrap, k):
        eng_wrap.sig(f"f{k}")

    @block.gpsimd
    def _(gr):
        g = Eng(pb, "g", gr)
        g.wait("barx", 10)
        g.dma(S["whq"][:, 0, :, :], T["Wh"][:, :].rearrange("(kt p) m -> p kt m", p=128), "p1w")
        g.dma(S["whq"][:, 1, :, :], T["Wq"][:, :].rearrange("(kt p) m -> p kt m", p=128), "p1w")
        g.dma(S["whq"][:, 2, :, :], T["Wo0"][:, :].rearrange("(kt p) m -> p kt m", p=128), "p1w")

    @block.sync
    def _(y):
        pb.wait("y", y, "barx", 10)
        y.dma_start(out=bass.AP(S["bcols"], 0, [[64, 128], [8, 8], [1, 8]]),
                    in_=bass.AP(T["bvecT"], 0, [[8, 128], [1024, 8], [1, 8]])
                    ).then_inc(pb.sem("sa"), 16)
        pb.tick("sa", 16)
    sa_bcols = pb.cnt["sa"]

    @block.vector
    def _(vr):
        v = Eng(pb, "v", vr)
        v.wait("sa", 16)
        bc = S["bcols"][:, :].rearrange("p (mt j) -> p mt j", j=8)
        v.tensor_tensor(S["hbcol"][:, :], bc[:, :, 0], bc[:, :, 4], op=ALU.add)
        v.sig("ve")
    ve_hb = pb.cnt["ve"]

    for c in range(NCH):
        c0 = c * CH

        qmm = []
        hmm = []
        @block.tensor
        def _(tr, qmm=qmm, hmm=hmm):
            pb.wait("t", tr, "p1w", pb.cnt["p1w"])
            for which, wsl, rec in (("q", 1, qmm), ("h", 0, hmm)):
                for mt in range(8):
                    k = mt % 2
                    t_use(tr, k)
                    ps = P[f"pp{k}"]
                    for kt in range(8):
                        inst = tr.matmul(ps[:, :], S["whq"][:, wsl, kt, mt * 128:(mt + 1) * 128],
                                         S["xnT"][:, kt * HALF + c0:kt * HALF + c0 + CH],
                                         start=(kt == 0), stop=(kt == 7), skip_group_check=True)
                    inst.then_inc(pb.sems["mm"], 1); pb.tick("mm")
                    rec.append((pb.cnt["mm"], k))

        @block.scalar
        def _(sr, qmm=qmm, hmm=hmm):
            s = Eng(pb, "s", sr)
            s.wait("sa", sa_bcols)
            for mt in range(8):
                mmv, k = qmm[mt]
                s.wait("mm", mmv)
                s.activation(S["qc"][:, mt, :], P[f"pp{k}"][:, :], AF.Sigmoid,
                             bias=S["bcols"][:, mt * 8 + 1:mt * 8 + 2])
                done(s, k)
            s.wait("ve", ve_hb)
            for mt in range(8):
                mmv, k = hmm[mt]
                s.wait("mm", mmv)
                s.activation(S["hc"][:, mt, :], P[f"pp{k}"][:, :], AF.Identity,
                             bias=S["hbcol"][:, mt:mt + 1])
                done(s, k)
            s.sig("p1k")

        ndmm = []
        @block.tensor
        def _(tr, ndmm=ndmm):
            t = Eng(pb, "t", tr, chained=False)
            t.wait("p1k", pb.cnt["p1k"])
            for h in range(16):
                k = 2 + h % 4
                t_use(tr, k)
                kvs = S["kv_sb"] if h < 8 else S["kv_sb2"]
                b = 64 * (h % 2)
                inst = tr.matmul(P[f"pp{k}"][0:65, :], kvs[b:b + 64, 65 * (h % 8):65 * (h % 8) + 65],
                                 S["qc"][b:b + 64, h // 2, :],
                                 start=True, stop=True, skip_group_check=True)
                inst.then_inc(pb.sems["mm"], 1); pb.tick("mm")
                ndmm.append((pb.cnt["mm"], k))

        @block.vector
        def _(vr, ndmm=ndmm):
            v = Eng(pb, "v", vr)
            for h in range(16):
                mmv, k = ndmm[h]
                v.wait("mm", mmv)
                ps = P[f"pp{k}"]
                v.tensor_copy(S["den_pack"][64 * (h % 2):64 * (h % 2) + 1,
                                            (h // 2) * CH:(h // 2) * CH + CH],
                              ps[64:65, :])
                v.tensor_copy(S["attnc"][64 * (h % 2):64 * (h % 2) + 64, h // 2, :],
                              ps[0:64, :])
                done(v, k)
            v.sig("ve")

        @block.sync
        def _(y):
            pb.wait("y", y, "ve", pb.cnt["ve"])
            for bq in range(2):
                y.dma_start(out=S["den_sb"][8 * bq:8 * bq + 8, :],
                            in_=S["den_pack"][64 * bq:64 * bq + 1, :]
                            ).then_inc(pb.sem("sa"), 16)
                pb.tick("sa", 16)

        @block.scalar
        def _(sr):
            s = Eng(pb, "s", sr)
            s.wait("sa", pb.cnt["sa"])
            s.activation(S["recipb"][0:16, :], S["den_sb"][0:16, :], AF.Ln,
                         bias=S["sb_gates"][0:16, 2:3])
            s.activation(S["recipb"][0:16, :], S["recipb"][0:16, :], AF.Exp, scale=-1.0)
            s.sig("ve")

        bcmm = []
        @block.tensor
        def _(tr, bcmm=bcmm):
            pb.wait("t", tr, "ve", pb.cnt["ve"])
            for mt in range(8):
                k = mt % 2
                t_use(tr, k)
                inst = tr.matmul(P[f"pp{k}"][:, :], S["sb_bmapPf"][:, mt * 128:(mt + 1) * 128],
                                 S["recipb"][:, :], start=True, stop=True,
                                 skip_group_check=True)
                inst.then_inc(pb.sems["mm"], 1); pb.tick("mm")
                bcmm.append((pb.cnt["mm"], k))

        @block.vector
        def _(vr, bcmm=bcmm):
            v = Eng(pb, "v", vr)
            for mt in range(8):
                mmv, k = bcmm[mt]
                v.wait("mm", mmv)
                v.tensor_tensor(S["attnc"][:, mt, :], S["attnc"][:, mt, :],
                                P[f"pp{k}"][:, :], op=ALU.mult)
                done(v, k)
            v.sig("ve")

        @block.scalar
        def _(sr):
            s = Eng(pb, "s", sr)
            s.wait("ve", pb.cnt["ve"])
            s.activation(S["attn2c"][:, :, :], S["attnc"][:, :, :], AF.Square)
            s.sig("se")

        stmm = []
        @block.tensor
        def _(tr, stmm=stmm):
            pb.wait("t", tr, "se", pb.cnt["se"])
            for which, srcn in ((2, "attnc"), (3, "attn2c")):
                t_use(tr, which)
                for kt in range(8):
                    inst = tr.matmul(P[f"pp{which}"][0:1, :], S["sb_onesb"][:, 0:1],
                                     S[srcn][:, kt, :], start=(kt == 0), stop=(kt == 7),
                                     skip_group_check=True)
                inst.then_inc(pb.sems["mm"], 1); pb.tick("mm")
                stmm.append(pb.cnt["mm"])

        @block.vector
        def _(vr, stmm=stmm):
            v = Eng(pb, "v", vr)
            v.wait("mm", stmm[-1])
            L = S["lnrow"]
            v.tensor_scalar_mul(L[0:1, 0:CH], P["pp2"][0:1, :], 1.0 / D)
            v.tensor_scalar_mul(L[0:1, CH:2 * CH], P["pp3"][0:1, :], 1.0 / D)
            done(v, 2); done(v, 3)
            v.tensor_tensor(L[0:1, 2 * CH:3 * CH], L[0:1, 0:CH], L[0:1, 0:CH], op=ALU.mult)
            v.tensor_tensor(L[0:1, CH:2 * CH], L[0:1, CH:2 * CH], L[0:1, 2 * CH:3 * CH], op=ALU.subtract)
            v.tensor_scalar_add(L[0:1, CH:2 * CH], L[0:1, CH:2 * CH], 1e-5)
            v.sig("ve")

        @block.scalar
        def _(sr):
            s = Eng(pb, "s", sr)
            s.wait("ve", pb.cnt["ve"])
            s.activation(S["lnrow"][0:1, 3 * CH:4 * CH], S["lnrow"][0:1, CH:2 * CH], AF.Sqrt)
            s.sig("se")

        @block.vector
        def _(vr):
            v = Eng(pb, "v", vr)
            v.wait("se", pb.cnt["se"])
            L = S["lnrow"]
            v.reciprocal(L[0:1, 2 * CH:3 * CH], L[0:1, 3 * CH:4 * CH])
            v.tensor_tensor(L[0:1, 0:CH], L[0:1, 0:CH], L[0:1, 2 * CH:3 * CH], op=ALU.mult)
            v.sig("ve")

        mursmm = []
        @block.tensor
        def _(tr, mursmm=mursmm):
            pb.wait("t", tr, "ve", pb.cnt["ve"])
            for which, off in ((4, 0), (5, 2 * CH)):
                t_use(tr, which)
                inst = tr.matmul(P[f"pp{which}"][:, :], S["sb_bc0f"][:, :],
                                 S["lnrow"][:, off:off + CH], start=True, stop=True,
                                 skip_group_check=True)
                inst.then_inc(pb.sems["mm"], 1); pb.tick("mm")
                mursmm.append(pb.cnt["mm"])

        @block.vector
        def _(vr, mursmm=mursmm):
            v = Eng(pb, "v", vr)
            v.wait("mm", mursmm[-1])
            v.wait("sa", sa_bcols)
            v.tensor_copy(S["mursb"][:, 0:CH], P["pp4"][:, :])
            v.tensor_copy(S["mursb"][:, CH:2 * CH], P["pp5"][:, :])
            done(v, 4); done(v, 5)
            for mt in range(8):
                v.tensor_tensor(S["tmpf"][:, :], S["attnc"][:, mt, :],
                                S["mursb"][:, CH:2 * CH], op=ALU.mult)
                v.tensor_tensor(S["tmpf"][:, :], S["tmpf"][:, :],
                                S["mursb"][:, 0:CH], op=ALU.subtract)
                v.tensor_scalar(S["tmpf"][:, :], S["tmpf"][:, :],
                                S["bcols"][:, mt * 8 + 5:mt * 8 + 6],
                                S["bcols"][:, mt * 8 + 6:mt * 8 + 7],
                                op0=ALU.mult, op1=ALU.add)
                v.tensor_tensor(S["gatec"][:, mt, :], S["tmpf"][:, :],
                                S["hc"][:, mt, :], op=ALU.mult)
            v.sig("ve")

        womm = []
        @block.tensor
        def _(tr, womm=womm):
            pb.wait("t", tr, "ve", pb.cnt["ve"])
            if c >= 1:
                pb.wait("t", tr, "sa", pb.cnt["sa"])
            for mt in range(8):
                k = mt % 2
                t_use(tr, k)
                for kt in range(8):
                    inst = tr.matmul(P[f"pp{k}"][:, :], S["whq"][:, 2, kt, mt * 128:(mt + 1) * 128],
                                     S["gatec"][:, kt, :], start=(kt == 0), stop=(kt == 7),
                                     skip_group_check=True)
                inst.then_inc(pb.sems["mm"], 1); pb.tick("mm")
                womm.append((pb.cnt["mm"], k))

        @block.scalar
        def _(sr, womm=womm):
            s = Eng(pb, "s", sr)
            for mt in range(8):
                mmv, k = womm[mt]
                s.wait("mm", mmv)
                s.activation(S["astage"][:, mt, :], P[f"pp{k}"][:, :], AF.Relu,
                             bias=S["bcols"][:, mt * 8 + 7:mt * 8 + 8])
                done(s, k)
            s.sig("se")

        @block.sync
        def _(y):
            pb.wait("y", y, "se", pb.cnt["se"])
            y.dma_start(out=bass.AP(T["A_T"], c0, [[HALF, 128], [128 * HALF, 8], [1, CH]]),
                        in_=S["astage"][:, :, :]).then_inc(pb.sem("sa"), 16)
            pb.tick("sa", 16)

    pb.barrier("p1", drain=("sa",))
    pb.close_phase("p1b")

    if "A_T" in pb.dbg:
        dd = pb.dout("d_A_T", (D, HALF), F32)
        def dumpf(gr, dd=dd):
            g = Eng(pb, "g", gr)
            g.dma(dd[:, :], T["A_T"][:, :], "gd")
        block.gpsimd(dumpf)


P2_LAYERS = FFN_DEPTH
HCH = 11


def p2_exchange(pb, st):
    nc, block = pb.nc, pb.block
    S, T = pb.S, pb.T
    groups = [[0, 1], [2, 3], [4, 5], [6, 7]]

    @block.gpsimd
    def _(gr):
        g = Eng(pb, "g", gr)
        g.wait("xw", pb.cnt["xw"])
        g.dma(T["slab_own"][:, :],
              bass.AP(T["xinnerT"], DOMF - SLAB, [[DOMF, D], [1, SLAB]]), "gd")
        g.wait("gd", pb.cnt["gd"])
        gr.collective_compute(
            "AllGather", ALU.bypass, replica_groups=groups,
            ins=[T["slab_own"][:, :].opt()], outs=[T["slab_all"][:, :].opt()],
        ).then_inc(pb.sem("cc"), 1)
        pb.tick("cc")
        g.wait("cc", pb.cnt["cc"])
        g.wait("shc", st["shc"])
        g.dma(S["shad"][:, :, 0:SLAB],
              bass.AP(T["slab_all"], 0, [[SLAB, 128], [128 * SLAB, 8], [1, SLAB]]), "gd")

    @block.vector
    def _(vr):
        v = Eng(pb, "v", vr)
        v.wait("gd", pb.cnt["gd"])
        v.wait("xw2", st["xw2"])
        v.tensor_scalar_mul(S["xch"][:, :, 0:SLAB], S["shad"][:, :, 0:SLAB],
                            S["sb_gates"][:, 0:1])
        v.sig("ve")
        v.sig("shc"); st["shc"] += 1

    @block.scalar
    def _(sr):
        s = Eng(pb, "s", sr)
        s.wait("ve", pb.cnt["ve"])
        s.dma(bass.AP(T["xinnerT"], 0, [[DOMF, 128], [128 * DOMF, 8], [1, SLAB]]),
              S["xch"][:, :, 0:SLAB], "xw")
        s.sig("xw2"); st["xw2"] += 1


def p2_attn(pb, l, st):
    nc, block = pb.nc, pb.block
    S, T, P = pb.S, pb.T, pb.P

    @block.gpsimd
    def _(gr):
        g = Eng(pb, "g", gr)
        g.dma(S["wqkv"][:, :, :], T["Wqkv"][l].rearrange("(kt p) m -> p kt m", p=128), "wld")
        g.dma(S["wof"][:, :, :], T["Wo_f"][l].rearrange("(kt p) m -> p kt m", p=128), "wld")
        g.dma(S["qkcols"][:, :], T["qsksc"][l], "wld")
        g.dma(S["rott"][:, :], T["rot"][:, :], "wld")
        g.dma(S["s_maskP"][:, :, :], T["maskP"][:, :, :], "wld")
        g.dma(S["s_maskF"][:, :, :], T["maskF"][:, :, :], "wld")
        g.dma(S["s_two2f"][:, :], T["two2f"][:, :], "wld")
        g.dma(S["s_two2fT"][:, :], T["two2fT"][:, :], "wld")
    wld_l = pb.cnt["wld"]

    @block.vector
    def _(vr):
        v = Eng(pb, "v", vr)
        v.wait("wld", wld_l)
        Lr = S["nrow"]
        v.tensor_tensor(Lr[0:64, 0:1], S["qkcols"][0:64, 0:1], S["qkcols"][0:64, 1:2], op=ALU.mult)
        v.tensor_scalar(Lr[0:64, 1:2], Lr[0:64, 0:1], -1.0, None, op0=ALU.mult)
        v.tensor_tensor(Lr[0:64, 0:1], Lr[0:64, 0:1], Lr[0:64, 1:2], op=ALU.max)
        v.tensor_copy(Lr[0:32, 2:3], Lr[32:64, 0:1])
        v.tensor_tensor(Lr[0:32, 0:1], Lr[0:32, 0:1], Lr[0:32, 2:3], op=ALU.max)
        v.sig("ve")

    @block.sync
    def _(y):
        pb.wait("y", y, "ve", pb.cnt["ve"])
        y.dma_start(out=S["nrow"][0:1, 32:64], in_=S["nrow"][0:32, 0:1]
                    ).then_inc(pb.sem("sa"), 16)
        pb.tick("sa", 16)

    @block.vector
    def _(vr):
        v = Eng(pb, "v", vr)
        v.wait("sa", pb.cnt["sa"])
        v.reduce_max(S["nrow"][0:1, 0:1], S["nrow"][0:1, 32:64], axis=AXX)
        v.tensor_scalar(S["nrow"][0:1, 1:2], S["nrow"][0:1, 0:1], -8.0, None, op0=ALU.mult)
        v.sig("ve")

    @block.tensor
    def _(tr):
        pb.wait("t", tr, "ve", pb.cnt["ve"])
        pb.wait("t", tr, "pne", st["pne"])
        inst = tr.matmul(P["pe5"][:, 0:1], S["sb_bc0f"][:, :],
                         S["nrow"][:, 1:2], start=True, stop=True, skip_group_check=True)
        inst.then_inc(pb.sems["mm"], 1); pb.tick("mm")

    @block.vector
    def _(vr):
        v = Eng(pb, "v", vr)
        v.wait("mm", pb.cnt["mm"])
        v.tensor_copy(S["negc"][:, :], P["pe5"][:, 0:1])
        v.sig("pne"); st["pne"] += 1

    for seg in range(-1, 4):
        p2_attn_seg(pb, l, seg, wld_l, st)


def p2_attn_seg(pb, l, seg, wld_l, st):
    nc, block = pb.nc, pb.block
    S, T, P = pb.S, pb.T, pb.P
    if seg < 0:
        ntok, xlo = WIN, 1
        kslot = [2]
        tiles = list(range(8, 16))
    else:
        ntok, xlo = 2 * WIN, SLAB + seg * 512
        kslot = [(seg * 2) % 3, (seg * 2 + 1) % 3]
        tiles = list(range(16))
    nt_t = ntok // 128

    @block.sync
    def _(y):
        pb.wait("y", y, "shc", st["shc"])
        pb.wait("y", y, "xw2", st["xw2"])
        pb.wait("y", y, "xw", pb.cnt["xw"])
        y.dma_start(out=S["xch"][:, :, 0:ntok + 1],
                    in_=bass.AP(T["xinnerT"], xlo - 1,
                                [[DOMF, 128], [128 * DOMF, 8], [1, ntok + 1]])
                    ).then_inc(pb.sem("xr"), 16)
        pb.tick("xr", 16)

    @block.vector
    def _(vr):
        v = Eng(pb, "v", vr)
        v.wait("xr", pb.cnt["xr"])
        v.tensor_copy(S["shad"][:, :, 0:ntok + 1], S["xch"][:, :, 0:ntok + 1])
        v.sig("shr")
    shr_now = pb.cnt["shr"]

    n_tl = len(tiles)
    qmm = {}
    nrmm = {}
    bcmm = {}

    @block.tensor
    def _(tr):
        pb.wait("t", tr, "wld", wld_l)
        pb.wait("t", tr, "shr", shr_now)
        for i in range(n_tl + 2):
            if i >= 2:
                j = i - 2
                pb.wait("t", tr, "nr", st["nr_b"] + j + 1)
                pb.wait("t", tr, "fb%d" % (j % 2), st["fb"][j % 2]); st["fb"][j % 2] += 1
                pbc = P["pg7"] if j % 2 == 0 else P["ph8"]
                inst = tr.matmul(pbc[:, 0:ntok], S["s_two2f"][0:2, :],
                                 S["rn2"][0:2, (j % 2) * 512:(j % 2) * 512 + ntok],
                                 start=True, stop=True, skip_group_check=True)
                inst.then_inc(pb.sems["mm"], 1); pb.tick("mm")
                bcmm[j] = pb.cnt["mm"]
            if i >= 1 and i - 1 < n_tl:
                j = i - 1
                pb.wait("t", tr, "sq", st["sq_b"] + j + 1)
                pb.wait("t", tr, "fn%d" % (j % 2), st["fn"][j % 2]); st["fn"][j % 2] += 1
                pn = P["pe5"] if j % 2 == 0 else P["pf6"]
                inst = tr.matmul(pn[0:2, 0:ntok], S["s_two2fT"][:, 0:2],
                                 S["sqs"][:, (j % 2) * 512:(j % 2) * 512 + ntok],
                                 start=True, stop=True, skip_group_check=True)
                inst.then_inc(pb.sems["mm"], 1); pb.tick("mm")
                nrmm[j] = pb.cnt["mm"]
            if i < n_tl:
                mt = tiles[i]
                k = i % 2
                pb.wait("t", tr, "fq%d" % k, st["fq"][k]); st["fq"][k] += 1
                ps = P["pqa"] if k == 0 else P["pqb"]
                for kt in range(8):
                    off = 1 if kt < 4 else 0
                    inst = tr.matmul(ps[:, 0:ntok], S["wqkv"][:, kt, mt * 128:(mt + 1) * 128],
                                     S["shad"][:, kt, off:off + ntok],
                                     start=(kt == 0), stop=(kt == 7), skip_group_check=True)
                inst.then_inc(pb.sems["mm"], 1); pb.tick("mm")
                qmm[i] = pb.cnt["mm"]

    @block.scalar
    def _(sr):
        s = Eng(pb, "s", sr)
        for i in range(n_tl):
            s.wait("mm", qmm[i])
            if i >= 2:
                s.wait("mm", nrmm[i - 2])
            s.activation(S["sqs"][:, (i % 2) * 512:(i % 2) * 512 + ntok],
                         (P["pqa"] if i % 2 == 0 else P["pqb"])[:, 0:ntok], AF.Square)
            s.sig("sq")
            s.wait("mm", nrmm[i])
            if i >= 2:
                s.wait("nr", st["nr_b"] + i - 1)
            sl_ = (i % 2) * 512
            s.activation(S["rn1"][0:2, sl_:sl_ + ntok],
                         (P["pe5"] if i % 2 == 0 else P["pf6"])[0:2, 0:ntok], AF.Ln,
                         bias=S["sb_gates"][0:2, 2:3])
            s.activation(S["rn1"][0:2, sl_:sl_ + ntok], S["rn1"][0:2, sl_:sl_ + ntok],
                         AF.Exp, scale=-0.5)
            s.sig("sn")
    st["sq_b"] += n_tl

    @block.vector
    def _(vr):
        v = Eng(pb, "v", vr)
        for i in range(n_tl):
            mt = tiles[i]
            is_q = mt < 8
            k = i % 2
            sl = k * 512
            v.wait("sn", st["sn_b"] + i + 1)
            v.tensor_scalar_min(S["rn2"][0:2, sl:sl + ntok], S["rn1"][0:2, sl:sl + ntok], 1e12)
            v.sig("nr")
            v.sig("fn%d" % k)
            v.wait("mm", bcmm[i])
            v.tensor_copy(S["rnb"][:, 0:ntok], (P["pg7"] if k == 0 else P["ph8"])[:, 0:ntok])
            v.sig("fb%d" % k)
            col = 0 if is_q else 1
            ps = P["pqa"] if k == 0 else P["pqb"]
            v.scalar_tensor_tensor(S["t1f"][:, 0:ntok], ps[:, 0:ntok],
                                   S["qkcols"][:, col:col + 1], S["rnb"][:, 0:ntok],
                                   op0=ALU.mult, op1=ALU.mult)
            for hw in range(4):
                swp = (hw // 2) * 64 + (1 - hw % 2) * 32
                dwp = (hw // 2) * 64 + (hw % 2) * 32
                v.scalar_tensor_tensor(S["t3f"][dwp:dwp + 32, 0:ntok],
                                       ps[swp:swp + 32, 0:ntok],
                                       S["qkcols"][swp:swp + 32, col:col + 1],
                                       S["rnb"][swp:swp + 32, 0:ntok],
                                       op0=ALU.mult, op1=ALU.mult)
            v.sig("fq%d" % k)
            tb = (0 if is_q else 2 * DOMF) + xlo
            bb = (DOMF if is_q else 3 * DOMF) + xlo
            v.tensor_tensor(S["t2f"][:, 0:ntok], S["t1f"][:, 0:ntok],
                            S["rott"][:, tb:tb + ntok], op=ALU.mult)
            v.tensor_tensor(S["t3f"][:, 0:ntok], S["t3f"][:, 0:ntok],
                            S["rott"][:, bb:bb + ntok], op=ALU.mult)
            if is_q:
                v.tensor_tensor(S["qpb"][:, mt, 0:ntok], S["t2f"][:, 0:ntok],
                                S["t3f"][:, 0:ntok], op=ALU.add)
            else:
                mtk = mt - 8
                for w in range(len(kslot)):
                    v.tensor_tensor(S["kring"][:, kslot[w], mtk, :],
                                    S["t2f"][:, w * WIN:(w + 1) * WIN],
                                    S["t3f"][:, w * WIN:(w + 1) * WIN], op=ALU.add)
            v.sig("qkr")
    st["sn_b"] += n_tl
    st["nr_b"] += n_tl

    vmm = {}
    @block.tensor
    def _(tr):
        for tc in range(nt_t):
            for nh in range(2):
                k = nh
                pb.wait("t", tr, "fv%d" % k, st["fv"][k]); st["fv"][k] += 1
                ps = P["pvA"] if k == 0 else P["pvB"]
                for kt in range(8):
                    off = 1 if kt < 4 else 0
                    inst = tr.matmul(ps[:, :],
                                     S["shad"][:, kt, off + tc * 128:off + tc * 128 + 128],
                                     S["wqkv"][:, kt, 2048 + nh * 512:2048 + nh * 512 + 512],
                                     start=(kt == 0), stop=(kt == 7), skip_group_check=True)
                inst.then_inc(pb.sems["mm"], 1); pb.tick("mm")
                vmm[(tc, nh)] = pb.cnt["mm"]

    @block.vector
    def _(vr):
        v = Eng(pb, "v", vr)
        for tc in range(nt_t):
            ws = kslot[0] if seg < 0 else kslot[tc // 2]
            tt = tc if seg < 0 else tc % 2
            for nh in range(2):
                v.wait("mm", vmm[(tc, nh)])
                ps = P["pvA"] if nh == 0 else P["pvB"]
                v.tensor_copy(
                    S["vring"][:, ws, tt, nh * 520:nh * 520 + 520].rearrange(
                        "p (h m) -> p h m", m=65)[:, :, 0:64],
                    ps[:, :].rearrange("p (h m) -> p h m", m=64))
                v.sig("fv%d" % nh)
        v.sig("vkr")

    if seg < 0:
        @block.vector
        def _(vr):
            v = Eng(pb, "v", vr)
            v.wait("qkr", pb.cnt["qkr"])
            v.wait("vkr", pb.cnt["vkr"])
            v.sig("shc")
        st["shc"] += 1
        return

    qkr_now = pb.cnt["qkr"]
    vkr_now = pb.cnt["vkr"]
    for w in range(2):
        wslot = kslot[w]
        pslot = (wslot - 1) % 3
        first = (seg == 0 and w == 0)
        qoff = w * WIN
        scmm = {}
        vomm = {}
        PSLOT = {0: ("pg7", 0), 1: ("ph8", 0), 2: ("pg7", 256), 3: ("ph8", 256)}

        def emit_vmm(tr, h):
            pb.wait("t", tr, "pex", st["pex_b"] + h + 1)
            nmp, coff = PSLOT[h % 4]
            pb.wait("t", tr, "fo%d" % (h % 4), st["fo"][h % 4]); st["fo"][h % 4] += 1
            for kc in range(4):
                sl = pslot if kc < 2 else wslot
                inst = tr.matmul(P[nmp][0:65, coff:coff + WIN],
                                 S["vring"][:, sl, kc % 2, 65 * h:65 * h + 65],
                                 S["probs"][:, (h % 2) * 4 + kc, :],
                                 start=(kc == 0), stop=(kc == 3), skip_group_check=True)
            inst.then_inc(pb.sems["pvm"], 1); pb.tick("pvm")
            vomm[h] = pb.cnt["pvm"]

        @block.tensor
        def _(tr):
            pb.wait("t", tr, "qkr", qkr_now)
            pb.wait("t", tr, "vkr", vkr_now)
            for h in range(16):
                for kc in range(4):
                    k = (h * 4 + kc) % 2
                    sl = pslot if kc < 2 else wslot
                    pb.wait("t", tr, "fs%d" % k, st["fs"][k]); st["fs"][k] += 1
                    b = 64 * (h % 2)
                    pn = P["pe5"] if k == 0 else P["pf6"]
                    inst = tr.matmul(pn[:, 0:WIN],
                                     S["kring"][b:b + 64, sl, h // 2,
                                                (kc % 2) * 128:(kc % 2) * 128 + 128],
                                     S["qpb"][b:b + 64, h // 2, qoff:qoff + WIN],
                                     start=True, stop=True, skip_group_check=True)
                    inst.then_inc(pb.sems["mm"], 1); pb.tick("mm")
                    scmm[(h, kc)] = pb.cnt["mm"]
                if h >= 2:
                    emit_vmm(tr, h - 2)
            emit_vmm(tr, 14)
            emit_vmm(tr, 15)

        @block.vector
        def _(vr):
            v = Eng(pb, "v", vr)
            def evac(h):
                nmp, coff = PSLOT[h % 4]
                v.wait("pvm", vomm[h])
                v.tensor_copy(S["den_pack"][64 * (h % 2):64 * (h % 2) + 1,
                                            (h // 2) * WIN:(h // 2) * WIN + WIN],
                              P[nmp][64:65, coff:coff + WIN])
                v.tensor_copy(S["oall"][64 * (h % 2):64 * (h % 2) + 64, h // 2, :],
                              P[nmp][0:64, coff:coff + WIN])
                v.sig("fo%d" % (h % 4))
            for h in range(16):
                if h >= 2:
                    v.wait("pex", st["pex_b"] + h - 1)
                for kc in range(4):
                    k = (h * 4 + kc) % 2
                    v.wait("mm", scmm[(h, kc)])
                    mk = S["s_maskF"] if first else S["s_maskP"]
                    v.scalar_tensor_tensor(S["scf"][:, (h % 2) * 4 + kc, :],
                                           (P["pe5"] if k == 0 else P["pf6"])[:, 0:WIN],
                                           8.0, mk[:, kc, :], op0=ALU.mult, op1=ALU.add)
                    v.sig("fs%d" % k)
                v.sig("sce")
                if h >= 2:
                    evac(h - 2)
            evac(14)
            evac(15)
            v.sig("ve")

        @block.scalar
        def _(sr):
            s = Eng(pb, "s", sr)
            s.wait("pne", st["pne"])
            for h in range(16):
                s.wait("sce", st["sce_b"] + h + 1)
                if h >= 2:
                    s.wait("pvm", st["pvm_b"] + h - 1)
                s.activation(S["probs"][:, (h % 2) * 4:(h % 2) * 4 + 4, :],
                             S["scf"][:, (h % 2) * 4:(h % 2) * 4 + 4, :], AF.Exp,
                             bias=S["negc"][:, 0:1])
                s.sig("pex")
        st["sce_b"] += 16

        st["pex_b"] += 16

        st["pvm_b"] += 16

        @block.sync
        def _(y):
            pb.wait("y", y, "ve", pb.cnt["ve"])
            for bq in range(2):
                y.dma_start(out=S["den_sb"][8 * bq:8 * bq + 8, 0:WIN],
                            in_=S["den_pack"][64 * bq:64 * bq + 1, 0:8 * WIN]
                            ).then_inc(pb.sem("sa"), 16)
                pb.tick("sa", 16)

        @block.scalar
        def _(sr):
            s = Eng(pb, "s", sr)
            s.wait("sa", pb.cnt["sa"])
            s.activation(S["recipb"][0:16, 0:WIN], S["den_sb"][0:16, 0:WIN], AF.Ln,
                         bias=S["sb_gates"][0:16, 2:3])
            s.activation(S["recipb"][0:16, 0:WIN], S["recipb"][0:16, 0:WIN], AF.Exp, scale=-1.0)
            s.sig("ve")

        rbmm = {}
        @block.tensor
        def _(tr):
            pb.wait("t", tr, "ve", pb.cnt["ve"])
            for mt in range(8):
                k = mt % 2
                pb.wait("t", tr, "fq%d" % k, st["fq"][k]); st["fq"][k] += 1
                ps = P["pqa"] if k == 0 else P["pqb"]
                inst = tr.matmul(ps[:, 0:WIN],
                                 S["sb_bmapPf"][:, mt * 128:(mt + 1) * 128],
                                 S["recipb"][:, 0:WIN], start=True, stop=True,
                                 skip_group_check=True)
                inst.then_inc(pb.sems["mm"], 1); pb.tick("mm")
                rbmm[mt] = pb.cnt["mm"]

        @block.vector
        def _(vr):
            v = Eng(pb, "v", vr)
            for mt in range(8):
                k = mt % 2
                v.wait("mm", rbmm[mt])
                v.tensor_copy(S["rbc2"][:, :], (P["pqa"] if k == 0 else P["pqb"])[:, 0:WIN])
                v.sig("fq%d" % k)
                v.tensor_tensor(S["concat"][:, mt, qoff:qoff + WIN],
                                S["oall"][:, mt, :], S["rbc2"][:, :], op=ALU.mult)
            v.sig("cc2")

    cc2_now = pb.cnt["cc2"]
    wommv = {}
    @block.tensor
    def _(tr):
        pb.wait("t", tr, "cc2", cc2_now)
        for mt in range(8):
            k = mt % 2
            pb.wait("t", tr, "fv%d" % k, st["fv"][k]); st["fv"][k] += 1
            ps = P["pvA"] if k == 0 else P["pvB"]
            for kt in range(8):
                inst = tr.matmul(ps[:, :], S["wof"][:, kt, mt * 128:(mt + 1) * 128],
                                 S["concat"][:, kt, :],
                                 start=(kt == 0), stop=(kt == 7), skip_group_check=True)
            inst.then_inc(pb.sems["mm"], 1); pb.tick("mm")
            wommv[mt] = pb.cnt["mm"]

    @block.vector
    def _(vr):
        v = Eng(pb, "v", vr)
        for mt in range(8):
            k = mt % 2
            v.wait("mm", wommv[mt])
            v.tensor_tensor(S["xch"][:, mt, 1:513],
                            (P["pvA"] if k == 0 else P["pvB"])[:, :],
                            S["xch"][:, mt, 1:513], op=ALU.add)
            v.sig("fv%d" % k)
        v.sig("shc"); st["shc"] += 1
        v.sig("ve")

    @block.scalar
    def _(sr):
        s = Eng(pb, "s", sr)
        s.wait("ve", pb.cnt["ve"])
        s.dma(bass.AP(T["xinnerT"], xlo, [[DOMF, 128], [128 * DOMF, 8], [1, 512]]),
              S["xch"][:, :, 1:513], "xw")
        s.sig("xw2"); st["xw2"] += 1


def p2_ffn(pb, l, st):
    nc, block = pb.nc, pb.block
    S, T, P = pb.S, pb.T, pb.P

    @block.gpsimd
    def _(gr):
        g = Eng(pb, "g", gr)
        g.dma(S["w1s"][:, :, :], T["W1"][l].rearrange("(kt p) m -> p kt m", p=128), "wld")
        g.dma(S["w2s"][:, 0:21, :], T["W2"][l][0:2688, :].rearrange("(kt p) m -> p kt m", p=128), "wld")
        g.dma(S["w2s"][0:42, 21, :], T["W2"][l][2688:2730, :], "wld")
        g.dma(S["s_b1c"][:, :], T["b1c"][l], "wld")
        g.dma(S["s_b2c"][:, :], T["b2c"][l], "wld")
        g.dma(S["s_lnc"][:, :], T["lnpc"][l], "wld")
    wld_l = pb.cnt["wld"]

    for tc in range(4):
        xlo = SLAB + tc * 512

        @block.sync
        def _(y):
            pb.wait("y", y, "shc", st["shc"])
            pb.wait("y", y, "xw2", st["xw2"])
            pb.wait("y", y, "xw", pb.cnt["xw"])
            y.dma_start(out=S["xch"][:, :, 0:513],
                        in_=bass.AP(T["xinnerT"], xlo - 1,
                                    [[DOMF, 128], [128 * DOMF, 8], [1, 513]])
                        ).then_inc(pb.sem("xr"), 16)
            pb.tick("xr", 16)

        @block.vector
        def _(vr):
            v = Eng(pb, "v", vr)
            v.wait("xr", pb.cnt["xr"])
            v.tensor_copy(S["shad"][:, :, 0:513], S["xch"][:, :, 0:513])
            v.sig("shr")
        shr_now = pb.cnt["shr"]

        @block.scalar
        def _(sr):
            s = Eng(pb, "s", sr)
            s.wait("shr", shr_now)
            s.wait("hgf", st["hgf"])
            for kt in range(8):
                off = 1 if kt < 4 else 0
                s.activation(S["hg"][:, 4 + kt, :], S["shad"][:, kt, off:off + 512], AF.Square)
            s.sig("se")

        stm = {}
        @block.tensor
        def _(tr):
            pb.wait("t", tr, "wld", wld_l)
            pb.wait("t", tr, "se", pb.cnt["se"])
            pb.wait("t", tr, "shr", shr_now)
            pb.wait("t", tr, "fst", st["fst"])
            for which in range(2):
                ps = P["pa00"] if which == 0 else P["pa10"]
                for kt in range(8):
                    off = 1 if kt < 4 else 0
                    rhs = (S["shad"][:, kt, off:off + 512] if which == 0
                           else S["hg"][:, 4 + kt, :])
                    inst = tr.matmul(ps[0:1, :], S["sb_onesb"][:, 0:1], rhs,
                                     start=(kt == 0), stop=(kt == 7), skip_group_check=True)
                inst.then_inc(pb.sems["mm"], 1); pb.tick("mm")
                stm[which] = pb.cnt["mm"]

        frow = S["hg"][:, 0:4, :].bitcast(F32)

        @block.vector
        def _(vr):
            v = Eng(pb, "v", vr)
            v.wait("mm", stm[1])
            v.tensor_scalar_mul(frow[0:1, 0, 0:256], P["pa00"][0:1, 0:256], 1.0 / D)
            v.tensor_scalar_mul(frow[0:1, 1, 0:256], P["pa00"][0:1, 256:512], 1.0 / D)
            v.tensor_scalar_mul(frow[0:1, 2, 0:256], P["pa10"][0:1, 0:256], 1.0 / D)
            v.tensor_scalar_mul(frow[0:1, 3, 0:256], P["pa10"][0:1, 256:512], 1.0 / D)
            v.sig("fst"); st["fst"] += 1
            v.tensor_tensor(S["t1f"][0:1, 0:256], frow[0:1, 0, 0:256], frow[0:1, 0, 0:256], op=ALU.mult)
            v.tensor_tensor(S["t1f"][0:1, 256:512], frow[0:1, 1, 0:256], frow[0:1, 1, 0:256], op=ALU.mult)
            v.tensor_tensor(frow[0:1, 2, 0:256], frow[0:1, 2, 0:256], S["t1f"][0:1, 0:256], op=ALU.subtract)
            v.tensor_tensor(frow[0:1, 3, 0:256], frow[0:1, 3, 0:256], S["t1f"][0:1, 256:512], op=ALU.subtract)
            v.tensor_scalar_add(frow[0:1, 2, 0:256], frow[0:1, 2, 0:256], 1e-5)
            v.tensor_scalar_add(frow[0:1, 3, 0:256], frow[0:1, 3, 0:256], 1e-5)
            v.sig("ve")

        @block.scalar
        def _(sr):
            s = Eng(pb, "s", sr)
            s.wait("ve", pb.cnt["ve"])
            s.activation(S["t1f"][0:1, 0:512], frow[0:1, 2:4, :].rearrange("p a b -> p (a b)"), AF.Sqrt)
            s.sig("se")

        @block.vector
        def _(vr):
            v = Eng(pb, "v", vr)
            v.wait("se", pb.cnt["se"])
            v.reciprocal(S["t2f"][0:1, 0:512], S["t1f"][0:1, 0:512])
            v.tensor_tensor(S["t1f"][0:1, 0:256], frow[0:1, 0, 0:256], S["t2f"][0:1, 0:256], op=ALU.mult)
            v.tensor_tensor(S["t1f"][0:1, 256:512], frow[0:1, 1, 0:256], S["t2f"][0:1, 256:512], op=ALU.mult)
            v.sig("ve")

        bcm = {}
        @block.tensor
        def _(tr):
            pb.wait("t", tr, "ve", pb.cnt["ve"])
            pb.wait("t", tr, "fbc", st["fbc"])
            for which, srcrow in ((0, "t1f"), (1, "t2f")):
                ps = P["pg00"] if which == 0 else P["pg10"]
                inst = tr.matmul(ps[:, :], S["sb_bc0f"][:, :],
                                 S[srcrow][:, 0:512], start=True, stop=True,
                                 skip_group_check=True)
                inst.then_inc(pb.sems["mm"], 1); pb.tick("mm")
                bcm[which] = pb.cnt["mm"]

        @block.vector
        def _(vr):
            v = Eng(pb, "v", vr)
            v.wait("mm", bcm[1])
            v.tensor_copy(S["t3f"][:, 0:512], P["pg00"][:, :])
            for kt in range(8):
                off = 1 if kt < 4 else 0
                v.tensor_tensor(S["t1f"][:, 0:512], S["shad"][:, kt, off:off + 512],
                                P["pg10"][:, :], op=ALU.mult)
                v.tensor_tensor(S["t1f"][:, 0:512], S["t1f"][:, 0:512], S["t3f"][:, 0:512],
                                op=ALU.subtract)
                v.tensor_scalar(S["xn2"][:, kt, :], S["t1f"][:, 0:512],
                                S["s_lnc"][:, kt * 2:kt * 2 + 1], S["s_lnc"][:, kt * 2 + 1:kt * 2 + 2],
                                op0=ALU.mult, op1=ALU.add)
            v.sig("fbc"); st["fbc"] += 1
            v.sig("xn2r")
            v.sig("shc"); st["shc"] += 1

        xn2_now = pb.cnt["xn2r"]

        for hch in range(HCH):
            at0 = 2 * hch
            amm = {}
            @block.tensor
            def _(tr, hch=hch, at0=at0, amm=amm):
                pb.wait("t", tr, "xn2r", xn2_now)
                kk = hch % 2
                pb.wait("t", tr, "fpa%d" % kk, st["fpa"][kk]); st["fpa"][kk] += 1
                for gate in range(2):
                    for sub in range(2):
                        nrow_ = 128 if (hch < 10 or sub == 0) else 42
                        c0 = (2730 if gate else 0) + hch * 256 + sub * 128
                        ps = P["p%s%d%d" % ("g" if gate else "a", kk, sub)]
                        for kt in range(8):
                            inst = tr.matmul(ps[0:nrow_, :], S["w1s"][:, kt, c0:c0 + nrow_],
                                             S["xn2"][:, kt, :],
                                             start=(kt == 0), stop=(kt == 7), skip_group_check=True)
                        inst.then_inc(pb.sems["mm"], 1); pb.tick("mm")
                amm[0] = pb.cnt["mm"]

            @block.scalar
            def _(sr, hch=hch, at0=at0, amm=amm):
                s = Eng(pb, "s", sr)
                kk = hch % 2
                s.wait("mm", amm[0])
                for sub in range(2):
                    nrow_ = 128 if (hch < 10 or sub == 0) else 42
                    s.activation(S["absc"][0:nrow_, 4 * kk + sub, :], P["pa%d%d" % (kk, sub)][0:nrow_, :],
                                 AF.Identity, bias=S["s_b1c"][0:nrow_, at0 + sub:at0 + sub + 1])
                    s.activation(S["absc"][0:nrow_, 4 * kk + 2 + sub, :], P["pg%d%d" % (kk, sub)][0:nrow_, :],
                                 AF.Gelu, bias=S["s_b1c"][0:nrow_, 22 + at0 + sub:22 + at0 + sub + 1])
                s.sig("se")

            @block.vector
            def _(vr, hch=hch, amm=amm):
                v = Eng(pb, "v", vr)
                kk = hch % 2
                v.wait("se", pb.cnt["se"])
                for sub in range(2):
                    nrow_ = 128 if (hch < 10 or sub == 0) else 42
                    v.tensor_tensor(S["hg"][0:nrow_, 2 * hch + sub, :],
                                    S["absc"][0:nrow_, 4 * kk + sub, :],
                                    S["absc"][0:nrow_, 4 * kk + 2 + sub, :], op=ALU.mult)
                v.sig("fpa%d" % kk)
                v.sig("hgr")
        hgr_now = pb.cnt["hgr"]

        w2m = {}
        @block.tensor
        def _(tr):
            pb.wait("t", tr, "hgr", hgr_now)
            for mt in range(8):
                kk = mt % 2
                pb.wait("t", tr, "fpa%d" % kk, st["fpa"][kk]); st["fpa"][kk] += 1
                ps = P["pa%d0" % kk]
                for kt in range(22):
                    nrow_ = 128 if kt < 21 else 42
                    inst = tr.matmul(ps[:, :], S["w2s"][0:nrow_, kt, mt * 128:(mt + 1) * 128],
                                     S["hg"][0:nrow_, kt, :],
                                     start=(kt == 0), stop=(kt == 21), skip_group_check=True)
                inst.then_inc(pb.sems["mm"], 1); pb.tick("mm")
                w2m[mt] = pb.cnt["mm"]

        @block.vector
        def _(vr):
            v = Eng(pb, "v", vr)
            for mt in range(8):
                kk = mt % 2
                v.wait("mm", w2m[mt])
                v.scalar_tensor_tensor(S["xch"][:, mt, 1:513], P["pa%d0" % kk][:, :],
                                       S["s_b2c"][:, mt:mt + 1], S["xch"][:, mt, 1:513],
                                       op0=ALU.add, op1=ALU.add)
                v.sig("fpa%d" % kk)
            v.sig("ve")
            v.sig("hgf"); st["hgf"] += 1

        @block.scalar
        def _(sr, xlo=xlo):
            s = Eng(pb, "s", sr)
            s.wait("ve", pb.cnt["ve"])
            s.dma(bass.AP(T["xinnerT"], xlo, [[DOMF, 128], [128 * DOMF, 8], [1, 512]]),
                  S["xch"][:, :, 1:513], "xw")
            s.sig("xw2"); st["xw2"] += 1


def p2(pb):
    nc, block = pb.nc, pb.block
    S, T = pb.S, pb.T

    pb.open_phase("p2")
    pb.sbuf("xch", (128, 8, 513), F32, ph="p2")
    pb.sbuf("shad", (128, 8, 513), BF16, ph="p2")
    pb.sbuf("nrow", (128, 64), F32, ph="p2")
    pb.sbuf("negc", (128, 1), F32, ph="p2")
    pb.sbuf("qkcols", (128, 2), F32, ph="p2")
    pb.sbuf("t1f", (128, 512), F32, ph="p2")
    pb.sbuf("t2f", (128, 512), F32, ph="p2")
    pb.sbuf("t3f", (128, 512), F32, ph="p2")
    for s in ("xw", "xw2", "xr", "shr", "shc", "sq", "sn", "nr", "qkr", "vkr",
              "sce", "pex", "pvm", "cc2", "pne", "hgr", "hgf", "xn2r",
              "fst", "fbc"):
        pb.sem(s)
    for k in range(2):
        pb.sem("fq%d" % k); pb.sem("fn%d" % k); pb.sem("fb%d" % k)
        pb.sem("fv%d" % k); pb.sem("fs%d" % k); pb.sem("fpa%d" % k)
    for k in range(4):
        pb.sem("fo%d" % k)

    def p2_nrow_ms(vr):
        v = Eng(pb, "v", vr)
        v.memset(S["nrow"][:, :], 0.0)
        v.sig("ve")
    block.vector(p2_nrow_ms)

    st = {"pne": 0, "shc": 0, "xw2": 0, "sq_b": 0, "sn_b": 0, "nr_b": 0,
          "sce_b": 0, "pex_b": 0, "pvm_b": 0, "hgf": 0, "fst": 0, "fbc": 0,
          "fq": [0, 0], "fn": [0, 0], "fb": [0, 0], "fv": [0, 0], "fs": [0, 0],
          "fo": [0, 0, 0, 0], "fpa": [0, 0]}

    for l in range(P2_LAYERS):
        if l > 0:
            p2_exchange(pb, st)
        pb.open_phase("p2a")
        pb.sbuf("s_maskP", (128, 4, WIN), F32, ph="p2a")
        pb.sbuf("s_maskF", (128, 4, WIN), F32, ph="p2a")
        pb.sbuf("s_two2f", (2, 128), F32, ph="p2a")
        pb.sbuf("s_two2fT", (128, 2), BF16, ph="p2a")
        pb.sbuf("wqkv", (128, 8, 3 * D), BF16, ph="p2a")
        pb.sbuf("wof", (128, 8, D), BF16, ph="p2a")
        pb.sbuf("rott", (128, 4 * DOMF), BF16, ph="p2a")
        pb.sbuf("kring", (128, 3, 8, WIN), BF16, ph="p2a")
        pb.sbuf("vring", (128, 3, 2, 1040), BF16, ph="p2a")
        pb.sbuf("qpb", (128, 8, 512), BF16, ph="p2a")
        pb.sbuf("probs", (128, 8, WIN), BF16, ph="p2a")
        pb.sbuf("scf", (128, 8, WIN), F32, ph="p2a")
        pb.sbuf("oall", (128, 8, WIN), BF16, ph="p2a")
        pb.sbuf("concat", (128, 8, 512), BF16, ph="p2a")
        pb.sbuf("sqs", (128, 1024), BF16, ph="p2a")
        pb.sbuf("rn1", (2, 1024), F32, ph="p2a")
        pb.sbuf("rn2", (2, 1024), F32, ph="p2a")
        pb.sbuf("rnb", (128, 512), F32, ph="p2a")
        pb.sbuf("recipb", (128, WIN), F32, ph="p2a")
        pb.sbuf("den_sb", (128, WIN), F32, ph="p2a")
        pb.sbuf("den_pack", (128, 2048), F32, ph="p2a")
        pb.sbuf("rbc2", (128, WIN), F32, ph="p2a")
        pb.psum("pqa", (128, 512), F32, ph="p2a")
        pb.psum("pqb", (128, 512), F32, ph="p2a")
        pb.psum("pvA", (128, 512), F32, ph="p2a")
        pb.psum("pvB", (128, 512), F32, ph="p2a")
        pb.psum("pe5", (128, 512), F32, ph="p2a")
        pb.psum("pf6", (128, 512), F32, ph="p2a")
        pb.psum("pg7", (128, 512), F32, ph="p2a")
        pb.psum("ph8", (128, 512), F32, ph="p2a")

        @block.vector
        def _(vr):
            v = Eng(pb, "v", vr)
            v.memset(S["vring"][:, :, :, :], 0.0)
            v.memset(S["vring"][:, :, :, :].rearrange("p s t (h m) -> p s t h m", m=65)[:, :, :, :, 64:65], 1.0)
            v.memset(S["kring"][:, :, :, :], 0.0)
            v.memset(S["den_sb"][:, :], 0.0)
            v.memset(S["recipb"][:, :], 0.0)
            v.sig("ve")
            v.sig("vkr")
            v.sig("qkr")

        p2_attn(pb, l, st)
        pb.barrier("p2a%d" % l, drain=("xw", "xr", "wld", "gd", "sa"))
        pb.close_phase("p2a")

        p2_exchange(pb, st)
        pb.open_phase("p2f")
        pb.sbuf("w1s", (128, 8, W1C), BF16, ph="p2f")
        pb.sbuf("w2s", (128, 22, D), BF16, ph="p2f")
        pb.sbuf("hg", (128, 22, 512), BF16, ph="p2f")
        pb.sbuf("xn2", (128, 8, 512), BF16, ph="p2f")
        pb.sbuf("absc", (128, 8, 512), BF16, ph="p2f")
        pb.sbuf("s_b1c", (128, 44), F32, ph="p2f")
        pb.sbuf("s_b2c", (128, 8), F32, ph="p2f")
        pb.sbuf("s_lnc", (128, 16), F32, ph="p2f")
        pb.psum("pa00", (128, 512), F32, ph="p2f")
        pb.psum("pa01", (128, 512), F32, ph="p2f")
        pb.psum("pa10", (128, 512), F32, ph="p2f")
        pb.psum("pa11", (128, 512), F32, ph="p2f")
        pb.psum("pg00", (128, 512), F32, ph="p2f")
        pb.psum("pg01", (128, 512), F32, ph="p2f")
        pb.psum("pg10", (128, 512), F32, ph="p2f")
        pb.psum("pg11", (128, 512), F32, ph="p2f")

        p2_ffn(pb, l, st)
        pb.barrier("p2f%d" % l, drain=("xw", "xr", "wld", "gd"))
        pb.close_phase("p2f")

    pb.close_phase("p2")

    if "xinner_final" in pb.dbg:
        dd = pb.dout("d_xinner", (D, DOMF), F32)
        def dumpf(gr, dd=dd):
            g = Eng(pb, "g", gr)
            g.wait("xw", pb.cnt["xw"])
            g.dma(dd[:, :], T["xinnerT"][:, :], "gd")
        block.gpsimd(dumpf)


def p3(pb):
    nc, block = pb.nc, pb.block
    S, T, P = pb.S, pb.T, pb.P
    groups = [[0, 1], [2, 3], [4, 5], [6, 7]]

    pb.open_phase("p3")
    pb.sbuf("p3x", (128, 2, D), F32, ph="p3")
    pb.sbuf("p3af", (128, 2, 2, D), F32, ph="p3")
    pb.sbuf("p3sum", (128, 2, D), F32, ph="p3")
    pb.sbuf("p3x1", (128, 2, D), F32, ph="p3")
    pb.sbuf("p3xn", (128, 2, D), BF16, ph="p3")
    pb.sbuf("p3sq", (128, D), F32, ph="p3")
    pb.sbuf("p3st", (128, 8), F32, ph="p3")
    pb.psum("p3ta", (128, 512), F32, ph="p3")
    pb.psum("p3tb", (128, 512), F32, ph="p3")
    for s in ("p0n", "p0v", "p0s", "p0e"):
        pb.sem(s)
    xs3 = pb.stream("xs", 2)
    af3 = pb.stream("af3", 4)

    mmb = [0]
    st0 = pb.cnt["st"]
    n0 = pb.cnt["p0n"]
    v0 = pb.cnt["p0v"]
    s0 = pb.cnt["p0s"]
    e0 = pb.cnt["p0e"]

    @block.sync
    def _(y_raw):
        for i in range(NTO):
            if i >= 2:
                pb.wait("y", y_raw, "p0e", e0 + i - 1)
            xs3.issue(y_raw, S["p3x"][:, i % 2, :], T["x"][i * 128:(i + 1) * 128, :])
            if i >= 2:
                pb.wait("y", y_raw, "p0n", n0 + i - 1)
            af3.issue(y_raw, S["p3af"][:, i % 2, 0, :].rearrange("p (f c) -> p f c", f=8),
                      bass.AP(T["A_T"], i * 128, [[HALF, 128], [128 * HALF, 8], [1, 128]]))
            af3.issue(y_raw, S["p3af"][:, i % 2, 1, :].rearrange("p (f c) -> p f c", f=8),
                      bass.AP(T["xinnerT"], SLAB + i * 128, [[DOMF, 128], [128 * DOMF, 8], [1, 128]]))

    @block.tensor
    def _(tr):
        t = Eng(pb, "t", tr, chained=False)
        mmb[0] = pb.cnt["mm"]
        for i in range(NTO):
            t.wait("p0n", n0 + i + 1)
            if i >= 1:
                t.wait("p0e", e0 + i)
            for fg in range(2):
                ps = P["p3ta"] if fg == 0 else P["p3tb"]
                for f in range(4):
                    ft = fg * 4 + f
                    inst = tr.matmul(ps[:, f * 128:(f + 1) * 128],
                                     S["p3sum"][:, i % 2, ft * 128:(ft + 1) * 128],
                                     S["sb_identf"][:, :], start=True, stop=True,
                                     is_transpose=True)
            inst.then_inc(pb.sems["mm"], 1)
            pb.tick("mm")

    @block.vector
    def _(vr):
        v = Eng(pb, "v", vr)
        for i in range(NTO):
            af3.wait("v", vr, 2 * i)
            af3.wait("v", vr, 2 * i + 1)
            v.tensor_tensor(S["p3sum"][:, i % 2, :], S["p3af"][:, i % 2, 0, :],
                            S["p3af"][:, i % 2, 1, :], op=ALU.add)
            v.sig("p0n")
            v.wait("mm", mmb[0] + i + 1)
            xs3.wait("v", vr, i)
            if i >= 2:
                v.wait("st", st0 + 32 * (i - 1))
            xt = S["p3x"][:, i % 2, :]
            x1 = S["p3x1"][:, i % 2, :]
            for fg in range(2):
                ps = P["p3ta"] if fg == 0 else P["p3tb"]
                for f in range(4):
                    ft = fg * 4 + f
                    v.tensor_tensor(x1[:, ft * 128:(ft + 1) * 128], ps[:, f * 128:(f + 1) * 128],
                                    xt[:, ft * 128:(ft + 1) * 128], op=ALU.add)
            st = S["p3st"]
            v.reduce_sum(st[:, 0:1], x1, axis=AXX)
            v.tensor_scalar_mul(st[:, 0:1], st[:, 0:1], 1.0 / D)
            v.tensor_tensor(st[:, 4:5], st[:, 0:1], st[:, 0:1], op=ALU.mult)
            v.tensor_tensor(S["p3sq"][:, :], x1, x1, op=ALU.mult)
            v.reduce_sum(st[:, 1:2], S["p3sq"][:, :], axis=AXX)
            v.tensor_scalar_mul(st[:, 1:2], st[:, 1:2], 1.0 / D)
            v.tensor_tensor(st[:, 1:2], st[:, 1:2], st[:, 4:5], op=ALU.subtract)
            v.tensor_scalar_add(st[:, 2:3], st[:, 1:2], 1e-5)
            v.sig("p0v")
            v.wait("p0s", s0 + i + 1)
            v.reciprocal(st[:, 3:4], st[:, 6:7])
            v.tensor_scalar(S["p3xn"][:, i % 2, :], x1, st[:, 0:1], st[:, 3:4],
                            op0=ALU.subtract, op1=ALU.mult)
            v.sig("p0e")
            v.sig("ve")

    ve0 = pb.cnt["ve"] - NTO

    @block.scalar
    def _(sr):
        s = Eng(pb, "s", sr)
        for i in range(NTO + 1):
            if i < NTO:
                s.wait("p0v", v0 + i + 1)
                s.activation(S["p3st"][:, 6:7], S["p3st"][:, 2:3], AF.Sqrt)
                s.sig("p0s")
            if i >= 1:
                j = i - 1
                s.wait("ve", ve0 + j + 1)
                s.dma(T["x1_tm"][j * 128:(j + 1) * 128, :], S["p3x1"][:, j % 2, :], "st")
                s.dma(T["xn1_own"][j * 128:(j + 1) * 128, :], S["p3xn"][:, j % 2, :], "st")

    @block.gpsimd
    def _(gr):
        g = Eng(pb, "g", gr)
        g.wait("st", pb.cnt["st"])
        gr.collective_compute(
            "AllGather", ALU.bypass, replica_groups=groups,
            ins=[T["xn1_own"][:, :].opt()], outs=[T["xn1_all"][:, :].opt()],
        ).then_inc(pb.sem("cc"), 1)
        pb.tick("cc")

    pb.barrier("p3", drain=("st", "cc"))
    pb.close_phase("p3")

    if "p3" in pb.dbg:
        dd = pb.dout("d_xn1all", (N, D), BF16)
        dd2 = pb.dout("d_x1tm", (HALF, D), F32)
        def dumpf(gr, dd=dd, dd2=dd2):
            g = Eng(pb, "g", gr)
            g.dma(dd[:, :], T["xn1_all"][:, :], "gd")
            g.dma(dd2[:, :], T["x1_tm"][:, :], "gd")
        block.gpsimd(dumpf)


def p4_gate(pb):
    nc, block = pb.nc, pb.block
    S, T, P = pb.S, pb.T, pb.P
    pb.open_phase("p4g")
    pb.sbuf("xn1T", (128, 8, N), BF16, ph="p4g")
    pb.psum("pga", (128, 512), F32, ph="p4g")
    pb.psum("pgb", (128, 512), F32, ph="p4g")
    xt4 = pb.stream("xt4", 2)

    @block.gpsimd
    def _(gr):
        g = Eng(pb, "g", gr)
        g.dma(S["gw4"][:, :, :], T["gate_w"][:, :].rearrange("(kt p) e -> p kt e", p=128), "wld")
        g.dma(S["s_triu"][:, :], T["triu128"][:, :], "wld")
        g.dma(S["s_iotasel"][:, :], T["iotasel"][:, :], "wld")
        g.dma(S["s_iotacol"][:, :], T["iotacol"][:, :], "wld")
    wld4 = pb.cnt["wld"]

    @block.sync
    def _(y):
        pb.wait("y", y, "cc", pb.cnt["cc"])
        for tt4 in range(4):
            for ft in range(8):
                i = tt4 * 8 + ft
                if i >= 2:
                    xt4.wait("y", y, i - 2)
                xt4.issue(y, S["xn1T"][:, ft, tt4 * 1024:(tt4 + 1) * 1024],
                          T["xn1_all"][tt4 * 1024:(tt4 + 1) * 1024, ft * 128:(ft + 1) * 128],
                          transpose=True)

    mmg = [0]
    sce0 = pb.cnt["sce"]

    @block.tensor
    def _(tr):
        t = Eng(pb, "t", tr, chained=False)
        t.wait("wld", wld4)
        mmg[0] = pb.cnt["mm"]
        for tt in range(NT):
            xt4.wait("t", tr, 8 * (tt // 8) + 6)
            xt4.wait("t", tr, 8 * (tt // 8) + 7)
            if tt >= 2:
                t.wait("sce", sce0 + tt - 1)
            ps = P["pga"] if tt % 2 == 0 else P["pgb"]
            for kt in range(8):
                inst = tr.matmul(ps[:, 0:8], S["xn1T"][:, kt, tt * 128:(tt + 1) * 128],
                                 S["gw4"][:, kt, :], start=(kt == 0), stop=(kt == 7),
                                 skip_group_check=True)
            inst.then_inc(pb.sems["mm"], 1); pb.tick("mm")

    @block.vector
    def _(vr):
        v = Eng(pb, "v", vr)
        for tt in range(NT):
            v.wait("mm", mmg[0] + tt + 1)
            ps = P["pga"] if tt % 2 == 0 else P["pgb"]
            v.tensor_copy(S["probs4"][:, :, tt:tt + 1], ps[:, 0:8].rearrange("p (e o) -> p e o", o=1))
            v.sig("sce")

    pb.barrier("p4g", drain=("xt40", "xt41", "wld"))
    pb.close_phase("p4g")


def p4_select(pb):
    nc, block = pb.nc, pb.block
    S, T, P = pb.S, pb.T, pb.P
    pb.open_phase("p4b")
    pb.psum("pb4a", (128, 512), F32, ph="p4b")
    pb.psum("pb4b", (128, 512), F32, ph="p4b")
    pb.psum("pb4c", (128, 512), F32, ph="p4b")

    @block.scalar
    def _(sr):
        s = Eng(pb, "s", sr)
        s.activation(S["exp4"][:, :, :], S["probs4"][:, :, :], AF.Exp)
        s.sig("se")

    @block.vector
    def _(vr):
        v = Eng(pb, "v", vr)
        v.wait("se", pb.cnt["se"])
        v.reduce_sum(S["den4"][:, :], S["exp4"][:, :, :].rearrange("p e t -> p t e"), axis=AXX)
        v.reciprocal(S["den4"][:, :], S["den4"][:, :])
        for e in range(N_EXP):
            v.tensor_tensor(S["probs4"][:, e, :], S["exp4"][:, e, :], S["den4"][:, :], op=ALU.mult)
        br = S["brow"]
        v.memset(br[:, :], 0.0)
        v.memset(S["srow"][:, :, :], 0.0)
        v.memset(br[0:1, 8:16], 1.0)
        v.memset(br[0:1, 16:24], 0.5)
        v.sig("pvm")

    for it in range(30):
        @block.tensor
        def _(tr):
            t = Eng(pb, "t", tr, chained=False)
            t.wait("pvm", pb.cnt["pvm"])
            inst = tr.matmul(P["pb4a"][:, 0:8], S["sb_bc0f"][:, :], S["brow"][:, 16:24],
                             start=True, stop=True, skip_group_check=True)
            inst.then_inc(pb.sems["mm"], 1); pb.tick("mm")

        @block.vector
        def _(vr):
            v = Eng(pb, "v", vr)
            v.wait("mm", pb.cnt["mm"])
            v.tensor_copy(S["taubc"][:, :], P["pb4a"][:, 0:8])
            for e in range(N_EXP):
                v.tensor_scalar(S["cmp4"][:, e, :], S["probs4"][:, e, :],
                                S["taubc"][:, e:e + 1], None, op0=ALU.is_ge)
            v.reduce_sum(S["cnt4"][:, :], S["cmp4"][:, :, :], axis=AXX)
            v.sig("pvm")

        @block.tensor
        def _(tr):
            t = Eng(pb, "t", tr, chained=False)
            t.wait("pvm", pb.cnt["pvm"])
            inst = tr.matmul(P["pb4b"][0:1, 0:8], S["sb_onesf"][:, 0:1], S["cnt4"][:, :],
                             start=True, stop=True, skip_group_check=True)
            inst.then_inc(pb.sems["mm"], 1); pb.tick("mm")

        @block.vector
        def _(vr):
            v = Eng(pb, "v", vr)
            v.wait("mm", pb.cnt["mm"])
            br = S["brow"]
            v.tensor_copy(br[0:1, 24:32], P["pb4b"][0:1, 0:8])
            v.tensor_scalar(br[0:1, 32:40], br[0:1, 24:32], float(MOE_K), None, op0=ALU.is_ge)
            v.tensor_tensor(br[0:1, 40:48], br[0:1, 16:24], br[0:1, 0:8], op=ALU.subtract)
            v.tensor_tensor(br[0:1, 48:56], br[0:1, 40:48], br[0:1, 32:40], op=ALU.mult)
            v.tensor_tensor(br[0:1, 0:8], br[0:1, 0:8], br[0:1, 48:56], op=ALU.add)
            v.tensor_scalar(br[0:1, 56:64], br[0:1, 32:40], -1.0, 1.0, op0=ALU.mult, op1=ALU.add)
            v.tensor_tensor(br[0:1, 40:48], br[0:1, 16:24], br[0:1, 8:16], op=ALU.subtract)
            v.tensor_tensor(br[0:1, 48:56], br[0:1, 40:48], br[0:1, 56:64], op=ALU.mult)
            v.tensor_tensor(br[0:1, 8:16], br[0:1, 8:16], br[0:1, 48:56], op=ALU.add)
            v.tensor_tensor(br[0:1, 16:24], br[0:1, 0:8], br[0:1, 8:16], op=ALU.add)
            v.tensor_scalar_mul(br[0:1, 16:24], br[0:1, 16:24], 0.5)
            v.sig("pvm")

    @block.tensor
    def _(tr):
        t = Eng(pb, "t", tr, chained=False)
        t.wait("pvm", pb.cnt["pvm"])
        inst = tr.matmul(P["pb4a"][:, 0:8], S["sb_bc0f"][:, :], S["brow"][:, 0:8],
                         start=True, stop=True, skip_group_check=True)
        inst.then_inc(pb.sems["mm"], 1); pb.tick("mm")

    @block.vector
    def _(vr):
        v = Eng(pb, "v", vr)
        v.wait("mm", pb.cnt["mm"])
        v.tensor_copy(S["taubc"][:, :], P["pb4a"][:, 0:8])
        for e in range(N_EXP):
            v.tensor_scalar(S["maskf"][:, e, :], S["probs4"][:, e, :],
                            S["taubc"][:, e:e + 1], None, op0=ALU.is_ge)
        v.tensor_tensor(S["exp4"][:, :, :], S["probs4"][:, :, :], S["maskf"][:, :, :], op=ALU.mult)
        for j in range(4):
            v.tensor_scalar_mul(S["t4a"][:, :], S["exp4"][:, j, :], S["sb_gates"][:, 1:2])
            v.tensor_scalar_mul(S["t4b"][:, :], S["exp4"][:, j + 4, :], S["sb_gates"][:, 0:1])
            v.tensor_tensor(S["gmine"][:, j, :], S["t4a"][:, :], S["t4b"][:, :], op=ALU.add)
        v.tensor_copy(S["gmb"][:, :, :], S["gmine"][:, :, :])
        v.sig("pvm")

    for e in range(N_EXP):
        @block.tensor
        def _(tr, e=e):
            t = Eng(pb, "t", tr, chained=False)
            t.wait("pvm", pb.cnt["pvm"])
            if e >= 1:
                t.wait("sce", pb.cnt["sce"])
            tr.matmul(P["pb4a"][:, 0:32], S["s_triu"][:, :], S["maskf"][:, e, :],
                      start=True, stop=True, skip_group_check=True)
            inst = tr.matmul(P["pb4b"][0:1, 0:32], S["sb_onesf"][:, 0:1], S["maskf"][:, e, :],
                             start=True, stop=True, skip_group_check=True)
            inst.then_inc(pb.sems["mm"], 1); pb.tick("mm")

        @block.vector
        def _(vr, e=e):
            v = Eng(pb, "v", vr)
            v.wait("mm", pb.cnt["mm"])
            sr = S["srow"]
            v.tensor_copy(sr[0:1, 0, :], P["pb4b"][0:1, 0:32])
            for k, sh in enumerate((1, 2, 4, 8, 16)):
                src, dst = k % 2, (k + 1) % 2
                v.tensor_copy(sr[0:1, dst, 0:sh], sr[0:1, src, 0:sh])
                v.tensor_tensor(sr[0:1, dst, sh:32], sr[0:1, src, sh:32],
                                sr[0:1, src, 0:32 - sh], op=ALU.add)
            v.memset(sr[0:1, 2, 0:1], 0.0)
            v.tensor_copy(sr[0:1, 2, 1:32], sr[0:1, 1, 0:31])
            v.tensor_copy(S["cwt"][:, :], P["pb4a"][:, 0:32])
            v.sig("pne")

        @block.tensor
        def _(tr, e=e):
            t = Eng(pb, "t", tr, chained=False)
            t.wait("pne", pb.cnt["pne"])
            inst = tr.matmul(P["pb4c"][:, 0:32], S["sb_bc0f"][:, :], S["srow"][:, 2, :],
                             start=True, stop=True, skip_group_check=True)
            inst.then_inc(pb.sems["mm"], 1); pb.tick("mm")

        @block.vector
        def _(vr, e=e):
            v = Eng(pb, "v", vr)
            v.wait("mm", pb.cnt["mm"])
            v.tensor_tensor(S["cumm"][:, e, :], S["cwt"][:, :], P["pb4c"][:, 0:32], op=ALU.add)
            v.tensor_tensor(S["cumm"][:, e, :], S["cumm"][:, e, :], S["maskf"][:, e, :], op=ALU.mult)
            v.sig("sce")

    @block.vector
    def _(vr):
        v = Eng(pb, "v", vr)
        v.wait("sce", pb.cnt["sce"])
        for j in range(4):
            v.tensor_scalar_mul(S["t4a"][:, :], S["cumm"][:, j, :], S["sb_gates"][:, 1:2])
            v.tensor_scalar_mul(S["t4b"][:, :], S["cumm"][:, j + 4, :], S["sb_gates"][:, 0:1])
            v.tensor_tensor(S["cmine"][:, j, :], S["t4a"][:, :], S["t4b"][:, :], op=ALU.add)
        v.sig("pvm")

    pb.barrier("p4b")
    pb.close_phase("p4b")


def p4_experts(pb):
    nc, block = pb.nc, pb.block
    S, T, P = pb.S, pb.T, pb.P
    pb.open_phase("p4x")
    pb.sbuf("xn1_tm", (128, NT, D), BF16, ph="p4x")
    pb.sbuf("hgeT", (128, 32, MOE_K), BF16, ph="p4x")
    pb.sbuf("xinT", (128, 8, MOE_K), BF16, ph="p4x")
    pb.sbuf("w1s4", (128, 2, 8, 128), BF16, ph="p4x")
    pb.sbuf("w2s4", (128, 2, D), BF16, ph="p4x")
    pb.sbuf("Ptg", (128, 2, 512), BF16, ph="p4x")
    pb.sbuf("grow4", (1, MOE_K), F32, ph="p4x")
    pb.sbuf("gcols", (128, 8), F32, ph="p4x")
    pb.sbuf("h2st", (128, 8, D), BF16, ph="p4x")
    for i in range(8):
        pb.psum(f"px{i}", (128, 512), F32, ph="p4x")
    w14 = pb.stream("w14", 2)
    w24 = pb.stream("w24", 2)

    @block.gpsimd
    def _(gr):
        g = Eng(pb, "g", gr)
        g.wait("cc", pb.cnt["cc"])
        g.dma(S["xn1_tm"][:, :, :], T["xn1_all"][:, :].rearrange("(tt p) d -> p tt d", p=128), "xld")
    xld4 = pb.cnt["xld"]

    for j in range(4):
        st_j = {}

        @block.tensor
        def _(tr, j=j, st_j=st_j):
            t = Eng(pb, "t", tr, chained=False)
            t.wait("xld", xld4)
            t.wait("pvm", pb.cnt["pvm"])
            if j >= 1:
                t.wait("sce", pb.cnt["sce"])
            st_j["mm0"] = pb.cnt["mm"]
            st_j["pt0"] = pb.cnt["hgr"]
            st_j["g4x0"] = pb.cnt["hgf"]
            st_j["g4e0"] = pb.cnt["xn2r"]
            for nh in range(2):
                if nh == 1:
                    t.wait("xn2r", st_j["g4e0"] + 1)
                for tt in range(NT):
                    i = nh * NT + tt
                    t.wait("hgr", st_j["pt0"] + i + 1)
                    for fc in range(8):
                        inst = tr.matmul(P[f"px{fc}"][:, :],
                                         S["xn1_tm"][:, tt, fc * 128:(fc + 1) * 128],
                                         S["Ptg"][:, i % 2, :],
                                         start=(tt == 0), stop=(tt == NT - 1),
                                         skip_group_check=True)
                    inst.then_inc(pb.sems["mm"], 1); pb.tick("mm")

        @block.tensor
        def _(tr, j=j, st_j=st_j):
            t = Eng(pb, "t", tr, chained=False)
            st_j["gm0"] = pb.cnt["mm"]
            st_j["gmm0"] = pb.cnt["sq"]
            gpt0 = st_j["pt0"] + 64
            for nh in range(2):
                for tt in range(NT):
                    i = nh * NT + tt
                    t.wait("hgr", gpt0 + i + 1)
                    bank = P["px0"] if nh == 0 else P["px1"]
                    inst = tr.matmul(bank[0:1, 0:512],
                                     S["gmb"][:, j, tt:tt + 1], S["Ptg"][:, i % 2, :],
                                     start=(tt == 0), stop=(tt == NT - 1),
                                     skip_group_check=True)
                    inst.then_inc(pb.sems["sq"], 1); pb.tick("sq")

        @block.vector
        def _(vr, j=j, st_j=st_j):
            v = Eng(pb, "v", vr)
            for nh in range(2):
                for tt in range(NT):
                    i = nh * NT + tt
                    if i >= 2:
                        v.wait("mm", st_j["mm0"] + (i - 1))
                    v.tensor_scalar(S["Ptg"][:, i % 2, :], S["s_iotasel"][:, nh * 512:nh * 512 + 512],
                                    S["cmine"][:, j, tt:tt + 1], 0.0,
                                    op0=ALU.subtract, op1=ALU.is_equal)
                    v.sig("hgr")
                v.wait("mm", st_j["mm0"] + NT * (nh + 1))
                for fc in range(8):
                    v.tensor_copy(S["xinT"][:, fc, nh * 512:nh * 512 + 512], P[f"px{fc}"][:, :])
                v.sig("xn2r")
            for nh in range(2):
                for tt in range(NT):
                    i = nh * NT + tt
                    if i >= 2:
                        v.wait("sq", st_j["gmm0"] + (i - 1))
                    else:
                        v.wait("mm", st_j["mm0"] + 64)
                    v.tensor_scalar(S["Ptg"][:, i % 2, :], S["s_iotasel"][:, nh * 512:nh * 512 + 512],
                                    S["cmine"][:, j, tt:tt + 1], 0.0,
                                    op0=ALU.subtract, op1=ALU.is_equal)
                    v.sig("hgr")
            v.wait("sq", st_j["gmm0"] + 64)
            v.tensor_copy(S["grow4"][0:1, 0:512], P["px0"][0:1, 0:512])
            v.tensor_copy(S["grow4"][0:1, 512:1024], P["px1"][0:1, 0:512])
            v.sig("xn2r")

        @block.tensor
        def _(tr, j=j, st_j=st_j):
            t = Eng(pb, "t", tr, chained=False)
            t.wait("xn2r", pb.cnt["xn2r"])
            for c in range(8):
                inst = tr.matmul(P["px2"][:, c:c + 1], S["grow4"][0:1, c * 128:(c + 1) * 128],
                                 S["sb_identf"][0:1, 0:1], start=(c == 0), stop=(c == 7),
                                 is_transpose=True, skip_group_check=True)
            inst.then_inc(pb.sems["mm"], 1); pb.tick("mm")

        @block.vector
        def _(vr, j=j):
            v = Eng(pb, "v", vr)
            v.wait("mm", pb.cnt["mm"])
            v.tensor_copy(S["gcols"][:, :], P["px2"][:, 0:8])
            v.sig("xn2r")

        h140 = pb.cnt["nr"]

        @block.gpsimd
        def _(gr, j=j, h140=h140):
            g = Eng(pb, "g", gr)
            for ht in range(32):
                if ht >= 2:
                    pb.wait("g", gr, "nr", h140 + 2 * (ht - 1))
                w14.issue(gr, S["w1s4"][:, ht % 2, :, :],
                          T["moe_w1"][j, :, ht * 128:(ht + 1) * 128]
                          .rearrange("(kt p) c -> p kt c", p=128))

        @block.tensor
        def _(tr, j=j, st_j=st_j):
            t = Eng(pb, "t", tr, chained=False)
            t.wait("xn2r", pb.cnt["xn2r"])
            st_j["h10"] = pb.cnt["mm"]
            st_j["h1e0"] = pb.cnt["nr"]
            w0 = w14.nissued - 32
            for ht in range(32):
                w14.wait("t", tr, w0 + ht)
                if ht >= 2:
                    t.wait("nr", st_j["h1e0"] + 2 * (ht - 1))
                for nh in range(2):
                    bank = P[f"px{2 + (ht % 2) * 2 + nh}"]
                    for kt in range(8):
                        inst = tr.matmul(bank[:, :], S["w1s4"][:, ht % 2, kt, :],
                                         S["xinT"][:, kt, nh * 512:nh * 512 + 512],
                                         start=(kt == 0), stop=(kt == 7), skip_group_check=True)
                    inst.then_inc(pb.sems["mm"], 1); pb.tick("mm")

        @block.scalar
        def _(sr, j=j, st_j=st_j):
            s = Eng(pb, "s", sr)
            for ht in range(32):
                s.wait("mm", st_j["h10"] + 2 * ht + 2)
                for nh in range(2):
                    bank = P[f"px{2 + (ht % 2) * 2 + nh}"]
                    s.activation(S["hgeT"][:, ht, nh * 512:nh * 512 + 512], bank[:, :],
                                 AF.Gelu_apprx_tanh)
                s.sig("nr", 2)

        h240 = pb.cnt["pex"]

        @block.gpsimd
        def _(gr, j=j, h240=h240):
            g = Eng(pb, "g", gr)
            for g2 in range(2):
                for kt in range(32):
                    i = g2 * 32 + kt
                    if i >= 2:
                        pb.wait("g", gr, "pex", h240 + (i - 1) * 8)
                    w24.issue(gr, S["w2s4"][:, i % 2, :],
                              T["moe_w2"][j, kt * 128:(kt + 1) * 128, :])

        @block.tensor
        def _(tr, j=j, st_j=st_j):
            t = Eng(pb, "t", tr, chained=False)
            t.wait("nr", pb.cnt["nr"])
            st_j["h20"] = pb.cnt["mm"]
            st_j["pex0"] = pb.cnt["pex"]
            st_j["h2e0"] = pb.cnt["fst"]
            w0 = w24.nissued - 64
            for g2 in range(2):
                if g2 == 1:
                    t.wait("fst", st_j["h2e0"] + 8)
                for kt in range(32):
                    i = g2 * 32 + kt
                    w24.wait("t", tr, w0 + i)
                    for stl in range(4):
                        st_ = g2 * 4 + stl
                        for nh in range(2):
                            bank = P[f"px{stl * 2 + nh}"]
                            inst = tr.matmul(bank[:, :], S["hgeT"][:, kt, st_ * 128:(st_ + 1) * 128],
                                             S["w2s4"][:, i % 2, nh * 512:nh * 512 + 512],
                                             start=(kt == 0), stop=(kt == 31), skip_group_check=True)
                    inst.then_inc(pb.sems["pex"], 8); pb.tick("pex", 8)

        @block.vector
        def _(vr, j=j, st_j=st_j):
            v = Eng(pb, "v", vr)
            v.wait("st", pb.cnt["st"])
            for g2 in range(2):
                v.wait("pex", st_j["pex0"] + 256 * (g2 + 1))
                for stl in range(4):
                    st_ = g2 * 4 + stl
                    for nh in range(2):
                        bank = P[f"px{stl * 2 + nh}"]
                        v.tensor_scalar_mul(S["h2st"][:, st_, nh * 512:nh * 512 + 512],
                                            bank[:, :], S["gcols"][:, st_:st_ + 1])
                v.sig("fst", 8)

        @block.scalar
        def _(sr, j=j):
            s = Eng(pb, "s", sr)
            s.wait("fst", pb.cnt["fst"])
            s.dma(T["h2w_own"][j * MOE_K:(j + 1) * MOE_K, :].rearrange("(st p) d -> p st d", p=128),
                  S["h2st"][:, :, :], "st")
            s.sig("sce")

    @block.gpsimd
    def _(gr):
        g = Eng(pb, "g", gr)
        g.wait("st", pb.cnt["st"])
        gr.collective_compute(
            "AllGather", ALU.bypass, replica_groups=[[0, 1], [2, 3], [4, 5], [6, 7]],
            ins=[T["h2w_own"][:, :].opt()], outs=[T["h2w_all"][:, :].opt()],
        ).then_inc(pb.sem("cc"), 1)
        pb.tick("cc")

    pb.barrier("p4x", drain=("st", "cc", "xld", "w140", "w141", "w240", "w241"))
    pb.close_phase("p4x")


def p4_scatter(pb):
    nc, block = pb.nc, pb.block
    S, T, P = pb.S, pb.T, pb.P
    pb.open_phase("p4s")
    pb.sbuf("fst", (128, 2, 8, D), BF16, ph="p4s")
    pb.sbuf("cumbc8", (128, 8, 512), F32, ph="p4s")
    pb.sbuf("Psel", (128, 2, 512), BF16, ph="p4s")
    pb.sbuf("F1st", (128, 8, 512), BF16, ph="p4s")
    pb.sbuf("cmow", (128, 8, 16), F32, ph="p4s")
    pb.sbuf("c4row", (128, 512), F32, ph="p4s")
    for i in range(8):
        pb.psum(f"ps{i}", (128, 512), F32, ph="p4s")
    h2l = pb.stream("h2l", 2)

    @block.vector
    def _(vr):
        v = Eng(pb, "v", vr)
        v.memset(S["c4row"][:, :], 0.0)
        for e in range(N_EXP):
            v.tensor_scalar_mul(S["t4a"][:, 0:16], S["cumm"][:, e, 0:16], S["sb_gates"][:, 1:2])
            v.tensor_scalar_mul(S["t4b"][:, 0:16], S["cumm"][:, e, 16:32], S["sb_gates"][:, 0:1])
            v.tensor_tensor(S["cmow"][:, e, :], S["t4a"][:, 0:16], S["t4b"][:, 0:16], op=ALU.add)
        v.sig("fbc")

    s4m0 = pb.cnt["fq0"]

    @block.gpsimd
    def _(gr):
        g = Eng(pb, "g", gr)
        g.wait("cc", pb.cnt["cc"])
        for tc in range(4):
            for e in range(N_EXP):
                i = tc * N_EXP + e
                if i >= 2:
                    pb.wait("g", gr, "fq0", s4m0 + (i - 1) * 64)
                h2l.issue(gr, S["fst"][:, i % 2, :, :],
                          T["h2w_all"][e * MOE_K:(e + 1) * MOE_K, :]
                          .rearrange("(st p) d -> p st d", p=128))
    hl0 = h2l.nissued - 32

    for tc in range(4):
        st_t = {}

        @block.tensor
        def _(tr, tc=tc, st_t=st_t):
            t = Eng(pb, "t", tr, chained=False)
            t.wait("fbc", pb.cnt["fbc"])
            if tc >= 1:
                t.wait("fn0", pb.cnt["fn0"])
            st_t["mm0"] = pb.cnt["mm"]
            st_t["s4r0"] = pb.cnt["shc"]
            st_t["bc40"] = pb.cnt["fb0"]
            for e in range(N_EXP):
                if e >= 1:
                    t.wait("shc", st_t["s4r0"] + 2 * e - 1)
                for c2 in range(4):
                    inst = tr.matmul(P["ps0"][0:1, c2 * 128:(c2 + 1) * 128],
                                     S["cmow"][:, e, tc * 4 + c2: tc * 4 + c2 + 1],
                                     S["sb_identf"][:, :], start=(c2 == 0), stop=(c2 == 3),
                                     is_transpose=True, skip_group_check=True)
                inst.then_inc(pb.sems["mm"], 1); pb.tick("mm")
                t.wait("shc", st_t["s4r0"] + 2 * e + 1)
                inst = tr.matmul(P["ps1"][:, :], S["sb_bc0f"][:, :], S["c4row"][:, :],
                                 start=True, stop=True, skip_group_check=True)
                inst.then_inc(pb.sems["fb0"], 1); pb.tick("fb0")

        @block.vector
        def _(vr, tc=tc, st_t=st_t):
            v = Eng(pb, "v", vr)
            for e in range(N_EXP):
                v.wait("mm", st_t["mm0"] + e + 1)
                v.tensor_copy(S["c4row"][0:1, :], P["ps0"][0:1, :])
                v.sig("shc")
                v.wait("fb0", st_t["bc40"] + e + 1)
                v.tensor_copy(S["cumbc8"][:, e, :], P["ps1"][:, :])
                v.sig("shc")

        st_b = {}

        @block.tensor
        def _(tr, tc=tc, st_b=st_b, st_t=st_t):
            t = Eng(pb, "t", tr, chained=False)
            t.wait("shc", st_t["s4r0"] + 16)
            st_b["sp0"] = pb.cnt["xw2"]
            for e in range(N_EXP):
                h2l.wait("t", tr, hl0 + tc * N_EXP + e)
                for st_ in range(8):
                    i = e * 8 + st_
                    t.wait("xw2", st_b["sp0"] + i + 1)
                    for ft in range(8):
                        inst = tr.matmul(P[f"ps{ft}"][:, :],
                                         S["fst"][:, e % 2, st_, ft * 128:(ft + 1) * 128],
                                         S["Psel"][:, i % 2, :],
                                         start=(e == 0 and st_ == 0), stop=(e == 7 and st_ == 7),
                                         skip_group_check=True)
                    inst.then_inc(pb.sems["fq0"], 8); pb.tick("fq0", 8)

        @block.vector
        def _(vr, tc=tc, st_b=st_b):
            v = Eng(pb, "v", vr)
            sm0 = pb.cnt["fq0"] - 512
            for e in range(N_EXP):
                for st_ in range(8):
                    i = e * 8 + st_
                    if i >= 2:
                        v.wait("fq0", sm0 + 8 * (i - 1))
                    v.tensor_scalar(S["Psel"][:, i % 2, :], S["cumbc8"][:, e, :],
                                    S["s_iotacol"][:, 0:1], float(st_ * 128),
                                    op0=ALU.subtract, op1=ALU.is_equal)
                    v.sig("xw2")
            v.wait("fq0", pb.cnt["fq0"])
            v.wait("st", pb.cnt["st"])
            for ft in range(8):
                v.tensor_copy(S["F1st"][:, ft, :], P[f"ps{ft}"][:, :])
            v.sig("ve")

        @block.scalar
        def _(sr, tc=tc):
            s = Eng(pb, "s", sr)
            s.wait("ve", pb.cnt["ve"])
            s.dma(bass.AP(T["F1T"], tc * 512, [[HALF, 128], [128 * HALF, 8], [1, 512]]),
                  S["F1st"][:, :, :], "st")
            s.sig("fn0")

    pb.barrier("p4s", drain=("st", "h2l0", "h2l1"))
    pb.close_phase("p4s")


def p4(pb):
    nc, block = pb.nc, pb.block
    S, T = pb.S, pb.T
    pb.open_phase("p4")
    pb.sbuf("probs4", (128, N_EXP, NT), F32, ph="p4")
    pb.sbuf("exp4", (128, N_EXP, NT), F32, ph="p4")
    pb.sbuf("den4", (128, NT), F32, ph="p4")
    pb.sbuf("cumm", (128, N_EXP, NT), F32, ph="p4")
    pb.sbuf("maskf", (128, N_EXP, NT), F32, ph="p4")
    pb.sbuf("cmp4", (128, N_EXP, NT), BF16, ph="p4")
    pb.sbuf("cnt4", (128, N_EXP), F32, ph="p4")
    pb.sbuf("brow", (128, 64), F32, ph="p4")
    pb.sbuf("taubc", (128, N_EXP), F32, ph="p4")
    pb.sbuf("gmine", (128, 4, NT), F32, ph="p4")
    pb.sbuf("gmb", (128, 4, NT), BF16, ph="p4")
    pb.sbuf("cmine", (128, 4, NT), F32, ph="p4")
    pb.sbuf("srow", (128, 3, NT), F32, ph="p4")
    pb.sbuf("cwt", (128, NT), F32, ph="p4")
    pb.sbuf("t4a", (128, NT), F32, ph="p4")
    pb.sbuf("t4b", (128, NT), F32, ph="p4")
    pb.sbuf("gw4", (128, 8, N_EXP), BF16, ph="p4")
    pb.sbuf("s_triu", (128, 128), F32, ph="p4")
    pb.sbuf("s_iotasel", (128, MOE_K), F32, ph="p4")
    pb.sbuf("s_iotacol", (128, 1), F32, ph="p4")
    for s in ("sce", "pvm", "pne", "hgr", "hgf", "xn2r", "sq", "sn", "nr", "pex",
              "fst", "fbc", "shc", "xw2", "fq0", "fn0", "fb0"):
        pb.sem(s)

    p4_gate(pb)
    p4_select(pb)
    p4_experts(pb)
    p4_scatter(pb)
    pb.close_phase("p4")

    if "p4" in pb.dbg:
        dd = pb.dout("d_F1T", (D, HALF), BF16)
        def dumpf(gr, dd=dd):
            g = Eng(pb, "g", gr)
            g.dma(dd[:, :], T["F1T"][:, :], "gd")
        block.gpsimd(dumpf)


def p5_prep(pb):
    nc, block = pb.nc, pb.block
    S, T, P = pb.S, pb.T, pb.P
    pb.open_phase("p5x")
    pb.sbuf("wkv5", (128, 2, 8, D), BF16, ph="p5x")
    pb.sbuf("wi1sb", (128, 8, 512), BF16, ph="p5x")
    pb.sbuf("wi2sb", (128, 4, 1), BF16, ph="p5x")
    pb.sbuf("himp5", (128, 4, 512), BF16, ph="p5x")
    pb.sbuf("improw", (1, 512), F32, ph="p5x")
    pb.sbuf("bvbc", (128, D), F32, ph="p5x")
    pb.sbuf("brsrc", (1, D), F32, ph="p5x")
    pb.sbuf("wi2r", (1, 512), BF16, ph="p5x")
    pb.sbuf("xc5", (128, 2, 8, 512), BF16, ph="p5x")

    @block.gpsimd
    def _(gr):
        g = Eng(pb, "g", gr)
        g.dma(S["wkv5"][:, 0, :, :], T["wk1"][:, :].rearrange("(kt p) m -> p kt m", p=128), "wld")
        g.dma(S["wkv5"][:, 1, :, :], T["wv1"][:, :].rearrange("(kt p) m -> p kt m", p=128), "wld")
        g.dma(S["wi1sb"][:, :, :], T["wi1"][:, :].rearrange("(kt p) m -> p kt m", p=128), "wld")
        g.dma(S["wi2r"][0:1, :], T["wi2"][:, :], "wld")
        g.wait("wld", pb.cnt["wld"])
        for mt in range(4):
            g.dma(S["wi2sb"][:, mt, 0:1], S["wi2r"][0:1, mt * 128:(mt + 1) * 128], "wld")
        g.dma(S["s_t2f"][:, :], T["two2f128"][:, :], "wld")
        g.dma(S["b5c"][:, :, :],
              bass.AP(T["b1vecT"], 0, [[8, 128], [1024, 8], [1, 8]]), "wld")
        g.dma(S["brsrc"][0:1, :], T["b1vec"][2:3, :], "wld")
    wld5 = pb.cnt["wld"]

    def xchunk_loader(stream, mm_sem, mm_base, per_ch):
        @block.sync
        def _(y):
            pb.wait("y", y, "cc", pb.cnt["cc"])
            s0 = stream.nissued
            for ch in range(8):
                if ch >= 2:
                    pb.wait("y", y, mm_sem, mm_base[0] + per_ch * (ch - 1))
                else:
                    pb.wait("y", y, mm_sem, mm_base[0])
                for ft in range(8):
                    i = s0 + ch * 8 + ft
                    if ch * 8 + ft >= 2:
                        stream.wait("y", y, i - 2)
                    stream.issue(y, S["xc5"][:, ch % 2, ft, :],
                                 T["xn1_all"][ch * 512:(ch + 1) * 512, ft * 128:(ft + 1) * 128],
                                 transpose=True)

    @block.tensor
    def _(tr):
        t = Eng(pb, "t", tr, chained=False)
        t.wait("wld", wld5)
        dd0 = pb.cnt["dd"]
        for nh in range(2):
            if nh:
                t.wait("dd", dd0 + 1)
            inst = tr.matmul(P["pq0"][:, :], S["sb_onesf"][0:1, :],
                             S["brsrc"][0:1, nh * 512:nh * 512 + 512],
                             start=True, stop=True, skip_group_check=True)
            inst.then_inc(pb.sems["mm"], 1); pb.tick("mm")

    @block.vector
    def _(vr):
        v = Eng(pb, "v", vr)
        mm0 = pb.cnt["mm"] - 2
        for nh in range(2):
            v.wait("mm", mm0 + nh + 1)
            v.tensor_copy(S["bvbc"][:, nh * 512:nh * 512 + 512], P["pq0"][:, :])
            v.sig("dd")
        v.memset(S["kn2"][:, :], -1e30)
        v.memset(S["vt5"][:, :, :].rearrange("p t (h m) -> p t h m", m=65)[:, :, :, 64:65], 1.0)
        v.sig("dd")

    mmi = [0, 0]
    xs_i = pb.stream("xt4", 2)

    @block.tensor
    def _(tr):
        t = Eng(pb, "t", tr, chained=False)
        t.wait("wld", wld5)
        t.wait("dd", pb.cnt["dd"])
        mmi[0] = pb.cnt["mm"]
        mmi[1] = pb.cnt["vkr"]
        p1k0 = pb.cnt["p1k"]
        qk0 = pb.cnt["qkr"]
        x0 = 0
        for ch in range(8):
            xs_i.wait("t", tr, x0 + ch * 8 + 6)
            xs_i.wait("t", tr, x0 + ch * 8 + 7)
            for mt in range(4):
                if ch >= 1:
                    t.wait("p1k", p1k0 + 4 * (ch - 1) + mt + 1)
                bank = P[f"pq{mt}"]
                for kt in range(8):
                    inst = tr.matmul(bank[:, :], S["wi1sb"][:, kt, mt * 128:(mt + 1) * 128],
                                     S["xc5"][:, ch % 2, kt, :],
                                     start=(kt == 0), stop=(kt == 7), skip_group_check=True)
                inst.then_inc(pb.sems["mm"], 1); pb.tick("mm")
            t.wait("p1k", p1k0 + 4 * (ch + 1))
            if ch >= 1:
                t.wait("qkr", qk0 + ch)
            for mt in range(4):
                inst = tr.matmul(P["pq4"][0:1, 0:512], S["wi2sb"][:, mt, :],
                                 S["himp5"][:, mt, :], start=(mt == 0), stop=(mt == 3),
                                 skip_group_check=True)
            inst.then_inc(pb.sems["mm"], 1); pb.tick("mm")
            if ch >= 1:
                for c in range(4):
                    inst = tr.matmul(P["pq5"][:, (ch - 1) * 4 + c:(ch - 1) * 4 + c + 1],
                                     S["improw"][0:1, c * 128:(c + 1) * 128],
                                     S["sb_identf"][0:1, 0:1], start=(ch == 1 and c == 0),
                                     stop=False, is_transpose=True, skip_group_check=True)
                inst.then_inc(pb.sems["vkr"], 1); pb.tick("vkr")
        t.wait("qkr", qk0 + 8)
        for c in range(4):
            inst = tr.matmul(P["pq5"][:, 28 + c:28 + c + 1],
                             S["improw"][0:1, c * 128:(c + 1) * 128],
                             S["sb_identf"][0:1, 0:1], start=False,
                             stop=(c == 3), is_transpose=True, skip_group_check=True)
        inst.then_inc(pb.sems["vkr"], 1); pb.tick("vkr")

    xchunk_loader(xs_i, "mm", mmi, 5)

    @block.scalar
    def _(sr):
        s = Eng(pb, "s", sr)
        for ch in range(8):
            for mt in range(4):
                s.wait("mm", mmi[0] + 5 * ch + mt + 1)
                bank = P[f"pq{mt}"]
                s.activation(S["himp5"][:, mt, :], bank[:, :], AF.Relu,
                             bias=S["b5c"][:, mt, 4:5])
                s.sig("p1k")

    @block.vector
    def _(vr):
        v = Eng(pb, "v", vr)
        vk0 = mmi[1]
        for ch in range(8):
            v.wait("mm", mmi[0] + 5 * (ch + 1))
            if ch >= 1:
                v.wait("vkr", vk0 + ch)
            v.tensor_scalar(S["improw"][0:1, :], P["pq4"][0:1, 0:512],
                            S["b5c"][0:1, 0, 5:6], None, op0=ALU.add)
            v.sig("qkr")

    @block.vector
    def _(vr):
        v = Eng(pb, "v", vr)
        v.wait("vkr", pb.cnt["vkr"])
        v.tensor_copy(S["imp_tm"][:, :], P["pq5"][:, 0:32])
        c = S["colsc"]
        v.tensor_reduce(c[:, 0:1], S["imp_tm"][:, :], axis=AXX, op=ALU.min)
        v.reduce_max(c[:, 1:2], S["imp_tm"][:, :], axis=AXX)
        for col in range(2):
            op = ALU.min if col == 0 else ALU.max
            v.tensor_copy(c[0:64, 2:3], c[64:128, col:col + 1])
            v.tensor_tensor(c[0:64, col:col + 1], c[0:64, col:col + 1], c[0:64, 2:3], op=op)
            v.tensor_copy(c[0:32, 2:3], c[32:64, col:col + 1])
            v.tensor_tensor(c[0:32, col:col + 1], c[0:32, col:col + 1], c[0:32, 2:3], op=op)
        v.sig("shr")

    @block.tensor
    def _(tr):
        t = Eng(pb, "t", tr, chained=False)
        t.wait("shr", pb.cnt["shr"])
        for col in range(2):
            inst = tr.matmul(P["pq6"][0:1, col * 32:col * 32 + 32],
                             S["colsc"][0:32, col:col + 1], S["sb_identf"][0:32, 0:32],
                             start=(col == 0), stop=(col == 1), is_transpose=True,
                             skip_group_check=True)
        inst.then_inc(pb.sems["mm"], 1); pb.tick("mm")

    @block.vector
    def _(vr):
        v = Eng(pb, "v", vr)
        v.wait("mm", pb.cnt["mm"])
        br = S["brow5"]
        v.memset(br[:, :], 0.0)
        v.tensor_reduce(br[0:1, 0:1], P["pq6"][0:1, 0:32], axis=AXX, op=ALU.min)
        v.reduce_max(br[0:1, 1:2], P["pq6"][0:1, 32:64], axis=AXX)
        v.tensor_scalar_add(br[0:1, 1:2], br[0:1, 1:2], 1.0)
        v.tensor_tensor(br[0:1, 2:3], br[0:1, 0:1], br[0:1, 1:2], op=ALU.add)
        v.tensor_scalar_mul(br[0:1, 2:3], br[0:1, 2:3], 0.5)
        v.sig("pvm")

    for it in range(36):
        final = it == 35

        @block.tensor
        def _(tr, final=final):
            t = Eng(pb, "t", tr, chained=False)
            t.wait("pvm", pb.cnt["pvm"])
            src = S["brow5"][:, 0:1] if final else S["brow5"][:, 2:3]
            inst = tr.matmul(P["pq7"][:, 0:1], S["sb_bc0f"][:, :], src,
                             start=True, stop=True, skip_group_check=True)
            inst.then_inc(pb.sems["mm"], 1); pb.tick("mm")

        @block.vector
        def _(vr, final=final):
            v = Eng(pb, "v", vr)
            v.wait("mm", pb.cnt["mm"])
            v.tensor_copy(S["tbc5"][:, :], P["pq7"][:, 0:1])
            if final:
                v.tensor_scalar(S["kmb5"][:, :], S["imp_tm"][:, :], S["tbc5"][:, 0:1],
                                None, op0=ALU.is_ge)
                v.tensor_scalar(S["kmb5"][:, :], S["kmb5"][:, :], -NEG_BIG, NEG_BIG,
                                op0=ALU.mult, op1=ALU.add)
                v.sig("pvm")
            else:
                v.tensor_scalar(S["cmpi"][:, :], S["imp_tm"][:, :], S["tbc5"][:, 0:1],
                                None, op0=ALU.is_ge)
                v.reduce_sum(S["cnti"][:, :], S["cmpi"][:, :], axis=AXX)
                v.sig("pvm")

        if not final:
            @block.tensor
            def _(tr):
                t = Eng(pb, "t", tr, chained=False)
                t.wait("pvm", pb.cnt["pvm"])
                inst = tr.matmul(P["pq7"][0:1, 32:33], S["sb_onesf"][:, 0:1], S["cnti"][:, :],
                                 start=True, stop=True, skip_group_check=True)
                inst.then_inc(pb.sems["mm"], 1); pb.tick("mm")

            @block.vector
            def _(vr):
                v = Eng(pb, "v", vr)
                v.wait("mm", pb.cnt["mm"])
                br = S["brow5"]
                v.tensor_copy(br[0:1, 3:4], P["pq7"][0:1, 32:33])
                v.tensor_scalar(br[0:1, 4:5], br[0:1, 3:4], float(L1_K), None, op0=ALU.is_ge)
                v.tensor_tensor(br[0:1, 5:6], br[0:1, 2:3], br[0:1, 0:1], op=ALU.subtract)
                v.tensor_tensor(br[0:1, 6:7], br[0:1, 5:6], br[0:1, 4:5], op=ALU.mult)
                v.tensor_tensor(br[0:1, 0:1], br[0:1, 0:1], br[0:1, 6:7], op=ALU.add)
                v.tensor_scalar(br[0:1, 7:8], br[0:1, 4:5], -1.0, 1.0, op0=ALU.mult, op1=ALU.add)
                v.tensor_tensor(br[0:1, 5:6], br[0:1, 2:3], br[0:1, 1:2], op=ALU.subtract)
                v.tensor_tensor(br[0:1, 6:7], br[0:1, 5:6], br[0:1, 7:8], op=ALU.mult)
                v.tensor_tensor(br[0:1, 1:2], br[0:1, 1:2], br[0:1, 6:7], op=ALU.add)
                v.tensor_tensor(br[0:1, 2:3], br[0:1, 0:1], br[0:1, 1:2], op=ALU.add)
                v.tensor_scalar_mul(br[0:1, 2:3], br[0:1, 2:3], 0.5)
                v.sig("pvm")

    mmk = [0]
    xs_k = pb.stream("xt4", 2)

    @block.tensor
    def _(tr):
        t = Eng(pb, "t", tr, chained=False)
        mmk[0] = pb.cnt["mm"]
        pk0 = pb.cnt["p1k"]
        fpa0b = pb.cnt["fpa0"]
        x0 = 0
        for ch in range(8):
            xs_k.wait("t", tr, x0 + ch * 8 + 6)
            xs_k.wait("t", tr, x0 + ch * 8 + 7)
            for mt in range(8):
                i = ch * 8 + mt
                if i >= 2:
                    t.wait("p1k", pk0 + 2 * (i - 1))
                bank = P[f"pq{i % 2}"]
                for kt in range(8):
                    inst = tr.matmul(bank[:, :], S["wkv5"][:, 0, kt, mt * 128:(mt + 1) * 128],
                                     S["xc5"][:, ch % 2, kt, :],
                                     start=(kt == 0), stop=(kt == 7), skip_group_check=True)
                inst.then_inc(pb.sems["mm"], 1); pb.tick("mm")
                t.wait("p1k", pk0 + 2 * i + 2)
                if i >= 1:
                    t.wait("fpa0", fpa0b + i)
                inst = tr.matmul(P["pq2"][:, :], S["s_t2f"][:, :], S["qsq5"][:, :],
                                 start=True, stop=True, skip_group_check=True)
                inst.then_inc(pb.sems["p1f"], 1); pb.tick("p1f")

    xchunk_loader(xs_k, "mm", mmk, 8)

    @block.scalar
    def _(sr):
        s = Eng(pb, "s", sr)
        pf0 = pb.cnt["p1f"] - 64
        for ch in range(8):
            for mt in range(8):
                i = ch * 8 + mt
                s.wait("mm", mmk[0] + i + 1)
                bank = P[f"pq{i % 2}"]
                s.activation(S["kp5"][:, mt, ch * 512:ch * 512 + 512], bank[:, :],
                             AF.Identity, bias=S["b5c"][:, mt, 1:2])
                s.sig("p1k")
                if i >= 1:
                    s.wait("p1f", pf0 + i)
                s.activation(S["qsq5"][:, :], bank[:, :], AF.Square,
                             bias=S["b5c"][:, mt, 1:2])
                s.sig("p1k")

    @block.vector
    def _(vr):
        v = Eng(pb, "v", vr)
        pf0 = pb.cnt["p1f"] - 64
        for ch in range(8):
            for mt in range(8):
                i = ch * 8 + mt
                v.wait("p1f", pf0 + i + 1)
                v.reduce_max(S["colsc"][:, 3:4], P["pq2"][:, :], axis=AXX)
                v.tensor_tensor(S["kn2"][:, mt:mt + 1], S["kn2"][:, mt:mt + 1],
                                S["colsc"][:, 3:4], op=ALU.max)
                v.sig("fpa0")

    mmv = [0]
    xs_v = pb.stream("xt4", 2)

    @block.tensor
    def _(tr):
        t = Eng(pb, "t", tr, chained=False)
        t.wait("dd", pb.cnt["dd"])
        mmv[0] = pb.cnt["mm"]
        fo0 = pb.cnt["fo0"]
        x0 = 0
        for ch in range(8):
            xs_v.wait("t", tr, x0 + ch * 8 + 6)
            xs_v.wait("t", tr, x0 + ch * 8 + 7)
            for twc in range(4):
                tt = ch * 4 + twc
                for nh in range(2):
                    i = tt * 2 + nh
                    if i >= 2:
                        t.wait("fo0", fo0 + i - 1)
                    bank = P[f"pq{3 + i % 2}"]
                    for kt in range(8):
                        inst = tr.matmul(bank[:, :], S["xc5"][:, ch % 2, kt, twc * 128:(twc + 1) * 128],
                                         S["wkv5"][:, 1, kt, nh * 512:nh * 512 + 512],
                                         start=(kt == 0), stop=(kt == 7), skip_group_check=True)
                    inst.then_inc(pb.sems["mm"], 1); pb.tick("mm")

    xchunk_loader(xs_v, "mm", mmv, 8)

    @block.vector
    def _(vr):
        v = Eng(pb, "v", vr)
        for tt in range(NT):
            for nh in range(2):
                i = tt * 2 + nh
                v.wait("mm", mmv[0] + i + 1)
                bank = P[f"pq{3 + i % 2}"]
                dst = S["vt5"][:, tt, :].rearrange("p (h m) -> p h m", m=65)[:, nh * 8:nh * 8 + 8, 0:64]
                v.tensor_tensor(dst, bank[:, :].rearrange("p (h m) -> p h m", m=64),
                                S["bvbc"][:, nh * 512:nh * 512 + 512].rearrange("p (h m) -> p h m", m=64),
                                op=ALU.add)
                v.sig("fo0")

    pb.barrier("p5x", drain=("wld", "xt40", "xt41"))
    pb.close_phase("p5x")


def p5_qc(pb, qc):
    nc, block = pb.nc, pb.block
    S, T, P = pb.S, pb.T, pb.P
    xq5 = pb.stream("xs", 2)
    wq5s = pb.stream("w14", 2)

    @block.gpsimd
    def _(gr):
        g = Eng(pb, "g", gr)
        p1kb = pb.cnt["p1k"]
        for mt in range(8):
            if mt >= 2:
                pb.wait("g", gr, "p1k", p1kb + 2 * (mt - 1))
            else:
                pb.wait("g", gr, "p1k", p1kb)
            wq5s.issue(gr, S["wqst"][:, mt % 2, :, :],
                       T["wq1"][:, mt * 128:(mt + 1) * 128].rearrange("(kt p) c -> p kt c", p=128))

    @block.sync
    def _(y):
        pb.wait("y", y, "p1e", pb.cnt["p1e"])
        for ft in range(8):
            if ft >= 2:
                xq5.wait("y", y, xq5.nissued - 2)
            xq5.issue(y, S["scrA"][:, ft, :],
                      T["xn1_own"][qc * 512:(qc + 1) * 512, ft * 128:(ft + 1) * 128],
                      transpose=True)
    xq50 = xq5.nissued - 8

    mmq = [0]

    @block.tensor
    def _(tr):
        t = Eng(pb, "t", tr, chained=False)
        xq5.wait("t", tr, xq50 + 6)
        xq5.wait("t", tr, xq50 + 7)
        t.wait("fs0", pb.cnt["fs0"])
        mmq[0] = pb.cnt["mm"]
        pk0 = pb.cnt["p1k"]
        fpa0q = pb.cnt["fpa0"]
        for mt in range(8):
            if mt >= 2:
                t.wait("p1k", pk0 + 2 * (mt - 1))
            bank = P[f"pq{6 + mt % 2}"]
            wq5s.wait("t", tr, wq5s.nissued - 8 + mt)
            for kt in range(8):
                inst = tr.matmul(bank[:, :], S["wqst"][:, mt % 2, kt, :],
                                 S["scrA"][:, kt, :], start=(kt == 0), stop=(kt == 7),
                                 skip_group_check=True)
            inst.then_inc(pb.sems["mm"], 1); pb.tick("mm")
            t.wait("p1k", pk0 + 2 * mt + 2)
            inst = tr.matmul(P["pq2"][:, :], S["s_t2f"][:, :], S["qsq5"][:, :],
                             start=True, stop=True, skip_group_check=True)
            inst.then_inc(pb.sems["p1f"], 1); pb.tick("p1f")
            if mt < 7:
                t.wait("fpa0", fpa0q + mt + 1)

    @block.scalar
    def _(sr):
        s = Eng(pb, "s", sr)
        pf0q = pb.cnt["p1f"] - 8
        for mt in range(8):
            s.wait("mm", mmq[0] + mt + 1)
            bank = P[f"pq{6 + mt % 2}"]
            s.activation(S["qc5"][:, mt, :], bank[:, :], AF.Identity,
                         bias=S["b5c"][:, mt, 0:1])
            s.sig("p1k")
            if mt >= 1:
                s.wait("p1f", pf0q + mt)
            s.activation(S["qsq5"][:, :], bank[:, :], AF.Square, bias=S["b5c"][:, mt, 0:1])
            s.sig("p1k")

    @block.vector
    def _(vr):
        v = Eng(pb, "v", vr)
        pf0 = pb.cnt["p1f"] - 8
        v.memset(S["qn2"][:, :], -1e30)
        for mt in range(8):
            v.wait("p1f", pf0 + mt + 1)
            v.reduce_max(S["colsc"][:, 3:4], P["pq2"][:, :], axis=AXX)
            v.tensor_tensor(S["qn2"][:, mt:mt + 1], S["qn2"][:, mt:mt + 1],
                            S["colsc"][:, 3:4], op=ALU.max)
            v.sig("fpa0")
        v.tensor_tensor(S["cm5"][:, 0:8], S["qn2"][:, :], S["kn2"][:, :], op=ALU.mult)
        v.sig("sn")

    @block.scalar
    def _(sr):
        s = Eng(pb, "s", sr)
        s.wait("sn", pb.cnt["sn"])
        s.activation(S["cm5"][:, 8:16], S["cm5"][:, 0:8], AF.Sqrt, scale=0.015625)
        s.sig("sn")

    @block.tensor
    def _(tr):
        t = Eng(pb, "t", tr, chained=False)
        t.wait("sn", pb.cnt["sn"])
        for r in range(2):
            inst = tr.matmul(P["pq2"][:, r * 8:r * 8 + 8], S["sb_onesf"][64 * r:64 * r + 1, :],
                             S["cm5"][64 * r:64 * r + 1, 8:16], start=(r == 0), stop=(r == 1),
                             skip_group_check=True)
        inst.then_inc(pb.sems["mm"], 1); pb.tick("mm")

    @block.vector
    def _(vr):
        v = Eng(pb, "v", vr)
        v.wait("mm", pb.cnt["mm"])
        v.tensor_copy(S["cbc5"][:, :], P["pq2"][:, 0:16])
        for h in range(HEADS):
            v.tensor_scalar(S["kmbc"][:, :, h], S["kmb5"][:, :],
                            S["cbc5"][:, 8 * (h % 2) + h // 2: 8 * (h % 2) + h // 2 + 1],
                            None, op0=ALU.subtract)
        v.sig("shr")

    st5 = {"mm0": 0, "pk0": 0, "fv0": 0, "qk0": 0, "fb0": 0, "vk0": 0, "p1e0": 0}

    @block.tensor
    def _(tr):
        t = Eng(pb, "t", tr, chained=False)
        t.wait("shr", pb.cnt["shr"])
        st5["mm0"] = pb.cnt["mm"]
        st5["pk0"] = pb.cnt["p1k"]
        st5["fv0"] = pb.cnt["fv1"]
        st5["qk0"] = pb.cnt["qkr"]
        st5["fb0"] = pb.cnt["fb1"]
        st5["vk0"] = pb.cnt["vkr"]
        st5["p1e0"] = pb.cnt["p1e"]
        st5["sq0"] = pb.cnt["sq"]
        for h in range(HEADS):
            r = 64 * (h % 2)
            f = h // 2
            if h >= 2:
                t.wait("p1e", st5["p1e0"] + h - 1)
            nb = P[f"pq{4 + h % 2}"]
            for g in range(9):
                if g < 8:
                    for kt4 in range(4):
                        kt = g * 4 + kt4
                        if kt >= 4:
                            t.wait("p1k", st5["pk0"] + h * NT + kt - 3)
                        sb = P[f"pq{kt % 4}"]
                        inst = tr.matmul(sb[:, :], S["kp5"][r:r + 64, f, kt * 128:(kt + 1) * 128],
                                         S["qc5"][r:r + 64, f, :], start=True, stop=True,
                                         skip_group_check=True)
                        inst.then_inc(pb.sems["mm"], 1); pb.tick("mm")
                if g >= 1:
                    for kt4 in range(4):
                        k2 = (g - 1) * 4 + kt4
                        t.wait("p1k", st5["pk0"] + h * NT + k2 + 1)
                        inst = tr.matmul(nb[0:65, :], S["vt5"][:, k2, 65 * h:65 * h + 65],
                                         S["scrA"][:, k2 % 8, :], start=(k2 == 0), stop=(k2 == NT - 1),
                                         skip_group_check=True)
                        inst.then_inc(pb.sems["fv1"], 1); pb.tick("fv1")
            t.wait("qkr", st5["qk0"] + h + 1)
            inst = tr.matmul(P["pq6"][:, :], S["sb_bc64f"][:, :], S["dsb5"][:, 512:1024],
                             start=True, stop=True, skip_group_check=True)
            inst.then_inc(pb.sems["fb1"], 1); pb.tick("fb1")

    @block.scalar
    def _(sr):
        s = Eng(pb, "s", sr)
        for h in range(HEADS):
            for kt in range(NT):
                s.wait("mm", st5["mm0"] + h * NT + kt + 1)
                if h * NT + kt >= 8:
                    s.wait("fv1", st5["fv0"] + h * NT + kt - 7)
                sb = P[f"pq{kt % 4}"]
                s.activation(S["scrA"][:, kt % 8, :], sb[:, :], AF.Exp,
                             bias=S["kmbc"][:, kt, h:h + 1], scale=0.125)
                s.sig("p1k")
            s.wait("sq", st5["sq0"] + h + 1)
            s.activation(S["dsb5"][64:65, 512:1024], S["dsb5"][64:65, 0:512], AF.Ln,
                         bias=S["sb_gates"][64:65, 2:3])
            s.activation(S["dsb5"][64:65, 512:1024], S["dsb5"][64:65, 512:1024], AF.Exp, scale=-1.0)
            s.sig("qkr")


    @block.vector
    def _(vr):
        v = Eng(pb, "v", vr)
        for h in range(HEADS):
            nb = P[f"pq{4 + h % 2}"]
            v.wait("fv1", st5["fv0"] + NT * (h + 1))
            if h >= 1:
                v.wait("fb1", st5["fb0"] + h)
            v.tensor_copy(S["dsb5"][64:65, 0:512], nb[64:65, :])
            v.sig("sq")
            v.wait("fb1", st5["fb0"] + h + 1)
            v.tensor_copy(S["rbc5"][:, :], P["pq6"][:, :])
            v.sig("vkr")
            r = 64 * (h % 2)
            f = h // 2
            v.tensor_tensor(S["conc5"][r:r + 64, f, :], nb[0:64, :], S["rbc5"][r:r + 64, :],
                            op=ALU.mult)
            v.sig("p1e")

    mmw = [0]

    @block.sync
    def _(y):
        fs0s = pb.cnt["fs0"]
        for tt in range(4):
            pb.wait("y", y, "fs0", fs0s + tt)
            tile = qc * 4 + tt
            y.dma_start(out=S["x1c5"][:, :], in_=T["x1_tm"][tile * 128:(tile + 1) * 128, :]
                        ).then_inc(pb.sem("xr"), 16)
            pb.tick("xr", 16)
            y.dma_start(out=S["f1b5"][:, :], in_=T["F1T"][:, tile * 128:(tile + 1) * 128],
                        transpose=True).then_inc(pb.sem("xr"), 16)
            pb.tick("xr", 16)

    xr0 = pb.cnt["xr"] - 128

    @block.tensor
    def _(tr):
        t = Eng(pb, "t", tr, chained=False)
        t.wait("p1e", pb.cnt["p1e"])
        t.wait("wld", pb.cnt["wld"])
        mmw[0] = pb.cnt["mm"]
        fs00 = pb.cnt["fs0"]
        for tt in range(4):
            if tt >= 1:
                t.wait("fs0", fs00 + tt)
            for nh in range(2):
                bank = P[f"pq{nh}"]
                for kt in range(8):
                    inst = tr.matmul(bank[:, :], S["conc5"][:, kt, tt * 128:(tt + 1) * 128],
                                     S["wo5"][:, kt, nh * 512:nh * 512 + 512],
                                     start=(kt == 0), stop=(kt == 7), skip_group_check=True)
                inst.then_inc(pb.sems["mm"], 1); pb.tick("mm")

    @block.scalar
    def _(sr):
        s = Eng(pb, "s", sr)
        fs0b = pb.cnt["fs0"]
        for tt in range(4):
            s.wait("xr", xr0 + 32 * tt + 32)
            if tt >= 1:
                s.wait("fs0", fs0b + tt)
            s.activation(S["f1f5"][:, :], S["f1b5"][:, :], AF.Identity)
            s.sig("fs1")
            s.wait("fs0", fs0b + tt + 1)
            s.dma(T["out"][(qc * 4 + tt) * 128:(qc * 4 + tt + 1) * 128, :],
                  S["outst"][:, :], "so")

    @block.vector
    def _(vr):
        v = Eng(pb, "v", vr)
        fs10 = pb.cnt["fs1"] - 4
        so0 = pb.cnt["so"] - 64
        for tt in range(4):
            v.wait("mm", mmw[0] + 2 * tt + 2)
            v.wait("fs1", fs10 + tt + 1)
            v.wait("so", so0 + 16 * tt)
            for nh in range(2):
                bank = P[f"pq{nh}"]
                sl = slice(nh * 512, nh * 512 + 512)
                v.tensor_tensor(S["outst"][:, sl], bank[:, :], S["x1c5"][:, sl], op=ALU.add)
                v.tensor_tensor(S["outst"][:, sl], S["outst"][:, sl], S["borow"][:, sl], op=ALU.add)
                v.tensor_tensor(S["outst"][:, sl], S["outst"][:, sl], S["f1f5"][:, sl], op=ALU.add)
            v.sig("fs0")


def p5(pb):
    nc, block = pb.nc, pb.block
    S, T, P = pb.S, pb.T, pb.P
    pb.open_phase("p5")
    pb.sbuf("kp5", (128, 8, N), BF16, ph="p5")
    pb.sbuf("vt5", (128, NT, 1040), BF16, ph="p5")
    pb.sbuf("b5c", (128, 8, 8), F32, ph="p5")
    pb.sbuf("s_t2f", (128, 128), F32, ph="p5")
    pb.sbuf("kmb5", (128, 32), F32, ph="p5")
    pb.sbuf("imp_tm", (128, 32), F32, ph="p5")
    pb.sbuf("qn2", (128, 8), F32, ph="p5")
    pb.sbuf("kn2", (128, 8), F32, ph="p5")
    pb.sbuf("cm5", (128, 16), F32, ph="p5")
    pb.sbuf("cbc5", (128, 16), F32, ph="p5")
    pb.sbuf("colsc", (128, 4), F32, ph="p5")
    pb.sbuf("brow5", (128, 16), F32, ph="p5")
    pb.sbuf("tbc5", (128, 1), F32, ph="p5")
    pb.sbuf("cmpi", (128, 32), BF16, ph="p5")
    pb.sbuf("cnti", (128, 1), F32, ph="p5")
    pb.sbuf("qsq5", (128, 512), F32, ph="p5")
    for i in range(8):
        pb.psum(f"pq{i}", (128, 512), F32, ph="p5")
    for s in ("so", "fv1", "fb1", "fs1"):
        pb.sem(s)

    p5_prep(pb)

    pb.open_phase("p5m")
    pb.sbuf("wqst", (128, 2, 8, 128), BF16, ph="p5m")
    pb.sbuf("wo5", (128, 8, D), BF16, ph="p5m")
    pb.sbuf("kmbc", (128, 32, HEADS), F32, ph="p5m")
    pb.sbuf("scrA", (128, 8, 512), BF16, ph="p5m")
    pb.sbuf("qc5", (128, 8, 512), BF16, ph="p5m")
    pb.sbuf("conc5", (128, 8, 512), BF16, ph="p5m")
    pb.sbuf("dsb5", (128, 1024), F32, ph="p5m")
    pb.sbuf("rbc5", (128, 512), F32, ph="p5m")
    pb.sbuf("borow", (128, D), F32, ph="p5m")
    pb.sbuf("x1c5", (128, D), F32, ph="p5m")
    pb.sbuf("f1b5", (128, D), BF16, ph="p5m")
    pb.sbuf("f1f5", (128, D), F32, ph="p5m")
    pb.sbuf("outst", (128, D), F32, ph="p5m")

    @block.gpsimd
    def _(gr):
        g = Eng(pb, "g", gr)
        g.dma(S["wo5"][:, :, :], T["wo1"][:, :].rearrange("(kt p) m -> p kt m", p=128), "wld")

    @block.gpsimd
    def _(gr):
        g = Eng(pb, "g", gr)
        g.wait("dd", pb.cnt["dd"] + 1)
        g.dma(S["dsb5"][0:1, :], T["b1vec"][3:4, :], "xld")

    @block.tensor
    def _(tr):
        t = Eng(pb, "t", tr, chained=False)
        t.wait("xld", pb.cnt["xld"])
        dd0b = pb.cnt["dd"]
        for nh in range(2):
            if nh:
                t.wait("dd", dd0b + 2)
            inst = tr.matmul(P["pq3"][:, :], S["sb_onesf"][0:1, :],
                             S["dsb5"][0:1, nh * 512:nh * 512 + 512],
                             start=True, stop=True, skip_group_check=True)
            inst.then_inc(pb.sems["mm"], 1); pb.tick("mm")

    @block.vector
    def _(vr):
        v = Eng(pb, "v", vr)
        v.memset(S["dsb5"][:, :], 0.0)
        v.sig("dd")
        mm0 = pb.cnt["mm"] - 2
        for nh in range(2):
            v.wait("mm", mm0 + nh + 1)
            v.tensor_copy(S["borow"][:, nh * 512:nh * 512 + 512], P["pq3"][:, :])
            v.sig("dd")

    for qc in range(4):
        p5_qc(pb, qc)

    @block.sync
    def _(y):
        y.wait_ge(pb.sems["so"], pb.cnt["so"])

    pb.close_phase("p5m")
    pb.close_phase("p5")


def finish_placeholder(pb):
    block = pb.block
    S, T = pb.S, pb.T
    pb.open_phase("fin")
    pb.sbuf("zout", (128, D), F32, ph="fin")
    pb.sem("so")

    @block.vector
    def _(vr):
        v = Eng(pb, "v", vr)
        v.memset(S["zout"][:, :], 0.0)
        v.sig("ve")

    @block.sync
    def _(y):
        pb.wait("y", y, "ve", pb.cnt["ve"])
        pb.wait("y", y, "gd", pb.cnt["gd"])
        for i in range(NTO):
            y.dma_start(out=T["out"][i * 128:(i + 1) * 128, :],
                        in_=S["zout"][:, :]).then_inc(pb.sem("so"), 16); pb.tick("so", 16)
        y.wait_ge(pb.sem("so"), pb.cnt["so"])
    pb.close_phase("fin")


def build_program(dbg=()):
    pb = PB()
    pb.dbg = set(dbg)
    nc = pb.nc

    pb.din("x", (N, D))
    pb.din("Wh", (D, D)); pb.din("Wq", (D, D)); pb.din("Wk", (D, D)); pb.din("Wv", (D, D))
    pb.din("Wo0", (D, D))
    pb.din("bvec", (8, D))
    pb.din("Wqkv", (FFN_DEPTH, D, 3 * D))
    pb.din("Wo_f", (FFN_DEPTH, D, D))
    pb.din("W1", (FFN_DEPTH, D, W1C))
    pb.din("W2", (FFN_DEPTH, DIM_INNER, D))
    pb.din("wq1", (D, D)); pb.din("wk1", (D, D)); pb.din("wv1", (D, D)); pb.din("wo1", (D, D))
    pb.din("b1vec", (8, D)); pb.din("b1vecT", (D, 8))
    pb.din("wi1", (D, 512)); pb.din("wi2", (1, 512))
    pb.din("gate_w", (D, N_EXP))
    pb.din("moe_w1", (4, D, 4 * D))
    pb.din("moe_w2", (4, 4 * D, D))
    pb.din("identb", (128, 128), BF16)
    pb.din("rot", (128, 4 * DOMF), BF16)
    pb.din("maskP", (128, 4, WIN))
    pb.din("maskF", (128, 4, WIN))
    pb.din("two2f", (2, 128))
    pb.din("two2fT", (128, 2), BF16)
    pb.din("qsksc", (FFN_DEPTH, 128, 2))
    pb.din("lnpc", (FFN_DEPTH, 128, 16))
    pb.din("b1c", (FFN_DEPTH, 128, 44))
    pb.din("b2c", (FFN_DEPTH, 128, 8))
    pb.din("onesb", (128, 128), BF16)
    pb.din("triu128", (128, 128))
    pb.din("iotasel", (128, MOE_K))
    pb.din("iotacol", (128, 1))
    pb.din("two2f128", (128, 128))
    pb.din("offs", (1, 8), I32)
    pb.din("gates", (128, 8))

    pb.dout("out", (HALF, D))

    pb.dscr("xinnerT", (D, DOMF))
    pb.dscr("A_T", (D, HALF))
    pb.dscr("x1_tm", (HALF, D))
    pb.dscr("F1T", (D, HALF), BF16)
    pb.dscr("xn1_own", (HALF, D), BF16)
    pb.dscr("xn1_all", (N, D), BF16)
    pb.dscr("slab_own", (SLAB, D), BF16)
    pb.dscr("slab_all", (2 * SLAB, D), BF16)
    pb.dscr("h2w_own", (4 * MOE_K, D), BF16)
    pb.dscr("h2w_all", (8 * MOE_K, D), BF16)

    pb.din("bmapP", (128, D), BF16)
    pb.din("bc0f", (128, 128))
    pb.din("bc64f", (128, 128))
    pb.din("bvecT", (D, 8))

    pb.sbuf("sb_identf", (128, 128), F32)
    pb.sbuf("sb_gates", (128, 8), F32)
    pb.sbuf("sb_onesf", (128, 128), F32)
    pb.sbuf("sb_bc0f", (128, 128), F32)
    pb.sbuf("sb_bc64f", (128, 128), F32)
    pb.open_phase("g2")
    pb.sbuf("sb_onesb", (128, 128), BF16, ph="g2")
    pb.sbuf("sb_bmapPf", (128, D), F32, ph="g2")

    pb.block = pb.ctx.enter_context(nc.Block())
    import contextlib as _cl
    scope = (lambda n: nc.named_scope(n)) if SCOPES else (lambda n: _cl.nullcontext())

    pb.open_phase("x01")
    pb.sbuf("xnT", (128, 8 * HALF), BF16, ph="x01")
    pb.open_phase("p1p")
    pb.sbuf("sb_bkbv", (1, 2 * D), BF16, ph="p1p")
    pb.sbuf("kv_sb", (128, 520), BF16, ph="p1p")
    pb.sbuf("kv_sb2", (128, 520), BF16, ph="p1p")
    pb.open_phase("xoth")
    pb.sbuf("xnTo", (128, 8 * HALF), BF16, ph="xoth")
    pb.open_phase("p0")
    with scope("p0"):
        p0(pb)
    pb.close_phase("p0")
    if STOP_AFTER != "P0":
        with scope("p1"):
            p1(pb)
    else:
        pb.close_phase("xoth")
    pb.close_phase("p1p")
    pb.close_phase("x01")
    if STOP_AFTER not in ("P0", "P1"):
        with scope("p2"):
            p2(pb)
    pb.close_phase("g2")
    if STOP_AFTER not in ("P0", "P1", "P2"):
        with scope("p3"):
            p3(pb)
    if STOP_AFTER not in ("P0", "P1", "P2", "P3"):
        with scope("p4"):
            p4(pb)
    if STOP_AFTER is None:
        with scope("p5"):
            p5(pb)
    else:
        finish_placeholder(pb)

    pb.ctx.close()
    return pb


def _prep_inputs(inputs):
    perm = _sigma_perm()
    A_q, B_q, A_k, B_k = _rotary_tables()
    identb = np.eye(128, dtype=np.float32).astype(ml_dtypes.bfloat16)
    onesb = np.ones((128, 128), dtype=np.float32).astype(ml_dtypes.bfloat16)
    bmapP = np.zeros((128, D), dtype=np.float32)
    for r in range(HEADS):
        h = 2 * (r % 8) + r // 8
        bmapP[r, h * DH:(h + 1) * DH] = 1.0
    bmapP = bmapP.astype(ml_dtypes.bfloat16)
    bc0f = np.zeros((128, 128), np.float32); bc0f[0, :] = 1.0
    bc64f = np.zeros((128, 128), np.float32); bc64f[64, :] = 1.0
    triu128 = np.triu(np.ones((128, 128), dtype=np.float32))
    iotasel = np.broadcast_to(np.arange(1, MOE_K + 1, dtype=np.float32)[None, :],
                              (128, MOE_K)).copy()
    iotacol = np.arange(1, 129, dtype=np.float32)[:, None].copy()
    two2f128 = np.zeros((128, 128), np.float32)
    two2f128[0:64, 0] = 1.0
    two2f128[64:128, 64] = 1.0

    x_all = np.asarray(inputs["L0_attn_x"], dtype=np.float32)
    bvec = np.stack([
        np.asarray(inputs["L0_attn_bh"])[perm],
        np.asarray(inputs["L0_attn_bq"])[perm],
        np.asarray(inputs["L0_attn_bk"])[perm],
        np.asarray(inputs["L0_attn_bv"])[perm],
        np.asarray(inputs["L0_attn_beta"])[perm],
        np.asarray(inputs["L0_attn_ln_w"])[perm],
        np.asarray(inputs["L0_attn_ln_b"])[perm],
        np.asarray(inputs["L0_attn_bo"]),
    ]).astype(np.float32)
    b1vec = np.zeros((8, D), dtype=np.float32)
    b1vec[0, :] = np.asarray(inputs["L1_attn_bq"])
    b1vec[1, :] = np.asarray(inputs["L1_attn_bk"])
    b1vec[2, :] = np.asarray(inputs["L1_attn_bv"])
    b1vec[3, :] = np.asarray(inputs["L1_attn_bo"])
    b1vec[4, 0:512] = np.asarray(inputs["L1_attn_bi1"])
    b1vec[5, 0:1] = np.asarray(inputs["L1_attn_bi2"])
    qs_ = np.asarray(inputs["L0_ffn_q_scale"]).astype(np.float32)
    ks_ = np.asarray(inputs["L0_ffn_k_scale"]).astype(np.float32)
    qsksc = np.zeros((FFN_DEPTH, 128, 2), np.float32)
    for l in range(FFN_DEPTH):
        qsksc[l, 0:64, 0] = qs_[l]; qsksc[l, 64:128, 0] = qs_[l]
        qsksc[l, 0:64, 1] = ks_[l]; qsksc[l, 64:128, 1] = ks_[l]
    lng = np.asarray(inputs["L0_ffn_ln_g"]).astype(np.float32)
    lnb_ = np.asarray(inputs["L0_ffn_ln_b"]).astype(np.float32)
    lnpc = np.zeros((FFN_DEPTH, 128, 16), np.float32)
    for l in range(FFN_DEPTH):
        for mt in range(8):
            lnpc[l, :, 2 * mt] = lng[l, mt * 128:(mt + 1) * 128]
            lnpc[l, :, 2 * mt + 1] = lnb_[l, mt * 128:(mt + 1) * 128]
    b1_ = np.asarray(inputs["L0_ffn_b1"]).astype(np.float32)
    b1c = np.zeros((FFN_DEPTH, 128, 44), np.float32)
    for l in range(FFN_DEPTH):
        for part, base in ((0, 0), (22, 2730)):
            for t_ in range(22):
                nrow = 128 if t_ < 21 else 42
                b1c[l, 0:nrow, part + t_] = b1_[l, base + t_ * 128: base + t_ * 128 + nrow]
    b2_ = np.asarray(inputs["L0_ffn_b2"]).astype(np.float32)
    b2c = np.zeros((FFN_DEPTH, 128, 8), np.float32)
    for l in range(FFN_DEPTH):
        for mt in range(8):
            b2c[l, :, mt] = b2_[l, mt * 128:(mt + 1) * 128]
    two2f = np.zeros((2, 128), np.float32)
    two2f[0, 0:64] = 1.0; two2f[1, 64:128] = 1.0
    two2fT = two2f.T.copy().astype(ml_dtypes.bfloat16)
    kk_ = np.arange(2 * WIN)[:, None]
    qq_ = np.arange(WIN)[None, :]
    band = ((kk_ >= qq_) & (kk_ <= qq_ + WIN))
    maskP_add = np.where(band, 0.0, NEG_BIG).astype(np.float32)
    maskF_add = maskP_add.copy()
    maskF_add[0:WIN, :] = NEG_BIG
    maskP_P = maskP_add.reshape(4, 128, WIN).transpose(1, 0, 2).copy()
    maskF_P = maskF_add.reshape(4, 128, WIN).transpose(1, 0, 2).copy()
    rots = [A_q, B_q, A_k, B_k]

    in_maps = []
    for core in range(8):
        b, half = core // 2, core % 2
        own_lo = half * HALF
        x_rel = np.concatenate([x_all[b, own_lo:own_lo + HALF],
                                x_all[b, HALF - own_lo:HALF - own_lo + HALF]], axis=0)
        lo = own_lo - SLAB
        idx = np.clip(np.arange(lo, own_lo + HALF), 0, N - 1)
        rot = np.concatenate([np.vstack([t[:, idx], t[:, idx]]) for t in rots],
                             axis=1).astype(ml_dtypes.bfloat16)
        offs = np.zeros((1, 8), dtype=np.int32)
        offs[0, 0] = HALF - own_lo
        gates = np.zeros((128, 8), dtype=np.float32)
        gates[:, 0] = float(half)
        gates[:, 1] = 1.0 - float(half)
        gates[:, 2] = 1e-24
        m = {
            "x": np.ascontiguousarray(x_rel),
            "Wh": np.ascontiguousarray(np.asarray(inputs["L0_attn_Wh"])[:, perm]),
            "Wq": np.ascontiguousarray(np.asarray(inputs["L0_attn_Wq"])[:, perm]),
            "Wk": np.ascontiguousarray(np.asarray(inputs["L0_attn_Wk"])[:, perm]),
            "Wv": np.ascontiguousarray(np.asarray(inputs["L0_attn_Wv"])[:, perm]),
            "Wo0": np.ascontiguousarray(np.asarray(inputs["L0_attn_Wo"])[perm, :]),
            "bvec": bvec,
            "Wqkv": np.asarray(inputs["L0_ffn_Wqkv"]),
            "Wo_f": np.asarray(inputs["L0_ffn_Wo"]),
            "W1": np.asarray(inputs["L0_ffn_W1"]),
            "W2": np.asarray(inputs["L0_ffn_W2"]),
            "qsksc": qsksc, "lnpc": lnpc, "b1c": b1c, "b2c": b2c,
            "two2f": two2f, "two2fT": two2fT,
            "maskP": maskP_P, "maskF": (maskF_P if half == 0 else maskP_P),
            "wq1": np.ascontiguousarray(np.asarray(inputs["L1_attn_wq"]).T),
            "wk1": np.ascontiguousarray(np.asarray(inputs["L1_attn_wk"]).T),
            "wv1": np.ascontiguousarray(np.asarray(inputs["L1_attn_wv"]).T),
            "wo1": np.ascontiguousarray(np.asarray(inputs["L1_attn_wo"]).T),
            "b1vec": b1vec,
            "wi1": np.ascontiguousarray(np.asarray(inputs["L1_attn_wi1"]).T),
            "wi2": np.ascontiguousarray(np.asarray(inputs["L1_attn_wi2"])),
            "gate_w": np.asarray(inputs["L1_ffn_gate_w"]),
            "moe_w1": np.asarray(inputs["L1_ffn_w1"])[half * 4:half * 4 + 4],
            "moe_w2": np.asarray(inputs["L1_ffn_w2"])[half * 4:half * 4 + 4],
            "identb": identb,
            "rot": rot,
            "onesb": onesb, "bmapP": bmapP, "bc0f": bc0f, "bc64f": bc64f,
            "bvecT": np.ascontiguousarray(bvec.T),
            "b1vecT": np.ascontiguousarray(b1vec.T),
            "triu128": triu128, "iotasel": iotasel, "iotacol": iotacol,
            "two2f128": two2f128,
            "offs": offs, "gates": gates,
        }
        in_maps.append(m)
    return in_maps


def kernel(**inputs):
    in_maps = _prep_inputs(inputs)
    pb = build_program(dbg=DEBUG_DUMPS)
    res = run_bass_kernel_spmd(pb.nc, in_maps, core_ids=list(range(8)))
    out = np.zeros((B, N, D), dtype=np.float32)
    for core in range(8):
        b, half = core // 2, core % 2
        out[b, half * HALF:(half + 1) * HALF, :] = res.results[core]["out"]
    if DEBUG_DUMPS:
        kernel._dbg = res.results
    return out
